# version 07
# baseline (speedup 1.0000x reference)
import math
import os
from contextlib import ExitStack

import numpy as np
import ml_dtypes

import concourse.bass as bass
import concourse.mybir as mybir
from concourse.bass_utils import run_bass_kernel_spmd

F32 = mybir.dt.float32
BF16 = mybir.dt.bfloat16
I32 = mybir.dt.int32
AF = mybir.ActivationFunctionType
ALU = mybir.AluOpType
AX = mybir.AxisListType

NCORES = 8
D = 1024
DC = D // 128
H16 = 16
DK = 64
HID0 = 4096
NEXP = 8
EHID = 2048
NKV = 4

STOP_AFTER = 99
SEQ = [1, 2, 3, 4, 5, 6]
TAP = None
ATT0_PHASE = 9
FFN0_PHASE = 9

ALLSEMS = ["T", "V", "S", "G", "D0", "D1", "D2", "D3", "DG", "DG2", "CC"]


def ts(i, n=512):
    return slice(i * n, (i + 1) * n)


class Prog:
    def __init__(self, nc, S, block, es):
        self.nc, self.S, self.block, self.es = nc, S, block, es
        self.sems = {}
        for nm in ALLSEMS:
            self.sems[nm] = [es.enter_context(nc.semaphore(nm)), 0]
        self.emit = {"t": block.tensor, "v": block.vector, "s": block.scalar,
                     "g": block.gpsimd, "y": block.sync}
        self.osem = {"t": "T", "v": "V", "s": "S", "g": "G", "y": None}
        self.nsteps = 0

    def inc(self, name, instr, k=1):
        s = self.sems[name]
        instr.then_inc(s[0], k)
        s[1] += k
        return s[1]

    def val(self, name):
        return self.sems[name][1]

    def step(self, kind, fn, waits=None):
        if waits is None:
            wl = [(nm, self.sems[nm][1]) for nm in ALLSEMS
                  if self.sems[nm][1] > 0]
        else:
            wl = []
            for w in waits:
                if isinstance(w, tuple):
                    if w[1] > 0:
                        wl.append(w)
                elif self.sems[w][1] > 0:
                    wl.append((w, self.sems[w][1]))
        myname = self.osem[kind]
        self.nsteps += 1

        def _(e):
            for nm, v in wl:
                e.wait_ge(self.sems[nm][0], v)
            last = fn(e)
            if myname is not None and last is not None:
                self.inc(myname, last)
        self.emit[kind](_)

    def barrier_waits(self):
        return [(nm, self.sems[nm][1]) for nm in ALLSEMS]


def mm(eng, out, lhsT, rhs, start, stop, tp="auto"):
    if tp == "auto":
        bp_l = lhsT.base_partition()
        bp_o = out.base_partition()
        tp = (bp_l, bp_o) if (bp_l or bp_o) else None
    return eng.matmul(out, lhsT, rhs, start=start, stop=stop, tile_position=tp)


def build_program(S=2048, stop_after=None):
    stop = STOP_AFTER if stop_after is None else stop_after
    nc = bass.Bass("TRN2", target_bir_lowering=False, debug=False)
    es = ExitStack()
    NQ = S // 512

    dr = {}

    def din(name, shape, dt=F32):
        dr[name] = nc.dram_tensor(name, shape, dt, kind="ExternalInput")
        return dr[name]

    din("x_fm", [D, S])
    din("w1T", [D, HID0]); din("w2T", [HID0, D])
    din("convw", [128, 3 * HID0 // 128]); din("convb", [128, HID0 // 128])
    din("wqkvT", [D, D]); din("woT", [D, D]); din("qkv_b", [128, DC])
    din("u8", [128, 256]); din("cs4", [128, 4 * S])
    din("a1_qw", [D, D]); din("a1_kw", [D, D]); din("a1_vw", [D, D]); din("a1_ow", [D, D])
    din("g_qw", [D, D]); din("g_qb", [128, DC])
    din("g_kw", [D, NKV * DK]); din("g_kb", [128, 2])
    din("g_vw", [D, NKV * DK]); din("g_vb", [128, 2]); din("g_vbr", [1, NKV * DK])
    din("g_ow", [D, D]); din("g_ob", [128, DC])
    din("gate_w", [D, NEXP])
    din("moe_w1", [NEXP, D, EHID]); din("moe_w2", [NEXP, EHID, D])
    din("norm1_w", [128, DC]); din("norm2_w", [128, DC])
    din("ident_f", [128, 128]); din("ident_b", [128, 128], BF16)
    din("ones_b", [128, 128], BF16); din("ones_f", [128, 128])
    din("ut_b", [128, 128], BF16)
    din("bd64_b", [128, 128], BF16)
    din("bd32_b", [128, 128], BF16)
    din("eo_b", [128, 128], BF16)
    din("trsel_b", [128, 128], BF16)
    din("iotaE", [128, NEXP])
    din("cmask", [128, 2048], BF16)
    din("prefmask", [NEXP, 1])
    din("onecol_b", [128, 1], BF16)

    y_out = nc.dram_tensor("y_fm", [D, S], F32, kind="ExternalOutput")
    dbg_out = nc.dram_tensor("dbg", [128, 16384], F32, kind="ExternalOutput") if TAP else None
    x_mir = nc.dram_tensor("x_mirror", [D, S], F32, kind="Internal")
    w1q_d = nc.dram_tensor("w1q_stash", [D, HID0], BF16, kind="Internal")
    w2q_d = nc.dram_tensor("w2q_stash", [HID0, D], BF16, kind="Internal")
    wqkvq_d = nc.dram_tensor("wqkvq_stash", [D, D], BF16, kind="Internal")
    woq_d = nc.dram_tensor("woq_stash", [D, D], BF16, kind="Internal")
    cc_in_d = nc.dram_tensor("cc_in", [1, NEXP], F32, kind="Internal")
    cc_out_d = nc.dram_tensor("cc_out", [NCORES, NEXP], F32, kind="Internal")

    big = 16384
    B1 = es.enter_context(nc.sbuf_tensor("B1", [128, big], F32))
    C1 = es.enter_context(nc.sbuf_tensor("C1", [128, big], BF16))
    C2 = es.enter_context(nc.sbuf_tensor("C2", [128, big], BF16))
    W1 = es.enter_context(nc.sbuf_tensor("W1", [128, 16384], BF16))
    W2 = es.enter_context(nc.sbuf_tensor("W2", [128, 16384], BF16))
    SM = es.enter_context(nc.sbuf_tensor("SM", [128, max(S, 2048)], F32))
    SM2 = es.enter_context(nc.sbuf_tensor("SM2", [128, 960], F32))
    CNf = es.enter_context(nc.sbuf_tensor("CNf", [128, 512 + NEXP], F32))
    CNb = es.enter_context(nc.sbuf_tensor("CNb", [128, 897], BF16))
    PSA = es.enter_context(nc.psum_tensor("PSA", [128, 2048], F32))
    PSB = es.enter_context(nc.psum_tensor("PSB", [128, 2048], F32))

    block = es.enter_context(nc.Block())
    P = Prog(nc, S, block, es)

    def bank(i, n=1):
        assert 0 <= i and i + n <= 8
        if i < 4:
            assert i + n <= 4
            return PSA[:, i * 512:(i + n) * 512]
        assert i >= 4
        return PSB[:, (i - 4) * 512:(i - 4 + n) * 512]

    IDF = CNf[:, 0:128]
    ONESF = CNf[:, 128:256]
    IOTAE = CNf[:, 256:256 + NEXP]
    IDB = CNb[:, 0:128]
    ONESB = CNb[:, 128:256]
    UTB = CNb[:, 256:384]
    BD64 = CNb[:, 384:512]
    EOB4 = CNb[:, 512:640]
    ONECOL = CNb[:, 640:641]
    BD32 = CNb[:, 641:769]
    TRSEL = CNb[:, 769:897]
    U8 = CNf[:, 256 + NEXP:512 + NEXP]

    sm2p = {"n": 0}

    def sm2col(name, n=1):
        c = sm2p["n"]; sm2p["n"] = c + n
        assert sm2p["n"] <= 960, (name, sm2p["n"])
        sm2p[name] = SM2[:, c:c + n]
        return sm2p[name]

    def _(e):
        for dst, src in [(IDF, "ident_f"), (ONESF, "ones_f"), (IOTAE, "iotaE"),
                         (IDB, "ident_b"), (ONESB, "ones_b"), (UTB, "ut_b"),
                         (BD64, "bd64_b"), (BD32, "bd32_b"), (EOB4, "eo_b"),
                         (TRSEL, "trsel_b"),
                         (ONECOL, "onecol_b"), (U8, "u8")]:
            P.inc("D0", e.dma_start(out=dst, in_=dr[src].ap()), 16)
    P.step("y", _)

    def load_cols(src_ap, dst_cols):
        def _(e):
            P.inc("D0", e.dma_start(out=dst_cols, in_=src_ap), 16)
        P.step("y", _)

    def rows_to_cols(src_sbuf, n, dst_cols, scale=1.0, pbank=6):
        for g0 in range(0, n, 4):
            g = min(4, n - g0)

            def _t(e, g0=g0, g=g):
                last = None
                for j in range(g):
                    last = e.transpose(bank(pbank)[:, ts(j, 128)],
                                       src_sbuf[:, ts(g0 + j, 128)], IDF)
                return last
            P.step("t", _t)

            def _s(e, g0=g0, g=g):
                last = None
                for j in range(g):
                    last = e.activation(out=dst_cols[:, g0 + j:g0 + j + 1],
                                        in_=bank(pbank)[:, ts(j, 128)][:, 0:1],
                                        func=AF.Copy, scale=float(scale))
                return last
            P.step("s", _s)

    def bcast_tm_cols(tm_cols, nsub, scr128, out_psum, tbank=7):
        def _t1(e):
            last = None
            for j in range(nsub):
                last = e.transpose(bank(tbank)[0:1, j * 128:(j + 1) * 128],
                                   tm_cols[:, j:j + 1], IDF)
            return last
        P.step("t", _t1)
        P.step("v", lambda e: e.tensor_copy(out=scr128[0:1, 0:128 * nsub],
                                            in_=bank(tbank)[0:1, 0:128 * nsub]))

        def _t2(e):
            last = None
            for j in range(nsub):
                last = mm(e, out_psum[:, ts(j, 128)], ONESF[0:1, :],
                          scr128[0:1, j * 128:(j + 1) * 128], True, True)
            return last
        P.step("t", _t2)

    def stash_bitlinear(win, stash, rows, cols, tag):
        nch, npc = rows // 128, cols // 1024
        npan = nch * npc
        sl, scr, scr2 = B1[:, 0:1024], B1[:, 1024:2048], B1[:, 2048:3072]
        acc = sm2col(tag + "_abs", npan)
        tot = sm2col(tag + "_tot")
        inv = sm2col(tag + "_inv")
        scol = sm2col(tag + "_s")
        cs = sm2col(tag + "_cs", cols // 128)
        win_r = win.ap().rearrange("(a p) c -> p a c", p=128)
        st_r = stash.ap().rearrange("(a p) c -> p a c", p=128)

        for i in range(npan):
            a, j = divmod(i, npc)
            P.step("y", lambda e, a=a, j=j: P.inc(
                "D0", e.dma_start(out=sl, in_=win_r[:, a, ts(j, 1024)]), 16) and None)
            P.step("s", lambda e, i=i: e.activation(
                out=scr, in_=sl, func=AF.Abs, accum_out=acc[:, i:i + 1]))
        P.step("v", lambda e: e.tensor_reduce(out=tot, in_=acc, axis=AX.X, op=ALU.add))
        P.step("t", lambda e: mm(e, bank(7)[:, 0:1], ONESF, tot, True, True))
        P.step("v", lambda e: e.tensor_scalar(
            out=inv, in0=bank(7)[:, 0:1], scalar1=1.0 / (rows * cols),
            scalar2=1e-5, op0=ALU.mult, op1=ALU.max))
        P.step("v", lambda e: e.reciprocal(out=scol, in_=inv))

        qp = C1[:, 0:1024]
        for i in range(npan):
            a, j = divmod(i, npc)
            P.step("y", lambda e, a=a, j=j: P.inc(
                "D0", e.dma_start(out=sl, in_=win_r[:, a, ts(j, 1024)]), 16) and None)
            P.step("v", lambda e: e.tensor_scalar(
                out=scr, in0=sl, scalar1=scol, scalar2=1.0,
                op0=ALU.mult, op1=ALU.add))
            P.step("v", lambda e: e.tensor_copy(
                out=B1[:, 3072:4096].bitcast(I32), in_=scr))
            P.step("v", lambda e: e.tensor_copy(
                out=scr2, in_=B1[:, 3072:4096].bitcast(I32)))
            P.step("v", lambda e: e.tensor_scalar(
                out=scr2, in0=scr2, scalar1=1.0, scalar2=1.0,
                op0=ALU.subtract, op1=ALU.min))
            P.step("v", lambda e: e.tensor_scalar(
                out=qp, in0=scr2, scalar1=-1.0, scalar2=None, op0=ALU.max))
            P.step("y", lambda e, a=a, j=j: P.inc(
                "D0", e.dma_start(out=st_r[:, a, ts(j, 1024)], in_=qp), 16) and None)

        wp = W1[:, 0:1024]
        csb = B1[:, 0:1024]
        for j in range(npc):
            for a in range(nch):
                P.step("y", lambda e, a=a, j=j: P.inc(
                    "D0", e.dma_start(out=wp, in_=st_r[:, a, ts(j, 1024)]), 16) and None)
                P.step("t", lambda e, a=a: [
                    mm(e, bank(4), ONESB, wp[:, ts(0)], a == 0, a == nch - 1),
                    mm(e, bank(5), ONESB, wp[:, ts(1)], a == 0, a == nch - 1)][-1])
            P.step("v", lambda e: e.tensor_copy(out=csb[:, 0:512], in_=bank(4)))
            P.step("v", lambda e: e.tensor_copy(out=csb[:, 512:1024], in_=bank(5)))
            rows_to_cols(csb, 8, cs[:, j * 8:(j + 1) * 8], 127.0)
        return dict(inv=inv, s=scol, cs=cs)

    def stash_ternary(win, stash, tag, qoff=7.0):
        rows = cols = D
        nch, npc = rows // 128, cols // 1024
        sl, scr = B1[:, 0:1024], B1[:, 1024:2048]
        acc = sm2col(tag + "_abs", nch * npc)
        tot = sm2col(tag + "_tot")
        thr = sm2col(tag + "_thr")
        alpha = sm2col(tag + "_alpha", cols // 128)
        cs = sm2col(tag + "_cs", cols // 128)
        win_r = win.ap().rearrange("(a p) c -> p a c", p=128)
        st_r = stash.ap().rearrange("(a p) c -> p a c", p=128)
        absb = B1[:, 2048:3072]

        for j in range(npc):
            for a in range(nch):
                P.step("y", lambda e, a=a, j=j: P.inc(
                    "D0", e.dma_start(out=sl, in_=win_r[:, a, ts(j, 1024)]), 16) and None)
                P.step("s", lambda e, i=a * npc + j: e.activation(
                    out=scr, in_=sl, func=AF.Abs, accum_out=acc[:, i:i + 1]))
                P.step("v", lambda e: e.tensor_copy(out=C1[:, 0:1024], in_=scr))
                P.step("t", lambda e, a=a: [
                    mm(e, bank(4), ONESB, C1[:, 0:512], a == 0, a == nch - 1),
                    mm(e, bank(5), ONESB, C1[:, 512:1024], a == 0, a == nch - 1)][-1])
            P.step("v", lambda e: e.tensor_copy(out=absb[:, 0:512], in_=bank(4)))
            P.step("v", lambda e: e.tensor_copy(out=absb[:, 512:1024], in_=bank(5)))
            rows_to_cols(absb, 8, alpha[:, j * 8:(j + 1) * 8], 1.0 / rows)
        P.step("v", lambda e: e.tensor_reduce(out=tot, in_=acc, axis=AX.X, op=ALU.add))
        P.step("t", lambda e: mm(e, bank(7)[:, 0:1], ONESF, tot, True, True))
        P.step("v", lambda e: e.tensor_scalar(
            out=thr, in0=bank(7)[:, 0:1], scalar1=0.05 / (rows * cols),
            scalar2=None, op0=ALU.mult))

        qp = C1[:, 0:1024]
        sgn = B1[:, 2048:3072]
        msk = B1[:, 3072:4096]
        for i in range(nch * npc):
            a, j = divmod(i, npc)
            P.step("y", lambda e, a=a, j=j: P.inc(
                "D0", e.dma_start(out=sl, in_=win_r[:, a, ts(j, 1024)]), 16) and None)
            P.step("s", lambda e: e.sign(out=sgn, in_=sl))
            P.step("s", lambda e: e.activation(out=scr, in_=sl, func=AF.Abs))
            P.step("v", lambda e: e.tensor_scalar(
                out=msk, in0=scr, scalar1=thr, scalar2=None, op0=ALU.is_ge))
            P.step("v", lambda e: e.tensor_tensor(out=qp, in0=sgn, in1=msk, op=ALU.mult))
            P.step("y", lambda e, a=a, j=j: P.inc(
                "D0", e.dma_start(out=st_r[:, a, ts(j, 1024)], in_=qp), 16) and None)

        wp = W1[:, 0:1024]
        csb = B1[:, 0:1024]
        for j in range(npc):
            for a in range(nch):
                P.step("y", lambda e, a=a, j=j: P.inc(
                    "D0", e.dma_start(out=wp, in_=st_r[:, a, ts(j, 1024)]), 16) and None)
                P.step("t", lambda e, a=a: [
                    mm(e, bank(4), ONESB, wp[:, ts(0)], a == 0, a == nch - 1),
                    mm(e, bank(5), ONESB, wp[:, ts(1)], a == 0, a == nch - 1)][-1])
            P.step("v", lambda e: e.tensor_copy(out=csb[:, 0:512], in_=bank(4)))
            P.step("v", lambda e: e.tensor_copy(out=csb[:, 512:1024], in_=bank(5)))
            rows_to_cols(csb, 8, cs[:, j * 8:(j + 1) * 8], float(qoff))
        return dict(alpha=alpha, cs=cs)

    def stage_ln(x_src, mode, save_stats=False):
        Qv = {"q8": 127.0, "q4": 7.0}.get(mode)
        floor_ = {"q8": 1e-5, "q4": 1e-6}.get(mode)
        x_r = x_src.ap().rearrange("(c p) t -> p c t", p=128)
        C13 = C1.rearrange("p (c t) -> p c t", c=DC)
        for q in range(NQ):
            xq = B1[:, 0:4096]
            xq3 = xq.rearrange("p (c t) -> p c t", c=DC)
            lnq = B1[:, 4096:8192]
            lnq3 = lnq.rearrange("p (c t) -> p c t", c=DC)
            rsc = B1[:, 8192:10240]
            slotA = B1[:, 12288:12800]
            slotB = B1[:, 12800:13312]
            slotC = B1[:, 13312:13824]
            slotM = B1[:, 13824:14336]
            tmcol = B1[:, 14336:14464]
            tmcol2 = B1[:, 14464:14592]
            scr128 = B1[:, 14592:15104]
            xb = C2[:, 0:4096]
            xb3 = xb.rearrange("p (c t) -> p c t", c=DC)
            sq = C2[:, 4096:8192]
            sq3 = sq.rearrange("p (c t) -> p c t", c=DC)

            P.step("y", lambda e, q=q: P.inc(
                "D0", e.dma_start(out=xq3, in_=x_r[:, :, ts(q)]), 16) and None)
            P.step("v", lambda e: e.tensor_copy(out=xb, in_=xq))
            P.step("v", lambda e: e.tensor_tensor(out=sq, in0=xb, in1=xb, op=ALU.mult))

            def _mm(e):
                last = None
                for c in range(DC):
                    mm(e, bank(0), ONESB, xb3[:, c, :], c == 0, c == DC - 1)
                    last = mm(e, bank(1), ONESB, sq3[:, c, :], c == 0, c == DC - 1)
                return last
            P.step("t", _mm)
            P.step("v", lambda e: [
                e.tensor_scalar(out=slotA, in0=bank(0), scalar1=1.0 / D,
                                scalar2=None, op0=ALU.mult),
                e.tensor_scalar(out=slotB, in0=bank(1), scalar1=1.0 / D,
                                scalar2=None, op0=ALU.mult)][-1])
            P.step("v", lambda e: e.tensor_tensor(out=slotC, in0=slotA,
                                                  in1=slotA, op=ALU.mult))
            P.step("v", lambda e: e.scalar_tensor_tensor(
                out=slotB, in0=slotB, scalar=1e-5, in1=slotC,
                op0=ALU.add, op1=ALU.subtract))
            P.step("v", lambda e: e.reciprocal(out=slotB, in_=slotB))
            P.step("s", lambda e: e.activation(out=slotC, in_=slotB, func=AF.Sqrt))
            if save_stats:
                rows_to_cols(slotA, 4, sm2p["lnmean_tm"][:, q * 4:q * 4 + 4])
                rows_to_cols(slotC, 4, sm2p["lnrstd_tm"][:, q * 4:q * 4 + 4])

            for c in range(DC):
                P.step("v", lambda e, c=c: e.tensor_tensor(
                    out=lnq3[:, c, :], in0=xq3[:, c, :], in1=slotA, op=ALU.subtract))
                P.step("v", lambda e, c=c: e.tensor_tensor(
                    out=lnq3[:, c, :], in0=lnq3[:, c, :], in1=slotC, op=ALU.mult))
                if mode == "bf":
                    P.step("v", lambda e, c=c, q=q: e.tensor_copy(
                        out=C13[:, c, ts(q)], in_=lnq3[:, c, :]))
                else:
                    slotAB = B1[:, 10240:10752]
                    P.step("s", lambda e, c=c: e.activation(
                        out=slotAB, in_=lnq3[:, c, :], func=AF.Abs))
                    if c == 0:
                        P.step("v", lambda e: e.tensor_copy(out=slotM, in_=slotAB))
                    else:
                        P.step("v", lambda e: e.tensor_tensor(
                            out=slotM, in0=slotM, in1=slotAB, op=ALU.max))
            if mode == "bf":
                continue

            def _t(e):
                last = None
                for j in range(4):
                    last = e.transpose(bank(2)[:, ts(j, 128)],
                                       slotM[:, ts(j, 128)], IDF)
                return last
            P.step("t", _t)
            P.step("v", lambda e: [
                e.tensor_reduce(out=tmcol[:, j:j + 1],
                                in_=bank(2)[:, ts(j, 128)], axis=AX.X, op=ALU.max)
                for j in range(4)][-1])
            P.step("v", lambda e: e.tensor_single_scalar(
                out=tmcol[:, 0:4], in_=tmcol[:, 0:4], scalar=floor_, op=ALU.max))
            P.step("v", lambda e: e.tensor_scalar(
                out=tmcol2[:, 0:4], in0=tmcol[:, 0:4], scalar1=1.0 / Qv,
                scalar2=None, op0=ALU.mult))
            P.step("v", lambda e: e.reciprocal(out=tmcol[:, 0:4], in_=tmcol[:, 0:4]))
            P.step("v", lambda e: e.tensor_scalar(
                out=tmcol[:, 0:4], in0=tmcol[:, 0:4], scalar1=Qv,
                scalar2=None, op0=ALU.mult))
            bcast_tm_cols(tmcol[:, 0:4], 4, scr128, bank(3))
            bcast_tm_cols(tmcol2[:, 0:4], 4, scr128, bank(2))
            P.step("v", lambda e, q=q: e.tensor_copy(out=SM[:, ts(q)], in_=bank(2)))

            for hh in range(4):
                def _r1(e, hh=hh):
                    last = None
                    for c in range(2):
                        last = e.tensor_tensor(
                            out=rsc[:, ts(c)], in0=lnq3[:, hh * 2 + c, :],
                            in1=bank(3), op=ALU.mult)
                    return last
                P.step("v", _r1)
                P.step("v", lambda e: e.tensor_scalar(
                    out=rsc[:, 0:1024], in0=rsc[:, 0:1024], scalar1=Qv,
                    scalar2=None, op0=ALU.add))
                P.step("v", lambda e: e.tensor_copy(
                    out=B1[:, 15104:16128].bitcast(I32), in_=rsc[:, 0:1024]))

                def _w(e, hh=hh, q=q):
                    last = None
                    for c in range(2):
                        last = e.tensor_copy(
                            out=C13[:, hh * 2 + c, ts(q)],
                            in_=B1[:, 15104 + 512 * c:15104 + 512 * (c + 1)].bitcast(I32))
                    return last
                P.step("v", _w)

    def cq3_pa(mc):
        return B1.rearrange("p (m t) -> p m t", m=8)[:, mc, 0:S]

    def stage_ffn0(x_src, x_dst, w1m, w2m):
        r13 = C1.rearrange("p (c t) -> p c t", c=DC)
        w1q_r = w1q_d.ap().rearrange("(a p) c -> p a c", p=128)
        w2f_r = w2q_d.ap().rearrange("(a p) c -> p a c", p=128)
        inv1 = SM[:, 0:S]
        cw0, cw1, cw2 = sm2p["cw0"], sm2p["cw1"], sm2p["cw2"]
        cbc = sm2p["cb"]
        nb = NQ - 1
        hedge = sm2p["hedge"]
        amax2_tm = sm2p["amax2_tm"]
        W2f = W2.bitcast(F32)
        scrA = [W2f[:, 0:S], W2f[:, 2048:2048 + S]]
        GtmpA = W2f[:, 4096:4096 + S]
        AMX = W2f[:, 6144:6144 + S]
        W1UPf = W1.bitcast(F32)
        bw = P.barrier_waits()
        first = {k: True for k in "tvsgy"}

        def stw(kind, fn, waits):
            if first[kind]:
                first[kind] = False
                waits = bw + list(waits)
            P.step(kind, fn, waits=waits)

        if "c127" not in sm2p:
            sm2col("c127")
        c127 = sm2p["c127"]

        def _vinit(e):
            e.memset(c127, 127.0)
            return e.tensor_scalar(
                out=inv1, in0=inv1, scalar1=w1m["inv"], scalar2=None,
                op0=ALU.mult)
        stw("v", _vinit, waits=[])
        v_inv1 = P.val("V")
        stw("g", lambda e: e.memset(AMX, 0.0), waits=[])
        g_amx0 = P.val("G")

        d1_wb = {}
        tm_mm = {}
        vm_evac_ps = {0: 0, 1: 0}
        sm_abs = {}
        gm_conv2 = {}
        vm_max = 0
        for qh in range(4):
            half = qh % 2
            wb3 = W1[:, half * 8192:half * 8192 + 8192].rearrange(
                "p (k m) -> p k m", k=DC)
            lw = [("T", tm_mm[(qh - 2, 7)])] if qh >= 2 else []
            lw.append(("D1", P.val("D1")))
            stw("y", lambda e, wb3=wb3, qh=qh: P.inc("D1", e.dma_start(
                out=wb3, in_=w1q_r[:, :, ts(qh, 1024)]), 16) and None, waits=lw)
            d1_wb[qh] = P.val("D1")
            for mc in range(8):
                pid = mc % 2
                ps = (PSA if pid == 0 else PSB)[:, 0:S]
                wcol = w1m["cs"][:, qh * 8 + mc:qh * 8 + mc + 1]
                hcol = qh * 8 + mc
                src = cq3_pa(mc)

                def _m(e, wb3=wb3, mc=mc, ps=ps, qh=qh):
                    last = None
                    for nq_ in range(NQ):
                        for kc in range(DC):
                            last = mm(e, ps[:, ts(nq_)],
                                      wb3[:, kc, ts(mc, 128)],
                                      r13[:, kc, ts(nq_)], kc == 0, kc == DC - 1)
                    return last
                stw("t", _m, waits=[("D1", d1_wb[qh]),
                                    ("V", vm_evac_ps[pid])])
                tm_mm[(qh, mc)] = P.val("T")
                ew = [("T", tm_mm[(qh, mc)]), ("V", v_inv1), ("V", vm_max)]
                if (qh - 1, mc) in sm_abs:
                    ew.append(("S", sm_abs[(qh - 1, mc)]))
                    ew.append(("G", gm_conv2[(qh - 1, mc)]))
                stw("v", lambda e, src=src, ps=ps, wcol=wcol:
                    e.scalar_tensor_tensor(
                        out=src, in0=ps, scalar=wcol,
                        in1=inv1, op0=ALU.subtract, op1=ALU.mult), waits=ew)
                vm_evac_ps[pid] = P.val("V")
                vm_evac_cur = P.val("V")
                scr = scrA[mc % 2]
                stw("s", lambda e, src=src, scr=scr, hcol=hcol: e.activation(
                    out=scr, in_=src, func=AF.Identity,
                    scale=cw1[:, hcol:hcol + 1], bias=cbc[:, hcol:hcol + 1]),
                    waits=[("V", vm_evac_cur)])
                sm_c1 = P.val("S")

                def _g2a(e, src=src, hcol=hcol):
                    last = e.tensor_scalar(
                        out=GtmpA[:, 0:S - 1], in0=src[:, 0:S - 1],
                        scalar1=cw0[:, hcol:hcol + 1], scalar2=None, op0=ALU.mult)
                    if nb:
                        srcb = src.rearrange("p (b u) -> p b u", u=512)
                        hv = hedge[:, hcol * 2 * nb:(hcol + 1) * 2 * nb]
                        hv2 = hv.rearrange("p (b u) -> p b u", u=2)
                        e.tensor_copy(out=hv2[:, :, 0:1], in_=srcb[:, 0:nb, 511:512])
                        last = e.tensor_copy(out=hv2[:, :, 1:2],
                                             in_=srcb[:, 1:nb + 1, 0:1])
                    return last
                stw("g", _g2a, waits=[("S", sm_c1)])
                g_mid = P.val("G")
                stw("g", lambda e, scr=scr: e.tensor_tensor(
                    out=scr[:, 1:S], in0=scr[:, 1:S],
                    in1=GtmpA[:, 0:S - 1], op=ALU.add), waits=[("G", g_mid)])
                gm_conv2[(qh, mc)] = P.val("G")
                stw("v", lambda e, src=src, scr=scr, hcol=hcol:
                    e.scalar_tensor_tensor(
                        out=scr[:, 0:S - 1], in0=src[:, 1:S],
                        scalar=cw2[:, hcol:hcol + 1], in1=scr[:, 0:S - 1],
                        op0=ALU.mult, op1=ALU.add),
                    waits=[("G", gm_conv2[(qh, mc)])])
                vm_c3 = P.val("V")

                stw("s", lambda e, src=src, scr=scr: e.activation(
                    out=src, in_=scr, func=AF.Silu), waits=[("V", vm_c3)])
                s_mid = P.val("S")
                stw("s", lambda e, src=src: e.activation(
                    out=src, in_=src, func=AF.Abs), waits=[("S", s_mid)])
                sm_abs[(qh, mc)] = P.val("S")
                stw("v", lambda e, src=src: e.tensor_tensor(
                    out=AMX, in0=AMX, in1=src, op=ALU.max),
                    waits=[("S", sm_abs[(qh, mc)]), ("G", g_amx0),
                           ("V", vm_max)])
                vm_max = P.val("V")

        if FFN0_PHASE <= 1:
            return
        nt = S // 128
        sc2_tm = sm2p["sc2_tm"]
        inv2_tm = sm2p["inv2_tm"]
        for j in range(0, nt, 4):
            g = min(4, nt - j)

            def _t(e, j=j, g=g):
                last = None
                for k in range(g):
                    last = e.transpose(bank(k)[:, 0:128], AMX[:, ts(j + k, 128)], IDF)
                return last
            P.step("t", _t, waits=[("V", vm_max), ("S", P.val("S")),
                                   ("T", P.val("T"))])
            tm_tr = P.val("T")

            def _v(e, j=j, g=g):
                return [e.tensor_reduce(
                    out=amax2_tm[:, j + k:j + k + 1], in_=bank(k)[:, 0:128],
                    axis=AX.X, op=ALU.max) for k in range(g)][-1]
            P.step("v", _v, waits=[("T", tm_tr)])
        P.step("v", lambda e: e.tensor_single_scalar(
            out=amax2_tm[:, 0:nt], in_=amax2_tm[:, 0:nt], scalar=1e-5, op=ALU.max))
        P.step("v", lambda e: e.tensor_scalar(
            out=inv2_tm[:, 0:nt], in0=amax2_tm[:, 0:nt],
            scalar1=w2m["inv"], scalar2=0.5 / 127.0, op0=ALU.mult, op1=ALU.mult))
        P.step("v", lambda e: e.reciprocal(out=sc2_tm[:, 0:nt], in_=amax2_tm[:, 0:nt]))
        P.step("v", lambda e: e.tensor_scalar(
            out=sc2_tm[:, 0:nt], in0=sc2_tm[:, 0:nt], scalar1=127.0,
            scalar2=None, op0=ALU.mult))
        scr128b = B1[:, 14336:14848]
        for tc in range(NQ):
            for j, tmc in enumerate([sc2_tm, inv2_tm]):
                dst = W1UPf[:, 4096 + tc * 1024 + j * 512:
                            4096 + tc * 1024 + j * 512 + 512]
                bcast_tm_cols(tmc[:, tc * 4:tc * 4 + 4], 4, scr128b, bank(4))
                P.step("v", lambda e, dst=dst: e.tensor_copy(
                    out=dst, in_=bank(4)))

        if FFN0_PHASE <= 2:
            return
        cch3 = B1[:, 0:4096].rearrange("p (m t) -> p m t", m=8)
        cscr = [B1[:, 4096:4608], B1[:, 4608:5120]]
        rscr = [B1[:, 5120:5632], B1[:, 5632:6144]]
        fscr = [B1[:, 6144:6656], B1[:, 6656:7168]]
        iscr = [B1[:, 15360:15872].bitcast(I32), B1[:, 15872:16384].bitcast(I32)]
        xq = B1[:, 8192:12288]
        xq3 = xq.rearrange("p (c t) -> p c t", c=DC)
        x_r = x_src.ap().rearrange("(c p) t -> p c t", p=128)
        xd_r = x_dst.ap().rearrange("(c p) t -> p c t", p=128)
        r2f = C2.rearrange("p (k t) -> p k t", k=32)
        wb3 = W1[:, 0:8192].rearrange("p (k m) -> p k m", k=DC)
        w2h = [W2[:, 0:8192].rearrange("p (k m) -> p k m", k=8),
               W2[:, 8192:16384].rearrange("p (k m) -> p k m", k=8)]

        paT, paV, paS, paG = (P.val(n) for n in "TVSG")
        d3_last = P.val("D3")
        d1_last = P.val("D1")
        tm_mm1 = paT
        tm_mm2 = {}
        vm_bank_free = {b: paV for b in range(8)}
        sm_silu = {m: paS for m in range(8)}
        sm_c1d = {m: paS for m in range(8)}
        gm_cvt = {m: paG for m in range(8)}
        gm_cvt_par = {0: paG, 1: paG}
        gm_fadd = {f: paG for f in range(8)}
        gm_fadd_par = {0: paG, 1: paG}
        d2_w2 = {}
        for tc in range(NQ):
            sc2sb = W1UPf[:, 4096 + tc * 1024:4096 + tc * 1024 + 512]
            inv2sb = W1UPf[:, 4096 + tc * 1024 + 512:4096 + tc * 1024 + 1024]
            xw = [("D3", d3_last)]
            if tc == 0:
                xw += [("V", paV), ("S", paS), ("G", paG)]
            stw("y", lambda e, tc=tc: P.inc(
                "D3", e.dma_start(out=xq3, in_=x_r[:, :, ts(tc)]), 16) and None,
                waits=xw)
            d3_xq = P.val("D3")
            for pp in range(2):
                pw = [("T", tm_mm2[(tc - 1, pp + 2)])] if (tc - 1, pp + 2) in tm_mm2 else []
                pw.append(("D2", P.val("D2")))
                if tc == 0:
                    pw += [("V", paV), ("S", paS), ("T", paT)]
                stw("g", lambda e, pp=pp: P.inc("D2", e.dma_start(
                    out=w2h[pp], in_=w2f_r[:, pp * 8:pp * 8 + 8, :]), 16) and None,
                    waits=pw)
                d2_w2[(tc, pp)] = P.val("D2")

            for qh in range(4):
                stw("y", lambda e, qh=qh: P.inc("D1", e.dma_start(
                    out=wb3, in_=w1q_r[:, :, ts(qh, 1024)]), 16) and None,
                    waits=[("T", tm_mm1), ("D1", d1_last)])
                d1_cur = P.val("D1")
                d1_last = d1_cur
                for mc in range(8):
                    bk = mc % 3
                    wcol = w1m["cs"][:, qh * 8 + mc:qh * 8 + mc + 1]
                    hcol = qh * 8 + mc
                    src = cch3[:, mc, :]
                    scr = cscr[mc % 2]
                    rs = rscr[mc % 2]
                    isc = iscr[mc % 2]

                    def _m(e, mc=mc, bk=bk, tc=tc):
                        last = None
                        for kc in range(DC):
                            last = mm(e, bank(bk), wb3[:, kc, ts(mc, 128)],
                                      r13[:, kc, ts(tc)], kc == 0, kc == DC - 1)
                        return last
                    mw1 = [("D1", d1_cur), ("V", vm_bank_free[bk])]
                    if tc == 0 and qh == 0 and mc == 0:
                        mw1.append(("T", P.val("T")))
                    stw("t", _m, waits=mw1)
                    tm_mm1 = P.val("T")
                    stw("v", lambda e, src=src, bk=bk, wcol=wcol, tc=tc:
                        e.scalar_tensor_tensor(
                            out=src, in0=bank(bk), scalar=wcol,
                            in1=inv1[:, ts(tc)], op0=ALU.subtract, op1=ALU.mult),
                        waits=[("T", tm_mm1), ("S", sm_silu[mc]), ("V", v_inv1)])
                    vm_bank_free[bk] = P.val("V")
                    vm_evac = P.val("V")
                    stw("s", lambda e, src=src, scr=scr, hcol=hcol: e.activation(
                        out=scr, in_=src, func=AF.Identity,
                        scale=cw1[:, hcol:hcol + 1], bias=cbc[:, hcol:hcol + 1]),
                        waits=[("V", vm_evac)])
                    sm_c1d[mc] = P.val("S")

                    stw("v", lambda e, src=src, scr=scr, hcol=hcol:
                        e.scalar_tensor_tensor(
                            out=scr[:, 1:512], in0=src[:, 0:511],
                            scalar=cw0[:, hcol:hcol + 1], in1=scr[:, 1:512],
                            op0=ALU.mult, op1=ALU.add),
                        waits=[("S", sm_c1d[mc])])
                    v_mid = P.val("V")

                    def _c3(e, src=src, scr=scr, hcol=hcol, tc=tc):
                        last = e.scalar_tensor_tensor(
                            out=scr[:, 0:511], in0=src[:, 1:512],
                            scalar=cw2[:, hcol:hcol + 1], in1=scr[:, 0:511],
                            op0=ALU.mult, op1=ALU.add)
                        if tc < NQ - 1:
                            last = e.scalar_tensor_tensor(
                                out=scr[:, 511:512],
                                in0=hedge[:, hcol * 2 * nb + 2 * tc + 1:
                                          hcol * 2 * nb + 2 * tc + 2],
                                scalar=cw2[:, hcol:hcol + 1], in1=scr[:, 511:512],
                                op0=ALU.mult, op1=ALU.add)
                        if tc > 0:
                            last = e.scalar_tensor_tensor(
                                out=scr[:, 0:1],
                                in0=hedge[:, hcol * 2 * nb + 2 * (tc - 1):
                                          hcol * 2 * nb + 2 * (tc - 1) + 1],
                                scalar=cw0[:, hcol:hcol + 1], in1=scr[:, 0:1],
                                op0=ALU.mult, op1=ALU.add)
                        return last
                    stw("v", _c3, waits=[("V", v_mid)])
                    vm_c23 = P.val("V")
                    stw("s", lambda e, src=src, scr=scr: e.activation(
                        out=src, in_=scr, func=AF.Silu), waits=[("V", vm_c23)])
                    sm_silu[mc] = P.val("S")
                    stw("v", lambda e, src=src, rs=rs, sc2sb=sc2sb:
                        e.tensor_tensor(out=rs, in0=src, in1=sc2sb, op=ALU.mult),
                        waits=[("S", sm_silu[mc]), ("G", gm_cvt_par[mc % 2])])
                    vm_rm = P.val("V")
                    stw("s", lambda e, rs=rs: e.activation(
                        out=rs, in_=rs, func=AF.Identity, bias=c127),
                        waits=[("V", vm_rm)])
                    sm_127 = P.val("S")
                    stw("g", lambda e, rs=rs, isc=isc: e.tensor_copy(
                        out=isc, in_=rs), waits=[("S", sm_127)])
                    gm_cvt_par[mc % 2] = P.val("G")
                    g_cv = P.val("G")
                    grw = [("G", g_cv)]
                    if (tc - 1, qh) in tm_mm2:
                        grw.append(("T", tm_mm2[(tc - 1, qh)]))
                    stw("g", lambda e, isc=isc, kk=qh * 8 + mc: e.tensor_copy(
                        out=r2f[:, kk, :], in_=isc), waits=grw)
                    gm_cvt[mc] = P.val("G")

            if FFN0_PHASE <= 3:
                continue
            gm_r2 = {qh: gm_cvt[7] for qh in range(4)}
            for p in range(4):
                if p >= 2:
                    stw("g", lambda e, p=p: P.inc("D2", e.dma_start(
                        out=w2h[p % 2], in_=w2f_r[:, p * 8:p * 8 + 8, :]), 16)
                        and None, waits=[("T", tm_mm2[(tc, p - 2)]),
                                         ("D2", P.val("D2"))])
                    d2_w2[(tc, p)] = P.val("D2")

                def _m2(e, tc=tc, p=p):
                    last = None
                    for fm in range(8):
                        for kc in range(8):
                            k = p * 8 + kc
                            last = mm(e, bank(fm), w2h[p % 2][:, kc, ts(fm, 128)],
                                      r2f[:, k, :], k == 0, k == 31)
                    return last
                mw = [("D2", d2_w2[(tc, p)]), ("G", gm_cvt[7])]
                if p == 0:
                    mw.append(("V", P.val("V")))
                    mw += [("V", vm_bank_free[b]) for b in range(3, 8)]
                stw("t", _m2, waits=mw)
                tm_mm2[(tc, p)] = P.val("T")
            for fc in range(DC):
                fs = fscr[fc % 2]
                stw("v", lambda e, fc=fc, fs=fs, inv2sb=inv2sb:
                    e.scalar_tensor_tensor(
                        out=fs, in0=bank(fc), scalar=w2m["cs"][:, fc:fc + 1],
                        in1=inv2sb, op0=ALU.subtract, op1=ALU.mult),
                    waits=[("T", tm_mm2[(tc, 3)]), ("G", gm_fadd_par[fc % 2])])
                vm_f = P.val("V")
                stw("g", lambda e, fc=fc, fs=fs: e.tensor_tensor(
                    out=xq3[:, fc, :], in0=xq3[:, fc, :], in1=fs, op=ALU.add),
                    waits=[("V", vm_f), ("D3", d3_xq)])
                gm_fadd[fc] = P.val("G")
                gm_fadd_par[fc % 2] = P.val("G")
                vm_bank_free[fc] = vm_f
            stw("y", lambda e, tc=tc: P.inc(
                "D3", e.dma_start(out=xd_r[:, :, ts(tc)], in_=xq3), 16) and None,
                waits=[("G", gm_fadd[7]), ("D3", d3_xq)])
            d3_last = P.val("D3")

    def tr(e, out, in_, ident, tp="auto"):
        if tp == "auto":
            bp_i = in_.base_partition()
            bp_o = out.base_partition()
            tp = (bp_i, bp_o) if (bp_i or bp_o) else None
        return e.transpose(out, in_, ident, tile_position=tp)

    def tap(name, src_ap):
        if TAP != name:
            return
        n = src_ap.shape[-1] if src_ap.ndim == 2 else None
        def _(e):
            P.inc("DG", e.dma_start(out=dbg_out.ap()[:, 0:n], in_=src_ap), 16)
        P.step("g", _)

    def stage_att0(x_src, x_dst, qm, om):
        NT = S // 128
        TB = S // 16
        BPT = 128 // TB
        r13 = C1.rearrange("p (c t) -> p c t", c=DC)
        wq_r = wqkvq_d.ap().rearrange("(a p) c -> p a c", p=128)
        wo_r = woq_d.ap().rearrange("(a p) c -> p a c", p=128)
        wbuf = W1[:, 0:8192]
        wbuf3 = wbuf.rearrange("p (k m) -> p k m", k=DC)
        projT = C2.rearrange("p (c t) -> p c t", c=DC)
        inv4 = SM[:, 0:S]
        scrA = B1[:, 8192:8704]

        P.step("y", lambda e: P.inc("D0", e.dma_start(out=wbuf3, in_=wq_r), 16) and None)
        for fc in range(DC):
            for nq in range(NQ):
                bk = (fc * NQ + nq) % 3

                def _m(e, fc=fc, nq=nq, bk=bk):
                    last = None
                    for kc in range(DC):
                        last = mm(e, bank(bk), wbuf3[:, kc, ts(fc, 128)],
                                  r13[:, kc, ts(nq)], kc == 0, kc == DC - 1)
                    return last
                P.step("t", _m)
                P.step("v", lambda e, fc=fc, nq=nq, bk=bk: e.scalar_tensor_tensor(
                    out=scrA, in0=bank(bk), scalar=qm["cs"][:, fc:fc + 1],
                    in1=inv4[:, ts(nq)], op0=ALU.subtract, op1=ALU.mult))
                P.step("s", lambda e, fc=fc, nq=nq: e.activation(
                    out=projT[:, fc, ts(nq)], in_=scrA, func=AF.Identity,
                    scale=qm["alpha"][:, fc:fc + 1],
                    bias=sm2p["qkvb"][:, fc:fc + 1]))

        tap("proj", C2[:, 0:16384])
        def att0_epilogue(e):
            for nq in range(NQ):
                P.inc("D0", e.dma_start(out=x_dst.ap()[:, ts(nq)],
                                        in_=x_src.ap()[:, ts(nq)]), 16)
        if ATT0_PHASE < 2:
            P.step("y", att0_epilogue)
            return
        P.step("y", lambda e: P.inc("D0", e.dma_start(
            out=B1[:, 0:4 * S], in_=dr["cs4"].ap()), 16) and None)
        COSQ, SINQ = B1[:, 0:S], B1[:, S:2 * S]
        COSK, SINK = B1[:, 2 * S:3 * S], B1[:, 3 * S:4 * S]
        rotT, rotB = B1[:, 4 * S:5 * S], B1[:, 5 * S:6 * S]
        x1b, x2b = B1[:, 6 * S:7 * S], B1[:, 7 * S:8 * S]
        qlT = C1[:, 0:4 * S].rearrange("p (g t) -> p g t", g=4)
        klT = W2[:, 0:4 * S].rearrange("p (g t) -> p g t", g=4)

        for side in ("q", "k"):
            COS = COSQ if side == "q" else COSK
            SIN = SINQ if side == "q" else SINK
            dstT = qlT if side == "q" else klT
            for g in range(4):
                def _eo(e, g=g, side=side):
                    last = None
                    for j in range(4):
                        h = 4 * g + j
                        par = h % 2
                        lE = EOB4[:, 64 * par:64 * par + 32]
                        lO = EOB4[:, 64 * par + 32:64 * par + 64]
                        if side == "q":
                            for nq in range(NQ):
                                rhs = projT[:, h // 2, ts(nq)]
                                mm(e, PSA[32 * j:32 * j + 32, ts(nq)], lE, rhs,
                                   True, True, tp=(0, 32 * j))
                                last = mm(e, PSB[32 * j:32 * j + 32, ts(nq)], lO,
                                          rhs, True, True, tp=(0, 32 * j))
                        else:
                            for cb in range(16):
                                par2 = cb % 2
                                lE2 = EOB4[:, 64 * par2:64 * par2 + 32]
                                lO2 = EOB4[:, 64 * par2 + 32:64 * par2 + 64]
                                rhs = projT[:, cb // 2, h * TB:(h + 1) * TB]
                                mm(e, PSA[32 * j:32 * j + 32, cb * TB:(cb + 1) * TB],
                                   lE2, rhs, True, True, tp=(0, 32 * j))
                                last = mm(e, PSB[32 * j:32 * j + 32, cb * TB:(cb + 1) * TB],
                                          lO2, rhs, True, True, tp=(0, 32 * j))
                    return last
                P.step("t", _eo)
                if ATT0_PHASE == 21:
                    continue
                P.step("v", lambda e: e.tensor_tensor(out=x1b, in0=PSA[:, 0:S], in1=COS, op=ALU.mult))
                P.step("v", lambda e: e.tensor_tensor(out=x2b, in0=PSB[:, 0:S], in1=SIN, op=ALU.mult))
                P.step("v", lambda e: e.tensor_tensor(out=rotT, in0=x1b, in1=x2b, op=ALU.subtract))
                P.step("v", lambda e: e.tensor_tensor(out=x1b, in0=PSA[:, 0:S], in1=SIN, op=ALU.mult))
                P.step("v", lambda e: e.tensor_tensor(out=x2b, in0=PSB[:, 0:S], in1=COS, op=ALU.mult))
                P.step("v", lambda e: e.tensor_tensor(out=rotB, in0=x1b, in1=x2b, op=ALU.add))
                if ATT0_PHASE == 22:
                    continue

                def _u(e, g=g):
                    last = None
                    for j in range(4):
                        for nq in range(NQ):
                            mm(e, PSA[32 * j:32 * j + 32, ts(nq)],
                               U8[:, 64 * j:64 * j + 32],
                               rotT[:, ts(nq)], True, False, tp=(0, 32 * j))
                            last = mm(e, PSA[32 * j:32 * j + 32, ts(nq)],
                                      U8[:, 64 * j + 32:64 * j + 64],
                                      rotB[:, ts(nq)], False, True, tp=(0, 32 * j))
                    return last
                P.step("t", _u)
                if ATT0_PHASE == 23:
                    continue
                P.step("v", lambda e, g=g, dstT=dstT: e.tensor_copy(
                    out=dstT[:, g, :], in_=PSA[:, 0:S]))

        if ATT0_PHASE < 3:
            P.step("y", att0_epilogue)
            return
        tap("qlT", C1[:, 0:4 * S])
        tap("klT", W2[:, 0:4 * S])
        shcol = sm2p["att0_sh"]
        t1c = sm2p["att0_t1"]
        t2c = sm2p["att0_t2"]
        sqb = W2[:, 4 * S + 256:5 * S + 256]
        for nm, src_ in (("q", qlT), ("k", klT)):
            dst = t1c if nm == "q" else t2c
            first = [True]
            for g in range(4):
                P.step("v", lambda e, g=g, src_=src_: e.tensor_tensor(
                    out=sqb, in0=src_[:, g, :], in1=src_[:, g, :], op=ALU.mult))
                for nq in range(NQ):
                    P.step("t", lambda e, nq=nq: mm(
                        e, bank(7), BD32, sqb[:, ts(nq)], True, True))
                    P.step("v", lambda e: e.tensor_reduce(
                        out=sm2p["att0_tmp"], in_=bank(7), axis=AX.X, op=ALU.max))
                    if first[0]:
                        P.step("v", lambda e, dst=dst: e.tensor_copy(
                            out=dst, in_=sm2p["att0_tmp"]))
                        first[0] = False
                    else:
                        P.step("v", lambda e, dst=dst: e.tensor_tensor(
                            out=dst, in0=dst, in1=sm2p["att0_tmp"], op=ALU.max))
        P.step("v", lambda e: e.tensor_tensor(out=t1c, in0=t1c, in1=t2c, op=ALU.mult))
        P.step("t", lambda e: tr(e, bank(7)[0:1, 0:128], t1c, IDF))
        P.step("v", lambda e: e.tensor_reduce(
            out=sm2p["att0_tmp"][0:1, :], in_=bank(7)[0:1, 0:128], axis=AX.X, op=ALU.max))
        P.step("s", lambda e: e.activation(out=sm2p["att0_tmp"][0:1, :],
                                           in_=sm2p["att0_tmp"][0:1, :], func=AF.Sqrt))
        P.step("v", lambda e: e.tensor_scalar(
            out=sm2p["att0_tmp"][0:1, :], in0=sm2p["att0_tmp"][0:1, :],
            scalar1=-0.125, scalar2=None, op0=ALU.mult))
        P.step("t", lambda e: mm(e, bank(7)[:, 0:1], ONESF[0:1, :],
                                 sm2p["att0_tmp"][0:1, :], True, True))
        P.step("v", lambda e: e.tensor_copy(out=shcol, in_=bank(7)[:, 0:1]))

        if ATT0_PHASE < 4:
            P.step("y", att0_epilogue)
            return
        ytm = B1[:, 0:NT * 1024].rearrange("p (i f) -> p i f", i=NT)
        vkm = W2[:, 4 * S:4 * S + 65]
        Pt = W2[:, 4 * S + 256:4 * S + 256 + S]
        P.step("v", lambda e: e.memset(vkm[:, 64:65], 1.0))
        for h in range(16):
            for cb16 in range(NT):
                def _tv(e, h=h, cb16=cb16):
                    last = None
                    for sub in range(BPT):
                        cb = cb16 * BPT + sub
                        src_ = projT[:, cb // 2, h * TB:(h + 1) * TB]
                        last = mm(e, bank(4)[sub * TB:sub * TB + TB, 0:64], src_,
                                  TRSEL[:, 64 * (cb % 2):64 * (cb % 2) + 64],
                                  True, True, tp=(0, (sub * TB) % 128))
                    return last
                P.step("t", _tv)
                P.step("v", lambda e: e.tensor_copy(
                    out=vkm[:, 0:64], in_=bank(4)[:, 0:64]))

                def _st(e, h=h, cb16=cb16):
                    last = None
                    j, g = h % 4, h // 4
                    for qn in range(NQ):
                        last = mm(e, bank(4 + qn),
                                  klT[:, g, cb16 * 128:cb16 * 128 + 128][32 * j:32 * j + 32, :],
                                  qlT[:, g, ts(qn)][32 * j:32 * j + 32, :], True, True,
                                  tp=(32 * j, 0))
                    return last
                P.step("t", _st)
                P.step("s", lambda e: e.activation(
                    out=Pt, in_=PSB[:, 0:S], func=AF.Exp, bias=shcol, scale=0.125))

                def _y(e, h=h, cb16=cb16):
                    last = None
                    for qc in range(NQ):
                        last = mm(e, bank(qc)[0:65, :], vkm,
                                  Pt[:, ts(qc)], cb16 == 0, cb16 == NT - 1)
                    return last
                P.step("t", _y)
            for qc in range(NQ):
                P.step("v", lambda e, qc=qc: e.tensor_copy(
                    out=SM[0:65, 0:512], in_=bank(qc)[0:65, :]))

                def _tt(e):
                    last = None
                    for u in range(4):
                        last = tr(e, bank(4)[:, 65 * u:65 * u + 65],
                                  SM[0:65, 128 * u:128 * u + 128], IDF[0:65, 0:65])
                    return last
                P.step("t", _tt)
                P.step("v", lambda e: [
                    e.reciprocal(out=sm2p["att0_rc"][:, u:u + 1],
                                 in_=bank(4)[:, 65 * u + 64:65 * u + 65])
                    for u in range(4)][-1])
                P.step("v", lambda e, h=h, qc=qc: [
                    e.tensor_scalar(
                        out=ytm[:, qc * 4 + u, h * 64:(h + 1) * 64],
                        in0=bank(4)[:, 65 * u:65 * u + 64],
                        scalar1=sm2p["att0_rc"][:, u:u + 1], scalar2=None,
                        op0=ALU.mult)
                    for u in range(4)][-1])

        tap("ytm", B1[:, 0:16384])
        if ATT0_PHASE < 5:
            P.step("y", att0_epilogue)
            return
        inv4y = sm2p["att0_inv4y"]
        for i in range(NT):
            P.step("v", lambda e, i=i: e.tensor_reduce(
                out=sm2p["att0_tmp"], in_=ytm[:, i, :], axis=AX.X, op=ALU.max,
                apply_absolute_value=True))
            P.step("v", lambda e: e.tensor_single_scalar(
                out=sm2p["att0_tmp"], in_=sm2p["att0_tmp"], scalar=1e-6, op=ALU.max))
            P.step("v", lambda e, i=i: e.tensor_scalar(
                out=inv4y[:, i:i + 1], in0=sm2p["att0_tmp"], scalar1=1.0 / 7.0,
                scalar2=None, op0=ALU.mult))
            P.step("v", lambda e: e.reciprocal(out=sm2p["att0_tmp"], in_=sm2p["att0_tmp"]))
            P.step("v", lambda e, i=i: e.tensor_scalar(
                out=SM[:, 0:1024], in0=ytm[:, i, :], scalar1=sm2p["att0_tmp"],
                scalar2=7.0, op0=ALU.mult, op1=ALU.mult))
            P.step("v", lambda e: e.tensor_scalar(
                out=SM[:, 0:1024], in0=SM[:, 0:1024], scalar1=7.0, scalar2=None,
                op0=ALU.add))
            P.step("v", lambda e: e.tensor_copy(
                out=SM[:, 1024:2048].bitcast(I32), in_=SM[:, 0:1024]))
            P.step("v", lambda e, i=i: e.tensor_copy(
                out=C1[:, i * 1024:(i + 1) * 1024], in_=SM[:, 1024:2048].bitcast(I32)))
        for i in range(NT):
            def _tq(e, i=i):
                last = None
                for fc in range(DC):
                    last = tr(e, bank(fc % 4)[:, 0:128],
                              C1[:, i * 1024 + fc * 128:i * 1024 + (fc + 1) * 128], IDB)
                    if fc % 4 == 3:
                        pass
                return last
            def _tq1(e, i=i):
                last = None
                for fc in range(4):
                    last = tr(e, bank(fc).bitcast(BF16)[:, 0:128],
                              C1[:, i * 1024 + fc * 128:i * 1024 + (fc + 1) * 128], IDB)
                return last
            P.step("t", _tq1)
            P.step("v", lambda e, i=i: [
                e.tensor_copy(out=projT[:, fc, i * 128:(i + 1) * 128],
                              in_=bank(fc).bitcast(BF16)[:, 0:128]) for fc in range(4)][-1])

            def _tq2(e, i=i):
                last = None
                for fc in range(4, DC):
                    last = tr(e, bank(fc - 4).bitcast(BF16)[:, 0:128],
                              C1[:, i * 1024 + fc * 128:i * 1024 + (fc + 1) * 128], IDB)
                return last
            P.step("t", _tq2)
            P.step("v", lambda e, i=i: [
                e.tensor_copy(out=projT[:, fc, i * 128:(i + 1) * 128],
                              in_=bank(fc - 4).bitcast(BF16)[:, 0:128]) for fc in range(4, DC)][-1])

        tap("r4y", C1[:, 0:16384])
        tap("r4yfm", C2[:, 0:16384])
        if TAP == "sm2":
            def _tp(e):
                P.inc("DG", e.dma_start(out=dbg_out.ap()[:, 0:8], in_=om["alpha"]), 16)
                P.inc("DG", e.dma_start(out=dbg_out.ap()[:, 8:16], in_=om["cs"]), 16)
                P.inc("DG", e.dma_start(out=dbg_out.ap()[:, 16:16 + NT], in_=inv4y[:, 0:NT]), 16)
            P.step("g", _tp)
        P.step("y", lambda e: P.inc("D0", e.dma_start(out=wbuf3, in_=wo_r), 16) and None)
        xq = B1[:, 8192:12288]
        xq3 = xq.rearrange("p (c t) -> p c t", c=DC)
        x_r = x_src.ap().rearrange("(c p) t -> p c t", p=128)
        xd_r = x_dst.ap().rearrange("(c p) t -> p c t", p=128)
        inv4ysb = B1[:, 12288:12800]
        scr128c = B1[:, 12800:13312]
        scrA2 = B1[:, 13312:13824]
        for nq in range(NQ):
            P.step("y", lambda e, nq=nq: P.inc(
                "D0", e.dma_start(out=xq3, in_=x_r[:, :, ts(nq)]), 16) and None)
            bcast_tm_cols(inv4y[:, nq * 4:nq * 4 + 4], 4, scr128c, bank(4))
            P.step("v", lambda e: e.tensor_copy(out=inv4ysb, in_=bank(4)))
            for fc in range(DC):
                bk = fc % 3

                def _m(e, fc=fc, nq=nq, bk=bk):
                    last = None
                    for kc in range(DC):
                        last = mm(e, bank(bk), wbuf3[:, kc, ts(fc, 128)],
                                  projT[:, kc, ts(nq)], kc == 0, kc == DC - 1)
                    return last
                P.step("t", _m)
                P.step("v", lambda e, fc=fc, bk=bk: e.scalar_tensor_tensor(
                    out=scrA2, in0=bank(bk), scalar=om["cs"][:, fc:fc + 1],
                    in1=inv4ysb, op0=ALU.subtract, op1=ALU.mult))
                P.step("v", lambda e, fc=fc: e.scalar_tensor_tensor(
                    out=xq3[:, fc, :], in0=scrA2, scalar=om["alpha"][:, fc:fc + 1],
                    in1=xq3[:, fc, :], op0=ALU.mult, op1=ALU.add))
            P.step("y", lambda e, nq=nq: P.inc(
                "D0", e.dma_start(out=xd_r[:, :, ts(nq)], in_=xq3), 16) and None)

    def attn_core(qT3, kT3, vtm, heads, scale, causal, Pt, shiftcol, ytm3,
                  sqb, NT, cmask=None):
        t1c = sm2p["ac_t1"]
        t2c = sm2p["ac_t2"]
        tmpc = sm2p["ac_tmp"]
        nkc = kT3.shape[1]
        for nm, src_, nch in (("q", qT3, 8), ("k", kT3, nkc)):
            dst = t1c if nm == "q" else t2c
            first = True
            for cch in range(nch):
                P.step("v", lambda e, cch=cch, src_=src_: e.tensor_tensor(
                    out=sqb[:, 0:S], in0=src_[:, cch, 0:S], in1=src_[:, cch, 0:S],
                    op=ALU.mult))
                for nq in range(NQ):
                    P.step("t", lambda e, nq=nq: mm(
                        e, bank(7), BD64, sqb[:, ts(nq)], True, True))
                    P.step("v", lambda e: e.tensor_reduce(
                        out=tmpc, in_=bank(7), axis=AX.X, op=ALU.max))
                    if first:
                        P.step("v", lambda e, dst=dst: e.tensor_copy(out=dst, in_=tmpc))
                        first = False
                    else:
                        P.step("v", lambda e, dst=dst: e.tensor_tensor(
                            out=dst, in0=dst, in1=tmpc, op=ALU.max))
        P.step("v", lambda e: e.tensor_tensor(out=t1c, in0=t1c, in1=t2c, op=ALU.mult))
        P.step("t", lambda e: tr(e, bank(7)[0:1, 0:128], t1c, IDF))
        P.step("v", lambda e: e.tensor_reduce(
            out=tmpc[0:1, :], in_=bank(7)[0:1, 0:128], axis=AX.X, op=ALU.max))
        P.step("s", lambda e: e.activation(out=tmpc[0:1, :], in_=tmpc[0:1, :],
                                           func=AF.Sqrt))
        P.step("v", lambda e, scale=scale: e.tensor_scalar(
            out=tmpc[0:1, :], in0=tmpc[0:1, :], scalar1=-scale, scalar2=None,
            op0=ALU.mult))
        P.step("t", lambda e: mm(e, bank(7)[:, 0:1], ONESF[0:1, :],
                                 tmpc[0:1, :], True, True))
        P.step("v", lambda e: e.tensor_copy(out=shiftcol, in_=bank(7)[:, 0:1]))

        ew = P.barrier_waits()
        first_on = {"t": True, "s": True, "v": True}

        def stw(kind, fn, w):
            if first_on.get(kind):
                w = list(w) + ew
                first_on[kind] = False
            P.step(kind, fn, waits=w)

        vm_end = P.val("V")
        for (s, kchunk, kbase, vcol) in heads:
            sm_exp = {}
            tm_st = {}
            tm_y = {}
            vm_mask = {}
            prev_mt = None
            for mt in range(NT):
                qn0 = max(0, -(-(128 * mt - 511) // 512)) if causal else 0
                if causal and qn0 >= NQ:
                    continue
                pt = Pt[:, (mt % 2) * S:(mt % 2) * S + S]

                def _st(e, s=s, kchunk=kchunk, kbase=kbase, mt=mt, qn0=qn0):
                    last = None
                    ksl = kT3[:, kchunk, mt * 128:mt * 128 + 128][kbase:kbase + 64, :]
                    for qn in range(qn0, NQ):
                        last = mm(e, bank(4 + qn),
                                  ksl, qT3[:, s // 2, ts(qn)][64 * (s % 2):64 * (s % 2) + 64, :],
                                  True, True, tp=(kbase, 0))
                    return last
                w = [("V", vm_end)]
                if prev_mt is not None:
                    w.append(("S", sm_exp[prev_mt]))
                stw("t", _st, w)
                tm_st[mt] = P.val("T")

                def _exp(e, mt=mt, qn0=qn0, pt=pt):
                    return e.activation(
                        out=pt[:, qn0 * 512:S], in_=PSB[:, qn0 * 512:S],
                        func=AF.Exp, bias=shiftcol, scale=scale)
                w = [("T", tm_st[mt])]
                pp = prev_mt
                stw("s", _exp, w)
                sm_exp[mt] = P.val("S")
                if causal:
                    dq = mt // 4
                    stw("v", lambda e, mt=mt, dq=dq, pt=pt: e.tensor_tensor(
                        out=pt[:, ts(dq)], in0=pt[:, ts(dq)],
                        in1=cmask[:, ts(mt % 4)], op=ALU.mult),
                        [("S", sm_exp[mt])])
                    vm_mask[mt] = P.val("V")

                def _y(e, s=s, vcol=vcol, mt=mt, qn0=qn0, causal=causal, pt=pt):
                    last = None
                    for qc in range(qn0, NQ):
                        stop_mt = min(NT - 1, 4 * qc + 3) if causal else NT - 1
                        last = mm(e, bank(qc)[0:65, :], vtm(mt)[:, vcol:vcol + 65],
                                  pt[:, ts(qc)], mt == 0, mt == stop_mt)
                    return last
                w = [("S", sm_exp[mt])]
                if causal:
                    w.append(("V", vm_mask[mt]))
                stw("t", _y, w)
                tm_y[mt] = P.val("T")
                prev_mt = mt
            for qc in range(NQ):
                P.step("v", lambda e, qc=qc: e.tensor_copy(
                    out=SM[0:65, 0:512], in_=bank(qc)[0:65, :]),
                    waits=[("T", tm_y[prev_mt]), ("V", P.val("V"))])

                def _tt(e):
                    last = None
                    for u in range(4):
                        last = tr(e, bank(4)[:, 65 * u:65 * u + 65],
                                  SM[0:65, 128 * u:128 * u + 128], IDF[0:65, 0:65])
                    return last
                P.step("t", _tt, waits=[("V", P.val("V"))])
                P.step("v", lambda e: [
                    e.reciprocal(out=sm2p["ac_rc"][:, u:u + 1],
                                 in_=bank(4)[:, 65 * u + 64:65 * u + 65])
                    for u in range(4)][-1], waits=[("T", P.val("T"))])
                P.step("v", lambda e, s=s, qc=qc: [
                    e.tensor_scalar(
                        out=ytm3[:, qc * 4 + u, s * 64:(s + 1) * 64],
                        in0=bank(4)[:, 65 * u:65 * u + 64],
                        scalar1=sm2p["ac_rc"][:, u:u + 1], scalar2=None,
                        op0=ALU.mult)
                    for u in range(4)][-1], waits=[("V", P.val("V"))])
            vm_end = P.val("V")

    def y_to_fm(ytm3, NT):
        projT = C2.rearrange("p (c t) -> p c t", c=DC)
        for i in range(NT):
            P.step("v", lambda e, i=i: e.tensor_copy(
                out=C1[:, i * 1024:(i + 1) * 1024], in_=ytm3[:, i, :]))
        for i in range(NT):
            def _t1(e, i=i):
                last = None
                for fc in range(4):
                    last = tr(e, bank(fc).bitcast(BF16)[:, 0:128],
                              C1[:, i * 1024 + fc * 128:i * 1024 + (fc + 1) * 128], IDB)
                return last
            P.step("t", _t1)
            P.step("v", lambda e, i=i: [
                e.tensor_copy(out=projT[:, fc, i * 128:(i + 1) * 128],
                              in_=bank(fc).bitcast(BF16)[:, 0:128])
                for fc in range(4)][-1])

            def _t2(e, i=i):
                last = None
                for fc in range(4, DC):
                    last = tr(e, bank(fc - 4).bitcast(BF16)[:, 0:128],
                              C1[:, i * 1024 + fc * 128:i * 1024 + (fc + 1) * 128], IDB)
                return last
            P.step("t", _t2)
            P.step("v", lambda e, i=i: [
                e.tensor_copy(out=projT[:, fc, i * 128:(i + 1) * 128],
                              in_=bank(fc - 4).bitcast(BF16)[:, 0:128])
                for fc in range(4, DC)][-1])

    def proj_fm(rhs3, wbuf3, nmc, evac, kacc=DC):
        for fc in range(nmc):
            for nq in range(NQ):
                bk = (fc * NQ + nq) % 3

                def _m(e, fc=fc, nq=nq, bk=bk):
                    last = None
                    for kc in range(kacc):
                        last = mm(e, bank(bk), wbuf3[:, kc, ts(fc, 128)],
                                  rhs3[:, kc, ts(nq)], kc == 0, kc == kacc - 1)
                    return last
                P.step("t", _m)
                evac(fc, nq, bk)

    def stage_att1(x_src, x_dst):
        NT = S // 128
        ln3 = C1.rearrange("p (c t) -> p c t", c=DC)
        qT3 = C2.rearrange("p (c t) -> p c t", c=DC)
        kT3 = W2[:, 0:8 * S].rearrange("p (c t) -> p c t", c=DC)
        wb = W1[:, 0:8192]
        wb3 = wb.rearrange("p (k m) -> p k m", k=DC)

        Pt = W1[:, 8192:8192 + 2 * S]
        sqb = W1[:, 8192 + S:8192 + 2 * S]
        CMASK = W1[:, 8192 + 2 * S:8192 + 2 * S + 2048]
        vtm15 = W1[:, 8192 + 2 * S + 2048:8192 + 2 * S + 3088]
        P.step("g", lambda e: P.inc("DG", e.dma_start(
            out=CMASK, in_=dr["cmask"].ap()), 16) and None)

        P.step("g", lambda e: P.inc("DG", e.dma_start(
            out=wb3, in_=dr["a1_qw"].ap().rearrange("(a p) c -> p a c", p=128)), 16) and None)

        def ev_q(fc, nq, bk):
            P.step("v", lambda e, fc=fc, nq=nq, bk=bk: e.tensor_copy(
                out=qT3[:, fc, ts(nq)], in_=bank(bk)))
        proj_fm(ln3, wb3, DC, ev_q)
        P.step("g", lambda e: P.inc("DG", e.dma_start(
            out=wb3, in_=dr["a1_kw"].ap().rearrange("(a p) c -> p a c", p=128)), 16) and None)

        def ev_k(fc, nq, bk):
            P.step("v", lambda e, fc=fc, nq=nq, bk=bk: e.tensor_copy(
                out=kT3[:, fc, ts(nq)], in_=bank(bk)))
        proj_fm(ln3, wb3, DC, ev_k)
        vtmb = B1.bitcast(BF16)

        def vtm(i):
            if i == 15:
                return vtm15
            return vtmb[:, i * 1040:i * 1040 + 1040]
        P.step("g", lambda e: P.inc("DG", e.dma_start(
            out=wb3, in_=dr["a1_vw"].ap().rearrange("(a p) c -> p a c", p=128)), 16) and None)
        for i in range(NT):
            P.step("v", lambda e, i=i: e.memset(
                vtm(i).rearrange("p (h u) -> p h u", h=16)[:, :, 64:65], 1.0))
            for hv in range(16):
                def _mv(e, i=i, hv=hv):
                    last = None
                    for kc in range(DC):
                        last = mm(e, bank(3)[:, 0:64],
                                  ln3[:, kc, i * 128:(i + 1) * 128],
                                  wb3[:, kc, hv * 64:(hv + 1) * 64],
                                  kc == 0, kc == DC - 1)
                    return last
                P.step("t", _mv)
                P.step("v", lambda e, i=i, hv=hv: e.tensor_copy(
                    out=vtm(i)[:, hv * 65:hv * 65 + 64], in_=bank(3)[:, 0:64]))

        heads = [(h, h // 2, 64 * (h % 2), 65 * h) for h in range(16)]
        ytmb = vtmb[:, 16384:32768].rearrange("p (i f) -> p i f", i=16)
        attn_core(qT3, kT3, vtm, heads, 1.0 / 32.0, True, Pt,
                  sm2p["a1_sh"], ytmb, sqb, NT, cmask=CMASK)

        for i in range(NT):
            P.step("v", lambda e, i=i: e.tensor_copy(
                out=C1[:, i * 1024:(i + 1) * 1024], in_=ytmb[:, i, :]))
        projT = C2.rearrange("p (c t) -> p c t", c=DC)
        for i in range(NT):
            def _t1(e, i=i):
                last = None
                for fc in range(4):
                    last = tr(e, bank(fc).bitcast(BF16)[:, 0:128],
                              C1[:, i * 1024 + fc * 128:i * 1024 + (fc + 1) * 128], IDB)
                return last
            P.step("t", _t1)
            P.step("v", lambda e, i=i: [
                e.tensor_copy(out=projT[:, fc, i * 128:(i + 1) * 128],
                              in_=bank(fc).bitcast(BF16)[:, 0:128])
                for fc in range(4)][-1])

            def _t2(e, i=i):
                last = None
                for fc in range(4, DC):
                    last = tr(e, bank(fc - 4).bitcast(BF16)[:, 0:128],
                              C1[:, i * 1024 + fc * 128:i * 1024 + (fc + 1) * 128], IDB)
                return last
            P.step("t", _t2)
            P.step("v", lambda e, i=i: [
                e.tensor_copy(out=projT[:, fc, i * 128:(i + 1) * 128],
                              in_=bank(fc - 4).bitcast(BF16)[:, 0:128])
                for fc in range(4, DC)][-1])

        P.step("g", lambda e: P.inc("DG", e.dma_start(
            out=wb3, in_=dr["a1_ow"].ap().rearrange("(a p) c -> p a c", p=128)), 16) and None)
        xq = B1[:, 8192:12288]
        x_r = x_src.ap().rearrange("(c p) t -> p c t", p=128)
        xd_r = x_dst.ap().rearrange("(c p) t -> p c t", p=128)
        xq3 = xq.rearrange("p (c t) -> p c t", c=DC)
        for nq in range(NQ):
            P.step("y", lambda e, nq=nq: P.inc(
                "D0", e.dma_start(out=xq3, in_=x_r[:, :, ts(nq)]), 16) and None)
            for fc in range(DC):
                bk = fc % 3

                def _m(e, fc=fc, nq=nq, bk=bk):
                    last = None
                    for kc in range(DC):
                        last = mm(e, bank(bk), wb3[:, kc, ts(fc, 128)],
                                  projT[:, kc, ts(nq)], kc == 0, kc == DC - 1)
                    return last
                P.step("t", _m)
                P.step("v", lambda e, fc=fc, bk=bk: e.tensor_tensor(
                    out=xq3[:, fc, :], in0=xq3[:, fc, :], in1=bank(bk), op=ALU.add))
            P.step("y", lambda e, nq=nq: P.inc(
                "D0", e.dma_start(out=xd_r[:, :, ts(nq)], in_=xq3), 16) and None)

    def stage_mix1(x_src, x_dst):
        NT = S // 128
        CAP = float(int(1.25 * NCORES * S * 2 / NEXP))
        ln3 = C1.rearrange("p (c t) -> p c t", c=DC)
        qT3 = C2.rearrange("p (c t) -> p c t", c=DC)
        kT3 = W2[:, 0:2 * S].rearrange("p (c t) -> p c t", c=2)
        wb = W1[:, 0:8192]
        wb3 = wb.rearrange("p (k m) -> p k m", k=DC)
        Pt = W1[:, 8192:8192 + 2 * S]
        sqb = W1[:, 8192 + S:8192 + 2 * S]
        vbb = W1[:, 8192 + 2 * S:8192 + 2 * S + 256]
        gwsb = W1[:, 8192 + 2 * S + 256:8192 + 2 * S + 320]
        x_r = x_src.ap().rearrange("(c p) t -> p c t", p=128)
        xd_r = x_dst.ap().rearrange("(c p) t -> p c t", p=128)

        P.step("g", lambda e: P.inc("DG", e.dma_start(
            out=wb3, in_=dr["g_qw"].ap().rearrange("(a p) c -> p a c", p=128)), 16) and None)

        def ev_q(fc, nq, bk):
            P.step("s", lambda e, fc=fc, nq=nq, bk=bk: e.activation(
                out=qT3[:, fc, ts(nq)], in_=bank(bk), func=AF.Identity,
                bias=sm2p["g_qb"][:, fc:fc + 1]))
        proj_fm(ln3, wb3, DC, ev_q)
        P.step("g", lambda e: P.inc("DG", e.dma_start(
            out=wb3[:, :, 0:256],
            in_=dr["g_kw"].ap().rearrange("(a p) c -> p a c", p=128)), 16) and None)

        def ev_k(fc, nq, bk):
            P.step("s", lambda e, fc=fc, nq=nq, bk=bk: e.activation(
                out=kT3[:, fc, ts(nq)], in_=bank(bk), func=AF.Identity,
                bias=sm2p["g_kb"][:, fc:fc + 1]))
        for fc in range(2):
            for nq in range(NQ):
                bk = (fc * NQ + nq) % 3

                def _m(e, fc=fc, nq=nq, bk=bk):
                    last = None
                    for kc in range(DC):
                        last = mm(e, bank(bk), wb3[:, kc, fc * 128:fc * 128 + 128],
                                  ln3[:, kc, ts(nq)], kc == 0, kc == DC - 1)
                    return last
                P.step("t", _m)
                ev_k(fc, nq, bk)
        P.step("y", lambda e: P.inc("D0", e.dma_start(
            out=SM[0:1, 0:256], in_=dr["g_vbr"].ap()), 16) and None)
        P.step("t", lambda e: mm(e, bank(3)[:, 0:256], ONESF[0:1, :],
                                 SM[0:1, 0:256], True, True))
        P.step("v", lambda e: e.tensor_copy(out=vbb, in_=bank(3)[:, 0:256]))
        P.step("g", lambda e: P.inc("DG", e.dma_start(
            out=wb3[:, :, 0:256],
            in_=dr["g_vw"].ap().rearrange("(a p) c -> p a c", p=128)), 16) and None)
        vtmr = W2[:, 2 * S:2 * S + NT * 260]

        def vtm(i):
            return vtmr[:, i * 260:i * 260 + 260]
        for i in range(NT):
            P.step("v", lambda e, i=i: e.memset(
                vtm(i).rearrange("p (h u) -> p h u", h=4)[:, :, 64:65], 1.0))
            for kv in range(NKV):
                def _mv(e, i=i, kv=kv):
                    last = None
                    for kc in range(DC):
                        last = mm(e, bank(3)[:, 0:64],
                                  ln3[:, kc, i * 128:(i + 1) * 128],
                                  wb3[:, kc, kv * 64:kv * 64 + 64],
                                  kc == 0, kc == DC - 1)
                    return last
                P.step("t", _mv)
                P.step("v", lambda e, i=i, kv=kv: e.tensor_tensor(
                    out=vtm(i)[:, kv * 65:kv * 65 + 64], in0=bank(3)[:, 0:64],
                    in1=vbb[:, kv * 64:kv * 64 + 64], op=ALU.add))

        def sig(h):
            return 8 * (h // 8) + 2 * (h % 4) + (h // 4) % 2
        heads = [(sig(h), (h // 4) // 2, 64 * ((h // 4) % 2), 65 * (h // 4))
                 for h in range(16)]
        ytmf = B1[:, 0:NT * 1024].rearrange("p (i f) -> p i f", i=NT)
        attn_core(qT3, kT3, vtm, heads, 1.0 / 8.0, False, Pt,
                  sm2p["gq_sh"], ytmf, sqb, NT)
        y_to_fm(ytmf, NT)
        projT = C2.rearrange("p (c t) -> p c t", c=DC)

        P.step("g", lambda e: P.inc("DG", e.dma_start(
            out=wb3, in_=dr["g_ow"].ap().rearrange("(a p) c -> p a c", p=128)), 16) and None)
        s13 = B1.rearrange("p (c t) -> p c t", c=DC)
        xch = SM[:, 0:512]
        scrS = SM[:, 512:1024]
        meanb = SM[:, 1024:1536]
        rstdb = SM[:, 1536:2048]
        scr128m = C2[:, 0:1024].bitcast(F32) if False else None
        for nq in range(NQ):
            bcast_tm_cols(sm2p["lnmean_tm"][:, nq * 4:nq * 4 + 4], 4,
                          B1[:, 0:512] if False else SM[:, 512:1024], bank(4))
            P.step("v", lambda e: e.tensor_copy(out=meanb, in_=bank(4)))
            bcast_tm_cols(sm2p["lnrstd_tm"][:, nq * 4:nq * 4 + 4], 4,
                          SM[:, 512:1024], bank(4))
            P.step("v", lambda e: e.tensor_copy(out=rstdb, in_=bank(4)))
            for fc in range(DC):
                bk = fc % 3

                def _m(e, fc=fc, nq=nq, bk=bk):
                    last = None
                    for kc in range(DC):
                        last = mm(e, bank(bk), wb3[:, kc, ts(fc, 128)],
                                  projT[:, kc, ts(nq)], kc == 0, kc == DC - 1)
                    return last
                P.step("t", _m)
                P.step("y", lambda e, fc=fc, nq=nq: P.inc(
                    "D0", e.dma_start(out=xch, in_=x_r[:, fc, ts(nq)]), 16) and None)
                P.step("s", lambda e, fc=fc, bk=bk: e.activation(
                    out=scrS, in_=bank(bk), func=AF.Identity,
                    bias=sm2p["g_ob"][:, fc:fc + 1]))
                P.step("v", lambda e: e.tensor_tensor(
                    out=xch, in0=xch, in1=meanb, op=ALU.subtract))
                P.step("v", lambda e: e.tensor_tensor(
                    out=xch, in0=xch, in1=rstdb, op=ALU.mult))
                P.step("v", lambda e, fc=fc, nq=nq: e.tensor_tensor(
                    out=s13[:, fc, ts(nq)], in0=xch, in1=scrS, op=ALU.add))

        for nq in range(NQ):
            for fc in range(DC):
                P.step("v", lambda e, fc=fc, nq=nq: e.tensor_copy(
                    out=ln3[:, fc, ts(nq)], in_=s13[:, fc, ts(nq)]))
            for fc in range(DC):
                P.step("v", lambda e, fc=fc, nq=nq: e.tensor_tensor(
                    out=C2[:, fc * 512:(fc + 1) * 512], in0=ln3[:, fc, ts(nq)],
                    in1=ln3[:, fc, ts(nq)], op=ALU.mult))

            def _mmv(e, nq=nq):
                last = None
                for fc in range(DC):
                    last = mm(e, bank(0), ONESB, C2[:, fc * 512:(fc + 1) * 512],
                              fc == 0, fc == DC - 1)
                return last
            P.step("t", _mmv)
            P.step("v", lambda e: e.tensor_scalar(
                out=rstdb, in0=bank(0), scalar1=1.0 / D, scalar2=1e-6,
                op0=ALU.mult, op1=ALU.add))
            P.step("v", lambda e: e.reciprocal(out=rstdb, in_=rstdb))
            P.step("s", lambda e: e.activation(out=rstdb, in_=rstdb, func=AF.Sqrt))
            for fc in range(DC):
                P.step("v", lambda e, fc=fc, nq=nq: e.scalar_tensor_tensor(
                    out=s13[:, fc, ts(nq)], in0=s13[:, fc, ts(nq)],
                    scalar=sm2p["n1w"][:, fc:fc + 1], in1=rstdb,
                    op0=ALU.mult, op1=ALU.mult))
                P.step("v", lambda e, fc=fc, nq=nq: e.tensor_copy(
                    out=ln3[:, fc, ts(nq)], in_=s13[:, fc, ts(nq)]))

        P.step("y", lambda e: P.inc("D0", e.dma_start(
            out=sm2p["gwf"].rearrange("p (k g) -> p k g", k=DC),
            in_=dr["gate_w"].ap().rearrange("(a p) g -> p a g", p=128)), 16) and None)
        gw3 = sm2p["gwf"].rearrange("p (k g) -> p k g", k=DC)
        gpt = SM[:, 0:128]
        srt = SM[:, 128:256]
        em1 = SM[:, 384:512]
        em2 = SM[:, 512:640]
        vm = SM[:, 1152:1280]
        mfin = SM[:, 1280:1408]
        emtb = C2[:, 0:128]

        def _gmm(e):
            last = None
            for i in range(NT):
                for kc in range(DC):
                    last = mm(e, bank(0)[:, 8 * i:8 * i + 8],
                              s13[:, kc, i * 128:(i + 1) * 128],
                              gw3[:, kc, :], kc == 0, kc == DC - 1)
            return last
        P.step("t", _gmm)
        P.step("s", lambda e: e.activation(out=gpt, in_=bank(0)[:, 0:128],
                                           func=AF.Sigmoid))
        for i in range(NT):
            sl8 = slice(8 * i, 8 * i + 8)
            P.step("v", lambda e, sl8=sl8: e.max(out=srt[:, sl8], in_=gpt[:, sl8]))
            P.step("v", lambda e, sl8=sl8, i=i: e.tensor_scalar(
                out=em1[:, sl8], in0=gpt[:, sl8], scalar1=srt[:, 8 * i:8 * i + 1],
                scalar2=None, op0=ALU.is_equal))
            P.step("v", lambda e, sl8=sl8, i=i: e.tensor_scalar(
                out=em2[:, sl8], in0=gpt[:, sl8], scalar1=srt[:, 8 * i + 1:8 * i + 2],
                scalar2=None, op0=ALU.is_equal))
            P.step("v", lambda e, i=i: e.tensor_tensor(
                out=sm2p["mx_ws"][:, i:i + 1], in0=srt[:, 8 * i:8 * i + 1],
                in1=srt[:, 8 * i + 1:8 * i + 2], op=ALU.add))
            P.step("v", lambda e, i=i: e.tensor_scalar(
                out=sm2p["mx_ws"][:, i:i + 1], in0=sm2p["mx_ws"][:, i:i + 1],
                scalar1=1e-6, scalar2=None, op0=ALU.add))
            P.step("v", lambda e, i=i: e.reciprocal(
                out=sm2p["mx_ws"][:, i:i + 1], in_=sm2p["mx_ws"][:, i:i + 1]))
            P.step("v", lambda e, i=i: e.tensor_tensor(
                out=sm2p["mx_w1n"][:, i:i + 1], in0=srt[:, 8 * i:8 * i + 1],
                in1=sm2p["mx_ws"][:, i:i + 1], op=ALU.mult))
            P.step("v", lambda e, i=i: e.tensor_tensor(
                out=sm2p["mx_w2n"][:, i:i + 1], in0=srt[:, 8 * i + 1:8 * i + 2],
                in1=sm2p["mx_ws"][:, i:i + 1], op=ALU.mult))
            P.step("v", lambda e, sl8=sl8: e.tensor_tensor(
                out=emtb[:, sl8], in0=em1[:, sl8], in1=em2[:, sl8], op=ALU.add))
            P.step("t", lambda e, sl8=sl8: mm(
                e, bank(1)[:, sl8], UTB, emtb[:, sl8], True, True))
            P.step("v", lambda e, sl8=sl8: e.tensor_copy(
                out=SM[:, 256:384][:, sl8], in_=bank(1)[:, sl8]))
            P.step("t", lambda e, sl8=sl8: mm(
                e, bank(2)[:, sl8], ONESB, emtb[:, sl8], True, True))
            P.step("v", lambda e, sl8=sl8, i=i: e.tensor_copy(
                out=SM[0:1, 640 + 8 * i:640 + 8 * i + 8], in_=bank(2)[0:1, sl8]))
        totE = SM[0:1, 768:768 + 256].rearrange("p (g u) -> p g u", g=NEXP)
        P.step("v", lambda e: e.memset(SM[0:1, 768:1024], 0.0))
        P.step("v", lambda e: e.tensor_copy(
            out=totE[:, :, 0:NT],
            in_=SM[0:1, 640:640 + 8 * NT].rearrange("p (i g) -> p g i", g=NEXP)))
        for sh in (1, 2, 4, 8):
            if sh >= NT:
                continue
            P.step("v", lambda e, sh=sh: e.tensor_tensor(
                out=totE[:, :, sh:NT], in0=totE[:, :, sh:NT],
                in1=totE[:, :, 0:NT - sh], op=ALU.add))
        P.step("v", lambda e: e.tensor_copy(
            out=SM[0:1, 1041:1049],
            in_=totE[:, :, NT - 1:NT].rearrange("p g u -> p (g u)")))
        P.step("y", lambda e: P.inc("D0", e.dma_start(
            out=cc_in_d.ap(), in_=SM[0:1, 1041:1049]), 16) and None)
        P.step("g", lambda e: P.inc("CC", e.collective_compute(
            "AllGather", ALU.bypass, ins=[cc_in_d.ap()], outs=[cc_out_d.ap()],
            replica_groups=[list(range(NCORES))]), 1) and None)
        P.step("y", lambda e: P.inc("D0", e.dma_start(
            out=SM[0:NCORES, 1024:1032], in_=cc_out_d.ap()), 16) and None)
        P.step("y", lambda e: P.inc("D0", e.dma_start(
            out=SM[0:NEXP, 1040:1041], in_=dr["prefmask"].ap()), 16) and None)
        P.step("t", lambda e: mm(e, bank(3)[0:1, 0:8], SM[0:NEXP, 1040:1041],
                                 SM[0:NCORES, 1024:1032], True, True))
        P.step("v", lambda e: e.tensor_copy(out=SM[0:1, 1032:1040],
                                            in_=bank(3)[0:1, 0:8]))
        for i in range(NT):
            sl8 = slice(8 * i, 8 * i + 8)
            if i == 0:
                P.step("v", lambda e: e.tensor_copy(
                    out=SM[0:1, 1048:1056], in_=SM[0:1, 1032:1040]))
            else:
                P.step("v", lambda e, i=i: e.tensor_tensor(
                    out=SM[0:1, 1048:1056], in0=SM[0:1, 1032:1040],
                    in1=totE[:, :, i - 1:i].rearrange("p g u -> p (g u)"),
                    op=ALU.add))
            P.step("v", lambda e: e.tensor_scalar(
                out=SM[0:1, 1048:1056], in0=SM[0:1, 1048:1056],
                scalar1=-1.0, scalar2=CAP, op0=ALU.mult, op1=ALU.add))
            P.step("t", lambda e: mm(
                e, bank(3)[:, 0:8], ONESF[0:1, :], SM[0:1, 1048:1056],
                True, True))
            P.step("v", lambda e, sl8=sl8: e.tensor_tensor(
                out=vm[:, sl8], in0=SM[:, 256:384][:, sl8], in1=bank(3)[:, 0:8],
                op=ALU.is_lt))
            P.step("v", lambda e, sl8=sl8: e.tensor_tensor(
                out=mfin[:, sl8], in0=em1[:, sl8], in1=vm[:, sl8], op=ALU.mult))
            P.step("v", lambda e, sl8=sl8, i=i: e.tensor_scalar(
                out=mfin[:, sl8], in0=mfin[:, sl8],
                scalar1=sm2p["mx_w1n"][:, i:i + 1], scalar2=None, op0=ALU.mult))
            P.step("v", lambda e, sl8=sl8: e.tensor_tensor(
                out=vm[:, sl8], in0=em2[:, sl8], in1=vm[:, sl8], op=ALU.mult))
            P.step("v", lambda e, sl8=sl8, i=i: e.tensor_scalar(
                out=vm[:, sl8], in0=vm[:, sl8],
                scalar1=sm2p["mx_w2n"][:, i:i + 1], scalar2=None, op0=ALU.mult))
            P.step("v", lambda e, sl8=sl8: e.tensor_tensor(
                out=mfin[:, sl8], in0=mfin[:, sl8], in1=vm[:, sl8], op=ALU.add))

        w1e3 = W1.rearrange("p (k m) -> p k m", k=DC)
        w2e3 = W2.rearrange("p (k m) -> p k m", k=16)
        hb3 = C2[:, 0:8192].rearrange("p (k t) -> p k t", k=16)
        mfb = SM[:, 1408:1920]
        P.step("g", lambda e: P.inc("DG", e.dma_start(
            out=w1e3, in_=dr["moe_w1"].ap()[0].rearrange(
                "(a p) c -> p a c", p=128)), 16) and None)
        P.step("g", lambda e: P.inc("DG2", e.dma_start(
            out=w2e3, in_=dr["moe_w2"].ap()[0].rearrange(
                "(a p) c -> p a c", p=128)), 16) and None)
        dg_w1 = P.val("DG")
        dg_w2 = P.val("DG2")
        bank_user_v = [P.val("V")] * 8
        gelu_mark = [P.val("S")] * 16
        tm_mm2_prev = P.val("T")
        for ex in range(NEXP):
            tm_mm1_last = None
            for nq in range(NQ):
                bcast_tm_cols(SM[:, 1280 + ex:1280 + ex + 8 * NT:8][:, nq * 4:nq * 4 + 4],
                              4, SM[:, 512:1024], bank(4))
                P.step("v", lambda e: e.tensor_copy(out=mfb, in_=bank(4)))
                vm_mfb = P.val("V")
                bank_user_v[4] = vm_mfb
                for mc in range(16):
                    bk = mc % 3

                    def _m1(e, mc=mc, nq=nq, bk=bk):
                        last = None
                        for kc in range(DC):
                            last = mm(e, bank(bk), w1e3[:, kc, mc * 128:(mc + 1) * 128],
                                      ln3[:, kc, ts(nq)], kc == 0, kc == DC - 1)
                        return last
                    P.step("t", _m1, waits=[("DG", dg_w1),
                                            ("S", gelu_mark[mc - 3] if mc >= 3 else gelu_mark[13 + mc]),
                                            ("V", bank_user_v[bk])])
                    tmark = P.val("T")
                    P.step("s", lambda e, mc=mc, bk=bk: e.activation(
                        out=hb3[:, mc, :], in_=bank(bk), func=AF.Gelu),
                        waits=[("T", tmark), ("T", tm_mm2_prev)])
                    gelu_mark[mc] = P.val("S")
                tm_mm1_last = P.val("T")
                if nq == NQ - 1 and ex < NEXP - 1:
                    P.step("g", lambda e, ex=ex: P.inc("DG", e.dma_start(
                        out=w1e3, in_=dr["moe_w1"].ap()[ex + 1].rearrange(
                            "(a p) c -> p a c", p=128)), 16) and None,
                        waits=[("T", tm_mm1_last)])
                    dg_w1 = P.val("DG")
                for fm in range(DC):
                    def _m2(e, fm=fm, nq=nq):
                        last = None
                        for kc in range(16):
                            last = mm(e, bank(fm), w2e3[:, kc, fm * 128:(fm + 1) * 128],
                                      hb3[:, kc, :], kc == 0, kc == 15)
                        return last
                    P.step("t", _m2, waits=[("DG2", dg_w2), ("S", gelu_mark[15]),
                                            ("V", bank_user_v[fm])])
                    tm2 = P.val("T")
                    P.step("v", lambda e, fm=fm: e.tensor_tensor(
                        out=bank(fm), in0=bank(fm), in1=mfb, op=ALU.mult),
                        waits=[("T", tm2), ("V", vm_mfb)])
                    P.step("v", lambda e, fm=fm, nq=nq: e.tensor_tensor(
                        out=s13[:, fm, ts(nq)], in0=s13[:, fm, ts(nq)],
                        in1=bank(fm), op=ALU.add), waits=[("V", P.val("V"))])
                    bank_user_v[fm] = P.val("V")
                tm_mm2_prev = P.val("T")
                if nq == NQ - 1 and ex < NEXP - 1:
                    P.step("g", lambda e, ex=ex: P.inc("DG2", e.dma_start(
                        out=w2e3, in_=dr["moe_w2"].ap()[ex + 1].rearrange(
                            "(a p) c -> p a c", p=128)), 16) and None,
                        waits=[("T", tm_mm2_prev)])
                    dg_w2 = P.val("DG2")

        for nq in range(NQ):
            for fc in range(DC):
                P.step("v", lambda e, fc=fc, nq=nq: e.tensor_copy(
                    out=C1[:, fc * 512:(fc + 1) * 512], in_=s13[:, fc, ts(nq)]))
            for fc in range(DC):
                P.step("v", lambda e, fc=fc: e.tensor_tensor(
                    out=C2[:, fc * 512:(fc + 1) * 512], in0=C1[:, fc * 512:(fc + 1) * 512],
                    in1=C1[:, fc * 512:(fc + 1) * 512], op=ALU.mult))

            def _mmv2(e, nq=nq):
                last = None
                for fc in range(DC):
                    last = mm(e, bank(0), ONESB, C2[:, fc * 512:(fc + 1) * 512],
                              fc == 0, fc == DC - 1)
                return last
            P.step("t", _mmv2)
            P.step("v", lambda e: e.tensor_scalar(
                out=rstdb, in0=bank(0), scalar1=1.0 / D, scalar2=1e-6,
                op0=ALU.mult, op1=ALU.add))
            P.step("v", lambda e: e.reciprocal(out=rstdb, in_=rstdb))
            P.step("s", lambda e: e.activation(out=rstdb, in_=rstdb, func=AF.Sqrt))
            for fc in range(DC):
                P.step("y", lambda e, fc=fc, nq=nq: P.inc(
                    "D0", e.dma_start(out=xch, in_=x_r[:, fc, ts(nq)]), 16) and None)
                P.step("v", lambda e, fc=fc, nq=nq: e.scalar_tensor_tensor(
                    out=scrS, in0=s13[:, fc, ts(nq)],
                    scalar=sm2p["hn2w"][:, fc:fc + 1], in1=rstdb,
                    op0=ALU.mult, op1=ALU.mult))
                P.step("v", lambda e, fc=fc, nq=nq: e.tensor_tensor(
                    out=xch, in0=xch, in1=scrS, op=ALU.add))
                P.step("y", lambda e, fc=fc, nq=nq: P.inc(
                    "D0", e.dma_start(out=xd_r[:, fc, ts(nq)], in_=xch), 16) and None)

    for nm, wdt, ncol in [("cw0", None, 32), ("cw1", None, 32), ("cw2", None, 32),
                          ("cb", None, 32)]:
        sm2col(nm, ncol)
    load_cols(dr["convw"].ap()[:, 0:32], sm2p["cw0"])
    load_cols(dr["convw"].ap()[:, 32:64], sm2p["cw1"])
    load_cols(dr["convw"].ap()[:, 64:96], sm2p["cw2"])
    load_cols(dr["convb"].ap(), sm2p["cb"])
    sm2col("hedge", 32 * 2 * max(NQ - 1, 1))
    sm2col("amax2_tm", S // 128)
    sm2col("sc2_tm", S // 128)
    sm2col("inv2_tm", S // 128)
    sm2col("qkvb", DC)
    load_cols(dr["qkv_b"].ap(), sm2p["qkvb"])
    sm2col("att0_sh", 1)
    sm2col("att0_t1", 1)
    sm2col("att0_t2", 1)
    sm2col("att0_tmp", 1)
    sm2col("att0_rc", 4)
    sm2col("att0_inv4y", S // 128)
    sm2col("ac_t1", 1); sm2col("ac_t2", 1); sm2col("ac_tmp", 1)
    sm2col("ac_rc", 4); sm2col("a1_sh", 1); sm2col("gq_sh", 1)
    sm2col("lnmean_tm", S // 128); sm2col("lnrstd_tm", S // 128)
    sm2col("g_qb", DC); load_cols(dr["g_qb"].ap(), sm2p["g_qb"])
    sm2col("g_kb", 2); load_cols(dr["g_kb"].ap(), sm2p["g_kb"])
    sm2col("g_ob", DC); load_cols(dr["g_ob"].ap(), sm2p["g_ob"])
    sm2col("n1w", DC); load_cols(dr["norm1_w"].ap(), sm2p["n1w"])
    sm2col("hn2w", DC); load_cols(dr["norm2_w"].ap(), sm2p["hn2w"])
    P.step("v", lambda e: e.tensor_scalar(
        out=sm2p["hn2w"], in0=sm2p["hn2w"], scalar1=0.5, scalar2=None,
        op0=ALU.mult))
    sm2col("gwf", 64)
    sm2col("mx_ws", S // 128)
    sm2col("mx_w1n", S // 128)
    sm2col("mx_w2n", S // 128)

    w1m = stash_bitlinear(dr["w1T"], w1q_d, D, HID0, "w1")
    w2m = stash_bitlinear(dr["w2T"], w2q_d, HID0, D, "w2")
    qm = stash_ternary(dr["wqkvT"], wqkvq_d, "wq", 7.0)
    om = stash_ternary(dr["woT"], woq_d, "wo", 7.0)

    marks = {}

    def mark(name):
        marks[name] = P.nsteps
    mark("setup")
    seq = [i for i in SEQ if i <= stop]
    for pos, sid in enumerate(seq):
        mark(f"pre{pos}_{sid}")
        src_t = dr["x_fm"] if pos == 0 else x_mir
        dst_t = y_out if pos == len(seq) - 1 else x_mir
        if sid in (1, 3):
            stage_ln(src_t, "q8")
            stage_ffn0(src_t, dst_t, w1m, w2m)
        elif sid == 2:
            stage_ln(src_t, "q4")
            tap("r4", C1[:, 0:16384])
            tap("inv4", SM[:, 0:S])
            stage_att0(src_t, dst_t, qm, om)
        elif sid == 5:
            stage_ln(src_t, "bf")
            stage_att1(src_t, dst_t)
        elif sid in (4, 6):
            stage_ln(src_t, "bf", save_stats=True)
            stage_mix1(src_t, dst_t)

    mark("end")
    if os.environ.get("KPROF"):
        prev = 0
        for k2, v2 in marks.items():
            print(f"  steps {k2}: +{v2 - prev} (cum {v2})")
            prev = v2
    P.step("y", lambda e: None)
    es.close()
    return nc


def _prep_inputs(inputs, S):
    f32 = np.float32
    g = {}
    g["w1T"] = np.ascontiguousarray(inputs["L0_ffn_w1"].T.astype(f32))
    g["w2T"] = np.ascontiguousarray(inputs["L0_ffn_w2"].T.astype(f32))
    def cols(v):
        v = np.asarray(v, f32).reshape(-1)
        return np.ascontiguousarray(v.reshape(-1, 128).T)
    cw = np.asarray(inputs["L0_ffn_conv_w"], f32).reshape(HID0, 3)
    g["convw"] = np.concatenate([cols(cw[:, k]) for k in range(3)], 1)
    g["convb"] = cols(inputs["L0_ffn_conv_b"])
    g["wqkvT"] = np.ascontiguousarray(np.asarray(inputs["L0_attn_Wqkv_w"], f32).T)
    g["woT"] = np.ascontiguousarray(np.asarray(inputs["L0_attn_Wo_w"], f32).T)
    g["qkv_b"] = cols(inputs["L0_attn_Wqkv_b"])
    U = np.asarray(inputs["L0_attn_U"], f32)
    u8 = np.zeros((128, 256), f32)
    for j in range(4):
        u8[32 * j:32 * j + 32, 64 * j:64 * j + 32] = U[0:32, :]
        u8[32 * j:32 * j + 32, 64 * j + 32:64 * j + 64] = U[32:64, :]
    g["u8"] = u8
    half = 32
    inv = 1.0 / 10000.0 ** (np.arange(half, dtype=f32) / half)
    pos_q = np.arange(S, dtype=f32)
    cb = np.arange(S) // (S // 16)
    tl = np.arange(S) % (S // 16)
    pos_k = (tl * 16 + cb).astype(f32)
    cs4 = np.empty((128, 4 * S), f32)
    cs4[:, 0 * S:1 * S] = np.tile(np.cos(pos_q[None, :] * inv[:, None]), (4, 1))
    cs4[:, 1 * S:2 * S] = np.tile(np.sin(pos_q[None, :] * inv[:, None]), (4, 1))
    cs4[:, 2 * S:3 * S] = np.tile(np.cos(pos_k[None, :] * inv[:, None]), (4, 1))
    cs4[:, 3 * S:4 * S] = np.tile(np.sin(pos_k[None, :] * inv[:, None]), (4, 1))
    g["cs4"] = cs4
    perm = np.empty(D, np.int64)
    for h in range(H16):
        for j in range(DK):
            perm[64 * h + j] = j * 16 + h
    g["a1_qw"] = np.ascontiguousarray(np.asarray(inputs["L1_attn_qw"], f32)[:, perm])
    g["a1_kw"] = np.ascontiguousarray(np.asarray(inputs["L1_attn_kw"], f32)[:, perm])
    g["a1_vw"] = np.ascontiguousarray(np.asarray(inputs["L1_attn_vw"], f32)[:, perm])
    g["a1_ow"] = np.ascontiguousarray(np.asarray(inputs["L1_attn_ow"], f32)[perm, :])
    def sig(h):
        return 8 * (h // 8) + 2 * (h % 4) + (h // 4) % 2
    qperm = np.empty(D, np.int64)
    for h in range(H16):
        for j in range(DK):
            qperm[64 * sig(h) + j] = 64 * h + j
    g["g_qw"] = np.ascontiguousarray(np.asarray(inputs["L1_ffn_qw"], f32)[:, qperm])
    g["g_qb"] = cols(np.asarray(inputs["L1_ffn_qb"], f32)[qperm])
    g["g_kw"] = np.asarray(inputs["L1_ffn_kw"], f32)
    g["g_kb"] = cols(inputs["L1_ffn_kb"])
    g["g_vw"] = np.asarray(inputs["L1_ffn_vw"], f32)
    g["g_vb"] = cols(inputs["L1_ffn_vb"])
    g["g_vbr"] = np.asarray(inputs["L1_ffn_vb"], f32).reshape(1, NKV * DK)
    g["g_ow"] = np.ascontiguousarray(np.asarray(inputs["L1_ffn_ow"], f32)[qperm, :])
    g["g_ob"] = cols(inputs["L1_ffn_ob"])
    g["gate_w"] = np.asarray(inputs["L1_ffn_gate_w"], f32)
    g["moe_w1"] = np.asarray(inputs["L1_ffn_w1"], f32)
    g["moe_w2"] = np.asarray(inputs["L1_ffn_w2"], f32)
    g["norm1_w"] = cols(inputs["L1_ffn_norm1_w"])
    g["norm2_w"] = cols(inputs["L1_ffn_norm2_w"])
    g["ident_f"] = np.eye(128, dtype=f32)
    g["ident_b"] = np.eye(128).astype(ml_dtypes.bfloat16)
    g["ones_b"] = np.ones((128, 128), ml_dtypes.bfloat16)
    g["ones_f"] = np.ones((128, 128), f32)
    g["ut_b"] = np.triu(np.ones((128, 128)), 1).astype(ml_dtypes.bfloat16)
    bd = np.zeros((128, 128), f32)
    bd[0:64, 0:64] = 1.0
    bd[64:128, 64:128] = 1.0
    g["bd64_b"] = bd.astype(ml_dtypes.bfloat16)
    eo4 = np.zeros((128, 128), f32)
    for j in range(32):
        eo4[2 * j, j] = 1.0; eo4[2 * j + 1, 32 + j] = 1.0
        eo4[64 + 2 * j, 64 + j] = 1.0; eo4[64 + 2 * j + 1, 96 + j] = 1.0
    g["eo_b"] = eo4.astype(ml_dtypes.bfloat16)
    trs = np.zeros((128, 128), f32)
    trs[0:64, 0:64] = np.eye(64)
    trs[64:128, 64:128] = np.eye(64)
    g["trsel_b"] = trs.astype(ml_dtypes.bfloat16)
    bd32 = np.zeros((128, 128), f32)
    for j in range(4):
        bd32[32 * j:32 * (j + 1), 32 * j:32 * (j + 1)] = 1.0
    g["bd32_b"] = bd32.astype(ml_dtypes.bfloat16)
    g["iotaE"] = np.broadcast_to(np.arange(NEXP, dtype=f32), (128, NEXP)).copy()
    cm = np.zeros((128, 2048), f32)
    for r in range(4):
        mi = np.arange(128)[:, None]
        qi = np.arange(512)[None, :]
        cm[:, r * 512:(r + 1) * 512] = (qi >= 128 * r + mi)
    g["cmask"] = cm.astype(ml_dtypes.bfloat16)
    g["onecol_b"] = np.ones((128, 1), ml_dtypes.bfloat16)

    x = np.asarray(inputs["L0_attn_x"], f32)
    per_core = []
    for c in range(NCORES):
        m = dict(g)
        m["x_fm"] = np.ascontiguousarray(x[c].T)
        pm = np.zeros((NEXP, 1), f32)
        pm[:c if c <= NEXP else NEXP] = 1.0
        m["prefmask"] = pm
        per_core.append(m)
    return per_core


def kernel(**inputs):
    S = np.asarray(inputs["L0_attn_x"]).shape[1]
    per_core = _prep_inputs(inputs, S)
    nc = build_program(S=S)
    res = run_bass_kernel_spmd(nc, per_core, core_ids=list(range(NCORES)))
    outs = [r["y_fm"] for r in res.results]
    return np.stack([o.T for o in outs], 0).astype(np.float32)


# version 08
# speedup vs baseline: 1.1743x; 1.1743x over previous
import math
import os
from contextlib import ExitStack

import numpy as np
import ml_dtypes

import concourse.bass as bass
import concourse.mybir as mybir
from concourse.bass_utils import run_bass_kernel_spmd

F32 = mybir.dt.float32
BF16 = mybir.dt.bfloat16
I32 = mybir.dt.int32
AF = mybir.ActivationFunctionType
ALU = mybir.AluOpType
AX = mybir.AxisListType

NCORES = 8
D = 1024
DC = D // 128
H16 = 16
DK = 64
HID0 = 4096
NEXP = 8
EHID = 2048
NKV = 4

STOP_AFTER = 99
SEQ = [1, 2, 3, 4, 5, 6]
TAP = None
ATT0_PHASE = 9
FFN0_PHASE = 9

ALLSEMS = ["T", "V", "S", "G", "D0", "D1", "D2", "D3", "DG", "DG2", "CC"]


def ts(i, n=512):
    return slice(i * n, (i + 1) * n)


class Prog:
    def __init__(self, nc, S, block, es):
        self.nc, self.S, self.block, self.es = nc, S, block, es
        self.sems = {}
        for nm in ALLSEMS:
            self.sems[nm] = [es.enter_context(nc.semaphore(nm)), 0]
        self.emit = {"t": block.tensor, "v": block.vector, "s": block.scalar,
                     "g": block.gpsimd, "y": block.sync}
        self.osem = {"t": "T", "v": "V", "s": "S", "g": "G", "y": None}
        self.nsteps = 0
        self.lastw = {k: {} for k in self.emit}

    def inc(self, name, instr, k=1):
        s = self.sems[name]
        instr.then_inc(s[0], k)
        s[1] += k
        return s[1]

    def val(self, name):
        return self.sems[name][1]

    def step(self, kind, fn, waits=None):
        if waits is None:
            wl = [(nm, self.sems[nm][1]) for nm in ALLSEMS
                  if self.sems[nm][1] > 0]
        else:
            wl = []
            for w in waits:
                if isinstance(w, tuple):
                    if w[1] > 0:
                        wl.append(w)
                elif self.sems[w][1] > 0:
                    wl.append((w, self.sems[w][1]))
        myname = self.osem[kind]
        self.nsteps += 1

        lw = self.lastw[kind]
        wl = [(nm, v) for nm, v in wl if lw.get(nm, 0) < v]
        for nm, v in wl:
            lw[nm] = v

        def _(e):
            for nm, v in wl:
                e.wait_ge(self.sems[nm][0], v)
            last = fn(e)
            if myname is not None and last is not None:
                self.inc(myname, last)
        self.emit[kind](_)

    def barrier_waits(self):
        return [(nm, self.sems[nm][1]) for nm in ALLSEMS]


def mm(eng, out, lhsT, rhs, start, stop, tp="auto"):
    if tp == "auto":
        bp_l = lhsT.base_partition()
        bp_o = out.base_partition()
        tp = (bp_l, bp_o) if (bp_l or bp_o) else None
    return eng.matmul(out, lhsT, rhs, start=start, stop=stop, tile_position=tp)


def build_program(S=2048, stop_after=None):
    stop = STOP_AFTER if stop_after is None else stop_after
    nc = bass.Bass("TRN2", target_bir_lowering=False, debug=False)
    es = ExitStack()
    NQ = S // 512

    dr = {}

    def din(name, shape, dt=F32):
        dr[name] = nc.dram_tensor(name, shape, dt, kind="ExternalInput")
        return dr[name]

    din("x_fm", [D, S])
    din("w1T", [D, HID0]); din("w2T", [HID0, D])
    din("convw", [128, 3 * HID0 // 128]); din("convb", [128, HID0 // 128])
    din("wqkvT", [D, D]); din("woT", [D, D]); din("qkv_b", [128, DC])
    din("u8", [128, 256]); din("cs4", [128, 4 * S])
    din("a1_qw", [D, D]); din("a1_kw", [D, D]); din("a1_vw", [D, D]); din("a1_ow", [D, D])
    din("g_qw", [D, D]); din("g_qb", [128, DC])
    din("g_kw", [D, NKV * DK]); din("g_kb", [128, 2])
    din("g_vw", [D, NKV * DK]); din("g_vb", [128, 2]); din("g_vbr", [1, NKV * DK])
    din("g_ow", [D, D]); din("g_ob", [128, DC])
    din("gate_w", [D, NEXP])
    din("moe_w1", [NEXP, D, EHID]); din("moe_w2", [NEXP, EHID, D])
    din("norm1_w", [128, DC]); din("norm2_w", [128, DC])
    din("ident_f", [128, 128]); din("ident_b", [128, 128], BF16)
    din("ones_b", [128, 128], BF16); din("ones_f", [128, 128])
    din("ut_b", [128, 128], BF16)
    din("bd64_b", [128, 128], BF16)
    din("bd32_b", [128, 128], BF16)
    din("eo_b", [128, 128], BF16)
    din("trsel_b", [128, 128], BF16)
    din("iotaE", [128, NEXP])
    din("cmask", [128, 2048], BF16)
    din("prefmask", [NEXP, 1])
    din("onecol_b", [128, 1], BF16)

    y_out = nc.dram_tensor("y_fm", [D, S], F32, kind="ExternalOutput")
    dbg_out = nc.dram_tensor("dbg", [128, 16384], F32, kind="ExternalOutput") if TAP else None
    x_mir = nc.dram_tensor("x_mirror", [D, S], F32, kind="Internal")
    w1q_d = nc.dram_tensor("w1q_stash", [D, HID0], BF16, kind="Internal")
    w2q_d = nc.dram_tensor("w2q_stash", [HID0, D], BF16, kind="Internal")
    wqkvq_d = nc.dram_tensor("wqkvq_stash", [D, D], BF16, kind="Internal")
    woq_d = nc.dram_tensor("woq_stash", [D, D], BF16, kind="Internal")
    cc_in_d = nc.dram_tensor("cc_in", [1, NEXP], F32, kind="Internal")
    cc_out_d = nc.dram_tensor("cc_out", [NCORES, NEXP], F32, kind="Internal")

    big = 16384
    B1 = es.enter_context(nc.sbuf_tensor("B1", [128, big], F32))
    C1 = es.enter_context(nc.sbuf_tensor("C1", [128, big], BF16))
    C2 = es.enter_context(nc.sbuf_tensor("C2", [128, big], BF16))
    W1 = es.enter_context(nc.sbuf_tensor("W1", [128, 16384], BF16))
    W2 = es.enter_context(nc.sbuf_tensor("W2", [128, 16384], BF16))
    SM = es.enter_context(nc.sbuf_tensor("SM", [128, max(S, 2048)], F32))
    SM2 = es.enter_context(nc.sbuf_tensor("SM2", [128, 960], F32))
    CNf = es.enter_context(nc.sbuf_tensor("CNf", [128, 512 + NEXP], F32))
    CNb = es.enter_context(nc.sbuf_tensor("CNb", [128, 897], BF16))
    PSA = es.enter_context(nc.psum_tensor("PSA", [128, 2048], F32))
    PSB = es.enter_context(nc.psum_tensor("PSB", [128, 2048], F32))

    block = es.enter_context(nc.Block())
    P = Prog(nc, S, block, es)

    def bank(i, n=1):
        assert 0 <= i and i + n <= 8
        if i < 4:
            assert i + n <= 4
            return PSA[:, i * 512:(i + n) * 512]
        assert i >= 4
        return PSB[:, (i - 4) * 512:(i - 4 + n) * 512]

    IDF = CNf[:, 0:128]
    ONESF = CNf[:, 128:256]
    IOTAE = CNf[:, 256:256 + NEXP]
    IDB = CNb[:, 0:128]
    ONESB = CNb[:, 128:256]
    UTB = CNb[:, 256:384]
    BD64 = CNb[:, 384:512]
    EOB4 = CNb[:, 512:640]
    ONECOL = CNb[:, 640:641]
    BD32 = CNb[:, 641:769]
    TRSEL = CNb[:, 769:897]
    U8 = CNf[:, 256 + NEXP:512 + NEXP]

    sm2p = {"n": 0}

    def sm2col(name, n=1):
        c = sm2p["n"]; sm2p["n"] = c + n
        assert sm2p["n"] <= 960, (name, sm2p["n"])
        sm2p[name] = SM2[:, c:c + n]
        return sm2p[name]

    def _(e):
        for dst, src in [(IDF, "ident_f"), (ONESF, "ones_f"), (IOTAE, "iotaE"),
                         (IDB, "ident_b"), (ONESB, "ones_b"), (UTB, "ut_b"),
                         (BD64, "bd64_b"), (BD32, "bd32_b"), (EOB4, "eo_b"),
                         (TRSEL, "trsel_b"),
                         (ONECOL, "onecol_b"), (U8, "u8")]:
            P.inc("D0", e.dma_start(out=dst, in_=dr[src].ap()), 16)
    P.step("y", _)

    def load_cols(src_ap, dst_cols):
        def _(e):
            P.inc("D0", e.dma_start(out=dst_cols, in_=src_ap), 16)
        P.step("y", _)

    def rows_to_cols(src_sbuf, n, dst_cols, scale=1.0, pbank=6):
        for g0 in range(0, n, 4):
            g = min(4, n - g0)

            def _t(e, g0=g0, g=g):
                last = None
                for j in range(g):
                    last = e.transpose(bank(pbank)[:, ts(j, 128)],
                                       src_sbuf[:, ts(g0 + j, 128)], IDF)
                return last
            P.step("t", _t)

            def _s(e, g0=g0, g=g):
                last = None
                for j in range(g):
                    last = e.activation(out=dst_cols[:, g0 + j:g0 + j + 1],
                                        in_=bank(pbank)[:, ts(j, 128)][:, 0:1],
                                        func=AF.Copy, scale=float(scale))
                return last
            P.step("s", _s)

    def bcast_tm_cols(tm_cols, nsub, scr128, out_psum, tbank=7):
        def _t1(e):
            last = None
            for j in range(nsub):
                last = e.transpose(bank(tbank)[0:1, j * 128:(j + 1) * 128],
                                   tm_cols[:, j:j + 1], IDF)
            return last
        P.step("t", _t1)
        P.step("v", lambda e: e.tensor_copy(out=scr128[0:1, 0:128 * nsub],
                                            in_=bank(tbank)[0:1, 0:128 * nsub]))

        def _t2(e):
            last = None
            for j in range(nsub):
                last = mm(e, out_psum[:, ts(j, 128)], ONESF[0:1, :],
                          scr128[0:1, j * 128:(j + 1) * 128], True, True)
            return last
        P.step("t", _t2)

    def stash_bitlinear(win, stash, rows, cols, tag):
        nch, npc = rows // 128, cols // 1024
        npan = nch * npc
        sl, scr, scr2 = B1[:, 0:1024], B1[:, 1024:2048], B1[:, 2048:3072]
        acc = sm2col(tag + "_abs", npan)
        tot = sm2col(tag + "_tot")
        inv = sm2col(tag + "_inv")
        scol = sm2col(tag + "_s")
        cs = sm2col(tag + "_cs", cols // 128)
        win_r = win.ap().rearrange("(a p) c -> p a c", p=128)
        st_r = stash.ap().rearrange("(a p) c -> p a c", p=128)

        for i in range(npan):
            a, j = divmod(i, npc)
            P.step("y", lambda e, a=a, j=j: P.inc(
                "D0", e.dma_start(out=sl, in_=win_r[:, a, ts(j, 1024)]), 16) and None)
            P.step("s", lambda e, i=i: e.activation(
                out=scr, in_=sl, func=AF.Abs, accum_out=acc[:, i:i + 1]))
        P.step("v", lambda e: e.tensor_reduce(out=tot, in_=acc, axis=AX.X, op=ALU.add))
        P.step("t", lambda e: mm(e, bank(7)[:, 0:1], ONESF, tot, True, True))
        P.step("v", lambda e: e.tensor_scalar(
            out=inv, in0=bank(7)[:, 0:1], scalar1=1.0 / (rows * cols),
            scalar2=1e-5, op0=ALU.mult, op1=ALU.max))
        P.step("v", lambda e: e.reciprocal(out=scol, in_=inv))

        qp = C1[:, 0:1024]
        for i in range(npan):
            a, j = divmod(i, npc)
            P.step("y", lambda e, a=a, j=j: P.inc(
                "D0", e.dma_start(out=sl, in_=win_r[:, a, ts(j, 1024)]), 16) and None)
            P.step("v", lambda e: e.tensor_scalar(
                out=scr, in0=sl, scalar1=scol, scalar2=1.0,
                op0=ALU.mult, op1=ALU.add))
            P.step("v", lambda e: e.tensor_copy(
                out=B1[:, 3072:4096].bitcast(I32), in_=scr))
            P.step("v", lambda e: e.tensor_copy(
                out=scr2, in_=B1[:, 3072:4096].bitcast(I32)))
            P.step("v", lambda e: e.tensor_scalar(
                out=scr2, in0=scr2, scalar1=1.0, scalar2=1.0,
                op0=ALU.subtract, op1=ALU.min))
            P.step("v", lambda e: e.tensor_scalar(
                out=qp, in0=scr2, scalar1=-1.0, scalar2=None, op0=ALU.max))
            P.step("y", lambda e, a=a, j=j: P.inc(
                "D0", e.dma_start(out=st_r[:, a, ts(j, 1024)], in_=qp), 16) and None)

        wp = W1[:, 0:1024]
        csb = B1[:, 0:1024]
        for j in range(npc):
            for a in range(nch):
                P.step("y", lambda e, a=a, j=j: P.inc(
                    "D0", e.dma_start(out=wp, in_=st_r[:, a, ts(j, 1024)]), 16) and None)
                P.step("t", lambda e, a=a: [
                    mm(e, bank(4), ONESB, wp[:, ts(0)], a == 0, a == nch - 1),
                    mm(e, bank(5), ONESB, wp[:, ts(1)], a == 0, a == nch - 1)][-1])
            P.step("v", lambda e: e.tensor_copy(out=csb[:, 0:512], in_=bank(4)))
            P.step("v", lambda e: e.tensor_copy(out=csb[:, 512:1024], in_=bank(5)))
            rows_to_cols(csb, 8, cs[:, j * 8:(j + 1) * 8], 127.0)
        return dict(inv=inv, s=scol, cs=cs)

    def stash_ternary(win, stash, tag, qoff=7.0):
        rows = cols = D
        nch, npc = rows // 128, cols // 1024
        sl, scr = B1[:, 0:1024], B1[:, 1024:2048]
        acc = sm2col(tag + "_abs", nch * npc)
        tot = sm2col(tag + "_tot")
        thr = sm2col(tag + "_thr")
        alpha = sm2col(tag + "_alpha", cols // 128)
        cs = sm2col(tag + "_cs", cols // 128)
        win_r = win.ap().rearrange("(a p) c -> p a c", p=128)
        st_r = stash.ap().rearrange("(a p) c -> p a c", p=128)
        absb = B1[:, 2048:3072]

        for j in range(npc):
            for a in range(nch):
                P.step("y", lambda e, a=a, j=j: P.inc(
                    "D0", e.dma_start(out=sl, in_=win_r[:, a, ts(j, 1024)]), 16) and None)
                P.step("s", lambda e, i=a * npc + j: e.activation(
                    out=scr, in_=sl, func=AF.Abs, accum_out=acc[:, i:i + 1]))
                P.step("v", lambda e: e.tensor_copy(out=C1[:, 0:1024], in_=scr))
                P.step("t", lambda e, a=a: [
                    mm(e, bank(4), ONESB, C1[:, 0:512], a == 0, a == nch - 1),
                    mm(e, bank(5), ONESB, C1[:, 512:1024], a == 0, a == nch - 1)][-1])
            P.step("v", lambda e: e.tensor_copy(out=absb[:, 0:512], in_=bank(4)))
            P.step("v", lambda e: e.tensor_copy(out=absb[:, 512:1024], in_=bank(5)))
            rows_to_cols(absb, 8, alpha[:, j * 8:(j + 1) * 8], 1.0 / rows)
        P.step("v", lambda e: e.tensor_reduce(out=tot, in_=acc, axis=AX.X, op=ALU.add))
        P.step("t", lambda e: mm(e, bank(7)[:, 0:1], ONESF, tot, True, True))
        P.step("v", lambda e: e.tensor_scalar(
            out=thr, in0=bank(7)[:, 0:1], scalar1=0.05 / (rows * cols),
            scalar2=None, op0=ALU.mult))

        qp = C1[:, 0:1024]
        sgn = B1[:, 2048:3072]
        msk = B1[:, 3072:4096]
        for i in range(nch * npc):
            a, j = divmod(i, npc)
            P.step("y", lambda e, a=a, j=j: P.inc(
                "D0", e.dma_start(out=sl, in_=win_r[:, a, ts(j, 1024)]), 16) and None)
            P.step("s", lambda e: e.sign(out=sgn, in_=sl))
            P.step("s", lambda e: e.activation(out=scr, in_=sl, func=AF.Abs))
            P.step("v", lambda e: e.tensor_scalar(
                out=msk, in0=scr, scalar1=thr, scalar2=None, op0=ALU.is_ge))
            P.step("v", lambda e: e.tensor_tensor(out=qp, in0=sgn, in1=msk, op=ALU.mult))
            P.step("y", lambda e, a=a, j=j: P.inc(
                "D0", e.dma_start(out=st_r[:, a, ts(j, 1024)], in_=qp), 16) and None)

        wp = W1[:, 0:1024]
        csb = B1[:, 0:1024]
        for j in range(npc):
            for a in range(nch):
                P.step("y", lambda e, a=a, j=j: P.inc(
                    "D0", e.dma_start(out=wp, in_=st_r[:, a, ts(j, 1024)]), 16) and None)
                P.step("t", lambda e, a=a: [
                    mm(e, bank(4), ONESB, wp[:, ts(0)], a == 0, a == nch - 1),
                    mm(e, bank(5), ONESB, wp[:, ts(1)], a == 0, a == nch - 1)][-1])
            P.step("v", lambda e: e.tensor_copy(out=csb[:, 0:512], in_=bank(4)))
            P.step("v", lambda e: e.tensor_copy(out=csb[:, 512:1024], in_=bank(5)))
            rows_to_cols(csb, 8, cs[:, j * 8:(j + 1) * 8], float(qoff))
        return dict(alpha=alpha, cs=cs)

    def stage_ln(x_src, mode, save_stats=False):
        Qv = {"q8": 127.0, "q4": 7.0}.get(mode)
        floor_ = {"q8": 1e-5, "q4": 1e-6}.get(mode)
        x_r = x_src.ap().rearrange("(c p) t -> p c t", p=128)
        C13 = C1.rearrange("p (c t) -> p c t", c=DC)
        for q in range(NQ):
            xq = B1[:, 0:4096]
            xq3 = xq.rearrange("p (c t) -> p c t", c=DC)
            lnq = B1[:, 4096:8192]
            lnq3 = lnq.rearrange("p (c t) -> p c t", c=DC)
            rsc = B1[:, 8192:10240]
            slotA = B1[:, 12288:12800]
            slotB = B1[:, 12800:13312]
            slotC = B1[:, 13312:13824]
            slotM = B1[:, 13824:14336]
            tmcol = B1[:, 14336:14464]
            tmcol2 = B1[:, 14464:14592]
            scr128 = B1[:, 14592:15104]
            xb = C2[:, 0:4096]
            xb3 = xb.rearrange("p (c t) -> p c t", c=DC)
            sq = C2[:, 4096:8192]
            sq3 = sq.rearrange("p (c t) -> p c t", c=DC)

            P.step("y", lambda e, q=q: P.inc(
                "D0", e.dma_start(out=xq3, in_=x_r[:, :, ts(q)]), 16) and None)
            P.step("v", lambda e: e.tensor_copy(out=xb, in_=xq))
            P.step("v", lambda e: e.tensor_tensor(out=sq, in0=xb, in1=xb, op=ALU.mult))

            def _mm(e):
                last = None
                for c in range(DC):
                    mm(e, bank(0), ONESB, xb3[:, c, :], c == 0, c == DC - 1)
                    last = mm(e, bank(1), ONESB, sq3[:, c, :], c == 0, c == DC - 1)
                return last
            P.step("t", _mm)
            P.step("v", lambda e: [
                e.tensor_scalar(out=slotA, in0=bank(0), scalar1=1.0 / D,
                                scalar2=None, op0=ALU.mult),
                e.tensor_scalar(out=slotB, in0=bank(1), scalar1=1.0 / D,
                                scalar2=None, op0=ALU.mult)][-1])
            P.step("v", lambda e: e.tensor_tensor(out=slotC, in0=slotA,
                                                  in1=slotA, op=ALU.mult))
            P.step("v", lambda e: e.scalar_tensor_tensor(
                out=slotB, in0=slotB, scalar=1e-5, in1=slotC,
                op0=ALU.add, op1=ALU.subtract))
            P.step("v", lambda e: e.reciprocal(out=slotB, in_=slotB))
            P.step("s", lambda e: e.activation(out=slotC, in_=slotB, func=AF.Sqrt))
            if save_stats:
                rows_to_cols(slotA, 4, sm2p["lnmean_tm"][:, q * 4:q * 4 + 4])
                rows_to_cols(slotC, 4, sm2p["lnrstd_tm"][:, q * 4:q * 4 + 4])

            for c in range(DC):
                P.step("v", lambda e, c=c: e.tensor_tensor(
                    out=lnq3[:, c, :], in0=xq3[:, c, :], in1=slotA, op=ALU.subtract))
                P.step("v", lambda e, c=c: e.tensor_tensor(
                    out=lnq3[:, c, :], in0=lnq3[:, c, :], in1=slotC, op=ALU.mult))
                if mode == "bf":
                    P.step("v", lambda e, c=c, q=q: e.tensor_copy(
                        out=C13[:, c, ts(q)], in_=lnq3[:, c, :]))
                else:
                    slotAB = B1[:, 10240:10752]
                    P.step("s", lambda e, c=c: e.activation(
                        out=slotAB, in_=lnq3[:, c, :], func=AF.Abs))
                    if c == 0:
                        P.step("v", lambda e: e.tensor_copy(out=slotM, in_=slotAB))
                    else:
                        P.step("v", lambda e: e.tensor_tensor(
                            out=slotM, in0=slotM, in1=slotAB, op=ALU.max))
            if mode == "bf":
                continue

            def _t(e):
                last = None
                for j in range(4):
                    last = e.transpose(bank(2)[:, ts(j, 128)],
                                       slotM[:, ts(j, 128)], IDF)
                return last
            P.step("t", _t)
            P.step("v", lambda e: [
                e.tensor_reduce(out=tmcol[:, j:j + 1],
                                in_=bank(2)[:, ts(j, 128)], axis=AX.X, op=ALU.max)
                for j in range(4)][-1])
            P.step("v", lambda e: e.tensor_single_scalar(
                out=tmcol[:, 0:4], in_=tmcol[:, 0:4], scalar=floor_, op=ALU.max))
            P.step("v", lambda e: e.tensor_scalar(
                out=tmcol2[:, 0:4], in0=tmcol[:, 0:4], scalar1=1.0 / Qv,
                scalar2=None, op0=ALU.mult))
            P.step("v", lambda e: e.reciprocal(out=tmcol[:, 0:4], in_=tmcol[:, 0:4]))
            P.step("v", lambda e: e.tensor_scalar(
                out=tmcol[:, 0:4], in0=tmcol[:, 0:4], scalar1=Qv,
                scalar2=None, op0=ALU.mult))
            bcast_tm_cols(tmcol[:, 0:4], 4, scr128, bank(3))
            bcast_tm_cols(tmcol2[:, 0:4], 4, scr128, bank(2))
            P.step("v", lambda e, q=q: e.tensor_copy(out=SM[:, ts(q)], in_=bank(2)))

            for hh in range(4):
                def _r1(e, hh=hh):
                    last = None
                    for c in range(2):
                        last = e.tensor_tensor(
                            out=rsc[:, ts(c)], in0=lnq3[:, hh * 2 + c, :],
                            in1=bank(3), op=ALU.mult)
                    return last
                P.step("v", _r1)
                P.step("v", lambda e: e.tensor_scalar(
                    out=rsc[:, 0:1024], in0=rsc[:, 0:1024], scalar1=Qv,
                    scalar2=None, op0=ALU.add))
                P.step("v", lambda e: e.tensor_copy(
                    out=B1[:, 15104:16128].bitcast(I32), in_=rsc[:, 0:1024]))

                def _w(e, hh=hh, q=q):
                    last = None
                    for c in range(2):
                        last = e.tensor_copy(
                            out=C13[:, hh * 2 + c, ts(q)],
                            in_=B1[:, 15104 + 512 * c:15104 + 512 * (c + 1)].bitcast(I32))
                    return last
                P.step("v", _w)

    def cq3_pa(mc):
        return B1.rearrange("p (m t) -> p m t", m=8)[:, mc, 0:S]

    def stage_ffn0(x_src, x_dst, w1m, w2m):
        r13 = C1.rearrange("p (c t) -> p c t", c=DC)
        w1q_r = w1q_d.ap().rearrange("(a p) c -> p a c", p=128)
        w2f_r = w2q_d.ap().rearrange("(a p) c -> p a c", p=128)
        inv1 = SM[:, 0:S]
        cw0, cw1, cw2 = sm2p["cw0"], sm2p["cw1"], sm2p["cw2"]
        cbc = sm2p["cb"]
        nb = NQ - 1
        hedge = sm2p["hedge"]
        amax2_tm = sm2p["amax2_tm"]
        W2f = W2.bitcast(F32)
        scrA = [W2f[:, 0:S], W2f[:, 2048:2048 + S]]
        GtmpA = W2f[:, 4096:4096 + S]
        AMX = W2f[:, 6144:6144 + S]
        W1UPf = W1.bitcast(F32)
        bw = P.barrier_waits()
        first = {k: True for k in "tvsgy"}

        def stw(kind, fn, waits):
            if first[kind]:
                first[kind] = False
                waits = bw + list(waits)
            P.step(kind, fn, waits=waits)

        if "c127" not in sm2p:
            sm2col("c127")
        c127 = sm2p["c127"]

        def _vinit(e):
            e.memset(c127, 127.0)
            return e.tensor_scalar(
                out=inv1, in0=inv1, scalar1=w1m["inv"], scalar2=None,
                op0=ALU.mult)
        stw("v", _vinit, waits=[])
        v_inv1 = P.val("V")
        stw("g", lambda e: e.memset(AMX, 0.0), waits=[])
        g_amx0 = P.val("G")

        d1_wb = {}
        tm_mm = {}
        vm_evac_ps = {0: 0, 1: 0}
        sm_abs = {}
        gm_conv2 = {}
        vm_max = 0
        for qh in range(4):
            half = qh % 2
            wb3 = W1[:, half * 8192:half * 8192 + 8192].rearrange(
                "p (k m) -> p k m", k=DC)
            lw = [("T", tm_mm[(qh - 2, 7)])] if qh >= 2 else []
            lw.append(("D1", P.val("D1")))
            stw("y", lambda e, wb3=wb3, qh=qh: P.inc("D1", e.dma_start(
                out=wb3, in_=w1q_r[:, :, ts(qh, 1024)]), 16) and None, waits=lw)
            d1_wb[qh] = P.val("D1")
            for mc in range(8):
                pid = mc % 2
                ps = (PSA if pid == 0 else PSB)[:, 0:S]
                wcol = w1m["cs"][:, qh * 8 + mc:qh * 8 + mc + 1]
                hcol = qh * 8 + mc
                src = cq3_pa(mc)

                def _m(e, wb3=wb3, mc=mc, ps=ps, qh=qh):
                    last = None
                    for nq_ in range(NQ):
                        for kc in range(DC):
                            last = mm(e, ps[:, ts(nq_)],
                                      wb3[:, kc, ts(mc, 128)],
                                      r13[:, kc, ts(nq_)], kc == 0, kc == DC - 1)
                    return last
                stw("t", _m, waits=[("D1", d1_wb[qh]),
                                    ("V", vm_evac_ps[pid])])
                tm_mm[(qh, mc)] = P.val("T")
                ew = [("T", tm_mm[(qh, mc)]), ("V", v_inv1), ("V", vm_max)]
                if (qh - 1, mc) in sm_abs:
                    ew.append(("S", sm_abs[(qh - 1, mc)]))
                    ew.append(("G", gm_conv2[(qh - 1, mc)]))
                stw("v", lambda e, src=src, ps=ps, wcol=wcol:
                    e.scalar_tensor_tensor(
                        out=src, in0=ps, scalar=wcol,
                        in1=inv1, op0=ALU.subtract, op1=ALU.mult), waits=ew)
                vm_evac_ps[pid] = P.val("V")
                vm_evac_cur = P.val("V")
                scr = scrA[mc % 2]
                stw("s", lambda e, src=src, scr=scr, hcol=hcol: e.activation(
                    out=scr, in_=src, func=AF.Identity,
                    scale=cw1[:, hcol:hcol + 1], bias=cbc[:, hcol:hcol + 1]),
                    waits=[("V", vm_evac_cur)])
                sm_c1 = P.val("S")

                def _g2a(e, src=src, hcol=hcol):
                    last = e.tensor_scalar(
                        out=GtmpA[:, 0:S - 1], in0=src[:, 0:S - 1],
                        scalar1=cw0[:, hcol:hcol + 1], scalar2=None, op0=ALU.mult)
                    if nb:
                        srcb = src.rearrange("p (b u) -> p b u", u=512)
                        hv = hedge[:, hcol * 2 * nb:(hcol + 1) * 2 * nb]
                        hv2 = hv.rearrange("p (b u) -> p b u", u=2)
                        e.tensor_copy(out=hv2[:, :, 0:1], in_=srcb[:, 0:nb, 511:512])
                        last = e.tensor_copy(out=hv2[:, :, 1:2],
                                             in_=srcb[:, 1:nb + 1, 0:1])
                    return last
                stw("g", _g2a, waits=[("S", sm_c1)])
                g_mid = P.val("G")
                stw("g", lambda e, scr=scr: e.tensor_tensor(
                    out=scr[:, 1:S], in0=scr[:, 1:S],
                    in1=GtmpA[:, 0:S - 1], op=ALU.add), waits=[("G", g_mid)])
                gm_conv2[(qh, mc)] = P.val("G")
                stw("v", lambda e, src=src, scr=scr, hcol=hcol:
                    e.scalar_tensor_tensor(
                        out=scr[:, 0:S - 1], in0=src[:, 1:S],
                        scalar=cw2[:, hcol:hcol + 1], in1=scr[:, 0:S - 1],
                        op0=ALU.mult, op1=ALU.add),
                    waits=[("G", gm_conv2[(qh, mc)])])
                vm_c3 = P.val("V")

                stw("s", lambda e, src=src, scr=scr: e.activation(
                    out=src, in_=scr, func=AF.Silu), waits=[("V", vm_c3)])
                s_mid = P.val("S")
                stw("s", lambda e, src=src: e.activation(
                    out=src, in_=src, func=AF.Abs), waits=[("S", s_mid)])
                sm_abs[(qh, mc)] = P.val("S")
                stw("v", lambda e, src=src: e.tensor_tensor(
                    out=AMX, in0=AMX, in1=src, op=ALU.max),
                    waits=[("S", sm_abs[(qh, mc)]), ("G", g_amx0),
                           ("V", vm_max)])
                vm_max = P.val("V")

        if FFN0_PHASE <= 1:
            return
        nt = S // 128
        sc2_tm = sm2p["sc2_tm"]
        inv2_tm = sm2p["inv2_tm"]
        for j in range(0, nt, 4):
            g = min(4, nt - j)

            def _t(e, j=j, g=g):
                last = None
                for k in range(g):
                    last = e.transpose(bank(k)[:, 0:128], AMX[:, ts(j + k, 128)], IDF)
                return last
            P.step("t", _t, waits=[("V", vm_max), ("S", P.val("S")),
                                   ("T", P.val("T"))])
            tm_tr = P.val("T")

            def _v(e, j=j, g=g):
                return [e.tensor_reduce(
                    out=amax2_tm[:, j + k:j + k + 1], in_=bank(k)[:, 0:128],
                    axis=AX.X, op=ALU.max) for k in range(g)][-1]
            P.step("v", _v, waits=[("T", tm_tr)])
        P.step("v", lambda e: e.tensor_single_scalar(
            out=amax2_tm[:, 0:nt], in_=amax2_tm[:, 0:nt], scalar=1e-5, op=ALU.max))
        P.step("v", lambda e: e.tensor_scalar(
            out=inv2_tm[:, 0:nt], in0=amax2_tm[:, 0:nt],
            scalar1=w2m["inv"], scalar2=0.5 / 127.0, op0=ALU.mult, op1=ALU.mult))
        P.step("v", lambda e: e.reciprocal(out=sc2_tm[:, 0:nt], in_=amax2_tm[:, 0:nt]))
        P.step("v", lambda e: e.tensor_scalar(
            out=sc2_tm[:, 0:nt], in0=sc2_tm[:, 0:nt], scalar1=127.0,
            scalar2=None, op0=ALU.mult))
        scr128b = B1[:, 14336:14848]
        for tc in range(NQ):
            for j, tmc in enumerate([sc2_tm, inv2_tm]):
                dst = W1UPf[:, 4096 + tc * 1024 + j * 512:
                            4096 + tc * 1024 + j * 512 + 512]
                bcast_tm_cols(tmc[:, tc * 4:tc * 4 + 4], 4, scr128b, bank(4))
                P.step("v", lambda e, dst=dst: e.tensor_copy(
                    out=dst, in_=bank(4)))

        if FFN0_PHASE <= 2:
            return
        cch3 = B1[:, 0:4096].rearrange("p (m t) -> p m t", m=8)
        cscr = [B1[:, 4096:4608], B1[:, 4608:5120]]
        rscr = [B1[:, 5120:5632], B1[:, 5632:6144]]
        fscr = [B1[:, 6144:6656], B1[:, 6656:7168]]
        iscr = [B1[:, 15360:15872].bitcast(I32), B1[:, 15872:16384].bitcast(I32)]
        xq = B1[:, 8192:12288]
        xq3 = xq.rearrange("p (c t) -> p c t", c=DC)
        x_r = x_src.ap().rearrange("(c p) t -> p c t", p=128)
        xd_r = x_dst.ap().rearrange("(c p) t -> p c t", p=128)
        r2f = C2.rearrange("p (k t) -> p k t", k=32)
        wb3 = W1[:, 0:8192].rearrange("p (k m) -> p k m", k=DC)
        w2h = [W2[:, 0:8192].rearrange("p (k m) -> p k m", k=8),
               W2[:, 8192:16384].rearrange("p (k m) -> p k m", k=8)]

        paT, paV, paS, paG = (P.val(n) for n in "TVSG")
        d3_last = P.val("D3")
        d1_last = P.val("D1")
        tm_mm1 = paT
        tm_mm2 = {}
        vm_bank_free = {b: paV for b in range(8)}
        sm_silu = {m: paS for m in range(8)}
        sm_c1d = {m: paS for m in range(8)}
        gm_cvt = {m: paG for m in range(8)}
        gm_cvt_par = {0: paG, 1: paG}
        gm_fadd = {f: paG for f in range(8)}
        gm_fadd_par = {0: paG, 1: paG}
        d2_w2 = {}
        for tc in range(NQ):
            sc2sb = W1UPf[:, 4096 + tc * 1024:4096 + tc * 1024 + 512]
            inv2sb = W1UPf[:, 4096 + tc * 1024 + 512:4096 + tc * 1024 + 1024]
            xw = [("D3", d3_last)]
            if tc == 0:
                xw += [("V", paV), ("S", paS), ("G", paG)]
            stw("y", lambda e, tc=tc: P.inc(
                "D3", e.dma_start(out=xq3, in_=x_r[:, :, ts(tc)]), 16) and None,
                waits=xw)
            d3_xq = P.val("D3")
            for pp in range(2):
                pw = [("T", tm_mm2[(tc - 1, pp + 2)])] if (tc - 1, pp + 2) in tm_mm2 else []
                pw.append(("D2", P.val("D2")))
                if tc == 0:
                    pw += [("V", paV), ("S", paS), ("T", paT)]
                stw("g", lambda e, pp=pp: P.inc("D2", e.dma_start(
                    out=w2h[pp], in_=w2f_r[:, pp * 8:pp * 8 + 8, :]), 16) and None,
                    waits=pw)
                d2_w2[(tc, pp)] = P.val("D2")

            for qh in range(4):
                stw("y", lambda e, qh=qh: P.inc("D1", e.dma_start(
                    out=wb3, in_=w1q_r[:, :, ts(qh, 1024)]), 16) and None,
                    waits=[("T", tm_mm1), ("D1", d1_last)])
                d1_cur = P.val("D1")
                d1_last = d1_cur
                for mc in range(8):
                    bk = mc % 3
                    wcol = w1m["cs"][:, qh * 8 + mc:qh * 8 + mc + 1]
                    hcol = qh * 8 + mc
                    src = cch3[:, mc, :]
                    scr = cscr[mc % 2]
                    rs = rscr[mc % 2]
                    isc = iscr[mc % 2]

                    def _m(e, mc=mc, bk=bk, tc=tc):
                        last = None
                        for kc in range(DC):
                            last = mm(e, bank(bk), wb3[:, kc, ts(mc, 128)],
                                      r13[:, kc, ts(tc)], kc == 0, kc == DC - 1)
                        return last
                    mw1 = [("D1", d1_cur), ("V", vm_bank_free[bk])]
                    if tc == 0 and qh == 0 and mc == 0:
                        mw1.append(("T", P.val("T")))
                    stw("t", _m, waits=mw1)
                    tm_mm1 = P.val("T")
                    stw("v", lambda e, src=src, bk=bk, wcol=wcol, tc=tc:
                        e.scalar_tensor_tensor(
                            out=src, in0=bank(bk), scalar=wcol,
                            in1=inv1[:, ts(tc)], op0=ALU.subtract, op1=ALU.mult),
                        waits=[("T", tm_mm1), ("S", sm_silu[mc]), ("V", v_inv1)])
                    vm_bank_free[bk] = P.val("V")
                    vm_evac = P.val("V")
                    stw("s", lambda e, src=src, scr=scr, hcol=hcol: e.activation(
                        out=scr, in_=src, func=AF.Identity,
                        scale=cw1[:, hcol:hcol + 1], bias=cbc[:, hcol:hcol + 1]),
                        waits=[("V", vm_evac)])
                    sm_c1d[mc] = P.val("S")

                    stw("v", lambda e, src=src, scr=scr, hcol=hcol:
                        e.scalar_tensor_tensor(
                            out=scr[:, 1:512], in0=src[:, 0:511],
                            scalar=cw0[:, hcol:hcol + 1], in1=scr[:, 1:512],
                            op0=ALU.mult, op1=ALU.add),
                        waits=[("S", sm_c1d[mc])])
                    v_mid = P.val("V")

                    def _c3(e, src=src, scr=scr, hcol=hcol, tc=tc):
                        last = e.scalar_tensor_tensor(
                            out=scr[:, 0:511], in0=src[:, 1:512],
                            scalar=cw2[:, hcol:hcol + 1], in1=scr[:, 0:511],
                            op0=ALU.mult, op1=ALU.add)
                        if tc < NQ - 1:
                            last = e.scalar_tensor_tensor(
                                out=scr[:, 511:512],
                                in0=hedge[:, hcol * 2 * nb + 2 * tc + 1:
                                          hcol * 2 * nb + 2 * tc + 2],
                                scalar=cw2[:, hcol:hcol + 1], in1=scr[:, 511:512],
                                op0=ALU.mult, op1=ALU.add)
                        if tc > 0:
                            last = e.scalar_tensor_tensor(
                                out=scr[:, 0:1],
                                in0=hedge[:, hcol * 2 * nb + 2 * (tc - 1):
                                          hcol * 2 * nb + 2 * (tc - 1) + 1],
                                scalar=cw0[:, hcol:hcol + 1], in1=scr[:, 0:1],
                                op0=ALU.mult, op1=ALU.add)
                        return last
                    stw("v", _c3, waits=[("V", v_mid)])
                    vm_c23 = P.val("V")
                    stw("s", lambda e, src=src, scr=scr: e.activation(
                        out=src, in_=scr, func=AF.Silu), waits=[("V", vm_c23)])
                    sm_silu[mc] = P.val("S")
                    stw("v", lambda e, src=src, rs=rs, sc2sb=sc2sb:
                        e.tensor_tensor(out=rs, in0=src, in1=sc2sb, op=ALU.mult),
                        waits=[("S", sm_silu[mc]), ("G", gm_cvt_par[mc % 2])])
                    vm_rm = P.val("V")
                    stw("s", lambda e, rs=rs: e.activation(
                        out=rs, in_=rs, func=AF.Identity, bias=c127),
                        waits=[("V", vm_rm)])
                    sm_127 = P.val("S")
                    stw("g", lambda e, rs=rs, isc=isc: e.tensor_copy(
                        out=isc, in_=rs), waits=[("S", sm_127)])
                    gm_cvt_par[mc % 2] = P.val("G")
                    g_cv = P.val("G")
                    grw = [("G", g_cv)]
                    if (tc - 1, qh) in tm_mm2:
                        grw.append(("T", tm_mm2[(tc - 1, qh)]))
                    stw("g", lambda e, isc=isc, kk=qh * 8 + mc: e.tensor_copy(
                        out=r2f[:, kk, :], in_=isc), waits=grw)
                    gm_cvt[mc] = P.val("G")

            if FFN0_PHASE <= 3:
                continue
            gm_r2 = {qh: gm_cvt[7] for qh in range(4)}
            for p in range(4):
                if p >= 2:
                    stw("g", lambda e, p=p: P.inc("D2", e.dma_start(
                        out=w2h[p % 2], in_=w2f_r[:, p * 8:p * 8 + 8, :]), 16)
                        and None, waits=[("T", tm_mm2[(tc, p - 2)]),
                                         ("D2", P.val("D2"))])
                    d2_w2[(tc, p)] = P.val("D2")

                def _m2(e, tc=tc, p=p):
                    last = None
                    for fm in range(8):
                        for kc in range(8):
                            k = p * 8 + kc
                            last = mm(e, bank(fm), w2h[p % 2][:, kc, ts(fm, 128)],
                                      r2f[:, k, :], k == 0, k == 31)
                    return last
                mw = [("D2", d2_w2[(tc, p)]), ("G", gm_cvt[7])]
                if p == 0:
                    mw.append(("V", P.val("V")))
                    mw += [("V", vm_bank_free[b]) for b in range(3, 8)]
                stw("t", _m2, waits=mw)
                tm_mm2[(tc, p)] = P.val("T")
            for fc in range(DC):
                fs = fscr[fc % 2]
                stw("v", lambda e, fc=fc, fs=fs, inv2sb=inv2sb:
                    e.scalar_tensor_tensor(
                        out=fs, in0=bank(fc), scalar=w2m["cs"][:, fc:fc + 1],
                        in1=inv2sb, op0=ALU.subtract, op1=ALU.mult),
                    waits=[("T", tm_mm2[(tc, 3)]), ("G", gm_fadd_par[fc % 2])])
                vm_f = P.val("V")
                stw("g", lambda e, fc=fc, fs=fs: e.tensor_tensor(
                    out=xq3[:, fc, :], in0=xq3[:, fc, :], in1=fs, op=ALU.add),
                    waits=[("V", vm_f), ("D3", d3_xq)])
                gm_fadd[fc] = P.val("G")
                gm_fadd_par[fc % 2] = P.val("G")
                vm_bank_free[fc] = vm_f
            stw("y", lambda e, tc=tc: P.inc(
                "D3", e.dma_start(out=xd_r[:, :, ts(tc)], in_=xq3), 16) and None,
                waits=[("G", gm_fadd[7]), ("D3", d3_xq)])
            d3_last = P.val("D3")

    def tr(e, out, in_, ident, tp="auto"):
        if tp == "auto":
            bp_i = in_.base_partition()
            bp_o = out.base_partition()
            tp = (bp_i, bp_o) if (bp_i or bp_o) else None
        return e.transpose(out, in_, ident, tile_position=tp)

    def tap(name, src_ap):
        if TAP != name:
            return
        n = src_ap.shape[-1] if src_ap.ndim == 2 else None
        def _(e):
            P.inc("DG", e.dma_start(out=dbg_out.ap()[:, 0:n], in_=src_ap), 16)
        P.step("g", _)

    def stage_att0(x_src, x_dst, qm, om):
        NT = S // 128
        TB = S // 16
        BPT = 128 // TB
        r13 = C1.rearrange("p (c t) -> p c t", c=DC)
        wq_r = wqkvq_d.ap().rearrange("(a p) c -> p a c", p=128)
        wo_r = woq_d.ap().rearrange("(a p) c -> p a c", p=128)
        wbuf = W1[:, 0:8192]
        wbuf3 = wbuf.rearrange("p (k m) -> p k m", k=DC)
        projT = C2.rearrange("p (c t) -> p c t", c=DC)
        inv4 = SM[:, 0:S]
        scrA = B1[:, 8192:8704]

        P.step("y", lambda e: P.inc("D0", e.dma_start(out=wbuf3, in_=wq_r), 16) and None)
        for fc in range(DC):
            for nq in range(NQ):
                bk = (fc * NQ + nq) % 3

                def _m(e, fc=fc, nq=nq, bk=bk):
                    last = None
                    for kc in range(DC):
                        last = mm(e, bank(bk), wbuf3[:, kc, ts(fc, 128)],
                                  r13[:, kc, ts(nq)], kc == 0, kc == DC - 1)
                    return last
                P.step("t", _m)
                P.step("v", lambda e, fc=fc, nq=nq, bk=bk: e.scalar_tensor_tensor(
                    out=scrA, in0=bank(bk), scalar=qm["cs"][:, fc:fc + 1],
                    in1=inv4[:, ts(nq)], op0=ALU.subtract, op1=ALU.mult))
                P.step("s", lambda e, fc=fc, nq=nq: e.activation(
                    out=projT[:, fc, ts(nq)], in_=scrA, func=AF.Identity,
                    scale=qm["alpha"][:, fc:fc + 1],
                    bias=sm2p["qkvb"][:, fc:fc + 1]))

        tap("proj", C2[:, 0:16384])
        def att0_epilogue(e):
            for nq in range(NQ):
                P.inc("D0", e.dma_start(out=x_dst.ap()[:, ts(nq)],
                                        in_=x_src.ap()[:, ts(nq)]), 16)
        if ATT0_PHASE < 2:
            P.step("y", att0_epilogue)
            return
        P.step("y", lambda e: P.inc("D0", e.dma_start(
            out=B1[:, 0:4 * S], in_=dr["cs4"].ap()), 16) and None)
        COSQ, SINQ = B1[:, 0:S], B1[:, S:2 * S]
        COSK, SINK = B1[:, 2 * S:3 * S], B1[:, 3 * S:4 * S]
        rotT, rotB = B1[:, 4 * S:5 * S], B1[:, 5 * S:6 * S]
        x1b, x2b = B1[:, 6 * S:7 * S], B1[:, 7 * S:8 * S]
        qlT = C1[:, 0:4 * S].rearrange("p (g t) -> p g t", g=4)
        klT = W2[:, 0:4 * S].rearrange("p (g t) -> p g t", g=4)

        for side in ("q", "k"):
            COS = COSQ if side == "q" else COSK
            SIN = SINQ if side == "q" else SINK
            dstT = qlT if side == "q" else klT
            for g in range(4):
                def _eo(e, g=g, side=side):
                    last = None
                    for j in range(4):
                        h = 4 * g + j
                        par = h % 2
                        lE = EOB4[:, 64 * par:64 * par + 32]
                        lO = EOB4[:, 64 * par + 32:64 * par + 64]
                        if side == "q":
                            for nq in range(NQ):
                                rhs = projT[:, h // 2, ts(nq)]
                                mm(e, PSA[32 * j:32 * j + 32, ts(nq)], lE, rhs,
                                   True, True, tp=(0, 32 * j))
                                last = mm(e, PSB[32 * j:32 * j + 32, ts(nq)], lO,
                                          rhs, True, True, tp=(0, 32 * j))
                        else:
                            for cb in range(16):
                                par2 = cb % 2
                                lE2 = EOB4[:, 64 * par2:64 * par2 + 32]
                                lO2 = EOB4[:, 64 * par2 + 32:64 * par2 + 64]
                                rhs = projT[:, cb // 2, h * TB:(h + 1) * TB]
                                mm(e, PSA[32 * j:32 * j + 32, cb * TB:(cb + 1) * TB],
                                   lE2, rhs, True, True, tp=(0, 32 * j))
                                last = mm(e, PSB[32 * j:32 * j + 32, cb * TB:(cb + 1) * TB],
                                          lO2, rhs, True, True, tp=(0, 32 * j))
                    return last
                P.step("t", _eo)
                if ATT0_PHASE == 21:
                    continue
                P.step("v", lambda e: e.tensor_tensor(out=x1b, in0=PSA[:, 0:S], in1=COS, op=ALU.mult))
                P.step("v", lambda e: e.tensor_tensor(out=x2b, in0=PSB[:, 0:S], in1=SIN, op=ALU.mult))
                P.step("v", lambda e: e.tensor_tensor(out=rotT, in0=x1b, in1=x2b, op=ALU.subtract))
                P.step("v", lambda e: e.tensor_tensor(out=x1b, in0=PSA[:, 0:S], in1=SIN, op=ALU.mult))
                P.step("v", lambda e: e.tensor_tensor(out=x2b, in0=PSB[:, 0:S], in1=COS, op=ALU.mult))
                P.step("v", lambda e: e.tensor_tensor(out=rotB, in0=x1b, in1=x2b, op=ALU.add))
                if ATT0_PHASE == 22:
                    continue

                def _u(e, g=g):
                    last = None
                    for j in range(4):
                        for nq in range(NQ):
                            mm(e, PSA[32 * j:32 * j + 32, ts(nq)],
                               U8[:, 64 * j:64 * j + 32],
                               rotT[:, ts(nq)], True, False, tp=(0, 32 * j))
                            last = mm(e, PSA[32 * j:32 * j + 32, ts(nq)],
                                      U8[:, 64 * j + 32:64 * j + 64],
                                      rotB[:, ts(nq)], False, True, tp=(0, 32 * j))
                    return last
                P.step("t", _u)
                if ATT0_PHASE == 23:
                    continue
                P.step("v", lambda e, g=g, dstT=dstT: e.tensor_copy(
                    out=dstT[:, g, :], in_=PSA[:, 0:S]))

        if ATT0_PHASE < 3:
            P.step("y", att0_epilogue)
            return
        tap("qlT", C1[:, 0:4 * S])
        tap("klT", W2[:, 0:4 * S])
        shcol = sm2p["att0_sh"]
        t1c = sm2p["att0_t1"]
        t2c = sm2p["att0_t2"]
        sqb = W2[:, 4 * S + 256:5 * S + 256]
        for nm, src_ in (("q", qlT), ("k", klT)):
            dst = t1c if nm == "q" else t2c
            first = [True]
            for g in range(4):
                P.step("v", lambda e, g=g, src_=src_: e.tensor_tensor(
                    out=sqb, in0=src_[:, g, :], in1=src_[:, g, :], op=ALU.mult))
                for nq in range(NQ):
                    P.step("t", lambda e, nq=nq: mm(
                        e, bank(7), BD32, sqb[:, ts(nq)], True, True))
                    P.step("v", lambda e: e.tensor_reduce(
                        out=sm2p["att0_tmp"], in_=bank(7), axis=AX.X, op=ALU.max))
                    if first[0]:
                        P.step("v", lambda e, dst=dst: e.tensor_copy(
                            out=dst, in_=sm2p["att0_tmp"]))
                        first[0] = False
                    else:
                        P.step("v", lambda e, dst=dst: e.tensor_tensor(
                            out=dst, in0=dst, in1=sm2p["att0_tmp"], op=ALU.max))
        P.step("v", lambda e: e.tensor_tensor(out=t1c, in0=t1c, in1=t2c, op=ALU.mult))
        P.step("t", lambda e: tr(e, bank(7)[0:1, 0:128], t1c, IDF))
        P.step("v", lambda e: e.tensor_reduce(
            out=sm2p["att0_tmp"][0:1, :], in_=bank(7)[0:1, 0:128], axis=AX.X, op=ALU.max))
        P.step("s", lambda e: e.activation(out=sm2p["att0_tmp"][0:1, :],
                                           in_=sm2p["att0_tmp"][0:1, :], func=AF.Sqrt))
        P.step("v", lambda e: e.tensor_scalar(
            out=sm2p["att0_tmp"][0:1, :], in0=sm2p["att0_tmp"][0:1, :],
            scalar1=-0.125, scalar2=None, op0=ALU.mult))
        P.step("t", lambda e: mm(e, bank(7)[:, 0:1], ONESF[0:1, :],
                                 sm2p["att0_tmp"][0:1, :], True, True))
        P.step("v", lambda e: e.tensor_copy(out=shcol, in_=bank(7)[:, 0:1]))

        if ATT0_PHASE < 4:
            P.step("y", att0_epilogue)
            return
        ytm = B1[:, 0:NT * 1024].rearrange("p (i f) -> p i f", i=NT)
        vkm = W2[:, 4 * S:4 * S + 65]
        Pt = W2[:, 4 * S + 256:4 * S + 256 + S]
        P.step("v", lambda e: e.memset(vkm[:, 64:65], 1.0))
        for h in range(16):
            for cb16 in range(NT):
                def _tv(e, h=h, cb16=cb16):
                    last = None
                    for sub in range(BPT):
                        cb = cb16 * BPT + sub
                        src_ = projT[:, cb // 2, h * TB:(h + 1) * TB]
                        last = mm(e, bank(4)[sub * TB:sub * TB + TB, 0:64], src_,
                                  TRSEL[:, 64 * (cb % 2):64 * (cb % 2) + 64],
                                  True, True, tp=(0, (sub * TB) % 128))
                    return last
                P.step("t", _tv)
                P.step("v", lambda e: e.tensor_copy(
                    out=vkm[:, 0:64], in_=bank(4)[:, 0:64]))

                def _st(e, h=h, cb16=cb16):
                    last = None
                    j, g = h % 4, h // 4
                    for qn in range(NQ):
                        last = mm(e, bank(4 + qn),
                                  klT[:, g, cb16 * 128:cb16 * 128 + 128][32 * j:32 * j + 32, :],
                                  qlT[:, g, ts(qn)][32 * j:32 * j + 32, :], True, True,
                                  tp=(32 * j, 0))
                    return last
                P.step("t", _st)
                P.step("s", lambda e: e.activation(
                    out=Pt, in_=PSB[:, 0:S], func=AF.Exp, bias=shcol, scale=0.125))

                def _y(e, h=h, cb16=cb16):
                    last = None
                    for qc in range(NQ):
                        last = mm(e, bank(qc)[0:65, :], vkm,
                                  Pt[:, ts(qc)], cb16 == 0, cb16 == NT - 1)
                    return last
                P.step("t", _y)
            for qc in range(NQ):
                P.step("v", lambda e, qc=qc: e.tensor_copy(
                    out=SM[0:65, 0:512], in_=bank(qc)[0:65, :]))

                def _tt(e):
                    last = None
                    for u in range(4):
                        last = tr(e, bank(4)[:, 65 * u:65 * u + 65],
                                  SM[0:65, 128 * u:128 * u + 128], IDF[0:65, 0:65])
                    return last
                P.step("t", _tt)
                P.step("v", lambda e: [
                    e.reciprocal(out=sm2p["att0_rc"][:, u:u + 1],
                                 in_=bank(4)[:, 65 * u + 64:65 * u + 65])
                    for u in range(4)][-1])
                P.step("v", lambda e, h=h, qc=qc: [
                    e.tensor_scalar(
                        out=ytm[:, qc * 4 + u, h * 64:(h + 1) * 64],
                        in0=bank(4)[:, 65 * u:65 * u + 64],
                        scalar1=sm2p["att0_rc"][:, u:u + 1], scalar2=None,
                        op0=ALU.mult)
                    for u in range(4)][-1])

        tap("ytm", B1[:, 0:16384])
        if ATT0_PHASE < 5:
            P.step("y", att0_epilogue)
            return
        inv4y = sm2p["att0_inv4y"]
        for i in range(NT):
            P.step("v", lambda e, i=i: e.tensor_reduce(
                out=sm2p["att0_tmp"], in_=ytm[:, i, :], axis=AX.X, op=ALU.max,
                apply_absolute_value=True))
            P.step("v", lambda e: e.tensor_single_scalar(
                out=sm2p["att0_tmp"], in_=sm2p["att0_tmp"], scalar=1e-6, op=ALU.max))
            P.step("v", lambda e, i=i: e.tensor_scalar(
                out=inv4y[:, i:i + 1], in0=sm2p["att0_tmp"], scalar1=1.0 / 7.0,
                scalar2=None, op0=ALU.mult))
            P.step("v", lambda e: e.reciprocal(out=sm2p["att0_tmp"], in_=sm2p["att0_tmp"]))
            P.step("v", lambda e, i=i: e.tensor_scalar(
                out=SM[:, 0:1024], in0=ytm[:, i, :], scalar1=sm2p["att0_tmp"],
                scalar2=7.0, op0=ALU.mult, op1=ALU.mult))
            P.step("v", lambda e: e.tensor_scalar(
                out=SM[:, 0:1024], in0=SM[:, 0:1024], scalar1=7.0, scalar2=None,
                op0=ALU.add))
            P.step("v", lambda e: e.tensor_copy(
                out=SM[:, 1024:2048].bitcast(I32), in_=SM[:, 0:1024]))
            P.step("v", lambda e, i=i: e.tensor_copy(
                out=C1[:, i * 1024:(i + 1) * 1024], in_=SM[:, 1024:2048].bitcast(I32)))
        for i in range(NT):
            def _tq(e, i=i):
                last = None
                for fc in range(DC):
                    last = tr(e, bank(fc % 4)[:, 0:128],
                              C1[:, i * 1024 + fc * 128:i * 1024 + (fc + 1) * 128], IDB)
                    if fc % 4 == 3:
                        pass
                return last
            def _tq1(e, i=i):
                last = None
                for fc in range(4):
                    last = tr(e, bank(fc).bitcast(BF16)[:, 0:128],
                              C1[:, i * 1024 + fc * 128:i * 1024 + (fc + 1) * 128], IDB)
                return last
            P.step("t", _tq1)
            P.step("v", lambda e, i=i: [
                e.tensor_copy(out=projT[:, fc, i * 128:(i + 1) * 128],
                              in_=bank(fc).bitcast(BF16)[:, 0:128]) for fc in range(4)][-1])

            def _tq2(e, i=i):
                last = None
                for fc in range(4, DC):
                    last = tr(e, bank(fc - 4).bitcast(BF16)[:, 0:128],
                              C1[:, i * 1024 + fc * 128:i * 1024 + (fc + 1) * 128], IDB)
                return last
            P.step("t", _tq2)
            P.step("v", lambda e, i=i: [
                e.tensor_copy(out=projT[:, fc, i * 128:(i + 1) * 128],
                              in_=bank(fc - 4).bitcast(BF16)[:, 0:128]) for fc in range(4, DC)][-1])

        tap("r4y", C1[:, 0:16384])
        tap("r4yfm", C2[:, 0:16384])
        if TAP == "sm2":
            def _tp(e):
                P.inc("DG", e.dma_start(out=dbg_out.ap()[:, 0:8], in_=om["alpha"]), 16)
                P.inc("DG", e.dma_start(out=dbg_out.ap()[:, 8:16], in_=om["cs"]), 16)
                P.inc("DG", e.dma_start(out=dbg_out.ap()[:, 16:16 + NT], in_=inv4y[:, 0:NT]), 16)
            P.step("g", _tp)
        P.step("y", lambda e: P.inc("D0", e.dma_start(out=wbuf3, in_=wo_r), 16) and None)
        xq = B1[:, 8192:12288]
        xq3 = xq.rearrange("p (c t) -> p c t", c=DC)
        x_r = x_src.ap().rearrange("(c p) t -> p c t", p=128)
        xd_r = x_dst.ap().rearrange("(c p) t -> p c t", p=128)
        inv4ysb = B1[:, 12288:12800]
        scr128c = B1[:, 12800:13312]
        scrA2 = B1[:, 13312:13824]
        for nq in range(NQ):
            P.step("y", lambda e, nq=nq: P.inc(
                "D0", e.dma_start(out=xq3, in_=x_r[:, :, ts(nq)]), 16) and None)
            bcast_tm_cols(inv4y[:, nq * 4:nq * 4 + 4], 4, scr128c, bank(4))
            P.step("v", lambda e: e.tensor_copy(out=inv4ysb, in_=bank(4)))
            for fc in range(DC):
                bk = fc % 3

                def _m(e, fc=fc, nq=nq, bk=bk):
                    last = None
                    for kc in range(DC):
                        last = mm(e, bank(bk), wbuf3[:, kc, ts(fc, 128)],
                                  projT[:, kc, ts(nq)], kc == 0, kc == DC - 1)
                    return last
                P.step("t", _m)
                P.step("v", lambda e, fc=fc, bk=bk: e.scalar_tensor_tensor(
                    out=scrA2, in0=bank(bk), scalar=om["cs"][:, fc:fc + 1],
                    in1=inv4ysb, op0=ALU.subtract, op1=ALU.mult))
                P.step("v", lambda e, fc=fc: e.scalar_tensor_tensor(
                    out=xq3[:, fc, :], in0=scrA2, scalar=om["alpha"][:, fc:fc + 1],
                    in1=xq3[:, fc, :], op0=ALU.mult, op1=ALU.add))
            P.step("y", lambda e, nq=nq: P.inc(
                "D0", e.dma_start(out=xd_r[:, :, ts(nq)], in_=xq3), 16) and None)

    def attn_core(qT3, kT3, vtm, heads, scale, causal, Pt, shiftcol, ytm3,
                  sqb, NT, cmask=None):
        t1c = sm2p["ac_t1"]
        t2c = sm2p["ac_t2"]
        tmpc = sm2p["ac_tmp"]
        nkc = kT3.shape[1]
        for nm, src_, nch in (("q", qT3, 8), ("k", kT3, nkc)):
            dst = t1c if nm == "q" else t2c
            first = True
            for cch in range(nch):
                P.step("v", lambda e, cch=cch, src_=src_: e.tensor_tensor(
                    out=sqb[:, 0:S], in0=src_[:, cch, 0:S], in1=src_[:, cch, 0:S],
                    op=ALU.mult))
                for nq in range(NQ):
                    P.step("t", lambda e, nq=nq: mm(
                        e, bank(7), BD64, sqb[:, ts(nq)], True, True))
                    P.step("v", lambda e: e.tensor_reduce(
                        out=tmpc, in_=bank(7), axis=AX.X, op=ALU.max))
                    if first:
                        P.step("v", lambda e, dst=dst: e.tensor_copy(out=dst, in_=tmpc))
                        first = False
                    else:
                        P.step("v", lambda e, dst=dst: e.tensor_tensor(
                            out=dst, in0=dst, in1=tmpc, op=ALU.max))
        P.step("v", lambda e: e.tensor_tensor(out=t1c, in0=t1c, in1=t2c, op=ALU.mult))
        P.step("t", lambda e: tr(e, bank(7)[0:1, 0:128], t1c, IDF))
        P.step("v", lambda e: e.tensor_reduce(
            out=tmpc[0:1, :], in_=bank(7)[0:1, 0:128], axis=AX.X, op=ALU.max))
        P.step("s", lambda e: e.activation(out=tmpc[0:1, :], in_=tmpc[0:1, :],
                                           func=AF.Sqrt))
        P.step("v", lambda e, scale=scale: e.tensor_scalar(
            out=tmpc[0:1, :], in0=tmpc[0:1, :], scalar1=-scale, scalar2=None,
            op0=ALU.mult))
        P.step("t", lambda e: mm(e, bank(7)[:, 0:1], ONESF[0:1, :],
                                 tmpc[0:1, :], True, True))
        P.step("v", lambda e: e.tensor_copy(out=shiftcol, in_=bank(7)[:, 0:1]))

        ew = P.barrier_waits()
        first_on = {"t": True, "s": True, "v": True}

        def stw(kind, fn, w):
            if first_on.get(kind):
                w = list(w) + ew
                first_on[kind] = False
            P.step(kind, fn, waits=w)

        vm_end = P.val("V")
        for (s, kchunk, kbase, vcol) in heads:
            sm_exp = {}
            tm_st = {}
            tm_y = {}
            vm_mask = {}
            prev_mt = None
            for mt in range(NT):
                qn0 = max(0, -(-(128 * mt - 511) // 512)) if causal else 0
                if causal and qn0 >= NQ:
                    continue
                pt = Pt[:, (mt % 2) * S:(mt % 2) * S + S]

                def _st(e, s=s, kchunk=kchunk, kbase=kbase, mt=mt, qn0=qn0):
                    last = None
                    ksl = kT3[:, kchunk, mt * 128:mt * 128 + 128][kbase:kbase + 64, :]
                    for qn in range(qn0, NQ):
                        last = mm(e, bank(4 + qn),
                                  ksl, qT3[:, s // 2, ts(qn)][64 * (s % 2):64 * (s % 2) + 64, :],
                                  True, True, tp=(kbase, 0))
                    return last
                w = [("V", vm_end)]
                if prev_mt is not None:
                    w.append(("S", sm_exp[prev_mt]))
                stw("t", _st, w)
                tm_st[mt] = P.val("T")

                def _exp(e, mt=mt, qn0=qn0, pt=pt):
                    return e.activation(
                        out=pt[:, qn0 * 512:S], in_=PSB[:, qn0 * 512:S],
                        func=AF.Exp, bias=shiftcol, scale=scale)
                w = [("T", tm_st[mt])]
                pp = prev_mt
                stw("s", _exp, w)
                sm_exp[mt] = P.val("S")
                if causal:
                    dq = mt // 4
                    stw("v", lambda e, mt=mt, dq=dq, pt=pt: e.tensor_tensor(
                        out=pt[:, ts(dq)], in0=pt[:, ts(dq)],
                        in1=cmask[:, ts(mt % 4)], op=ALU.mult),
                        [("S", sm_exp[mt])])
                    vm_mask[mt] = P.val("V")

                def _y(e, s=s, vcol=vcol, mt=mt, qn0=qn0, causal=causal, pt=pt):
                    last = None
                    for qc in range(qn0, NQ):
                        stop_mt = min(NT - 1, 4 * qc + 3) if causal else NT - 1
                        last = mm(e, bank(qc)[0:65, :], vtm(mt)[:, vcol:vcol + 65],
                                  pt[:, ts(qc)], mt == 0, mt == stop_mt)
                    return last
                w = [("S", sm_exp[mt])]
                if causal:
                    w.append(("V", vm_mask[mt]))
                stw("t", _y, w)
                tm_y[mt] = P.val("T")
                prev_mt = mt
            for qc in range(NQ):
                P.step("v", lambda e, qc=qc: e.tensor_copy(
                    out=SM[0:65, 0:512], in_=bank(qc)[0:65, :]),
                    waits=[("T", tm_y[prev_mt]), ("V", P.val("V"))])

                def _tt(e):
                    last = None
                    for u in range(4):
                        last = tr(e, bank(4)[:, 65 * u:65 * u + 65],
                                  SM[0:65, 128 * u:128 * u + 128], IDF[0:65, 0:65])
                    return last
                P.step("t", _tt, waits=[("V", P.val("V"))])
                P.step("v", lambda e: [
                    e.reciprocal(out=sm2p["ac_rc"][:, u:u + 1],
                                 in_=bank(4)[:, 65 * u + 64:65 * u + 65])
                    for u in range(4)][-1], waits=[("T", P.val("T"))])
                P.step("v", lambda e, s=s, qc=qc: [
                    e.tensor_scalar(
                        out=ytm3[:, qc * 4 + u, s * 64:(s + 1) * 64],
                        in0=bank(4)[:, 65 * u:65 * u + 64],
                        scalar1=sm2p["ac_rc"][:, u:u + 1], scalar2=None,
                        op0=ALU.mult)
                    for u in range(4)][-1], waits=[("V", P.val("V"))])
            vm_end = P.val("V")

    def y_to_fm(ytm3, NT):
        projT = C2.rearrange("p (c t) -> p c t", c=DC)
        for i in range(NT):
            P.step("v", lambda e, i=i: e.tensor_copy(
                out=C1[:, i * 1024:(i + 1) * 1024], in_=ytm3[:, i, :]))
        for i in range(NT):
            def _t1(e, i=i):
                last = None
                for fc in range(4):
                    last = tr(e, bank(fc).bitcast(BF16)[:, 0:128],
                              C1[:, i * 1024 + fc * 128:i * 1024 + (fc + 1) * 128], IDB)
                return last
            P.step("t", _t1)
            P.step("v", lambda e, i=i: [
                e.tensor_copy(out=projT[:, fc, i * 128:(i + 1) * 128],
                              in_=bank(fc).bitcast(BF16)[:, 0:128])
                for fc in range(4)][-1])

            def _t2(e, i=i):
                last = None
                for fc in range(4, DC):
                    last = tr(e, bank(fc - 4).bitcast(BF16)[:, 0:128],
                              C1[:, i * 1024 + fc * 128:i * 1024 + (fc + 1) * 128], IDB)
                return last
            P.step("t", _t2)
            P.step("v", lambda e, i=i: [
                e.tensor_copy(out=projT[:, fc, i * 128:(i + 1) * 128],
                              in_=bank(fc - 4).bitcast(BF16)[:, 0:128])
                for fc in range(4, DC)][-1])

    def proj_fm(rhs3, wbuf3, nmc, evac, kacc=DC):
        for fc in range(nmc):
            for nq in range(NQ):
                bk = (fc * NQ + nq) % 3

                def _m(e, fc=fc, nq=nq, bk=bk):
                    last = None
                    for kc in range(kacc):
                        last = mm(e, bank(bk), wbuf3[:, kc, ts(fc, 128)],
                                  rhs3[:, kc, ts(nq)], kc == 0, kc == kacc - 1)
                    return last
                P.step("t", _m)
                evac(fc, nq, bk)

    def stage_att1(x_src, x_dst):
        NT = S // 128
        ln3 = C1.rearrange("p (c t) -> p c t", c=DC)
        qT3 = C2.rearrange("p (c t) -> p c t", c=DC)
        kT3 = W2[:, 0:8 * S].rearrange("p (c t) -> p c t", c=DC)
        wb = W1[:, 0:8192]
        wb3 = wb.rearrange("p (k m) -> p k m", k=DC)

        Pt = W1[:, 8192:8192 + 2 * S]
        sqb = W1[:, 8192 + S:8192 + 2 * S]
        CMASK = W1[:, 8192 + 2 * S:8192 + 2 * S + 2048]
        vtm15 = W1[:, 8192 + 2 * S + 2048:8192 + 2 * S + 3088]
        P.step("g", lambda e: P.inc("DG", e.dma_start(
            out=CMASK, in_=dr["cmask"].ap()), 16) and None)

        P.step("g", lambda e: P.inc("DG", e.dma_start(
            out=wb3, in_=dr["a1_qw"].ap().rearrange("(a p) c -> p a c", p=128)), 16) and None)

        def ev_q(fc, nq, bk):
            P.step("v", lambda e, fc=fc, nq=nq, bk=bk: e.tensor_copy(
                out=qT3[:, fc, ts(nq)], in_=bank(bk)))
        proj_fm(ln3, wb3, DC, ev_q)
        P.step("g", lambda e: P.inc("DG", e.dma_start(
            out=wb3, in_=dr["a1_kw"].ap().rearrange("(a p) c -> p a c", p=128)), 16) and None)

        def ev_k(fc, nq, bk):
            P.step("v", lambda e, fc=fc, nq=nq, bk=bk: e.tensor_copy(
                out=kT3[:, fc, ts(nq)], in_=bank(bk)))
        proj_fm(ln3, wb3, DC, ev_k)
        vtmb = B1.bitcast(BF16)

        def vtm(i):
            if i == 15:
                return vtm15
            return vtmb[:, i * 1040:i * 1040 + 1040]
        P.step("g", lambda e: P.inc("DG", e.dma_start(
            out=wb3, in_=dr["a1_vw"].ap().rearrange("(a p) c -> p a c", p=128)), 16) and None)
        for i in range(NT):
            P.step("v", lambda e, i=i: e.memset(
                vtm(i).rearrange("p (h u) -> p h u", h=16)[:, :, 64:65], 1.0))
            for hv in range(16):
                def _mv(e, i=i, hv=hv):
                    last = None
                    for kc in range(DC):
                        last = mm(e, bank(3)[:, 0:64],
                                  ln3[:, kc, i * 128:(i + 1) * 128],
                                  wb3[:, kc, hv * 64:(hv + 1) * 64],
                                  kc == 0, kc == DC - 1)
                    return last
                P.step("t", _mv)
                P.step("v", lambda e, i=i, hv=hv: e.tensor_copy(
                    out=vtm(i)[:, hv * 65:hv * 65 + 64], in_=bank(3)[:, 0:64]))

        heads = [(h, h // 2, 64 * (h % 2), 65 * h) for h in range(16)]
        ytmb = vtmb[:, 16384:32768].rearrange("p (i f) -> p i f", i=16)
        attn_core(qT3, kT3, vtm, heads, 1.0 / 32.0, True, Pt,
                  sm2p["a1_sh"], ytmb, sqb, NT, cmask=CMASK)

        for i in range(NT):
            P.step("v", lambda e, i=i: e.tensor_copy(
                out=C1[:, i * 1024:(i + 1) * 1024], in_=ytmb[:, i, :]))
        projT = C2.rearrange("p (c t) -> p c t", c=DC)
        for i in range(NT):
            def _t1(e, i=i):
                last = None
                for fc in range(4):
                    last = tr(e, bank(fc).bitcast(BF16)[:, 0:128],
                              C1[:, i * 1024 + fc * 128:i * 1024 + (fc + 1) * 128], IDB)
                return last
            P.step("t", _t1)
            P.step("v", lambda e, i=i: [
                e.tensor_copy(out=projT[:, fc, i * 128:(i + 1) * 128],
                              in_=bank(fc).bitcast(BF16)[:, 0:128])
                for fc in range(4)][-1])

            def _t2(e, i=i):
                last = None
                for fc in range(4, DC):
                    last = tr(e, bank(fc - 4).bitcast(BF16)[:, 0:128],
                              C1[:, i * 1024 + fc * 128:i * 1024 + (fc + 1) * 128], IDB)
                return last
            P.step("t", _t2)
            P.step("v", lambda e, i=i: [
                e.tensor_copy(out=projT[:, fc, i * 128:(i + 1) * 128],
                              in_=bank(fc - 4).bitcast(BF16)[:, 0:128])
                for fc in range(4, DC)][-1])

        P.step("g", lambda e: P.inc("DG", e.dma_start(
            out=wb3, in_=dr["a1_ow"].ap().rearrange("(a p) c -> p a c", p=128)), 16) and None)
        xq = B1[:, 8192:12288]
        x_r = x_src.ap().rearrange("(c p) t -> p c t", p=128)
        xd_r = x_dst.ap().rearrange("(c p) t -> p c t", p=128)
        xq3 = xq.rearrange("p (c t) -> p c t", c=DC)
        for nq in range(NQ):
            P.step("y", lambda e, nq=nq: P.inc(
                "D0", e.dma_start(out=xq3, in_=x_r[:, :, ts(nq)]), 16) and None)
            for fc in range(DC):
                bk = fc % 3

                def _m(e, fc=fc, nq=nq, bk=bk):
                    last = None
                    for kc in range(DC):
                        last = mm(e, bank(bk), wb3[:, kc, ts(fc, 128)],
                                  projT[:, kc, ts(nq)], kc == 0, kc == DC - 1)
                    return last
                P.step("t", _m)
                P.step("v", lambda e, fc=fc, bk=bk: e.tensor_tensor(
                    out=xq3[:, fc, :], in0=xq3[:, fc, :], in1=bank(bk), op=ALU.add))
            P.step("y", lambda e, nq=nq: P.inc(
                "D0", e.dma_start(out=xd_r[:, :, ts(nq)], in_=xq3), 16) and None)

    def stage_mix1(x_src, x_dst):
        NT = S // 128
        CAP = float(int(1.25 * NCORES * S * 2 / NEXP))
        ln3 = C1.rearrange("p (c t) -> p c t", c=DC)
        qT3 = C2.rearrange("p (c t) -> p c t", c=DC)
        kT3 = W2[:, 0:2 * S].rearrange("p (c t) -> p c t", c=2)
        wb = W1[:, 0:8192]
        wb3 = wb.rearrange("p (k m) -> p k m", k=DC)
        Pt = W1[:, 8192:8192 + 2 * S]
        sqb = W1[:, 8192 + S:8192 + 2 * S]
        vbb = W1[:, 8192 + 2 * S:8192 + 2 * S + 256]
        gwsb = W1[:, 8192 + 2 * S + 256:8192 + 2 * S + 320]
        x_r = x_src.ap().rearrange("(c p) t -> p c t", p=128)
        xd_r = x_dst.ap().rearrange("(c p) t -> p c t", p=128)

        P.step("g", lambda e: P.inc("DG", e.dma_start(
            out=wb3, in_=dr["g_qw"].ap().rearrange("(a p) c -> p a c", p=128)), 16) and None)

        def ev_q(fc, nq, bk):
            P.step("s", lambda e, fc=fc, nq=nq, bk=bk: e.activation(
                out=qT3[:, fc, ts(nq)], in_=bank(bk), func=AF.Identity,
                bias=sm2p["g_qb"][:, fc:fc + 1]))
        proj_fm(ln3, wb3, DC, ev_q)
        P.step("g", lambda e: P.inc("DG", e.dma_start(
            out=wb3[:, :, 0:256],
            in_=dr["g_kw"].ap().rearrange("(a p) c -> p a c", p=128)), 16) and None)

        def ev_k(fc, nq, bk):
            P.step("s", lambda e, fc=fc, nq=nq, bk=bk: e.activation(
                out=kT3[:, fc, ts(nq)], in_=bank(bk), func=AF.Identity,
                bias=sm2p["g_kb"][:, fc:fc + 1]))
        for fc in range(2):
            for nq in range(NQ):
                bk = (fc * NQ + nq) % 3

                def _m(e, fc=fc, nq=nq, bk=bk):
                    last = None
                    for kc in range(DC):
                        last = mm(e, bank(bk), wb3[:, kc, fc * 128:fc * 128 + 128],
                                  ln3[:, kc, ts(nq)], kc == 0, kc == DC - 1)
                    return last
                P.step("t", _m)
                ev_k(fc, nq, bk)
        P.step("y", lambda e: P.inc("D0", e.dma_start(
            out=SM[0:1, 0:256], in_=dr["g_vbr"].ap()), 16) and None)
        P.step("t", lambda e: mm(e, bank(3)[:, 0:256], ONESF[0:1, :],
                                 SM[0:1, 0:256], True, True))
        P.step("v", lambda e: e.tensor_copy(out=vbb, in_=bank(3)[:, 0:256]))
        P.step("g", lambda e: P.inc("DG", e.dma_start(
            out=wb3[:, :, 0:256],
            in_=dr["g_vw"].ap().rearrange("(a p) c -> p a c", p=128)), 16) and None)
        vtmr = W2[:, 2 * S:2 * S + NT * 260]

        def vtm(i):
            return vtmr[:, i * 260:i * 260 + 260]
        for i in range(NT):
            P.step("v", lambda e, i=i: e.memset(
                vtm(i).rearrange("p (h u) -> p h u", h=4)[:, :, 64:65], 1.0))
            for kv in range(NKV):
                def _mv(e, i=i, kv=kv):
                    last = None
                    for kc in range(DC):
                        last = mm(e, bank(3)[:, 0:64],
                                  ln3[:, kc, i * 128:(i + 1) * 128],
                                  wb3[:, kc, kv * 64:kv * 64 + 64],
                                  kc == 0, kc == DC - 1)
                    return last
                P.step("t", _mv)
                P.step("v", lambda e, i=i, kv=kv: e.tensor_tensor(
                    out=vtm(i)[:, kv * 65:kv * 65 + 64], in0=bank(3)[:, 0:64],
                    in1=vbb[:, kv * 64:kv * 64 + 64], op=ALU.add))

        def sig(h):
            return 8 * (h // 8) + 2 * (h % 4) + (h // 4) % 2
        heads = [(sig(h), (h // 4) // 2, 64 * ((h // 4) % 2), 65 * (h // 4))
                 for h in range(16)]
        ytmf = B1[:, 0:NT * 1024].rearrange("p (i f) -> p i f", i=NT)
        attn_core(qT3, kT3, vtm, heads, 1.0 / 8.0, False, Pt,
                  sm2p["gq_sh"], ytmf, sqb, NT)
        y_to_fm(ytmf, NT)
        projT = C2.rearrange("p (c t) -> p c t", c=DC)

        P.step("g", lambda e: P.inc("DG", e.dma_start(
            out=wb3, in_=dr["g_ow"].ap().rearrange("(a p) c -> p a c", p=128)), 16) and None)
        s13 = B1.rearrange("p (c t) -> p c t", c=DC)
        xch = SM[:, 0:512]
        scrS = SM[:, 512:1024]
        meanb = SM[:, 1024:1536]
        rstdb = SM[:, 1536:2048]
        scr128m = C2[:, 0:1024].bitcast(F32) if False else None
        for nq in range(NQ):
            bcast_tm_cols(sm2p["lnmean_tm"][:, nq * 4:nq * 4 + 4], 4,
                          B1[:, 0:512] if False else SM[:, 512:1024], bank(4))
            P.step("v", lambda e: e.tensor_copy(out=meanb, in_=bank(4)))
            bcast_tm_cols(sm2p["lnrstd_tm"][:, nq * 4:nq * 4 + 4], 4,
                          SM[:, 512:1024], bank(4))
            P.step("v", lambda e: e.tensor_copy(out=rstdb, in_=bank(4)))
            for fc in range(DC):
                bk = fc % 3

                def _m(e, fc=fc, nq=nq, bk=bk):
                    last = None
                    for kc in range(DC):
                        last = mm(e, bank(bk), wb3[:, kc, ts(fc, 128)],
                                  projT[:, kc, ts(nq)], kc == 0, kc == DC - 1)
                    return last
                P.step("t", _m)
                P.step("y", lambda e, fc=fc, nq=nq: P.inc(
                    "D0", e.dma_start(out=xch, in_=x_r[:, fc, ts(nq)]), 16) and None)
                P.step("s", lambda e, fc=fc, bk=bk: e.activation(
                    out=scrS, in_=bank(bk), func=AF.Identity,
                    bias=sm2p["g_ob"][:, fc:fc + 1]))
                P.step("v", lambda e: e.tensor_tensor(
                    out=xch, in0=xch, in1=meanb, op=ALU.subtract))
                P.step("v", lambda e: e.tensor_tensor(
                    out=xch, in0=xch, in1=rstdb, op=ALU.mult))
                P.step("v", lambda e, fc=fc, nq=nq: e.tensor_tensor(
                    out=s13[:, fc, ts(nq)], in0=xch, in1=scrS, op=ALU.add))

        for nq in range(NQ):
            for fc in range(DC):
                P.step("v", lambda e, fc=fc, nq=nq: e.tensor_copy(
                    out=ln3[:, fc, ts(nq)], in_=s13[:, fc, ts(nq)]))
            for fc in range(DC):
                P.step("v", lambda e, fc=fc, nq=nq: e.tensor_tensor(
                    out=C2[:, fc * 512:(fc + 1) * 512], in0=ln3[:, fc, ts(nq)],
                    in1=ln3[:, fc, ts(nq)], op=ALU.mult))

            def _mmv(e, nq=nq):
                last = None
                for fc in range(DC):
                    last = mm(e, bank(0), ONESB, C2[:, fc * 512:(fc + 1) * 512],
                              fc == 0, fc == DC - 1)
                return last
            P.step("t", _mmv)
            P.step("v", lambda e: e.tensor_scalar(
                out=rstdb, in0=bank(0), scalar1=1.0 / D, scalar2=1e-6,
                op0=ALU.mult, op1=ALU.add))
            P.step("v", lambda e: e.reciprocal(out=rstdb, in_=rstdb))
            P.step("s", lambda e: e.activation(out=rstdb, in_=rstdb, func=AF.Sqrt))
            for fc in range(DC):
                P.step("v", lambda e, fc=fc, nq=nq: e.scalar_tensor_tensor(
                    out=s13[:, fc, ts(nq)], in0=s13[:, fc, ts(nq)],
                    scalar=sm2p["n1w"][:, fc:fc + 1], in1=rstdb,
                    op0=ALU.mult, op1=ALU.mult))
                P.step("v", lambda e, fc=fc, nq=nq: e.tensor_copy(
                    out=ln3[:, fc, ts(nq)], in_=s13[:, fc, ts(nq)]))

        P.step("y", lambda e: P.inc("D0", e.dma_start(
            out=sm2p["gwf"].rearrange("p (k g) -> p k g", k=DC),
            in_=dr["gate_w"].ap().rearrange("(a p) g -> p a g", p=128)), 16) and None)
        gw3 = sm2p["gwf"].rearrange("p (k g) -> p k g", k=DC)
        gpt = SM[:, 0:128]
        srt = SM[:, 128:256]
        em1 = SM[:, 384:512]
        em2 = SM[:, 512:640]
        vm = SM[:, 1152:1280]
        mfin = SM[:, 1280:1408]
        emtb = C2[:, 0:128]

        def _gmm(e):
            last = None
            for i in range(NT):
                for kc in range(DC):
                    last = mm(e, bank(0)[:, 8 * i:8 * i + 8],
                              s13[:, kc, i * 128:(i + 1) * 128],
                              gw3[:, kc, :], kc == 0, kc == DC - 1)
            return last
        P.step("t", _gmm)
        P.step("s", lambda e: e.activation(out=gpt, in_=bank(0)[:, 0:128],
                                           func=AF.Sigmoid))
        for i in range(NT):
            sl8 = slice(8 * i, 8 * i + 8)
            P.step("v", lambda e, sl8=sl8: e.max(out=srt[:, sl8], in_=gpt[:, sl8]))
            P.step("v", lambda e, sl8=sl8, i=i: e.tensor_scalar(
                out=em1[:, sl8], in0=gpt[:, sl8], scalar1=srt[:, 8 * i:8 * i + 1],
                scalar2=None, op0=ALU.is_equal))
            P.step("v", lambda e, sl8=sl8, i=i: e.tensor_scalar(
                out=em2[:, sl8], in0=gpt[:, sl8], scalar1=srt[:, 8 * i + 1:8 * i + 2],
                scalar2=None, op0=ALU.is_equal))
            P.step("v", lambda e, i=i: e.tensor_tensor(
                out=sm2p["mx_ws"][:, i:i + 1], in0=srt[:, 8 * i:8 * i + 1],
                in1=srt[:, 8 * i + 1:8 * i + 2], op=ALU.add))
            P.step("v", lambda e, i=i: e.tensor_scalar(
                out=sm2p["mx_ws"][:, i:i + 1], in0=sm2p["mx_ws"][:, i:i + 1],
                scalar1=1e-6, scalar2=None, op0=ALU.add))
            P.step("v", lambda e, i=i: e.reciprocal(
                out=sm2p["mx_ws"][:, i:i + 1], in_=sm2p["mx_ws"][:, i:i + 1]))
            P.step("v", lambda e, i=i: e.tensor_tensor(
                out=sm2p["mx_w1n"][:, i:i + 1], in0=srt[:, 8 * i:8 * i + 1],
                in1=sm2p["mx_ws"][:, i:i + 1], op=ALU.mult))
            P.step("v", lambda e, i=i: e.tensor_tensor(
                out=sm2p["mx_w2n"][:, i:i + 1], in0=srt[:, 8 * i + 1:8 * i + 2],
                in1=sm2p["mx_ws"][:, i:i + 1], op=ALU.mult))
            P.step("v", lambda e, sl8=sl8: e.tensor_tensor(
                out=emtb[:, sl8], in0=em1[:, sl8], in1=em2[:, sl8], op=ALU.add))
            P.step("t", lambda e, sl8=sl8: mm(
                e, bank(1)[:, sl8], UTB, emtb[:, sl8], True, True))
            P.step("v", lambda e, sl8=sl8: e.tensor_copy(
                out=SM[:, 256:384][:, sl8], in_=bank(1)[:, sl8]))
            P.step("t", lambda e, sl8=sl8: mm(
                e, bank(2)[:, sl8], ONESB, emtb[:, sl8], True, True))
            P.step("v", lambda e, sl8=sl8, i=i: e.tensor_copy(
                out=SM[0:1, 640 + 8 * i:640 + 8 * i + 8], in_=bank(2)[0:1, sl8]))
        totE = SM[0:1, 768:768 + 256].rearrange("p (g u) -> p g u", g=NEXP)
        P.step("v", lambda e: e.memset(SM[0:1, 768:1024], 0.0))
        P.step("v", lambda e: e.tensor_copy(
            out=totE[:, :, 0:NT],
            in_=SM[0:1, 640:640 + 8 * NT].rearrange("p (i g) -> p g i", g=NEXP)))
        for sh in (1, 2, 4, 8):
            if sh >= NT:
                continue
            P.step("v", lambda e, sh=sh: e.tensor_tensor(
                out=totE[:, :, sh:NT], in0=totE[:, :, sh:NT],
                in1=totE[:, :, 0:NT - sh], op=ALU.add))
        P.step("v", lambda e: e.tensor_copy(
            out=SM[0:1, 1041:1049],
            in_=totE[:, :, NT - 1:NT].rearrange("p g u -> p (g u)")))
        P.step("y", lambda e: P.inc("D0", e.dma_start(
            out=cc_in_d.ap(), in_=SM[0:1, 1041:1049]), 16) and None)
        P.step("g", lambda e: P.inc("CC", e.collective_compute(
            "AllGather", ALU.bypass, ins=[cc_in_d.ap()], outs=[cc_out_d.ap()],
            replica_groups=[list(range(NCORES))]), 1) and None)
        P.step("y", lambda e: P.inc("D0", e.dma_start(
            out=SM[0:NCORES, 1024:1032], in_=cc_out_d.ap()), 16) and None)
        P.step("y", lambda e: P.inc("D0", e.dma_start(
            out=SM[0:NEXP, 1040:1041], in_=dr["prefmask"].ap()), 16) and None)
        P.step("t", lambda e: mm(e, bank(3)[0:1, 0:8], SM[0:NEXP, 1040:1041],
                                 SM[0:NCORES, 1024:1032], True, True))
        P.step("v", lambda e: e.tensor_copy(out=SM[0:1, 1032:1040],
                                            in_=bank(3)[0:1, 0:8]))
        for i in range(NT):
            sl8 = slice(8 * i, 8 * i + 8)
            if i == 0:
                P.step("v", lambda e: e.tensor_copy(
                    out=SM[0:1, 1048:1056], in_=SM[0:1, 1032:1040]))
            else:
                P.step("v", lambda e, i=i: e.tensor_tensor(
                    out=SM[0:1, 1048:1056], in0=SM[0:1, 1032:1040],
                    in1=totE[:, :, i - 1:i].rearrange("p g u -> p (g u)"),
                    op=ALU.add))
            P.step("v", lambda e: e.tensor_scalar(
                out=SM[0:1, 1048:1056], in0=SM[0:1, 1048:1056],
                scalar1=-1.0, scalar2=CAP, op0=ALU.mult, op1=ALU.add))
            P.step("t", lambda e: mm(
                e, bank(3)[:, 0:8], ONESF[0:1, :], SM[0:1, 1048:1056],
                True, True))
            P.step("v", lambda e, sl8=sl8: e.tensor_tensor(
                out=vm[:, sl8], in0=SM[:, 256:384][:, sl8], in1=bank(3)[:, 0:8],
                op=ALU.is_lt))
            P.step("v", lambda e, sl8=sl8: e.tensor_tensor(
                out=mfin[:, sl8], in0=em1[:, sl8], in1=vm[:, sl8], op=ALU.mult))
            P.step("v", lambda e, sl8=sl8, i=i: e.tensor_scalar(
                out=mfin[:, sl8], in0=mfin[:, sl8],
                scalar1=sm2p["mx_w1n"][:, i:i + 1], scalar2=None, op0=ALU.mult))
            P.step("v", lambda e, sl8=sl8: e.tensor_tensor(
                out=vm[:, sl8], in0=em2[:, sl8], in1=vm[:, sl8], op=ALU.mult))
            P.step("v", lambda e, sl8=sl8, i=i: e.tensor_scalar(
                out=vm[:, sl8], in0=vm[:, sl8],
                scalar1=sm2p["mx_w2n"][:, i:i + 1], scalar2=None, op0=ALU.mult))
            P.step("v", lambda e, sl8=sl8: e.tensor_tensor(
                out=mfin[:, sl8], in0=mfin[:, sl8], in1=vm[:, sl8], op=ALU.add))

        w1e3 = W1.rearrange("p (k m) -> p k m", k=DC)
        w2e3 = W2.rearrange("p (k m) -> p k m", k=16)
        hb3 = C2[:, 0:8192].rearrange("p (k t) -> p k t", k=16)
        mfb = SM[:, 1408:1920]
        P.step("g", lambda e: P.inc("DG", e.dma_start(
            out=w1e3, in_=dr["moe_w1"].ap()[0].rearrange(
                "(a p) c -> p a c", p=128)), 16) and None)
        P.step("g", lambda e: P.inc("DG2", e.dma_start(
            out=w2e3, in_=dr["moe_w2"].ap()[0].rearrange(
                "(a p) c -> p a c", p=128)), 16) and None)
        dg_w1 = P.val("DG")
        dg_w2 = P.val("DG2")
        bank_user_v = [P.val("V")] * 8
        gelu_mark = [P.val("S")] * 16
        tm_mm2_prev = P.val("T")
        for ex in range(NEXP):
            tm_mm1_last = None
            for nq in range(NQ):
                bcast_tm_cols(SM[:, 1280 + ex:1280 + ex + 8 * NT:8][:, nq * 4:nq * 4 + 4],
                              4, SM[:, 512:1024], bank(4))
                P.step("v", lambda e: e.tensor_copy(out=mfb, in_=bank(4)))
                vm_mfb = P.val("V")
                bank_user_v[4] = vm_mfb
                for mc in range(16):
                    bk = mc % 3

                    def _m1(e, mc=mc, nq=nq, bk=bk):
                        last = None
                        for kc in range(DC):
                            last = mm(e, bank(bk), w1e3[:, kc, mc * 128:(mc + 1) * 128],
                                      ln3[:, kc, ts(nq)], kc == 0, kc == DC - 1)
                        return last
                    P.step("t", _m1, waits=[("DG", dg_w1),
                                            ("S", gelu_mark[mc - 3] if mc >= 3 else gelu_mark[13 + mc]),
                                            ("V", bank_user_v[bk])])
                    tmark = P.val("T")
                    P.step("s", lambda e, mc=mc, bk=bk: e.activation(
                        out=hb3[:, mc, :], in_=bank(bk), func=AF.Gelu),
                        waits=[("T", tmark), ("T", tm_mm2_prev)])
                    gelu_mark[mc] = P.val("S")
                tm_mm1_last = P.val("T")
                if nq == NQ - 1 and ex < NEXP - 1:
                    P.step("g", lambda e, ex=ex: P.inc("DG", e.dma_start(
                        out=w1e3, in_=dr["moe_w1"].ap()[ex + 1].rearrange(
                            "(a p) c -> p a c", p=128)), 16) and None,
                        waits=[("T", tm_mm1_last)])
                    dg_w1 = P.val("DG")
                for fm in range(DC):
                    def _m2(e, fm=fm, nq=nq):
                        last = None
                        for kc in range(16):
                            last = mm(e, bank(fm), w2e3[:, kc, fm * 128:(fm + 1) * 128],
                                      hb3[:, kc, :], kc == 0, kc == 15)
                        return last
                    P.step("t", _m2, waits=[("DG2", dg_w2), ("S", gelu_mark[15]),
                                            ("V", bank_user_v[fm])])
                    tm2 = P.val("T")
                    P.step("v", lambda e, fm=fm: e.tensor_tensor(
                        out=bank(fm), in0=bank(fm), in1=mfb, op=ALU.mult),
                        waits=[("T", tm2), ("V", vm_mfb)])
                    P.step("v", lambda e, fm=fm, nq=nq: e.tensor_tensor(
                        out=s13[:, fm, ts(nq)], in0=s13[:, fm, ts(nq)],
                        in1=bank(fm), op=ALU.add), waits=[("V", P.val("V"))])
                    bank_user_v[fm] = P.val("V")
                tm_mm2_prev = P.val("T")
                if nq == NQ - 1 and ex < NEXP - 1:
                    P.step("g", lambda e, ex=ex: P.inc("DG2", e.dma_start(
                        out=w2e3, in_=dr["moe_w2"].ap()[ex + 1].rearrange(
                            "(a p) c -> p a c", p=128)), 16) and None,
                        waits=[("T", tm_mm2_prev)])
                    dg_w2 = P.val("DG2")

        for nq in range(NQ):
            for fc in range(DC):
                P.step("v", lambda e, fc=fc, nq=nq: e.tensor_copy(
                    out=C1[:, fc * 512:(fc + 1) * 512], in_=s13[:, fc, ts(nq)]))
            for fc in range(DC):
                P.step("v", lambda e, fc=fc: e.tensor_tensor(
                    out=C2[:, fc * 512:(fc + 1) * 512], in0=C1[:, fc * 512:(fc + 1) * 512],
                    in1=C1[:, fc * 512:(fc + 1) * 512], op=ALU.mult))

            def _mmv2(e, nq=nq):
                last = None
                for fc in range(DC):
                    last = mm(e, bank(0), ONESB, C2[:, fc * 512:(fc + 1) * 512],
                              fc == 0, fc == DC - 1)
                return last
            P.step("t", _mmv2)
            P.step("v", lambda e: e.tensor_scalar(
                out=rstdb, in0=bank(0), scalar1=1.0 / D, scalar2=1e-6,
                op0=ALU.mult, op1=ALU.add))
            P.step("v", lambda e: e.reciprocal(out=rstdb, in_=rstdb))
            P.step("s", lambda e: e.activation(out=rstdb, in_=rstdb, func=AF.Sqrt))
            for fc in range(DC):
                P.step("y", lambda e, fc=fc, nq=nq: P.inc(
                    "D0", e.dma_start(out=xch, in_=x_r[:, fc, ts(nq)]), 16) and None)
                P.step("v", lambda e, fc=fc, nq=nq: e.scalar_tensor_tensor(
                    out=scrS, in0=s13[:, fc, ts(nq)],
                    scalar=sm2p["hn2w"][:, fc:fc + 1], in1=rstdb,
                    op0=ALU.mult, op1=ALU.mult))
                P.step("v", lambda e, fc=fc, nq=nq: e.tensor_tensor(
                    out=xch, in0=xch, in1=scrS, op=ALU.add))
                P.step("y", lambda e, fc=fc, nq=nq: P.inc(
                    "D0", e.dma_start(out=xd_r[:, fc, ts(nq)], in_=xch), 16) and None)

    for nm, wdt, ncol in [("cw0", None, 32), ("cw1", None, 32), ("cw2", None, 32),
                          ("cb", None, 32)]:
        sm2col(nm, ncol)
    load_cols(dr["convw"].ap()[:, 0:32], sm2p["cw0"])
    load_cols(dr["convw"].ap()[:, 32:64], sm2p["cw1"])
    load_cols(dr["convw"].ap()[:, 64:96], sm2p["cw2"])
    load_cols(dr["convb"].ap(), sm2p["cb"])
    sm2col("hedge", 32 * 2 * max(NQ - 1, 1))
    sm2col("amax2_tm", S // 128)
    sm2col("sc2_tm", S // 128)
    sm2col("inv2_tm", S // 128)
    sm2col("qkvb", DC)
    load_cols(dr["qkv_b"].ap(), sm2p["qkvb"])
    sm2col("att0_sh", 1)
    sm2col("att0_t1", 1)
    sm2col("att0_t2", 1)
    sm2col("att0_tmp", 1)
    sm2col("att0_rc", 4)
    sm2col("att0_inv4y", S // 128)
    sm2col("ac_t1", 1); sm2col("ac_t2", 1); sm2col("ac_tmp", 1)
    sm2col("ac_rc", 4); sm2col("a1_sh", 1); sm2col("gq_sh", 1)
    sm2col("lnmean_tm", S // 128); sm2col("lnrstd_tm", S // 128)
    sm2col("g_qb", DC); load_cols(dr["g_qb"].ap(), sm2p["g_qb"])
    sm2col("g_kb", 2); load_cols(dr["g_kb"].ap(), sm2p["g_kb"])
    sm2col("g_ob", DC); load_cols(dr["g_ob"].ap(), sm2p["g_ob"])
    sm2col("n1w", DC); load_cols(dr["norm1_w"].ap(), sm2p["n1w"])
    sm2col("hn2w", DC); load_cols(dr["norm2_w"].ap(), sm2p["hn2w"])
    P.step("v", lambda e: e.tensor_scalar(
        out=sm2p["hn2w"], in0=sm2p["hn2w"], scalar1=0.5, scalar2=None,
        op0=ALU.mult))
    sm2col("gwf", 64)
    sm2col("mx_ws", S // 128)
    sm2col("mx_w1n", S // 128)
    sm2col("mx_w2n", S // 128)

    w1m = stash_bitlinear(dr["w1T"], w1q_d, D, HID0, "w1")
    w2m = stash_bitlinear(dr["w2T"], w2q_d, HID0, D, "w2")
    qm = stash_ternary(dr["wqkvT"], wqkvq_d, "wq", 7.0)
    om = stash_ternary(dr["woT"], woq_d, "wo", 7.0)

    marks = {}

    def mark(name):
        marks[name] = P.nsteps
    mark("setup")
    seq = [i for i in SEQ if i <= stop]
    for pos, sid in enumerate(seq):
        mark(f"pre{pos}_{sid}")
        src_t = dr["x_fm"] if pos == 0 else x_mir
        dst_t = y_out if pos == len(seq) - 1 else x_mir
        if sid in (1, 3):
            stage_ln(src_t, "q8")
            stage_ffn0(src_t, dst_t, w1m, w2m)
        elif sid == 2:
            stage_ln(src_t, "q4")
            tap("r4", C1[:, 0:16384])
            tap("inv4", SM[:, 0:S])
            stage_att0(src_t, dst_t, qm, om)
        elif sid == 5:
            stage_ln(src_t, "bf")
            stage_att1(src_t, dst_t)
        elif sid in (4, 6):
            stage_ln(src_t, "bf", save_stats=True)
            stage_mix1(src_t, dst_t)

    mark("end")
    if os.environ.get("KPROF"):
        prev = 0
        for k2, v2 in marks.items():
            print(f"  steps {k2}: +{v2 - prev} (cum {v2})")
            prev = v2
    P.step("y", lambda e: None)
    es.close()
    return nc


def _prep_inputs(inputs, S):
    f32 = np.float32
    g = {}
    g["w1T"] = np.ascontiguousarray(inputs["L0_ffn_w1"].T.astype(f32))
    g["w2T"] = np.ascontiguousarray(inputs["L0_ffn_w2"].T.astype(f32))
    def cols(v):
        v = np.asarray(v, f32).reshape(-1)
        return np.ascontiguousarray(v.reshape(-1, 128).T)
    cw = np.asarray(inputs["L0_ffn_conv_w"], f32).reshape(HID0, 3)
    g["convw"] = np.concatenate([cols(cw[:, k]) for k in range(3)], 1)
    g["convb"] = cols(inputs["L0_ffn_conv_b"])
    g["wqkvT"] = np.ascontiguousarray(np.asarray(inputs["L0_attn_Wqkv_w"], f32).T)
    g["woT"] = np.ascontiguousarray(np.asarray(inputs["L0_attn_Wo_w"], f32).T)
    g["qkv_b"] = cols(inputs["L0_attn_Wqkv_b"])
    U = np.asarray(inputs["L0_attn_U"], f32)
    u8 = np.zeros((128, 256), f32)
    for j in range(4):
        u8[32 * j:32 * j + 32, 64 * j:64 * j + 32] = U[0:32, :]
        u8[32 * j:32 * j + 32, 64 * j + 32:64 * j + 64] = U[32:64, :]
    g["u8"] = u8
    half = 32
    inv = 1.0 / 10000.0 ** (np.arange(half, dtype=f32) / half)
    pos_q = np.arange(S, dtype=f32)
    cb = np.arange(S) // (S // 16)
    tl = np.arange(S) % (S // 16)
    pos_k = (tl * 16 + cb).astype(f32)
    cs4 = np.empty((128, 4 * S), f32)
    cs4[:, 0 * S:1 * S] = np.tile(np.cos(pos_q[None, :] * inv[:, None]), (4, 1))
    cs4[:, 1 * S:2 * S] = np.tile(np.sin(pos_q[None, :] * inv[:, None]), (4, 1))
    cs4[:, 2 * S:3 * S] = np.tile(np.cos(pos_k[None, :] * inv[:, None]), (4, 1))
    cs4[:, 3 * S:4 * S] = np.tile(np.sin(pos_k[None, :] * inv[:, None]), (4, 1))
    g["cs4"] = cs4
    perm = np.empty(D, np.int64)
    for h in range(H16):
        for j in range(DK):
            perm[64 * h + j] = j * 16 + h
    g["a1_qw"] = np.ascontiguousarray(np.asarray(inputs["L1_attn_qw"], f32)[:, perm])
    g["a1_kw"] = np.ascontiguousarray(np.asarray(inputs["L1_attn_kw"], f32)[:, perm])
    g["a1_vw"] = np.ascontiguousarray(np.asarray(inputs["L1_attn_vw"], f32)[:, perm])
    g["a1_ow"] = np.ascontiguousarray(np.asarray(inputs["L1_attn_ow"], f32)[perm, :])
    def sig(h):
        return 8 * (h // 8) + 2 * (h % 4) + (h // 4) % 2
    qperm = np.empty(D, np.int64)
    for h in range(H16):
        for j in range(DK):
            qperm[64 * sig(h) + j] = 64 * h + j
    g["g_qw"] = np.ascontiguousarray(np.asarray(inputs["L1_ffn_qw"], f32)[:, qperm])
    g["g_qb"] = cols(np.asarray(inputs["L1_ffn_qb"], f32)[qperm])
    g["g_kw"] = np.asarray(inputs["L1_ffn_kw"], f32)
    g["g_kb"] = cols(inputs["L1_ffn_kb"])
    g["g_vw"] = np.asarray(inputs["L1_ffn_vw"], f32)
    g["g_vb"] = cols(inputs["L1_ffn_vb"])
    g["g_vbr"] = np.asarray(inputs["L1_ffn_vb"], f32).reshape(1, NKV * DK)
    g["g_ow"] = np.ascontiguousarray(np.asarray(inputs["L1_ffn_ow"], f32)[qperm, :])
    g["g_ob"] = cols(inputs["L1_ffn_ob"])
    g["gate_w"] = np.asarray(inputs["L1_ffn_gate_w"], f32)
    g["moe_w1"] = np.asarray(inputs["L1_ffn_w1"], f32)
    g["moe_w2"] = np.asarray(inputs["L1_ffn_w2"], f32)
    g["norm1_w"] = cols(inputs["L1_ffn_norm1_w"])
    g["norm2_w"] = cols(inputs["L1_ffn_norm2_w"])
    g["ident_f"] = np.eye(128, dtype=f32)
    g["ident_b"] = np.eye(128).astype(ml_dtypes.bfloat16)
    g["ones_b"] = np.ones((128, 128), ml_dtypes.bfloat16)
    g["ones_f"] = np.ones((128, 128), f32)
    g["ut_b"] = np.triu(np.ones((128, 128)), 1).astype(ml_dtypes.bfloat16)
    bd = np.zeros((128, 128), f32)
    bd[0:64, 0:64] = 1.0
    bd[64:128, 64:128] = 1.0
    g["bd64_b"] = bd.astype(ml_dtypes.bfloat16)
    eo4 = np.zeros((128, 128), f32)
    for j in range(32):
        eo4[2 * j, j] = 1.0; eo4[2 * j + 1, 32 + j] = 1.0
        eo4[64 + 2 * j, 64 + j] = 1.0; eo4[64 + 2 * j + 1, 96 + j] = 1.0
    g["eo_b"] = eo4.astype(ml_dtypes.bfloat16)
    trs = np.zeros((128, 128), f32)
    trs[0:64, 0:64] = np.eye(64)
    trs[64:128, 64:128] = np.eye(64)
    g["trsel_b"] = trs.astype(ml_dtypes.bfloat16)
    bd32 = np.zeros((128, 128), f32)
    for j in range(4):
        bd32[32 * j:32 * (j + 1), 32 * j:32 * (j + 1)] = 1.0
    g["bd32_b"] = bd32.astype(ml_dtypes.bfloat16)
    g["iotaE"] = np.broadcast_to(np.arange(NEXP, dtype=f32), (128, NEXP)).copy()
    cm = np.zeros((128, 2048), f32)
    for r in range(4):
        mi = np.arange(128)[:, None]
        qi = np.arange(512)[None, :]
        cm[:, r * 512:(r + 1) * 512] = (qi >= 128 * r + mi)
    g["cmask"] = cm.astype(ml_dtypes.bfloat16)
    g["onecol_b"] = np.ones((128, 1), ml_dtypes.bfloat16)

    x = np.asarray(inputs["L0_attn_x"], f32)
    per_core = []
    for c in range(NCORES):
        m = dict(g)
        m["x_fm"] = np.ascontiguousarray(x[c].T)
        pm = np.zeros((NEXP, 1), f32)
        pm[:c if c <= NEXP else NEXP] = 1.0
        m["prefmask"] = pm
        per_core.append(m)
    return per_core


def kernel(**inputs):
    S = np.asarray(inputs["L0_attn_x"]).shape[1]
    per_core = _prep_inputs(inputs, S)
    nc = build_program(S=S)
    res = run_bass_kernel_spmd(nc, per_core, core_ids=list(range(NCORES)))
    outs = [r["y_fm"] for r in res.results]
    return np.stack([o.T for o in outs], 0).astype(np.float32)


# version 09
# speedup vs baseline: 1.1869x; 1.0108x over previous
import math
import os
from contextlib import ExitStack

import numpy as np
import ml_dtypes

import concourse.bass as bass
import concourse.mybir as mybir
from concourse.bass_utils import run_bass_kernel_spmd

F32 = mybir.dt.float32
BF16 = mybir.dt.bfloat16
I32 = mybir.dt.int32
AF = mybir.ActivationFunctionType
ALU = mybir.AluOpType
AX = mybir.AxisListType

NCORES = 8
D = 1024
DC = D // 128
H16 = 16
DK = 64
HID0 = 4096
NEXP = 8
EHID = 2048
NKV = 4

STOP_AFTER = 99
SEQ = [1, 2, 3, 4, 5, 6]
TAP = None
ATT0_PHASE = 9
FFN0_PHASE = 9

ALLSEMS = ["T", "V", "S", "G", "D0", "D1", "D2", "D3", "DG", "DG2", "CC"]


def ts(i, n=512):
    return slice(i * n, (i + 1) * n)


class Prog:
    def __init__(self, nc, S, block, es):
        self.nc, self.S, self.block, self.es = nc, S, block, es
        self.sems = {}
        for nm in ALLSEMS:
            self.sems[nm] = [es.enter_context(nc.semaphore(nm)), 0]
        self.emit = {"t": block.tensor, "v": block.vector, "s": block.scalar,
                     "g": block.gpsimd, "y": block.sync}
        self.osem = {"t": "T", "v": "V", "s": "S", "g": "G", "y": None}
        self.nsteps = 0
        self.lastw = {k: {} for k in self.emit}

    def inc(self, name, instr, k=1):
        s = self.sems[name]
        instr.then_inc(s[0], k)
        s[1] += k
        return s[1]

    def val(self, name):
        return self.sems[name][1]

    def step(self, kind, fn, waits=None):
        if waits is None:
            wl = [(nm, self.sems[nm][1]) for nm in ALLSEMS
                  if self.sems[nm][1] > 0]
        else:
            wl = []
            for w in waits:
                if isinstance(w, tuple):
                    if w[1] > 0:
                        wl.append(w)
                elif self.sems[w][1] > 0:
                    wl.append((w, self.sems[w][1]))
        myname = self.osem[kind]
        self.nsteps += 1

        lw = self.lastw[kind]
        wl = [(nm, v) for nm, v in wl if lw.get(nm, 0) < v]
        for nm, v in wl:
            lw[nm] = v

        def _(e):
            for nm, v in wl:
                e.wait_ge(self.sems[nm][0], v)
            last = fn(e)
            if myname is not None and last is not None:
                self.inc(myname, last)
        self.emit[kind](_)

    def barrier_waits(self):
        return [(nm, self.sems[nm][1]) for nm in ALLSEMS]


def mm(eng, out, lhsT, rhs, start, stop, tp="auto"):
    if tp == "auto":
        bp_l = lhsT.base_partition()
        bp_o = out.base_partition()
        tp = (bp_l, bp_o) if (bp_l or bp_o) else None
    return eng.matmul(out, lhsT, rhs, start=start, stop=stop, tile_position=tp)


def build_program(S=2048, stop_after=None):
    stop = STOP_AFTER if stop_after is None else stop_after
    nc = bass.Bass("TRN2", target_bir_lowering=False, debug=False)
    es = ExitStack()
    NQ = S // 512

    dr = {}

    def din(name, shape, dt=F32):
        dr[name] = nc.dram_tensor(name, shape, dt, kind="ExternalInput")
        return dr[name]

    din("x_fm", [D, S])
    din("w1T", [D, HID0]); din("w2T", [HID0, D])
    din("convw", [128, 3 * HID0 // 128]); din("convb", [128, HID0 // 128])
    din("wqkvT", [D, D]); din("woT", [D, D]); din("qkv_b", [128, DC])
    din("u8", [128, 256]); din("cs4", [128, 4 * S])
    din("a1_qw", [D, D]); din("a1_kw", [D, D]); din("a1_vw", [D, D]); din("a1_ow", [D, D])
    din("g_qw", [D, D]); din("g_qb", [128, DC])
    din("g_kw", [D, NKV * DK]); din("g_kb", [128, 2])
    din("g_vw", [D, NKV * DK]); din("g_vb", [128, 2]); din("g_vbr", [1, NKV * DK])
    din("g_ow", [D, D]); din("g_ob", [128, DC])
    din("gate_w", [D, NEXP])
    din("moe_w1", [NEXP, D, EHID]); din("moe_w2", [NEXP, EHID, D])
    din("norm1_w", [128, DC]); din("norm2_w", [128, DC])
    din("ident_f", [128, 128]); din("ident_b", [128, 128], BF16)
    din("ones_b", [128, 128], BF16); din("ones_f", [128, 128])
    din("ut_b", [128, 128], BF16)
    din("bd64_b", [128, 128], BF16)
    din("bd32_b", [128, 128], BF16)
    din("eo_b", [128, 128], BF16)
    din("trsel_b", [128, 128], BF16)
    din("iotaE", [128, NEXP])
    din("cmask", [128, 2048], BF16)
    din("prefmask", [NEXP, 1])
    din("onecol_b", [128, 1], BF16)

    y_out = nc.dram_tensor("y_fm", [D, S], F32, kind="ExternalOutput")
    dbg_out = nc.dram_tensor("dbg", [128, 16384], F32, kind="ExternalOutput") if TAP else None
    x_mir = nc.dram_tensor("x_mirror", [D, S], F32, kind="Internal")
    w1q_d = nc.dram_tensor("w1q_stash", [D, HID0], BF16, kind="Internal")
    w2q_d = nc.dram_tensor("w2q_stash", [HID0, D], BF16, kind="Internal")
    wqkvq_d = nc.dram_tensor("wqkvq_stash", [D, D], BF16, kind="Internal")
    woq_d = nc.dram_tensor("woq_stash", [D, D], BF16, kind="Internal")
    cc_in_d = nc.dram_tensor("cc_in", [1, NEXP], F32, kind="Internal")
    cc_out_d = nc.dram_tensor("cc_out", [NCORES, NEXP], F32, kind="Internal")

    big = 16384
    B1 = es.enter_context(nc.sbuf_tensor("B1", [128, big], F32))
    C1 = es.enter_context(nc.sbuf_tensor("C1", [128, big], BF16))
    C2 = es.enter_context(nc.sbuf_tensor("C2", [128, big], BF16))
    W1 = es.enter_context(nc.sbuf_tensor("W1", [128, 16384], BF16))
    W2 = es.enter_context(nc.sbuf_tensor("W2", [128, 16384], BF16))
    SM = es.enter_context(nc.sbuf_tensor("SM", [128, max(S, 2048)], F32))
    SM2 = es.enter_context(nc.sbuf_tensor("SM2", [128, 960], F32))
    CNf = es.enter_context(nc.sbuf_tensor("CNf", [128, 512 + NEXP], F32))
    CNb = es.enter_context(nc.sbuf_tensor("CNb", [128, 897], BF16))
    PSA = es.enter_context(nc.psum_tensor("PSA", [128, 2048], F32))
    PSB = es.enter_context(nc.psum_tensor("PSB", [128, 2048], F32))

    block = es.enter_context(nc.Block())
    P = Prog(nc, S, block, es)

    def bank(i, n=1):
        assert 0 <= i and i + n <= 8
        if i < 4:
            assert i + n <= 4
            return PSA[:, i * 512:(i + n) * 512]
        assert i >= 4
        return PSB[:, (i - 4) * 512:(i - 4 + n) * 512]

    IDF = CNf[:, 0:128]
    ONESF = CNf[:, 128:256]
    IOTAE = CNf[:, 256:256 + NEXP]
    IDB = CNb[:, 0:128]
    ONESB = CNb[:, 128:256]
    UTB = CNb[:, 256:384]
    BD64 = CNb[:, 384:512]
    EOB4 = CNb[:, 512:640]
    ONECOL = CNb[:, 640:641]
    BD32 = CNb[:, 641:769]
    TRSEL = CNb[:, 769:897]
    U8 = CNf[:, 256 + NEXP:512 + NEXP]

    sm2p = {"n": 0}

    def sm2col(name, n=1):
        c = sm2p["n"]; sm2p["n"] = c + n
        assert sm2p["n"] <= 960, (name, sm2p["n"])
        sm2p[name] = SM2[:, c:c + n]
        return sm2p[name]

    def _(e):
        for dst, src in [(IDF, "ident_f"), (ONESF, "ones_f"), (IOTAE, "iotaE"),
                         (IDB, "ident_b"), (ONESB, "ones_b"), (UTB, "ut_b"),
                         (BD64, "bd64_b"), (BD32, "bd32_b"), (EOB4, "eo_b"),
                         (TRSEL, "trsel_b"),
                         (ONECOL, "onecol_b"), (U8, "u8")]:
            P.inc("D0", e.dma_start(out=dst, in_=dr[src].ap()), 16)
    P.step("y", _)

    def load_cols(src_ap, dst_cols):
        def _(e):
            P.inc("D0", e.dma_start(out=dst_cols, in_=src_ap), 16)
        P.step("y", _)

    def rows_to_cols(src_sbuf, n, dst_cols, scale=1.0, pbank=6):
        for g0 in range(0, n, 4):
            g = min(4, n - g0)

            def _t(e, g0=g0, g=g):
                last = None
                for j in range(g):
                    last = e.transpose(bank(pbank)[:, ts(j, 128)],
                                       src_sbuf[:, ts(g0 + j, 128)], IDF)
                return last
            P.step("t", _t)

            def _s(e, g0=g0, g=g):
                last = None
                for j in range(g):
                    last = e.activation(out=dst_cols[:, g0 + j:g0 + j + 1],
                                        in_=bank(pbank)[:, ts(j, 128)][:, 0:1],
                                        func=AF.Copy, scale=float(scale))
                return last
            P.step("s", _s)

    def bcast_tm_cols(tm_cols, nsub, scr128, out_psum, tbank=7):
        def _t1(e):
            last = None
            for j in range(nsub):
                last = e.transpose(bank(tbank)[0:1, j * 128:(j + 1) * 128],
                                   tm_cols[:, j:j + 1], IDF)
            return last
        P.step("t", _t1)
        P.step("v", lambda e: e.tensor_copy(out=scr128[0:1, 0:128 * nsub],
                                            in_=bank(tbank)[0:1, 0:128 * nsub]))

        def _t2(e):
            last = None
            for j in range(nsub):
                last = mm(e, out_psum[:, ts(j, 128)], ONESF[0:1, :],
                          scr128[0:1, j * 128:(j + 1) * 128], True, True)
            return last
        P.step("t", _t2)

    def stash_bitlinear(win, stash, rows, cols, tag):
        nch, npc = rows // 128, cols // 1024
        npan = nch * npc
        sl, scr, scr2 = B1[:, 0:1024], B1[:, 1024:2048], B1[:, 2048:3072]
        acc = sm2col(tag + "_abs", npan)
        tot = sm2col(tag + "_tot")
        inv = sm2col(tag + "_inv")
        scol = sm2col(tag + "_s")
        cs = sm2col(tag + "_cs", cols // 128)
        win_r = win.ap().rearrange("(a p) c -> p a c", p=128)
        st_r = stash.ap().rearrange("(a p) c -> p a c", p=128)

        for i in range(npan):
            a, j = divmod(i, npc)
            P.step("y", lambda e, a=a, j=j: P.inc(
                "D0", e.dma_start(out=sl, in_=win_r[:, a, ts(j, 1024)]), 16) and None)
            P.step("s", lambda e, i=i: e.activation(
                out=scr, in_=sl, func=AF.Abs, accum_out=acc[:, i:i + 1]))
        P.step("v", lambda e: e.tensor_reduce(out=tot, in_=acc, axis=AX.X, op=ALU.add))
        P.step("t", lambda e: mm(e, bank(7)[:, 0:1], ONESF, tot, True, True))
        P.step("v", lambda e: e.tensor_scalar(
            out=inv, in0=bank(7)[:, 0:1], scalar1=1.0 / (rows * cols),
            scalar2=1e-5, op0=ALU.mult, op1=ALU.max))
        P.step("v", lambda e: e.reciprocal(out=scol, in_=inv))

        qp = C1[:, 0:1024]
        for i in range(npan):
            a, j = divmod(i, npc)
            P.step("y", lambda e, a=a, j=j: P.inc(
                "D0", e.dma_start(out=sl, in_=win_r[:, a, ts(j, 1024)]), 16) and None)
            P.step("v", lambda e: e.tensor_scalar(
                out=scr, in0=sl, scalar1=scol, scalar2=1.0,
                op0=ALU.mult, op1=ALU.add))
            P.step("v", lambda e: e.tensor_copy(
                out=B1[:, 3072:4096].bitcast(I32), in_=scr))
            P.step("v", lambda e: e.tensor_copy(
                out=scr2, in_=B1[:, 3072:4096].bitcast(I32)))
            P.step("v", lambda e: e.tensor_scalar(
                out=scr2, in0=scr2, scalar1=1.0, scalar2=1.0,
                op0=ALU.subtract, op1=ALU.min))
            P.step("v", lambda e: e.tensor_scalar(
                out=qp, in0=scr2, scalar1=-1.0, scalar2=None, op0=ALU.max))
            P.step("y", lambda e, a=a, j=j: P.inc(
                "D0", e.dma_start(out=st_r[:, a, ts(j, 1024)], in_=qp), 16) and None)

        wp = W1[:, 0:1024]
        csb = B1[:, 0:1024]
        for j in range(npc):
            for a in range(nch):
                P.step("y", lambda e, a=a, j=j: P.inc(
                    "D0", e.dma_start(out=wp, in_=st_r[:, a, ts(j, 1024)]), 16) and None)
                P.step("t", lambda e, a=a: [
                    mm(e, bank(4), ONESB, wp[:, ts(0)], a == 0, a == nch - 1),
                    mm(e, bank(5), ONESB, wp[:, ts(1)], a == 0, a == nch - 1)][-1])
            P.step("v", lambda e: e.tensor_copy(out=csb[:, 0:512], in_=bank(4)))
            P.step("v", lambda e: e.tensor_copy(out=csb[:, 512:1024], in_=bank(5)))
            rows_to_cols(csb, 8, cs[:, j * 8:(j + 1) * 8], 127.0)
        return dict(inv=inv, s=scol, cs=cs)

    def stash_ternary(win, stash, tag, qoff=7.0):
        rows = cols = D
        nch, npc = rows // 128, cols // 1024
        sl, scr = B1[:, 0:1024], B1[:, 1024:2048]
        acc = sm2col(tag + "_abs", nch * npc)
        tot = sm2col(tag + "_tot")
        thr = sm2col(tag + "_thr")
        alpha = sm2col(tag + "_alpha", cols // 128)
        cs = sm2col(tag + "_cs", cols // 128)
        win_r = win.ap().rearrange("(a p) c -> p a c", p=128)
        st_r = stash.ap().rearrange("(a p) c -> p a c", p=128)
        absb = B1[:, 2048:3072]

        for j in range(npc):
            for a in range(nch):
                P.step("y", lambda e, a=a, j=j: P.inc(
                    "D0", e.dma_start(out=sl, in_=win_r[:, a, ts(j, 1024)]), 16) and None)
                P.step("s", lambda e, i=a * npc + j: e.activation(
                    out=scr, in_=sl, func=AF.Abs, accum_out=acc[:, i:i + 1]))
                P.step("v", lambda e: e.tensor_copy(out=C1[:, 0:1024], in_=scr))
                P.step("t", lambda e, a=a: [
                    mm(e, bank(4), ONESB, C1[:, 0:512], a == 0, a == nch - 1),
                    mm(e, bank(5), ONESB, C1[:, 512:1024], a == 0, a == nch - 1)][-1])
            P.step("v", lambda e: e.tensor_copy(out=absb[:, 0:512], in_=bank(4)))
            P.step("v", lambda e: e.tensor_copy(out=absb[:, 512:1024], in_=bank(5)))
            rows_to_cols(absb, 8, alpha[:, j * 8:(j + 1) * 8], 1.0 / rows)
        P.step("v", lambda e: e.tensor_reduce(out=tot, in_=acc, axis=AX.X, op=ALU.add))
        P.step("t", lambda e: mm(e, bank(7)[:, 0:1], ONESF, tot, True, True))
        P.step("v", lambda e: e.tensor_scalar(
            out=thr, in0=bank(7)[:, 0:1], scalar1=0.05 / (rows * cols),
            scalar2=None, op0=ALU.mult))

        qp = C1[:, 0:1024]
        sgn = B1[:, 2048:3072]
        msk = B1[:, 3072:4096]
        for i in range(nch * npc):
            a, j = divmod(i, npc)
            P.step("y", lambda e, a=a, j=j: P.inc(
                "D0", e.dma_start(out=sl, in_=win_r[:, a, ts(j, 1024)]), 16) and None)
            P.step("s", lambda e: e.sign(out=sgn, in_=sl))
            P.step("s", lambda e: e.activation(out=scr, in_=sl, func=AF.Abs))
            P.step("v", lambda e: e.tensor_scalar(
                out=msk, in0=scr, scalar1=thr, scalar2=None, op0=ALU.is_ge))
            P.step("v", lambda e: e.tensor_tensor(out=qp, in0=sgn, in1=msk, op=ALU.mult))
            P.step("y", lambda e, a=a, j=j: P.inc(
                "D0", e.dma_start(out=st_r[:, a, ts(j, 1024)], in_=qp), 16) and None)

        wp = W1[:, 0:1024]
        csb = B1[:, 0:1024]
        for j in range(npc):
            for a in range(nch):
                P.step("y", lambda e, a=a, j=j: P.inc(
                    "D0", e.dma_start(out=wp, in_=st_r[:, a, ts(j, 1024)]), 16) and None)
                P.step("t", lambda e, a=a: [
                    mm(e, bank(4), ONESB, wp[:, ts(0)], a == 0, a == nch - 1),
                    mm(e, bank(5), ONESB, wp[:, ts(1)], a == 0, a == nch - 1)][-1])
            P.step("v", lambda e: e.tensor_copy(out=csb[:, 0:512], in_=bank(4)))
            P.step("v", lambda e: e.tensor_copy(out=csb[:, 512:1024], in_=bank(5)))
            rows_to_cols(csb, 8, cs[:, j * 8:(j + 1) * 8], float(qoff))
        return dict(alpha=alpha, cs=cs)

    def stage_ln(x_src, mode, save_stats=False):
        Qv = {"q8": 127.0, "q4": 7.0}.get(mode)
        floor_ = {"q8": 1e-5, "q4": 1e-6}.get(mode)
        x_r = x_src.ap().rearrange("(c p) t -> p c t", p=128)
        C13 = C1.rearrange("p (c t) -> p c t", c=DC)
        for q in range(NQ):
            xq = B1[:, 0:4096]
            xq3 = xq.rearrange("p (c t) -> p c t", c=DC)
            lnq = B1[:, 4096:8192]
            lnq3 = lnq.rearrange("p (c t) -> p c t", c=DC)
            rsc = B1[:, 8192:10240]
            slotA = B1[:, 12288:12800]
            slotB = B1[:, 12800:13312]
            slotC = B1[:, 13312:13824]
            slotM = B1[:, 13824:14336]
            tmcol = B1[:, 14336:14464]
            tmcol2 = B1[:, 14464:14592]
            scr128 = B1[:, 14592:15104]
            xb = C2[:, 0:4096]
            xb3 = xb.rearrange("p (c t) -> p c t", c=DC)
            sq = C2[:, 4096:8192]
            sq3 = sq.rearrange("p (c t) -> p c t", c=DC)

            P.step("y", lambda e, q=q: P.inc(
                "D0", e.dma_start(out=xq3, in_=x_r[:, :, ts(q)]), 16) and None)
            P.step("v", lambda e: e.tensor_copy(out=xb, in_=xq))
            P.step("v", lambda e: e.tensor_tensor(out=sq, in0=xb, in1=xb, op=ALU.mult))

            def _mm(e):
                last = None
                for c in range(DC):
                    mm(e, bank(0), ONESB, xb3[:, c, :], c == 0, c == DC - 1)
                    last = mm(e, bank(1), ONESB, sq3[:, c, :], c == 0, c == DC - 1)
                return last
            P.step("t", _mm)
            P.step("v", lambda e: [
                e.tensor_scalar(out=slotA, in0=bank(0), scalar1=1.0 / D,
                                scalar2=None, op0=ALU.mult),
                e.tensor_scalar(out=slotB, in0=bank(1), scalar1=1.0 / D,
                                scalar2=None, op0=ALU.mult)][-1])
            P.step("v", lambda e: e.tensor_tensor(out=slotC, in0=slotA,
                                                  in1=slotA, op=ALU.mult))
            P.step("v", lambda e: e.scalar_tensor_tensor(
                out=slotB, in0=slotB, scalar=1e-5, in1=slotC,
                op0=ALU.add, op1=ALU.subtract))
            P.step("v", lambda e: e.reciprocal(out=slotB, in_=slotB))
            P.step("s", lambda e: e.activation(out=slotC, in_=slotB, func=AF.Sqrt))
            if save_stats:
                rows_to_cols(slotA, 4, sm2p["lnmean_tm"][:, q * 4:q * 4 + 4])
                rows_to_cols(slotC, 4, sm2p["lnrstd_tm"][:, q * 4:q * 4 + 4])

            for c in range(DC):
                P.step("v", lambda e, c=c: e.tensor_tensor(
                    out=lnq3[:, c, :], in0=xq3[:, c, :], in1=slotA, op=ALU.subtract))
                P.step("v", lambda e, c=c: e.tensor_tensor(
                    out=lnq3[:, c, :], in0=lnq3[:, c, :], in1=slotC, op=ALU.mult))
                if mode == "bf":
                    P.step("v", lambda e, c=c, q=q: e.tensor_copy(
                        out=C13[:, c, ts(q)], in_=lnq3[:, c, :]))
                else:
                    slotAB = B1[:, 10240:10752]
                    P.step("s", lambda e, c=c: e.activation(
                        out=slotAB, in_=lnq3[:, c, :], func=AF.Abs))
                    if c == 0:
                        P.step("v", lambda e: e.tensor_copy(out=slotM, in_=slotAB))
                    else:
                        P.step("v", lambda e: e.tensor_tensor(
                            out=slotM, in0=slotM, in1=slotAB, op=ALU.max))
            if mode == "bf":
                continue

            def _t(e):
                last = None
                for j in range(4):
                    last = e.transpose(bank(2)[:, ts(j, 128)],
                                       slotM[:, ts(j, 128)], IDF)
                return last
            P.step("t", _t)
            P.step("v", lambda e: [
                e.tensor_reduce(out=tmcol[:, j:j + 1],
                                in_=bank(2)[:, ts(j, 128)], axis=AX.X, op=ALU.max)
                for j in range(4)][-1])
            P.step("v", lambda e: e.tensor_single_scalar(
                out=tmcol[:, 0:4], in_=tmcol[:, 0:4], scalar=floor_, op=ALU.max))
            P.step("v", lambda e: e.tensor_scalar(
                out=tmcol2[:, 0:4], in0=tmcol[:, 0:4], scalar1=1.0 / Qv,
                scalar2=None, op0=ALU.mult))
            P.step("v", lambda e: e.reciprocal(out=tmcol[:, 0:4], in_=tmcol[:, 0:4]))
            P.step("v", lambda e: e.tensor_scalar(
                out=tmcol[:, 0:4], in0=tmcol[:, 0:4], scalar1=Qv,
                scalar2=None, op0=ALU.mult))
            bcast_tm_cols(tmcol[:, 0:4], 4, scr128, bank(3))
            bcast_tm_cols(tmcol2[:, 0:4], 4, scr128, bank(2))
            P.step("v", lambda e, q=q: e.tensor_copy(out=SM[:, ts(q)], in_=bank(2)))

            for hh in range(4):
                def _r1(e, hh=hh):
                    last = None
                    for c in range(2):
                        last = e.tensor_tensor(
                            out=rsc[:, ts(c)], in0=lnq3[:, hh * 2 + c, :],
                            in1=bank(3), op=ALU.mult)
                    return last
                P.step("v", _r1)
                P.step("v", lambda e: e.tensor_scalar(
                    out=rsc[:, 0:1024], in0=rsc[:, 0:1024], scalar1=Qv,
                    scalar2=None, op0=ALU.add))
                P.step("v", lambda e: e.tensor_copy(
                    out=B1[:, 15104:16128].bitcast(I32), in_=rsc[:, 0:1024]))

                def _w(e, hh=hh, q=q):
                    last = None
                    for c in range(2):
                        last = e.tensor_copy(
                            out=C13[:, hh * 2 + c, ts(q)],
                            in_=B1[:, 15104 + 512 * c:15104 + 512 * (c + 1)].bitcast(I32))
                    return last
                P.step("v", _w)

    def cq3_pa(mc):
        return B1.rearrange("p (m t) -> p m t", m=8)[:, mc, 0:S]

    def stage_ffn0(x_src, x_dst, w1m, w2m):
        r13 = C1.rearrange("p (c t) -> p c t", c=DC)
        w1q_r = w1q_d.ap().rearrange("(a p) c -> p a c", p=128)
        w2f_r = w2q_d.ap().rearrange("(a p) c -> p a c", p=128)
        inv1 = SM[:, 0:S]
        cw0, cw1, cw2 = sm2p["cw0"], sm2p["cw1"], sm2p["cw2"]
        cbc = sm2p["cb"]
        nb = NQ - 1
        hedge = sm2p["hedge"]
        amax2_tm = sm2p["amax2_tm"]
        W2f = W2.bitcast(F32)
        scrA = [W2f[:, 0:S], W2f[:, 2048:2048 + S]]
        GtmpA = W2f[:, 4096:4096 + S]
        AMX = W2f[:, 6144:6144 + S]
        W1UPf = W1.bitcast(F32)
        bw = P.barrier_waits()
        first = {k: True for k in "tvsgy"}

        def stw(kind, fn, waits):
            if first[kind]:
                first[kind] = False
                waits = bw + list(waits)
            P.step(kind, fn, waits=waits)

        if "c127" not in sm2p:
            sm2col("c127")
        c127 = sm2p["c127"]

        def _vinit(e):
            e.memset(c127, 127.0)
            return e.tensor_scalar(
                out=inv1, in0=inv1, scalar1=w1m["inv"], scalar2=None,
                op0=ALU.mult)
        stw("v", _vinit, waits=[])
        v_inv1 = P.val("V")
        stw("g", lambda e: e.memset(AMX, 0.0), waits=[])
        g_amx0 = P.val("G")

        d1_wb = {}
        tm_mm = {}
        vm_evac_ps = {0: 0, 1: 0}
        sm_abs = {}
        gm_conv2 = {}
        vm_max = 0
        for qh in range(4):
            half = qh % 2
            wb3 = W1[:, half * 8192:half * 8192 + 8192].rearrange(
                "p (k m) -> p k m", k=DC)
            lw = [("T", tm_mm[(qh - 2, 7)])] if qh >= 2 else []
            lw.append(("D1", P.val("D1")))
            stw("y", lambda e, wb3=wb3, qh=qh: P.inc("D1", e.dma_start(
                out=wb3, in_=w1q_r[:, :, ts(qh, 1024)]), 16) and None, waits=lw)
            d1_wb[qh] = P.val("D1")
            for mc in range(8):
                pid = mc % 2
                ps = (PSA if pid == 0 else PSB)[:, 0:S]
                wcol = w1m["cs"][:, qh * 8 + mc:qh * 8 + mc + 1]
                hcol = qh * 8 + mc
                src = cq3_pa(mc)

                def _m(e, wb3=wb3, mc=mc, ps=ps, qh=qh):
                    last = None
                    for nq_ in range(NQ):
                        for kc in range(DC):
                            last = mm(e, ps[:, ts(nq_)],
                                      wb3[:, kc, ts(mc, 128)],
                                      r13[:, kc, ts(nq_)], kc == 0, kc == DC - 1)
                    return last
                stw("t", _m, waits=[("D1", d1_wb[qh]),
                                    ("V", vm_evac_ps[pid])])
                tm_mm[(qh, mc)] = P.val("T")
                ew = [("T", tm_mm[(qh, mc)]), ("V", v_inv1), ("V", vm_max)]
                if (qh - 1, mc) in sm_abs:
                    ew.append(("S", sm_abs[(qh - 1, mc)]))
                    ew.append(("G", gm_conv2[(qh - 1, mc)]))
                stw("v", lambda e, src=src, ps=ps, wcol=wcol:
                    e.scalar_tensor_tensor(
                        out=src, in0=ps, scalar=wcol,
                        in1=inv1, op0=ALU.subtract, op1=ALU.mult), waits=ew)
                vm_evac_ps[pid] = P.val("V")
                vm_evac_cur = P.val("V")
                scr = scrA[mc % 2]
                stw("s", lambda e, src=src, scr=scr, hcol=hcol: e.activation(
                    out=scr, in_=src, func=AF.Identity,
                    scale=cw1[:, hcol:hcol + 1], bias=cbc[:, hcol:hcol + 1]),
                    waits=[("V", vm_evac_cur)])
                sm_c1 = P.val("S")

                def _g2a(e, src=src, hcol=hcol):
                    last = e.tensor_scalar(
                        out=GtmpA[:, 0:S - 1], in0=src[:, 0:S - 1],
                        scalar1=cw0[:, hcol:hcol + 1], scalar2=None, op0=ALU.mult)
                    if nb:
                        srcb = src.rearrange("p (b u) -> p b u", u=512)
                        hv = hedge[:, hcol * 2 * nb:(hcol + 1) * 2 * nb]
                        hv2 = hv.rearrange("p (b u) -> p b u", u=2)
                        e.tensor_copy(out=hv2[:, :, 0:1], in_=srcb[:, 0:nb, 511:512])
                        last = e.tensor_copy(out=hv2[:, :, 1:2],
                                             in_=srcb[:, 1:nb + 1, 0:1])
                    return last
                stw("g", _g2a, waits=[("S", sm_c1)])
                g_mid = P.val("G")
                stw("g", lambda e, scr=scr: e.tensor_tensor(
                    out=scr[:, 1:S], in0=scr[:, 1:S],
                    in1=GtmpA[:, 0:S - 1], op=ALU.add), waits=[("G", g_mid)])
                gm_conv2[(qh, mc)] = P.val("G")
                stw("v", lambda e, src=src, scr=scr, hcol=hcol:
                    e.scalar_tensor_tensor(
                        out=scr[:, 0:S - 1], in0=src[:, 1:S],
                        scalar=cw2[:, hcol:hcol + 1], in1=scr[:, 0:S - 1],
                        op0=ALU.mult, op1=ALU.add),
                    waits=[("G", gm_conv2[(qh, mc)])])
                vm_c3 = P.val("V")

                stw("s", lambda e, src=src, scr=scr: e.activation(
                    out=src, in_=scr, func=AF.Silu), waits=[("V", vm_c3)])
                s_mid = P.val("S")
                stw("s", lambda e, src=src: e.activation(
                    out=src, in_=src, func=AF.Abs), waits=[("S", s_mid)])
                sm_abs[(qh, mc)] = P.val("S")
                stw("v", lambda e, src=src: e.tensor_tensor(
                    out=AMX, in0=AMX, in1=src, op=ALU.max),
                    waits=[("S", sm_abs[(qh, mc)]), ("G", g_amx0),
                           ("V", vm_max)])
                vm_max = P.val("V")

        if FFN0_PHASE <= 1:
            return
        nt = S // 128
        sc2_tm = sm2p["sc2_tm"]
        inv2_tm = sm2p["inv2_tm"]
        for j in range(0, nt, 4):
            g = min(4, nt - j)

            def _t(e, j=j, g=g):
                last = None
                for k in range(g):
                    last = e.transpose(bank(k)[:, 0:128], AMX[:, ts(j + k, 128)], IDF)
                return last
            P.step("t", _t, waits=[("V", vm_max), ("S", P.val("S")),
                                   ("T", P.val("T"))])
            tm_tr = P.val("T")

            def _v(e, j=j, g=g):
                return [e.tensor_reduce(
                    out=amax2_tm[:, j + k:j + k + 1], in_=bank(k)[:, 0:128],
                    axis=AX.X, op=ALU.max) for k in range(g)][-1]
            P.step("v", _v, waits=[("T", tm_tr)])
        P.step("v", lambda e: e.tensor_single_scalar(
            out=amax2_tm[:, 0:nt], in_=amax2_tm[:, 0:nt], scalar=1e-5, op=ALU.max))
        P.step("v", lambda e: e.tensor_scalar(
            out=inv2_tm[:, 0:nt], in0=amax2_tm[:, 0:nt],
            scalar1=w2m["inv"], scalar2=0.5 / 127.0, op0=ALU.mult, op1=ALU.mult))
        P.step("v", lambda e: e.reciprocal(out=sc2_tm[:, 0:nt], in_=amax2_tm[:, 0:nt]))
        P.step("v", lambda e: e.tensor_scalar(
            out=sc2_tm[:, 0:nt], in0=sc2_tm[:, 0:nt], scalar1=127.0,
            scalar2=None, op0=ALU.mult))
        scr128b = B1[:, 14336:14848]
        for tc in range(NQ):
            for j, tmc in enumerate([sc2_tm, inv2_tm]):
                dst = W1UPf[:, 4096 + tc * 1024 + j * 512:
                            4096 + tc * 1024 + j * 512 + 512]
                bcast_tm_cols(tmc[:, tc * 4:tc * 4 + 4], 4, scr128b, bank(4))
                P.step("v", lambda e, dst=dst: e.tensor_copy(
                    out=dst, in_=bank(4)))

        if FFN0_PHASE <= 2:
            return
        cch3 = B1[:, 0:4096].rearrange("p (m t) -> p m t", m=8)
        cscr = [B1[:, 4096:4608], B1[:, 4608:5120]]
        rscr = [B1[:, 5120:5632], B1[:, 5632:6144]]
        fscr = [B1[:, 6144:6656], B1[:, 6656:7168]]
        iscr = [B1[:, 15360:15872].bitcast(I32), B1[:, 15872:16384].bitcast(I32)]
        xq = B1[:, 8192:12288]
        xq3 = xq.rearrange("p (c t) -> p c t", c=DC)
        x_r = x_src.ap().rearrange("(c p) t -> p c t", p=128)
        xd_r = x_dst.ap().rearrange("(c p) t -> p c t", p=128)
        r2f = C2.rearrange("p (k t) -> p k t", k=32)
        wb3 = W1[:, 0:8192].rearrange("p (k m) -> p k m", k=DC)
        w2h = [W2[:, 0:8192].rearrange("p (k m) -> p k m", k=8),
               W2[:, 8192:16384].rearrange("p (k m) -> p k m", k=8)]

        paT, paV, paS, paG = (P.val(n) for n in "TVSG")
        d3_last = P.val("D3")
        d1_last = P.val("D1")
        tm_mm1 = paT
        tm_mm2 = {}
        vm_bank_free = {b: paV for b in range(8)}
        sm_silu = {m: paS for m in range(8)}
        sm_c1d = {m: paS for m in range(8)}
        gm_cvt = {m: paG for m in range(8)}
        gm_cvt_par = {0: paG, 1: paG}
        gm_fadd = {f: paG for f in range(8)}
        gm_fadd_par = {0: paG, 1: paG}
        d2_w2 = {}
        for tc in range(NQ):
            sc2sb = W1UPf[:, 4096 + tc * 1024:4096 + tc * 1024 + 512]
            inv2sb = W1UPf[:, 4096 + tc * 1024 + 512:4096 + tc * 1024 + 1024]
            xw = [("D3", d3_last)]
            if tc == 0:
                xw += [("V", paV), ("S", paS), ("G", paG)]
            stw("y", lambda e, tc=tc: P.inc(
                "D3", e.dma_start(out=xq3, in_=x_r[:, :, ts(tc)]), 16) and None,
                waits=xw)
            d3_xq = P.val("D3")
            for pp in range(2):
                pw = [("T", tm_mm2[(tc - 1, pp + 2)])] if (tc - 1, pp + 2) in tm_mm2 else []
                pw.append(("D2", P.val("D2")))
                if tc == 0:
                    pw += [("V", paV), ("S", paS), ("T", paT)]
                stw("g", lambda e, pp=pp: P.inc("D2", e.dma_start(
                    out=w2h[pp], in_=w2f_r[:, pp * 8:pp * 8 + 8, :]), 16) and None,
                    waits=pw)
                d2_w2[(tc, pp)] = P.val("D2")

            for qh in range(4):
                stw("y", lambda e, qh=qh: P.inc("D1", e.dma_start(
                    out=wb3, in_=w1q_r[:, :, ts(qh, 1024)]), 16) and None,
                    waits=[("T", tm_mm1), ("D1", d1_last)])
                d1_cur = P.val("D1")
                d1_last = d1_cur
                for mc in range(8):
                    bk = mc % 3
                    wcol = w1m["cs"][:, qh * 8 + mc:qh * 8 + mc + 1]
                    hcol = qh * 8 + mc
                    src = cch3[:, mc, :]
                    scr = cscr[mc % 2]
                    rs = rscr[mc % 2]
                    isc = iscr[mc % 2]

                    def _m(e, mc=mc, bk=bk, tc=tc):
                        last = None
                        for kc in range(DC):
                            last = mm(e, bank(bk), wb3[:, kc, ts(mc, 128)],
                                      r13[:, kc, ts(tc)], kc == 0, kc == DC - 1)
                        return last
                    mw1 = [("D1", d1_cur), ("V", vm_bank_free[bk])]
                    if tc == 0 and qh == 0 and mc == 0:
                        mw1.append(("T", P.val("T")))
                    stw("t", _m, waits=mw1)
                    tm_mm1 = P.val("T")
                    stw("v", lambda e, src=src, bk=bk, wcol=wcol, tc=tc:
                        e.scalar_tensor_tensor(
                            out=src, in0=bank(bk), scalar=wcol,
                            in1=inv1[:, ts(tc)], op0=ALU.subtract, op1=ALU.mult),
                        waits=[("T", tm_mm1), ("S", sm_silu[mc]), ("V", v_inv1)])
                    vm_bank_free[bk] = P.val("V")
                    vm_evac = P.val("V")
                    stw("s", lambda e, src=src, scr=scr, hcol=hcol: e.activation(
                        out=scr, in_=src, func=AF.Identity,
                        scale=cw1[:, hcol:hcol + 1], bias=cbc[:, hcol:hcol + 1]),
                        waits=[("V", vm_evac)])
                    sm_c1d[mc] = P.val("S")

                    stw("v", lambda e, src=src, scr=scr, hcol=hcol:
                        e.scalar_tensor_tensor(
                            out=scr[:, 1:512], in0=src[:, 0:511],
                            scalar=cw0[:, hcol:hcol + 1], in1=scr[:, 1:512],
                            op0=ALU.mult, op1=ALU.add),
                        waits=[("S", sm_c1d[mc])])
                    v_mid = P.val("V")

                    def _c3(e, src=src, scr=scr, hcol=hcol, tc=tc):
                        last = e.scalar_tensor_tensor(
                            out=scr[:, 0:511], in0=src[:, 1:512],
                            scalar=cw2[:, hcol:hcol + 1], in1=scr[:, 0:511],
                            op0=ALU.mult, op1=ALU.add)
                        if tc < NQ - 1:
                            last = e.scalar_tensor_tensor(
                                out=scr[:, 511:512],
                                in0=hedge[:, hcol * 2 * nb + 2 * tc + 1:
                                          hcol * 2 * nb + 2 * tc + 2],
                                scalar=cw2[:, hcol:hcol + 1], in1=scr[:, 511:512],
                                op0=ALU.mult, op1=ALU.add)
                        if tc > 0:
                            last = e.scalar_tensor_tensor(
                                out=scr[:, 0:1],
                                in0=hedge[:, hcol * 2 * nb + 2 * (tc - 1):
                                          hcol * 2 * nb + 2 * (tc - 1) + 1],
                                scalar=cw0[:, hcol:hcol + 1], in1=scr[:, 0:1],
                                op0=ALU.mult, op1=ALU.add)
                        return last
                    stw("v", _c3, waits=[("V", v_mid)])
                    vm_c23 = P.val("V")
                    stw("s", lambda e, src=src, scr=scr: e.activation(
                        out=src, in_=scr, func=AF.Silu), waits=[("V", vm_c23)])
                    sm_silu[mc] = P.val("S")
                    stw("v", lambda e, src=src, rs=rs, sc2sb=sc2sb:
                        e.tensor_tensor(out=rs, in0=src, in1=sc2sb, op=ALU.mult),
                        waits=[("S", sm_silu[mc]), ("G", gm_cvt_par[mc % 2])])
                    vm_rm = P.val("V")
                    stw("s", lambda e, rs=rs: e.activation(
                        out=rs, in_=rs, func=AF.Identity, bias=c127),
                        waits=[("V", vm_rm)])
                    sm_127 = P.val("S")
                    stw("g", lambda e, rs=rs, isc=isc: e.tensor_copy(
                        out=isc, in_=rs), waits=[("S", sm_127)])
                    gm_cvt_par[mc % 2] = P.val("G")
                    g_cv = P.val("G")
                    grw = [("G", g_cv)]
                    if (tc - 1, qh) in tm_mm2:
                        grw.append(("T", tm_mm2[(tc - 1, qh)]))
                    stw("g", lambda e, isc=isc, kk=qh * 8 + mc: e.tensor_copy(
                        out=r2f[:, kk, :], in_=isc), waits=grw)
                    gm_cvt[mc] = P.val("G")

            if FFN0_PHASE <= 3:
                continue
            gm_r2 = {qh: gm_cvt[7] for qh in range(4)}
            for p in range(4):
                if p >= 2:
                    stw("g", lambda e, p=p: P.inc("D2", e.dma_start(
                        out=w2h[p % 2], in_=w2f_r[:, p * 8:p * 8 + 8, :]), 16)
                        and None, waits=[("T", tm_mm2[(tc, p - 2)]),
                                         ("D2", P.val("D2"))])
                    d2_w2[(tc, p)] = P.val("D2")

                def _m2(e, tc=tc, p=p):
                    last = None
                    for fm in range(8):
                        for kc in range(8):
                            k = p * 8 + kc
                            last = mm(e, bank(fm), w2h[p % 2][:, kc, ts(fm, 128)],
                                      r2f[:, k, :], k == 0, k == 31)
                    return last
                mw = [("D2", d2_w2[(tc, p)]), ("G", gm_cvt[7])]
                if p == 0:
                    mw.append(("V", P.val("V")))
                    mw += [("V", vm_bank_free[b]) for b in range(3, 8)]
                stw("t", _m2, waits=mw)
                tm_mm2[(tc, p)] = P.val("T")
            for fc in range(DC):
                fs = fscr[fc % 2]
                stw("v", lambda e, fc=fc, fs=fs, inv2sb=inv2sb:
                    e.scalar_tensor_tensor(
                        out=fs, in0=bank(fc), scalar=w2m["cs"][:, fc:fc + 1],
                        in1=inv2sb, op0=ALU.subtract, op1=ALU.mult),
                    waits=[("T", tm_mm2[(tc, 3)]), ("G", gm_fadd_par[fc % 2])])
                vm_f = P.val("V")
                stw("g", lambda e, fc=fc, fs=fs: e.tensor_tensor(
                    out=xq3[:, fc, :], in0=xq3[:, fc, :], in1=fs, op=ALU.add),
                    waits=[("V", vm_f), ("D3", d3_xq)])
                gm_fadd[fc] = P.val("G")
                gm_fadd_par[fc % 2] = P.val("G")
                vm_bank_free[fc] = vm_f
            stw("y", lambda e, tc=tc: P.inc(
                "D3", e.dma_start(out=xd_r[:, :, ts(tc)], in_=xq3), 16) and None,
                waits=[("G", gm_fadd[7]), ("D3", d3_xq)])
            d3_last = P.val("D3")

    def tr(e, out, in_, ident, tp="auto"):
        if tp == "auto":
            bp_i = in_.base_partition()
            bp_o = out.base_partition()
            tp = (bp_i, bp_o) if (bp_i or bp_o) else None
        return e.transpose(out, in_, ident, tile_position=tp)

    def tap(name, src_ap):
        if TAP != name:
            return
        n = src_ap.shape[-1] if src_ap.ndim == 2 else None
        def _(e):
            P.inc("DG", e.dma_start(out=dbg_out.ap()[:, 0:n], in_=src_ap), 16)
        P.step("g", _)

    def stage_att0(x_src, x_dst, qm, om):
        NT = S // 128
        TB = S // 16
        BPT = 128 // TB
        r13 = C1.rearrange("p (c t) -> p c t", c=DC)
        wq_r = wqkvq_d.ap().rearrange("(a p) c -> p a c", p=128)
        wo_r = woq_d.ap().rearrange("(a p) c -> p a c", p=128)
        wbuf = W1[:, 0:8192]
        wbuf3 = wbuf.rearrange("p (k m) -> p k m", k=DC)
        projT = C2.rearrange("p (c t) -> p c t", c=DC)
        inv4 = SM[:, 0:S]
        scrA = B1[:, 8192:8704]

        P.step("y", lambda e: P.inc("D0", e.dma_start(out=wbuf3, in_=wq_r), 16) and None)
        for fc in range(DC):
            for nq in range(NQ):
                bk = (fc * NQ + nq) % 3

                def _m(e, fc=fc, nq=nq, bk=bk):
                    last = None
                    for kc in range(DC):
                        last = mm(e, bank(bk), wbuf3[:, kc, ts(fc, 128)],
                                  r13[:, kc, ts(nq)], kc == 0, kc == DC - 1)
                    return last
                P.step("t", _m)
                P.step("v", lambda e, fc=fc, nq=nq, bk=bk: e.scalar_tensor_tensor(
                    out=scrA, in0=bank(bk), scalar=qm["cs"][:, fc:fc + 1],
                    in1=inv4[:, ts(nq)], op0=ALU.subtract, op1=ALU.mult))
                P.step("s", lambda e, fc=fc, nq=nq: e.activation(
                    out=projT[:, fc, ts(nq)], in_=scrA, func=AF.Identity,
                    scale=qm["alpha"][:, fc:fc + 1],
                    bias=sm2p["qkvb"][:, fc:fc + 1]))

        tap("proj", C2[:, 0:16384])
        def att0_epilogue(e):
            for nq in range(NQ):
                P.inc("D0", e.dma_start(out=x_dst.ap()[:, ts(nq)],
                                        in_=x_src.ap()[:, ts(nq)]), 16)
        if ATT0_PHASE < 2:
            P.step("y", att0_epilogue)
            return
        P.step("y", lambda e: P.inc("D0", e.dma_start(
            out=B1[:, 0:4 * S], in_=dr["cs4"].ap()), 16) and None)
        COSQ, SINQ = B1[:, 0:S], B1[:, S:2 * S]
        COSK, SINK = B1[:, 2 * S:3 * S], B1[:, 3 * S:4 * S]
        rotT, rotB = B1[:, 4 * S:5 * S], B1[:, 5 * S:6 * S]
        x1b, x2b = B1[:, 6 * S:7 * S], B1[:, 7 * S:8 * S]
        qlT = C1[:, 0:4 * S].rearrange("p (g t) -> p g t", g=4)
        klT = W2[:, 0:4 * S].rearrange("p (g t) -> p g t", g=4)

        for side in ("q", "k"):
            COS = COSQ if side == "q" else COSK
            SIN = SINQ if side == "q" else SINK
            dstT = qlT if side == "q" else klT
            for g in range(4):
                def _eo(e, g=g, side=side):
                    last = None
                    for j in range(4):
                        h = 4 * g + j
                        par = h % 2
                        lE = EOB4[:, 64 * par:64 * par + 32]
                        lO = EOB4[:, 64 * par + 32:64 * par + 64]
                        if side == "q":
                            for nq in range(NQ):
                                rhs = projT[:, h // 2, ts(nq)]
                                mm(e, PSA[32 * j:32 * j + 32, ts(nq)], lE, rhs,
                                   True, True, tp=(0, 32 * j))
                                last = mm(e, PSB[32 * j:32 * j + 32, ts(nq)], lO,
                                          rhs, True, True, tp=(0, 32 * j))
                        else:
                            for cb in range(16):
                                par2 = cb % 2
                                lE2 = EOB4[:, 64 * par2:64 * par2 + 32]
                                lO2 = EOB4[:, 64 * par2 + 32:64 * par2 + 64]
                                rhs = projT[:, cb // 2, h * TB:(h + 1) * TB]
                                mm(e, PSA[32 * j:32 * j + 32, cb * TB:(cb + 1) * TB],
                                   lE2, rhs, True, True, tp=(0, 32 * j))
                                last = mm(e, PSB[32 * j:32 * j + 32, cb * TB:(cb + 1) * TB],
                                          lO2, rhs, True, True, tp=(0, 32 * j))
                    return last
                P.step("t", _eo)
                if ATT0_PHASE == 21:
                    continue
                P.step("v", lambda e: e.tensor_tensor(out=x1b, in0=PSA[:, 0:S], in1=COS, op=ALU.mult))
                P.step("v", lambda e: e.tensor_tensor(out=x2b, in0=PSB[:, 0:S], in1=SIN, op=ALU.mult))
                P.step("v", lambda e: e.tensor_tensor(out=rotT, in0=x1b, in1=x2b, op=ALU.subtract))
                P.step("v", lambda e: e.tensor_tensor(out=x1b, in0=PSA[:, 0:S], in1=SIN, op=ALU.mult))
                P.step("v", lambda e: e.tensor_tensor(out=x2b, in0=PSB[:, 0:S], in1=COS, op=ALU.mult))
                P.step("v", lambda e: e.tensor_tensor(out=rotB, in0=x1b, in1=x2b, op=ALU.add))
                if ATT0_PHASE == 22:
                    continue

                def _u(e, g=g):
                    last = None
                    for j in range(4):
                        for nq in range(NQ):
                            mm(e, PSA[32 * j:32 * j + 32, ts(nq)],
                               U8[:, 64 * j:64 * j + 32],
                               rotT[:, ts(nq)], True, False, tp=(0, 32 * j))
                            last = mm(e, PSA[32 * j:32 * j + 32, ts(nq)],
                                      U8[:, 64 * j + 32:64 * j + 64],
                                      rotB[:, ts(nq)], False, True, tp=(0, 32 * j))
                    return last
                P.step("t", _u)
                if ATT0_PHASE == 23:
                    continue
                P.step("v", lambda e, g=g, dstT=dstT: e.tensor_copy(
                    out=dstT[:, g, :], in_=PSA[:, 0:S]))

        if ATT0_PHASE < 3:
            P.step("y", att0_epilogue)
            return
        tap("qlT", C1[:, 0:4 * S])
        tap("klT", W2[:, 0:4 * S])
        shcol = sm2p["att0_sh"]
        t1c = sm2p["att0_t1"]
        t2c = sm2p["att0_t2"]
        sqb = W2[:, 4 * S + 256:5 * S + 256]
        for nm, src_ in (("q", qlT), ("k", klT)):
            dst = t1c if nm == "q" else t2c
            first = [True]
            for g in range(4):
                P.step("v", lambda e, g=g, src_=src_: e.tensor_tensor(
                    out=sqb, in0=src_[:, g, :], in1=src_[:, g, :], op=ALU.mult))
                for nq in range(NQ):
                    P.step("t", lambda e, nq=nq: mm(
                        e, bank(7), BD32, sqb[:, ts(nq)], True, True))
                    P.step("v", lambda e: e.tensor_reduce(
                        out=sm2p["att0_tmp"], in_=bank(7), axis=AX.X, op=ALU.max))
                    if first[0]:
                        P.step("v", lambda e, dst=dst: e.tensor_copy(
                            out=dst, in_=sm2p["att0_tmp"]))
                        first[0] = False
                    else:
                        P.step("v", lambda e, dst=dst: e.tensor_tensor(
                            out=dst, in0=dst, in1=sm2p["att0_tmp"], op=ALU.max))
        P.step("v", lambda e: e.tensor_tensor(out=t1c, in0=t1c, in1=t2c, op=ALU.mult))
        P.step("t", lambda e: tr(e, bank(7)[0:1, 0:128], t1c, IDF))
        P.step("v", lambda e: e.tensor_reduce(
            out=sm2p["att0_tmp"][0:1, :], in_=bank(7)[0:1, 0:128], axis=AX.X, op=ALU.max))
        P.step("s", lambda e: e.activation(out=sm2p["att0_tmp"][0:1, :],
                                           in_=sm2p["att0_tmp"][0:1, :], func=AF.Sqrt))
        P.step("v", lambda e: e.tensor_scalar(
            out=sm2p["att0_tmp"][0:1, :], in0=sm2p["att0_tmp"][0:1, :],
            scalar1=-0.125, scalar2=None, op0=ALU.mult))
        P.step("t", lambda e: mm(e, bank(7)[:, 0:1], ONESF[0:1, :],
                                 sm2p["att0_tmp"][0:1, :], True, True))
        P.step("v", lambda e: e.tensor_copy(out=shcol, in_=bank(7)[:, 0:1]))

        if ATT0_PHASE < 4:
            P.step("y", att0_epilogue)
            return
        ytm = B1[:, 0:NT * 1024].rearrange("p (i f) -> p i f", i=NT)
        Ptd = [W2[:, 4 * S + 256:4 * S + 256 + S],
               W2[:, 5 * S + 256:5 * S + 256 + S]]

        s_seed, v_seed, t_seed = P.val("S"), P.val("V"), P.val("T")
        vb3 = [W1[:, 0:NT * 65].rearrange("p (n c) -> p n c", c=65),
               W1[:, NT * 65:2 * NT * 65].rearrange("p (n c) -> p n c", c=65)]
        vm_bankuse = {5: v_seed, 6: v_seed}
        vm_vbuf = {0: v_seed, 1: v_seed}

        def fill_vbuf(h, w_extra):
            hb = h % 2
            for n0 in range(0, NT, 8):
                g_ = min(8, NT - n0)
                pb = 5 + (n0 // 8) % 2

                def _tvb(e, n0=n0, g_=g_, pb=pb, h=h):
                    last = None
                    for k in range(g_):
                        cb16 = n0 + k
                        for sub in range(BPT):
                            cb = cb16 * BPT + sub
                            src_ = projT[:, cb // 2, h * TB:(h + 1) * TB]
                            last = mm(e, bank(pb)[sub * TB:sub * TB + TB,
                                                  k * 64:k * 64 + 64], src_,
                                      TRSEL[:, 64 * (cb % 2):64 * (cb % 2) + 64],
                                      True, True, tp=(0, (sub * TB) % 128))
                    return last
                P.step("t", _tvb,
                       waits=w_extra + [("V", vm_bankuse[pb]),
                                        ("T", P.val("T"))])
                tv = P.val("T")

                def _vc(e, n0=n0, g_=g_, pb=pb, hb=hb):
                    e.tensor_copy(
                        out=vb3[hb][:, n0:n0 + g_, 0:64],
                        in_=bank(pb)[:, 0:g_ * 64].rearrange(
                            "p (n c) -> p n c", c=64))
                    return e.memset(vb3[hb][:, n0:n0 + g_, 64:65], 1.0)
                P.step("v", _vc, waits=[("T", tv)])
                vm_bankuse[pb] = P.val("V")
            vm_vbuf[hb] = P.val("V")

        fill_vbuf(0, [("S", s_seed), ("T", t_seed), ("V", v_seed)])

        sm_exp = P.val("S")
        vm_evac_all = max(vm_vbuf.values())
        tm_y_hist = {}
        tm_tt = {}
        vm_ts = {}
        first_st = True
        for h in range(16):
            j, g = h % 4, h // 4
            for cb16 in range(NT):
                Ptb = Ptd[cb16 % 2]

                def _st(e, j=j, g=g, cb16=cb16):
                    last = None
                    for qn in range(NQ):
                        last = mm(e, bank(4 + qn),
                                  klT[:, g, cb16 * 128:cb16 * 128 + 128][32 * j:32 * j + 32, :],
                                  qlT[:, g, ts(qn)][32 * j:32 * j + 32, :], True, True,
                                  tp=(32 * j, 0))
                    return last
                stww = [("S", sm_exp)]
                if cb16 == 0:
                    stww.append(("V", vm_vbuf[h % 2]))
                    stww.append(("V", vm_evac_all))
                if first_st:
                    stww += [("V", vm_evac_all), ("T", P.val("T"))]
                    first_st = False
                P.step("t", _st, waits=stww)
                tm_st = P.val("T")
                ew = [("T", tm_st)]
                if cb16 - 2 in tm_y_hist:
                    ew.append(("T", tm_y_hist[cb16 - 2]))
                P.step("s", lambda e, Ptb=Ptb: e.activation(
                    out=Ptb, in_=PSB[:, 0:S], func=AF.Exp, bias=shcol,
                    scale=0.125), waits=ew)
                sm_exp = P.val("S")

                def _y(e, h=h, cb16=cb16, Ptb=Ptb):
                    last = None
                    for qc in range(NQ):
                        last = mm(e, bank(qc)[0:65, :], vb3[h % 2][:, cb16, :],
                                  Ptb[:, ts(qc)], cb16 == 0, cb16 == NT - 1)
                    return last
                yw = [("S", sm_exp)]
                if cb16 == 0:
                    yw.append(("V", vm_evac_all))
                P.step("t", _y, waits=yw)
                tm_y_hist[cb16] = P.val("T")
            tm_y_fin = P.val("T")
            if h + 1 < 16:
                fill_vbuf(h + 1, [("S", sm_exp)])
            for qc in range(NQ):
                cw_ = [("T", tm_y_fin)]
                if qc - 1 in tm_tt:
                    cw_.append(("T", tm_tt[qc - 1]))
                P.step("v", lambda e, qc=qc: e.tensor_copy(
                    out=SM[0:65, 0:512], in_=bank(qc)[0:65, :]), waits=cw_)
                vm_cp = P.val("V")

                def _tt(e):
                    last = None
                    for u in range(4):
                        last = tr(e, bank(4)[:, 65 * u:65 * u + 65],
                                  SM[0:65, 128 * u:128 * u + 128], IDF[0:65, 0:65])
                    return last
                ttw = [("V", vm_cp), ("S", sm_exp), ("T", P.val("T"))]
                if (qc - 1) in vm_ts:
                    ttw.append(("V", vm_ts[qc - 1]))
                P.step("t", _tt, waits=ttw)
                tm_tt[qc] = P.val("T")
                P.step("v", lambda e: [
                    e.reciprocal(out=sm2p["att0_rc"][:, u:u + 1],
                                 in_=bank(4)[:, 65 * u + 64:65 * u + 65])
                    for u in range(4)][-1], waits=[("T", tm_tt[qc])])
                v_rc = P.val("V")
                P.step("v", lambda e, h=h, qc=qc: [
                    e.tensor_scalar(
                        out=ytm[:, qc * 4 + u, h * 64:(h + 1) * 64],
                        in0=bank(4)[:, 65 * u:65 * u + 64],
                        scalar1=sm2p["att0_rc"][:, u:u + 1], scalar2=None,
                        op0=ALU.mult)
                    for u in range(4)][-1], waits=[("V", v_rc)])
                vm_ts[qc] = P.val("V")
            vm_evac_all = P.val("V")
            tm_tt = {}
            vm_ts = {}

        tap("ytm", B1[:, 0:16384])
        if ATT0_PHASE < 5:
            P.step("y", att0_epilogue)
            return
        inv4y = sm2p["att0_inv4y"]
        for i in range(NT):
            P.step("v", lambda e, i=i: e.tensor_reduce(
                out=sm2p["att0_tmp"], in_=ytm[:, i, :], axis=AX.X, op=ALU.max,
                apply_absolute_value=True))
            P.step("v", lambda e: e.tensor_single_scalar(
                out=sm2p["att0_tmp"], in_=sm2p["att0_tmp"], scalar=1e-6, op=ALU.max))
            P.step("v", lambda e, i=i: e.tensor_scalar(
                out=inv4y[:, i:i + 1], in0=sm2p["att0_tmp"], scalar1=1.0 / 7.0,
                scalar2=None, op0=ALU.mult))
            P.step("v", lambda e: e.reciprocal(out=sm2p["att0_tmp"], in_=sm2p["att0_tmp"]))
            P.step("v", lambda e, i=i: e.tensor_scalar(
                out=SM[:, 0:1024], in0=ytm[:, i, :], scalar1=sm2p["att0_tmp"],
                scalar2=7.0, op0=ALU.mult, op1=ALU.mult))
            P.step("v", lambda e: e.tensor_scalar(
                out=SM[:, 0:1024], in0=SM[:, 0:1024], scalar1=7.0, scalar2=None,
                op0=ALU.add))
            P.step("v", lambda e: e.tensor_copy(
                out=SM[:, 1024:2048].bitcast(I32), in_=SM[:, 0:1024]))
            P.step("v", lambda e, i=i: e.tensor_copy(
                out=C1[:, i * 1024:(i + 1) * 1024], in_=SM[:, 1024:2048].bitcast(I32)))
        for i in range(NT):
            def _tq(e, i=i):
                last = None
                for fc in range(DC):
                    last = tr(e, bank(fc % 4)[:, 0:128],
                              C1[:, i * 1024 + fc * 128:i * 1024 + (fc + 1) * 128], IDB)
                    if fc % 4 == 3:
                        pass
                return last
            def _tq1(e, i=i):
                last = None
                for fc in range(4):
                    last = tr(e, bank(fc).bitcast(BF16)[:, 0:128],
                              C1[:, i * 1024 + fc * 128:i * 1024 + (fc + 1) * 128], IDB)
                return last
            P.step("t", _tq1)
            P.step("v", lambda e, i=i: [
                e.tensor_copy(out=projT[:, fc, i * 128:(i + 1) * 128],
                              in_=bank(fc).bitcast(BF16)[:, 0:128]) for fc in range(4)][-1])

            def _tq2(e, i=i):
                last = None
                for fc in range(4, DC):
                    last = tr(e, bank(fc - 4).bitcast(BF16)[:, 0:128],
                              C1[:, i * 1024 + fc * 128:i * 1024 + (fc + 1) * 128], IDB)
                return last
            P.step("t", _tq2)
            P.step("v", lambda e, i=i: [
                e.tensor_copy(out=projT[:, fc, i * 128:(i + 1) * 128],
                              in_=bank(fc - 4).bitcast(BF16)[:, 0:128]) for fc in range(4, DC)][-1])

        tap("r4y", C1[:, 0:16384])
        tap("r4yfm", C2[:, 0:16384])
        if TAP == "sm2":
            def _tp(e):
                P.inc("DG", e.dma_start(out=dbg_out.ap()[:, 0:8], in_=om["alpha"]), 16)
                P.inc("DG", e.dma_start(out=dbg_out.ap()[:, 8:16], in_=om["cs"]), 16)
                P.inc("DG", e.dma_start(out=dbg_out.ap()[:, 16:16 + NT], in_=inv4y[:, 0:NT]), 16)
            P.step("g", _tp)
        P.step("y", lambda e: P.inc("D0", e.dma_start(out=wbuf3, in_=wo_r), 16) and None)
        xq = B1[:, 8192:12288]
        xq3 = xq.rearrange("p (c t) -> p c t", c=DC)
        x_r = x_src.ap().rearrange("(c p) t -> p c t", p=128)
        xd_r = x_dst.ap().rearrange("(c p) t -> p c t", p=128)
        inv4ysb = B1[:, 12288:12800]
        scr128c = B1[:, 12800:13312]
        scrA2 = B1[:, 13312:13824]
        for nq in range(NQ):
            P.step("y", lambda e, nq=nq: P.inc(
                "D0", e.dma_start(out=xq3, in_=x_r[:, :, ts(nq)]), 16) and None)
            bcast_tm_cols(inv4y[:, nq * 4:nq * 4 + 4], 4, scr128c, bank(4))
            P.step("v", lambda e: e.tensor_copy(out=inv4ysb, in_=bank(4)))
            for fc in range(DC):
                bk = fc % 3

                def _m(e, fc=fc, nq=nq, bk=bk):
                    last = None
                    for kc in range(DC):
                        last = mm(e, bank(bk), wbuf3[:, kc, ts(fc, 128)],
                                  projT[:, kc, ts(nq)], kc == 0, kc == DC - 1)
                    return last
                P.step("t", _m)
                P.step("v", lambda e, fc=fc, bk=bk: e.scalar_tensor_tensor(
                    out=scrA2, in0=bank(bk), scalar=om["cs"][:, fc:fc + 1],
                    in1=inv4ysb, op0=ALU.subtract, op1=ALU.mult))
                P.step("v", lambda e, fc=fc: e.scalar_tensor_tensor(
                    out=xq3[:, fc, :], in0=scrA2, scalar=om["alpha"][:, fc:fc + 1],
                    in1=xq3[:, fc, :], op0=ALU.mult, op1=ALU.add))
            P.step("y", lambda e, nq=nq: P.inc(
                "D0", e.dma_start(out=xd_r[:, :, ts(nq)], in_=xq3), 16) and None)

    def attn_core(qT3, kT3, vtm, heads, scale, causal, Pt, shiftcol, ytm3,
                  sqb, NT, cmask=None):
        t1c = sm2p["ac_t1"]
        t2c = sm2p["ac_t2"]
        tmpc = sm2p["ac_tmp"]
        nkc = kT3.shape[1]
        for nm, src_, nch in (("q", qT3, 8), ("k", kT3, nkc)):
            dst = t1c if nm == "q" else t2c
            first = True
            for cch in range(nch):
                P.step("v", lambda e, cch=cch, src_=src_: e.tensor_tensor(
                    out=sqb[:, 0:S], in0=src_[:, cch, 0:S], in1=src_[:, cch, 0:S],
                    op=ALU.mult))
                for nq in range(NQ):
                    P.step("t", lambda e, nq=nq: mm(
                        e, bank(7), BD64, sqb[:, ts(nq)], True, True))
                    P.step("v", lambda e: e.tensor_reduce(
                        out=tmpc, in_=bank(7), axis=AX.X, op=ALU.max))
                    if first:
                        P.step("v", lambda e, dst=dst: e.tensor_copy(out=dst, in_=tmpc))
                        first = False
                    else:
                        P.step("v", lambda e, dst=dst: e.tensor_tensor(
                            out=dst, in0=dst, in1=tmpc, op=ALU.max))
        P.step("v", lambda e: e.tensor_tensor(out=t1c, in0=t1c, in1=t2c, op=ALU.mult))
        P.step("t", lambda e: tr(e, bank(7)[0:1, 0:128], t1c, IDF))
        P.step("v", lambda e: e.tensor_reduce(
            out=tmpc[0:1, :], in_=bank(7)[0:1, 0:128], axis=AX.X, op=ALU.max))
        P.step("s", lambda e: e.activation(out=tmpc[0:1, :], in_=tmpc[0:1, :],
                                           func=AF.Sqrt))
        P.step("v", lambda e, scale=scale: e.tensor_scalar(
            out=tmpc[0:1, :], in0=tmpc[0:1, :], scalar1=-scale, scalar2=None,
            op0=ALU.mult))
        P.step("t", lambda e: mm(e, bank(7)[:, 0:1], ONESF[0:1, :],
                                 tmpc[0:1, :], True, True))
        P.step("v", lambda e: e.tensor_copy(out=shiftcol, in_=bank(7)[:, 0:1]))

        ew = P.barrier_waits()
        first_on = {"t": True, "s": True, "v": True}

        def stw(kind, fn, w):
            if first_on.get(kind):
                w = list(w) + ew
                first_on[kind] = False
            P.step(kind, fn, waits=w)

        vm_end = P.val("V")
        for (s, kchunk, kbase, vcol) in heads:
            sm_exp = {}
            tm_st = {}
            tm_y = {}
            vm_mask = {}
            prev_mt = None
            for mt in range(NT):
                qn0 = max(0, -(-(128 * mt - 511) // 512)) if causal else 0
                if causal and qn0 >= NQ:
                    continue
                pt = Pt[:, (mt % 2) * S:(mt % 2) * S + S]

                def _st(e, s=s, kchunk=kchunk, kbase=kbase, mt=mt, qn0=qn0):
                    last = None
                    ksl = kT3[:, kchunk, mt * 128:mt * 128 + 128][kbase:kbase + 64, :]
                    for qn in range(qn0, NQ):
                        last = mm(e, bank(4 + qn),
                                  ksl, qT3[:, s // 2, ts(qn)][64 * (s % 2):64 * (s % 2) + 64, :],
                                  True, True, tp=(kbase, 0))
                    return last
                w = [("V", vm_end)]
                if prev_mt is not None:
                    w.append(("S", sm_exp[prev_mt]))
                stw("t", _st, w)
                tm_st[mt] = P.val("T")

                def _exp(e, mt=mt, qn0=qn0, pt=pt):
                    return e.activation(
                        out=pt[:, qn0 * 512:S], in_=PSB[:, qn0 * 512:S],
                        func=AF.Exp, bias=shiftcol, scale=scale)
                w = [("T", tm_st[mt])]
                pp = prev_mt
                stw("s", _exp, w)
                sm_exp[mt] = P.val("S")
                if causal:
                    dq = mt // 4
                    stw("v", lambda e, mt=mt, dq=dq, pt=pt: e.tensor_tensor(
                        out=pt[:, ts(dq)], in0=pt[:, ts(dq)],
                        in1=cmask[:, ts(mt % 4)], op=ALU.mult),
                        [("S", sm_exp[mt])])
                    vm_mask[mt] = P.val("V")

                def _y(e, s=s, vcol=vcol, mt=mt, qn0=qn0, causal=causal, pt=pt):
                    last = None
                    for qc in range(qn0, NQ):
                        stop_mt = min(NT - 1, 4 * qc + 3) if causal else NT - 1
                        last = mm(e, bank(qc)[0:65, :], vtm(mt)[:, vcol:vcol + 65],
                                  pt[:, ts(qc)], mt == 0, mt == stop_mt)
                    return last
                w = [("S", sm_exp[mt])]
                if causal:
                    w.append(("V", vm_mask[mt]))
                stw("t", _y, w)
                tm_y[mt] = P.val("T")
                prev_mt = mt
            for qc in range(NQ):
                P.step("v", lambda e, qc=qc: e.tensor_copy(
                    out=SM[0:65, 0:512], in_=bank(qc)[0:65, :]),
                    waits=[("T", tm_y[prev_mt]), ("V", P.val("V"))])

                def _tt(e):
                    last = None
                    for u in range(4):
                        last = tr(e, bank(4)[:, 65 * u:65 * u + 65],
                                  SM[0:65, 128 * u:128 * u + 128], IDF[0:65, 0:65])
                    return last
                P.step("t", _tt, waits=[("V", P.val("V"))])
                P.step("v", lambda e: [
                    e.reciprocal(out=sm2p["ac_rc"][:, u:u + 1],
                                 in_=bank(4)[:, 65 * u + 64:65 * u + 65])
                    for u in range(4)][-1], waits=[("T", P.val("T"))])
                P.step("v", lambda e, s=s, qc=qc: [
                    e.tensor_scalar(
                        out=ytm3[:, qc * 4 + u, s * 64:(s + 1) * 64],
                        in0=bank(4)[:, 65 * u:65 * u + 64],
                        scalar1=sm2p["ac_rc"][:, u:u + 1], scalar2=None,
                        op0=ALU.mult)
                    for u in range(4)][-1], waits=[("V", P.val("V"))])
            vm_end = P.val("V")

    def y_to_fm(ytm3, NT):
        projT = C2.rearrange("p (c t) -> p c t", c=DC)
        for i in range(NT):
            P.step("v", lambda e, i=i: e.tensor_copy(
                out=C1[:, i * 1024:(i + 1) * 1024], in_=ytm3[:, i, :]))
        for i in range(NT):
            def _t1(e, i=i):
                last = None
                for fc in range(4):
                    last = tr(e, bank(fc).bitcast(BF16)[:, 0:128],
                              C1[:, i * 1024 + fc * 128:i * 1024 + (fc + 1) * 128], IDB)
                return last
            P.step("t", _t1)
            P.step("v", lambda e, i=i: [
                e.tensor_copy(out=projT[:, fc, i * 128:(i + 1) * 128],
                              in_=bank(fc).bitcast(BF16)[:, 0:128])
                for fc in range(4)][-1])

            def _t2(e, i=i):
                last = None
                for fc in range(4, DC):
                    last = tr(e, bank(fc - 4).bitcast(BF16)[:, 0:128],
                              C1[:, i * 1024 + fc * 128:i * 1024 + (fc + 1) * 128], IDB)
                return last
            P.step("t", _t2)
            P.step("v", lambda e, i=i: [
                e.tensor_copy(out=projT[:, fc, i * 128:(i + 1) * 128],
                              in_=bank(fc - 4).bitcast(BF16)[:, 0:128])
                for fc in range(4, DC)][-1])

    def proj_fm(rhs3, wbuf3, nmc, evac, kacc=DC):
        for fc in range(nmc):
            for nq in range(NQ):
                bk = (fc * NQ + nq) % 3

                def _m(e, fc=fc, nq=nq, bk=bk):
                    last = None
                    for kc in range(kacc):
                        last = mm(e, bank(bk), wbuf3[:, kc, ts(fc, 128)],
                                  rhs3[:, kc, ts(nq)], kc == 0, kc == kacc - 1)
                    return last
                P.step("t", _m)
                evac(fc, nq, bk)

    def stage_att1(x_src, x_dst):
        NT = S // 128
        ln3 = C1.rearrange("p (c t) -> p c t", c=DC)
        qT3 = C2.rearrange("p (c t) -> p c t", c=DC)
        kT3 = W2[:, 0:8 * S].rearrange("p (c t) -> p c t", c=DC)
        wb = W1[:, 0:8192]
        wb3 = wb.rearrange("p (k m) -> p k m", k=DC)

        Pt = W1[:, 8192:8192 + 2 * S]
        sqb = W1[:, 8192 + S:8192 + 2 * S]
        CMASK = W1[:, 8192 + 2 * S:8192 + 2 * S + 2048]
        vtm15 = W1[:, 8192 + 2 * S + 2048:8192 + 2 * S + 3088]
        P.step("g", lambda e: P.inc("DG", e.dma_start(
            out=CMASK, in_=dr["cmask"].ap()), 16) and None)

        P.step("g", lambda e: P.inc("DG", e.dma_start(
            out=wb3, in_=dr["a1_qw"].ap().rearrange("(a p) c -> p a c", p=128)), 16) and None)

        def ev_q(fc, nq, bk):
            P.step("v", lambda e, fc=fc, nq=nq, bk=bk: e.tensor_copy(
                out=qT3[:, fc, ts(nq)], in_=bank(bk)))
        proj_fm(ln3, wb3, DC, ev_q)
        P.step("g", lambda e: P.inc("DG", e.dma_start(
            out=wb3, in_=dr["a1_kw"].ap().rearrange("(a p) c -> p a c", p=128)), 16) and None)

        def ev_k(fc, nq, bk):
            P.step("v", lambda e, fc=fc, nq=nq, bk=bk: e.tensor_copy(
                out=kT3[:, fc, ts(nq)], in_=bank(bk)))
        proj_fm(ln3, wb3, DC, ev_k)
        vtmb = B1.bitcast(BF16)

        def vtm(i):
            if i == 15:
                return vtm15
            return vtmb[:, i * 1040:i * 1040 + 1040]
        P.step("g", lambda e: P.inc("DG", e.dma_start(
            out=wb3, in_=dr["a1_vw"].ap().rearrange("(a p) c -> p a c", p=128)), 16) and None)
        for i in range(NT):
            P.step("v", lambda e, i=i: e.memset(
                vtm(i).rearrange("p (h u) -> p h u", h=16)[:, :, 64:65], 1.0))
            for hv in range(16):
                def _mv(e, i=i, hv=hv):
                    last = None
                    for kc in range(DC):
                        last = mm(e, bank(3)[:, 0:64],
                                  ln3[:, kc, i * 128:(i + 1) * 128],
                                  wb3[:, kc, hv * 64:(hv + 1) * 64],
                                  kc == 0, kc == DC - 1)
                    return last
                P.step("t", _mv)
                P.step("v", lambda e, i=i, hv=hv: e.tensor_copy(
                    out=vtm(i)[:, hv * 65:hv * 65 + 64], in_=bank(3)[:, 0:64]))

        heads = [(h, h // 2, 64 * (h % 2), 65 * h) for h in range(16)]
        ytmb = vtmb[:, 16384:32768].rearrange("p (i f) -> p i f", i=16)
        attn_core(qT3, kT3, vtm, heads, 1.0 / 32.0, True, Pt,
                  sm2p["a1_sh"], ytmb, sqb, NT, cmask=CMASK)

        for i in range(NT):
            P.step("v", lambda e, i=i: e.tensor_copy(
                out=C1[:, i * 1024:(i + 1) * 1024], in_=ytmb[:, i, :]))
        projT = C2.rearrange("p (c t) -> p c t", c=DC)
        for i in range(NT):
            def _t1(e, i=i):
                last = None
                for fc in range(4):
                    last = tr(e, bank(fc).bitcast(BF16)[:, 0:128],
                              C1[:, i * 1024 + fc * 128:i * 1024 + (fc + 1) * 128], IDB)
                return last
            P.step("t", _t1)
            P.step("v", lambda e, i=i: [
                e.tensor_copy(out=projT[:, fc, i * 128:(i + 1) * 128],
                              in_=bank(fc).bitcast(BF16)[:, 0:128])
                for fc in range(4)][-1])

            def _t2(e, i=i):
                last = None
                for fc in range(4, DC):
                    last = tr(e, bank(fc - 4).bitcast(BF16)[:, 0:128],
                              C1[:, i * 1024 + fc * 128:i * 1024 + (fc + 1) * 128], IDB)
                return last
            P.step("t", _t2)
            P.step("v", lambda e, i=i: [
                e.tensor_copy(out=projT[:, fc, i * 128:(i + 1) * 128],
                              in_=bank(fc - 4).bitcast(BF16)[:, 0:128])
                for fc in range(4, DC)][-1])

        P.step("g", lambda e: P.inc("DG", e.dma_start(
            out=wb3, in_=dr["a1_ow"].ap().rearrange("(a p) c -> p a c", p=128)), 16) and None)
        xq = B1[:, 8192:12288]
        x_r = x_src.ap().rearrange("(c p) t -> p c t", p=128)
        xd_r = x_dst.ap().rearrange("(c p) t -> p c t", p=128)
        xq3 = xq.rearrange("p (c t) -> p c t", c=DC)
        for nq in range(NQ):
            P.step("y", lambda e, nq=nq: P.inc(
                "D0", e.dma_start(out=xq3, in_=x_r[:, :, ts(nq)]), 16) and None)
            for fc in range(DC):
                bk = fc % 3

                def _m(e, fc=fc, nq=nq, bk=bk):
                    last = None
                    for kc in range(DC):
                        last = mm(e, bank(bk), wb3[:, kc, ts(fc, 128)],
                                  projT[:, kc, ts(nq)], kc == 0, kc == DC - 1)
                    return last
                P.step("t", _m)
                P.step("v", lambda e, fc=fc, bk=bk: e.tensor_tensor(
                    out=xq3[:, fc, :], in0=xq3[:, fc, :], in1=bank(bk), op=ALU.add))
            P.step("y", lambda e, nq=nq: P.inc(
                "D0", e.dma_start(out=xd_r[:, :, ts(nq)], in_=xq3), 16) and None)

    def stage_mix1(x_src, x_dst):
        NT = S // 128
        CAP = float(int(1.25 * NCORES * S * 2 / NEXP))
        ln3 = C1.rearrange("p (c t) -> p c t", c=DC)
        qT3 = C2.rearrange("p (c t) -> p c t", c=DC)
        kT3 = W2[:, 0:2 * S].rearrange("p (c t) -> p c t", c=2)
        wb = W1[:, 0:8192]
        wb3 = wb.rearrange("p (k m) -> p k m", k=DC)
        Pt = W1[:, 8192:8192 + 2 * S]
        sqb = W1[:, 8192 + S:8192 + 2 * S]
        vbb = W1[:, 8192 + 2 * S:8192 + 2 * S + 256]
        gwsb = W1[:, 8192 + 2 * S + 256:8192 + 2 * S + 320]
        x_r = x_src.ap().rearrange("(c p) t -> p c t", p=128)
        xd_r = x_dst.ap().rearrange("(c p) t -> p c t", p=128)

        P.step("g", lambda e: P.inc("DG", e.dma_start(
            out=wb3, in_=dr["g_qw"].ap().rearrange("(a p) c -> p a c", p=128)), 16) and None)

        def ev_q(fc, nq, bk):
            P.step("s", lambda e, fc=fc, nq=nq, bk=bk: e.activation(
                out=qT3[:, fc, ts(nq)], in_=bank(bk), func=AF.Identity,
                bias=sm2p["g_qb"][:, fc:fc + 1]))
        proj_fm(ln3, wb3, DC, ev_q)
        P.step("g", lambda e: P.inc("DG", e.dma_start(
            out=wb3[:, :, 0:256],
            in_=dr["g_kw"].ap().rearrange("(a p) c -> p a c", p=128)), 16) and None)

        def ev_k(fc, nq, bk):
            P.step("s", lambda e, fc=fc, nq=nq, bk=bk: e.activation(
                out=kT3[:, fc, ts(nq)], in_=bank(bk), func=AF.Identity,
                bias=sm2p["g_kb"][:, fc:fc + 1]))
        for fc in range(2):
            for nq in range(NQ):
                bk = (fc * NQ + nq) % 3

                def _m(e, fc=fc, nq=nq, bk=bk):
                    last = None
                    for kc in range(DC):
                        last = mm(e, bank(bk), wb3[:, kc, fc * 128:fc * 128 + 128],
                                  ln3[:, kc, ts(nq)], kc == 0, kc == DC - 1)
                    return last
                P.step("t", _m)
                ev_k(fc, nq, bk)
        P.step("y", lambda e: P.inc("D0", e.dma_start(
            out=SM[0:1, 0:256], in_=dr["g_vbr"].ap()), 16) and None)
        P.step("t", lambda e: mm(e, bank(3)[:, 0:256], ONESF[0:1, :],
                                 SM[0:1, 0:256], True, True))
        P.step("v", lambda e: e.tensor_copy(out=vbb, in_=bank(3)[:, 0:256]))
        P.step("g", lambda e: P.inc("DG", e.dma_start(
            out=wb3[:, :, 0:256],
            in_=dr["g_vw"].ap().rearrange("(a p) c -> p a c", p=128)), 16) and None)
        vtmr = W2[:, 2 * S:2 * S + NT * 260]

        def vtm(i):
            return vtmr[:, i * 260:i * 260 + 260]
        for i in range(NT):
            P.step("v", lambda e, i=i: e.memset(
                vtm(i).rearrange("p (h u) -> p h u", h=4)[:, :, 64:65], 1.0))
            for kv in range(NKV):
                def _mv(e, i=i, kv=kv):
                    last = None
                    for kc in range(DC):
                        last = mm(e, bank(3)[:, 0:64],
                                  ln3[:, kc, i * 128:(i + 1) * 128],
                                  wb3[:, kc, kv * 64:kv * 64 + 64],
                                  kc == 0, kc == DC - 1)
                    return last
                P.step("t", _mv)
                P.step("v", lambda e, i=i, kv=kv: e.tensor_tensor(
                    out=vtm(i)[:, kv * 65:kv * 65 + 64], in0=bank(3)[:, 0:64],
                    in1=vbb[:, kv * 64:kv * 64 + 64], op=ALU.add))

        def sig(h):
            return 8 * (h // 8) + 2 * (h % 4) + (h // 4) % 2
        heads = [(sig(h), (h // 4) // 2, 64 * ((h // 4) % 2), 65 * (h // 4))
                 for h in range(16)]
        ytmf = B1[:, 0:NT * 1024].rearrange("p (i f) -> p i f", i=NT)
        attn_core(qT3, kT3, vtm, heads, 1.0 / 8.0, False, Pt,
                  sm2p["gq_sh"], ytmf, sqb, NT)
        y_to_fm(ytmf, NT)
        projT = C2.rearrange("p (c t) -> p c t", c=DC)

        P.step("g", lambda e: P.inc("DG", e.dma_start(
            out=wb3, in_=dr["g_ow"].ap().rearrange("(a p) c -> p a c", p=128)), 16) and None)
        s13 = B1.rearrange("p (c t) -> p c t", c=DC)
        xch = SM[:, 0:512]
        scrS = SM[:, 512:1024]
        meanb = SM[:, 1024:1536]
        rstdb = SM[:, 1536:2048]
        scr128m = C2[:, 0:1024].bitcast(F32) if False else None
        for nq in range(NQ):
            bcast_tm_cols(sm2p["lnmean_tm"][:, nq * 4:nq * 4 + 4], 4,
                          B1[:, 0:512] if False else SM[:, 512:1024], bank(4))
            P.step("v", lambda e: e.tensor_copy(out=meanb, in_=bank(4)))
            bcast_tm_cols(sm2p["lnrstd_tm"][:, nq * 4:nq * 4 + 4], 4,
                          SM[:, 512:1024], bank(4))
            P.step("v", lambda e: e.tensor_copy(out=rstdb, in_=bank(4)))
            for fc in range(DC):
                bk = fc % 3

                def _m(e, fc=fc, nq=nq, bk=bk):
                    last = None
                    for kc in range(DC):
                        last = mm(e, bank(bk), wb3[:, kc, ts(fc, 128)],
                                  projT[:, kc, ts(nq)], kc == 0, kc == DC - 1)
                    return last
                P.step("t", _m)
                P.step("y", lambda e, fc=fc, nq=nq: P.inc(
                    "D0", e.dma_start(out=xch, in_=x_r[:, fc, ts(nq)]), 16) and None)
                P.step("s", lambda e, fc=fc, bk=bk: e.activation(
                    out=scrS, in_=bank(bk), func=AF.Identity,
                    bias=sm2p["g_ob"][:, fc:fc + 1]))
                P.step("v", lambda e: e.tensor_tensor(
                    out=xch, in0=xch, in1=meanb, op=ALU.subtract))
                P.step("v", lambda e: e.tensor_tensor(
                    out=xch, in0=xch, in1=rstdb, op=ALU.mult))
                P.step("v", lambda e, fc=fc, nq=nq: e.tensor_tensor(
                    out=s13[:, fc, ts(nq)], in0=xch, in1=scrS, op=ALU.add))

        for nq in range(NQ):
            for fc in range(DC):
                P.step("v", lambda e, fc=fc, nq=nq: e.tensor_copy(
                    out=ln3[:, fc, ts(nq)], in_=s13[:, fc, ts(nq)]))
            for fc in range(DC):
                P.step("v", lambda e, fc=fc, nq=nq: e.tensor_tensor(
                    out=C2[:, fc * 512:(fc + 1) * 512], in0=ln3[:, fc, ts(nq)],
                    in1=ln3[:, fc, ts(nq)], op=ALU.mult))

            def _mmv(e, nq=nq):
                last = None
                for fc in range(DC):
                    last = mm(e, bank(0), ONESB, C2[:, fc * 512:(fc + 1) * 512],
                              fc == 0, fc == DC - 1)
                return last
            P.step("t", _mmv)
            P.step("v", lambda e: e.tensor_scalar(
                out=rstdb, in0=bank(0), scalar1=1.0 / D, scalar2=1e-6,
                op0=ALU.mult, op1=ALU.add))
            P.step("v", lambda e: e.reciprocal(out=rstdb, in_=rstdb))
            P.step("s", lambda e: e.activation(out=rstdb, in_=rstdb, func=AF.Sqrt))
            for fc in range(DC):
                P.step("v", lambda e, fc=fc, nq=nq: e.scalar_tensor_tensor(
                    out=s13[:, fc, ts(nq)], in0=s13[:, fc, ts(nq)],
                    scalar=sm2p["n1w"][:, fc:fc + 1], in1=rstdb,
                    op0=ALU.mult, op1=ALU.mult))
                P.step("v", lambda e, fc=fc, nq=nq: e.tensor_copy(
                    out=ln3[:, fc, ts(nq)], in_=s13[:, fc, ts(nq)]))

        P.step("y", lambda e: P.inc("D0", e.dma_start(
            out=sm2p["gwf"].rearrange("p (k g) -> p k g", k=DC),
            in_=dr["gate_w"].ap().rearrange("(a p) g -> p a g", p=128)), 16) and None)
        gw3 = sm2p["gwf"].rearrange("p (k g) -> p k g", k=DC)
        gpt = SM[:, 0:128]
        srt = SM[:, 128:256]
        em1 = SM[:, 384:512]
        em2 = SM[:, 512:640]
        vm = SM[:, 1152:1280]
        mfin = SM[:, 1280:1408]
        emtb = C2[:, 0:128]

        def _gmm(e):
            last = None
            for i in range(NT):
                for kc in range(DC):
                    last = mm(e, bank(0)[:, 8 * i:8 * i + 8],
                              s13[:, kc, i * 128:(i + 1) * 128],
                              gw3[:, kc, :], kc == 0, kc == DC - 1)
            return last
        P.step("t", _gmm)
        P.step("s", lambda e: e.activation(out=gpt, in_=bank(0)[:, 0:128],
                                           func=AF.Sigmoid))
        for i in range(NT):
            sl8 = slice(8 * i, 8 * i + 8)
            P.step("v", lambda e, sl8=sl8: e.max(out=srt[:, sl8], in_=gpt[:, sl8]))
            P.step("v", lambda e, sl8=sl8, i=i: e.tensor_scalar(
                out=em1[:, sl8], in0=gpt[:, sl8], scalar1=srt[:, 8 * i:8 * i + 1],
                scalar2=None, op0=ALU.is_equal))
            P.step("v", lambda e, sl8=sl8, i=i: e.tensor_scalar(
                out=em2[:, sl8], in0=gpt[:, sl8], scalar1=srt[:, 8 * i + 1:8 * i + 2],
                scalar2=None, op0=ALU.is_equal))
            P.step("v", lambda e, i=i: e.tensor_tensor(
                out=sm2p["mx_ws"][:, i:i + 1], in0=srt[:, 8 * i:8 * i + 1],
                in1=srt[:, 8 * i + 1:8 * i + 2], op=ALU.add))
            P.step("v", lambda e, i=i: e.tensor_scalar(
                out=sm2p["mx_ws"][:, i:i + 1], in0=sm2p["mx_ws"][:, i:i + 1],
                scalar1=1e-6, scalar2=None, op0=ALU.add))
            P.step("v", lambda e, i=i: e.reciprocal(
                out=sm2p["mx_ws"][:, i:i + 1], in_=sm2p["mx_ws"][:, i:i + 1]))
            P.step("v", lambda e, i=i: e.tensor_tensor(
                out=sm2p["mx_w1n"][:, i:i + 1], in0=srt[:, 8 * i:8 * i + 1],
                in1=sm2p["mx_ws"][:, i:i + 1], op=ALU.mult))
            P.step("v", lambda e, i=i: e.tensor_tensor(
                out=sm2p["mx_w2n"][:, i:i + 1], in0=srt[:, 8 * i + 1:8 * i + 2],
                in1=sm2p["mx_ws"][:, i:i + 1], op=ALU.mult))
            P.step("v", lambda e, sl8=sl8: e.tensor_tensor(
                out=emtb[:, sl8], in0=em1[:, sl8], in1=em2[:, sl8], op=ALU.add))
            P.step("t", lambda e, sl8=sl8: mm(
                e, bank(1)[:, sl8], UTB, emtb[:, sl8], True, True))
            P.step("v", lambda e, sl8=sl8: e.tensor_copy(
                out=SM[:, 256:384][:, sl8], in_=bank(1)[:, sl8]))
            P.step("t", lambda e, sl8=sl8: mm(
                e, bank(2)[:, sl8], ONESB, emtb[:, sl8], True, True))
            P.step("v", lambda e, sl8=sl8, i=i: e.tensor_copy(
                out=SM[0:1, 640 + 8 * i:640 + 8 * i + 8], in_=bank(2)[0:1, sl8]))
        totE = SM[0:1, 768:768 + 256].rearrange("p (g u) -> p g u", g=NEXP)
        P.step("v", lambda e: e.memset(SM[0:1, 768:1024], 0.0))
        P.step("v", lambda e: e.tensor_copy(
            out=totE[:, :, 0:NT],
            in_=SM[0:1, 640:640 + 8 * NT].rearrange("p (i g) -> p g i", g=NEXP)))
        for sh in (1, 2, 4, 8):
            if sh >= NT:
                continue
            P.step("v", lambda e, sh=sh: e.tensor_tensor(
                out=totE[:, :, sh:NT], in0=totE[:, :, sh:NT],
                in1=totE[:, :, 0:NT - sh], op=ALU.add))
        P.step("v", lambda e: e.tensor_copy(
            out=SM[0:1, 1041:1049],
            in_=totE[:, :, NT - 1:NT].rearrange("p g u -> p (g u)")))
        P.step("y", lambda e: P.inc("D0", e.dma_start(
            out=cc_in_d.ap(), in_=SM[0:1, 1041:1049]), 16) and None)
        P.step("g", lambda e: P.inc("CC", e.collective_compute(
            "AllGather", ALU.bypass, ins=[cc_in_d.ap()], outs=[cc_out_d.ap()],
            replica_groups=[list(range(NCORES))]), 1) and None)
        P.step("y", lambda e: P.inc("D0", e.dma_start(
            out=SM[0:NCORES, 1024:1032], in_=cc_out_d.ap()), 16) and None)
        P.step("y", lambda e: P.inc("D0", e.dma_start(
            out=SM[0:NEXP, 1040:1041], in_=dr["prefmask"].ap()), 16) and None)
        P.step("t", lambda e: mm(e, bank(3)[0:1, 0:8], SM[0:NEXP, 1040:1041],
                                 SM[0:NCORES, 1024:1032], True, True))
        P.step("v", lambda e: e.tensor_copy(out=SM[0:1, 1032:1040],
                                            in_=bank(3)[0:1, 0:8]))
        for i in range(NT):
            sl8 = slice(8 * i, 8 * i + 8)
            if i == 0:
                P.step("v", lambda e: e.tensor_copy(
                    out=SM[0:1, 1048:1056], in_=SM[0:1, 1032:1040]))
            else:
                P.step("v", lambda e, i=i: e.tensor_tensor(
                    out=SM[0:1, 1048:1056], in0=SM[0:1, 1032:1040],
                    in1=totE[:, :, i - 1:i].rearrange("p g u -> p (g u)"),
                    op=ALU.add))
            P.step("v", lambda e: e.tensor_scalar(
                out=SM[0:1, 1048:1056], in0=SM[0:1, 1048:1056],
                scalar1=-1.0, scalar2=CAP, op0=ALU.mult, op1=ALU.add))
            P.step("t", lambda e: mm(
                e, bank(3)[:, 0:8], ONESF[0:1, :], SM[0:1, 1048:1056],
                True, True))
            P.step("v", lambda e, sl8=sl8: e.tensor_tensor(
                out=vm[:, sl8], in0=SM[:, 256:384][:, sl8], in1=bank(3)[:, 0:8],
                op=ALU.is_lt))
            P.step("v", lambda e, sl8=sl8: e.tensor_tensor(
                out=mfin[:, sl8], in0=em1[:, sl8], in1=vm[:, sl8], op=ALU.mult))
            P.step("v", lambda e, sl8=sl8, i=i: e.tensor_scalar(
                out=mfin[:, sl8], in0=mfin[:, sl8],
                scalar1=sm2p["mx_w1n"][:, i:i + 1], scalar2=None, op0=ALU.mult))
            P.step("v", lambda e, sl8=sl8: e.tensor_tensor(
                out=vm[:, sl8], in0=em2[:, sl8], in1=vm[:, sl8], op=ALU.mult))
            P.step("v", lambda e, sl8=sl8, i=i: e.tensor_scalar(
                out=vm[:, sl8], in0=vm[:, sl8],
                scalar1=sm2p["mx_w2n"][:, i:i + 1], scalar2=None, op0=ALU.mult))
            P.step("v", lambda e, sl8=sl8: e.tensor_tensor(
                out=mfin[:, sl8], in0=mfin[:, sl8], in1=vm[:, sl8], op=ALU.add))

        w1e3 = W1.rearrange("p (k m) -> p k m", k=DC)
        w2e3 = W2.rearrange("p (k m) -> p k m", k=16)
        hb3 = C2[:, 0:8192].rearrange("p (k t) -> p k t", k=16)
        mfb = SM[:, 1408:1920]
        P.step("g", lambda e: P.inc("DG", e.dma_start(
            out=w1e3, in_=dr["moe_w1"].ap()[0].rearrange(
                "(a p) c -> p a c", p=128)), 16) and None)
        P.step("g", lambda e: P.inc("DG2", e.dma_start(
            out=w2e3, in_=dr["moe_w2"].ap()[0].rearrange(
                "(a p) c -> p a c", p=128)), 16) and None)
        dg_w1 = P.val("DG")
        dg_w2 = P.val("DG2")
        bank_user_v = [P.val("V")] * 8
        gelu_mark = [P.val("S")] * 16
        tm_mm2_prev = P.val("T")
        for ex in range(NEXP):
            tm_mm1_last = None
            for nq in range(NQ):
                bcast_tm_cols(SM[:, 1280 + ex:1280 + ex + 8 * NT:8][:, nq * 4:nq * 4 + 4],
                              4, SM[:, 512:1024], bank(4))
                P.step("v", lambda e: e.tensor_copy(out=mfb, in_=bank(4)))
                vm_mfb = P.val("V")
                bank_user_v[4] = vm_mfb
                for mc in range(16):
                    bk = mc % 3

                    def _m1(e, mc=mc, nq=nq, bk=bk):
                        last = None
                        for kc in range(DC):
                            last = mm(e, bank(bk), w1e3[:, kc, mc * 128:(mc + 1) * 128],
                                      ln3[:, kc, ts(nq)], kc == 0, kc == DC - 1)
                        return last
                    P.step("t", _m1, waits=[("DG", dg_w1),
                                            ("S", gelu_mark[mc - 3] if mc >= 3 else gelu_mark[13 + mc]),
                                            ("V", bank_user_v[bk])])
                    tmark = P.val("T")
                    P.step("s", lambda e, mc=mc, bk=bk: e.activation(
                        out=hb3[:, mc, :], in_=bank(bk), func=AF.Gelu),
                        waits=[("T", tmark), ("T", tm_mm2_prev)])
                    gelu_mark[mc] = P.val("S")
                tm_mm1_last = P.val("T")
                if nq == NQ - 1 and ex < NEXP - 1:
                    P.step("g", lambda e, ex=ex: P.inc("DG", e.dma_start(
                        out=w1e3, in_=dr["moe_w1"].ap()[ex + 1].rearrange(
                            "(a p) c -> p a c", p=128)), 16) and None,
                        waits=[("T", tm_mm1_last)])
                    dg_w1 = P.val("DG")
                for fm in range(DC):
                    def _m2(e, fm=fm, nq=nq):
                        last = None
                        for kc in range(16):
                            last = mm(e, bank(fm), w2e3[:, kc, fm * 128:(fm + 1) * 128],
                                      hb3[:, kc, :], kc == 0, kc == 15)
                        return last
                    P.step("t", _m2, waits=[("DG2", dg_w2), ("S", gelu_mark[15]),
                                            ("V", bank_user_v[fm])])
                    tm2 = P.val("T")
                    P.step("v", lambda e, fm=fm: e.tensor_tensor(
                        out=bank(fm), in0=bank(fm), in1=mfb, op=ALU.mult),
                        waits=[("T", tm2), ("V", vm_mfb)])
                    P.step("v", lambda e, fm=fm, nq=nq: e.tensor_tensor(
                        out=s13[:, fm, ts(nq)], in0=s13[:, fm, ts(nq)],
                        in1=bank(fm), op=ALU.add), waits=[("V", P.val("V"))])
                    bank_user_v[fm] = P.val("V")
                tm_mm2_prev = P.val("T")
                if nq == NQ - 1 and ex < NEXP - 1:
                    P.step("g", lambda e, ex=ex: P.inc("DG2", e.dma_start(
                        out=w2e3, in_=dr["moe_w2"].ap()[ex + 1].rearrange(
                            "(a p) c -> p a c", p=128)), 16) and None,
                        waits=[("T", tm_mm2_prev)])
                    dg_w2 = P.val("DG2")

        for nq in range(NQ):
            for fc in range(DC):
                P.step("v", lambda e, fc=fc, nq=nq: e.tensor_copy(
                    out=C1[:, fc * 512:(fc + 1) * 512], in_=s13[:, fc, ts(nq)]))
            for fc in range(DC):
                P.step("v", lambda e, fc=fc: e.tensor_tensor(
                    out=C2[:, fc * 512:(fc + 1) * 512], in0=C1[:, fc * 512:(fc + 1) * 512],
                    in1=C1[:, fc * 512:(fc + 1) * 512], op=ALU.mult))

            def _mmv2(e, nq=nq):
                last = None
                for fc in range(DC):
                    last = mm(e, bank(0), ONESB, C2[:, fc * 512:(fc + 1) * 512],
                              fc == 0, fc == DC - 1)
                return last
            P.step("t", _mmv2)
            P.step("v", lambda e: e.tensor_scalar(
                out=rstdb, in0=bank(0), scalar1=1.0 / D, scalar2=1e-6,
                op0=ALU.mult, op1=ALU.add))
            P.step("v", lambda e: e.reciprocal(out=rstdb, in_=rstdb))
            P.step("s", lambda e: e.activation(out=rstdb, in_=rstdb, func=AF.Sqrt))
            for fc in range(DC):
                P.step("y", lambda e, fc=fc, nq=nq: P.inc(
                    "D0", e.dma_start(out=xch, in_=x_r[:, fc, ts(nq)]), 16) and None)
                P.step("v", lambda e, fc=fc, nq=nq: e.scalar_tensor_tensor(
                    out=scrS, in0=s13[:, fc, ts(nq)],
                    scalar=sm2p["hn2w"][:, fc:fc + 1], in1=rstdb,
                    op0=ALU.mult, op1=ALU.mult))
                P.step("v", lambda e, fc=fc, nq=nq: e.tensor_tensor(
                    out=xch, in0=xch, in1=scrS, op=ALU.add))
                P.step("y", lambda e, fc=fc, nq=nq: P.inc(
                    "D0", e.dma_start(out=xd_r[:, fc, ts(nq)], in_=xch), 16) and None)

    for nm, wdt, ncol in [("cw0", None, 32), ("cw1", None, 32), ("cw2", None, 32),
                          ("cb", None, 32)]:
        sm2col(nm, ncol)
    load_cols(dr["convw"].ap()[:, 0:32], sm2p["cw0"])
    load_cols(dr["convw"].ap()[:, 32:64], sm2p["cw1"])
    load_cols(dr["convw"].ap()[:, 64:96], sm2p["cw2"])
    load_cols(dr["convb"].ap(), sm2p["cb"])
    sm2col("hedge", 32 * 2 * max(NQ - 1, 1))
    sm2col("amax2_tm", S // 128)
    sm2col("sc2_tm", S // 128)
    sm2col("inv2_tm", S // 128)
    sm2col("qkvb", DC)
    load_cols(dr["qkv_b"].ap(), sm2p["qkvb"])
    sm2col("att0_sh", 1)
    sm2col("att0_t1", 1)
    sm2col("att0_t2", 1)
    sm2col("att0_tmp", 1)
    sm2col("att0_rc", 4)
    sm2col("att0_inv4y", S // 128)
    sm2col("ac_t1", 1); sm2col("ac_t2", 1); sm2col("ac_tmp", 1)
    sm2col("ac_rc", 4); sm2col("a1_sh", 1); sm2col("gq_sh", 1)
    sm2col("lnmean_tm", S // 128); sm2col("lnrstd_tm", S // 128)
    sm2col("g_qb", DC); load_cols(dr["g_qb"].ap(), sm2p["g_qb"])
    sm2col("g_kb", 2); load_cols(dr["g_kb"].ap(), sm2p["g_kb"])
    sm2col("g_ob", DC); load_cols(dr["g_ob"].ap(), sm2p["g_ob"])
    sm2col("n1w", DC); load_cols(dr["norm1_w"].ap(), sm2p["n1w"])
    sm2col("hn2w", DC); load_cols(dr["norm2_w"].ap(), sm2p["hn2w"])
    P.step("v", lambda e: e.tensor_scalar(
        out=sm2p["hn2w"], in0=sm2p["hn2w"], scalar1=0.5, scalar2=None,
        op0=ALU.mult))
    sm2col("gwf", 64)
    sm2col("mx_ws", S // 128)
    sm2col("mx_w1n", S // 128)
    sm2col("mx_w2n", S // 128)

    w1m = stash_bitlinear(dr["w1T"], w1q_d, D, HID0, "w1")
    w2m = stash_bitlinear(dr["w2T"], w2q_d, HID0, D, "w2")
    qm = stash_ternary(dr["wqkvT"], wqkvq_d, "wq", 7.0)
    om = stash_ternary(dr["woT"], woq_d, "wo", 7.0)

    marks = {}

    def mark(name):
        marks[name] = P.nsteps
    mark("setup")
    seq = [i for i in SEQ if i <= stop]
    for pos, sid in enumerate(seq):
        mark(f"pre{pos}_{sid}")
        src_t = dr["x_fm"] if pos == 0 else x_mir
        dst_t = y_out if pos == len(seq) - 1 else x_mir
        if sid in (1, 3):
            stage_ln(src_t, "q8")
            stage_ffn0(src_t, dst_t, w1m, w2m)
        elif sid == 2:
            stage_ln(src_t, "q4")
            tap("r4", C1[:, 0:16384])
            tap("inv4", SM[:, 0:S])
            stage_att0(src_t, dst_t, qm, om)
        elif sid == 5:
            stage_ln(src_t, "bf")
            stage_att1(src_t, dst_t)
        elif sid in (4, 6):
            stage_ln(src_t, "bf", save_stats=True)
            stage_mix1(src_t, dst_t)

    mark("end")
    if os.environ.get("KPROF"):
        prev = 0
        for k2, v2 in marks.items():
            print(f"  steps {k2}: +{v2 - prev} (cum {v2})")
            prev = v2
    P.step("y", lambda e: None)
    es.close()
    return nc


def _prep_inputs(inputs, S):
    f32 = np.float32
    g = {}
    g["w1T"] = np.ascontiguousarray(inputs["L0_ffn_w1"].T.astype(f32))
    g["w2T"] = np.ascontiguousarray(inputs["L0_ffn_w2"].T.astype(f32))
    def cols(v):
        v = np.asarray(v, f32).reshape(-1)
        return np.ascontiguousarray(v.reshape(-1, 128).T)
    cw = np.asarray(inputs["L0_ffn_conv_w"], f32).reshape(HID0, 3)
    g["convw"] = np.concatenate([cols(cw[:, k]) for k in range(3)], 1)
    g["convb"] = cols(inputs["L0_ffn_conv_b"])
    g["wqkvT"] = np.ascontiguousarray(np.asarray(inputs["L0_attn_Wqkv_w"], f32).T)
    g["woT"] = np.ascontiguousarray(np.asarray(inputs["L0_attn_Wo_w"], f32).T)
    g["qkv_b"] = cols(inputs["L0_attn_Wqkv_b"])
    U = np.asarray(inputs["L0_attn_U"], f32)
    u8 = np.zeros((128, 256), f32)
    for j in range(4):
        u8[32 * j:32 * j + 32, 64 * j:64 * j + 32] = U[0:32, :]
        u8[32 * j:32 * j + 32, 64 * j + 32:64 * j + 64] = U[32:64, :]
    g["u8"] = u8
    half = 32
    inv = 1.0 / 10000.0 ** (np.arange(half, dtype=f32) / half)
    pos_q = np.arange(S, dtype=f32)
    cb = np.arange(S) // (S // 16)
    tl = np.arange(S) % (S // 16)
    pos_k = (tl * 16 + cb).astype(f32)
    cs4 = np.empty((128, 4 * S), f32)
    cs4[:, 0 * S:1 * S] = np.tile(np.cos(pos_q[None, :] * inv[:, None]), (4, 1))
    cs4[:, 1 * S:2 * S] = np.tile(np.sin(pos_q[None, :] * inv[:, None]), (4, 1))
    cs4[:, 2 * S:3 * S] = np.tile(np.cos(pos_k[None, :] * inv[:, None]), (4, 1))
    cs4[:, 3 * S:4 * S] = np.tile(np.sin(pos_k[None, :] * inv[:, None]), (4, 1))
    g["cs4"] = cs4
    perm = np.empty(D, np.int64)
    for h in range(H16):
        for j in range(DK):
            perm[64 * h + j] = j * 16 + h
    g["a1_qw"] = np.ascontiguousarray(np.asarray(inputs["L1_attn_qw"], f32)[:, perm])
    g["a1_kw"] = np.ascontiguousarray(np.asarray(inputs["L1_attn_kw"], f32)[:, perm])
    g["a1_vw"] = np.ascontiguousarray(np.asarray(inputs["L1_attn_vw"], f32)[:, perm])
    g["a1_ow"] = np.ascontiguousarray(np.asarray(inputs["L1_attn_ow"], f32)[perm, :])
    def sig(h):
        return 8 * (h // 8) + 2 * (h % 4) + (h // 4) % 2
    qperm = np.empty(D, np.int64)
    for h in range(H16):
        for j in range(DK):
            qperm[64 * sig(h) + j] = 64 * h + j
    g["g_qw"] = np.ascontiguousarray(np.asarray(inputs["L1_ffn_qw"], f32)[:, qperm])
    g["g_qb"] = cols(np.asarray(inputs["L1_ffn_qb"], f32)[qperm])
    g["g_kw"] = np.asarray(inputs["L1_ffn_kw"], f32)
    g["g_kb"] = cols(inputs["L1_ffn_kb"])
    g["g_vw"] = np.asarray(inputs["L1_ffn_vw"], f32)
    g["g_vb"] = cols(inputs["L1_ffn_vb"])
    g["g_vbr"] = np.asarray(inputs["L1_ffn_vb"], f32).reshape(1, NKV * DK)
    g["g_ow"] = np.ascontiguousarray(np.asarray(inputs["L1_ffn_ow"], f32)[qperm, :])
    g["g_ob"] = cols(inputs["L1_ffn_ob"])
    g["gate_w"] = np.asarray(inputs["L1_ffn_gate_w"], f32)
    g["moe_w1"] = np.asarray(inputs["L1_ffn_w1"], f32)
    g["moe_w2"] = np.asarray(inputs["L1_ffn_w2"], f32)
    g["norm1_w"] = cols(inputs["L1_ffn_norm1_w"])
    g["norm2_w"] = cols(inputs["L1_ffn_norm2_w"])
    g["ident_f"] = np.eye(128, dtype=f32)
    g["ident_b"] = np.eye(128).astype(ml_dtypes.bfloat16)
    g["ones_b"] = np.ones((128, 128), ml_dtypes.bfloat16)
    g["ones_f"] = np.ones((128, 128), f32)
    g["ut_b"] = np.triu(np.ones((128, 128)), 1).astype(ml_dtypes.bfloat16)
    bd = np.zeros((128, 128), f32)
    bd[0:64, 0:64] = 1.0
    bd[64:128, 64:128] = 1.0
    g["bd64_b"] = bd.astype(ml_dtypes.bfloat16)
    eo4 = np.zeros((128, 128), f32)
    for j in range(32):
        eo4[2 * j, j] = 1.0; eo4[2 * j + 1, 32 + j] = 1.0
        eo4[64 + 2 * j, 64 + j] = 1.0; eo4[64 + 2 * j + 1, 96 + j] = 1.0
    g["eo_b"] = eo4.astype(ml_dtypes.bfloat16)
    trs = np.zeros((128, 128), f32)
    trs[0:64, 0:64] = np.eye(64)
    trs[64:128, 64:128] = np.eye(64)
    g["trsel_b"] = trs.astype(ml_dtypes.bfloat16)
    bd32 = np.zeros((128, 128), f32)
    for j in range(4):
        bd32[32 * j:32 * (j + 1), 32 * j:32 * (j + 1)] = 1.0
    g["bd32_b"] = bd32.astype(ml_dtypes.bfloat16)
    g["iotaE"] = np.broadcast_to(np.arange(NEXP, dtype=f32), (128, NEXP)).copy()
    cm = np.zeros((128, 2048), f32)
    for r in range(4):
        mi = np.arange(128)[:, None]
        qi = np.arange(512)[None, :]
        cm[:, r * 512:(r + 1) * 512] = (qi >= 128 * r + mi)
    g["cmask"] = cm.astype(ml_dtypes.bfloat16)
    g["onecol_b"] = np.ones((128, 1), ml_dtypes.bfloat16)

    x = np.asarray(inputs["L0_attn_x"], f32)
    per_core = []
    for c in range(NCORES):
        m = dict(g)
        m["x_fm"] = np.ascontiguousarray(x[c].T)
        pm = np.zeros((NEXP, 1), f32)
        pm[:c if c <= NEXP else NEXP] = 1.0
        m["prefmask"] = pm
        per_core.append(m)
    return per_core


def kernel(**inputs):
    S = np.asarray(inputs["L0_attn_x"]).shape[1]
    per_core = _prep_inputs(inputs, S)
    nc = build_program(S=S)
    res = run_bass_kernel_spmd(nc, per_core, core_ids=list(range(NCORES)))
    outs = [r["y_fm"] for r in res.results]
    return np.stack([o.T for o in outs], 0).astype(np.float32)


# version 10
# speedup vs baseline: 1.1929x; 1.0051x over previous
import math
import os
from contextlib import ExitStack

import numpy as np
import ml_dtypes

import concourse.bass as bass
import concourse.mybir as mybir
from concourse.bass_utils import run_bass_kernel_spmd

F32 = mybir.dt.float32
BF16 = mybir.dt.bfloat16
I32 = mybir.dt.int32
AF = mybir.ActivationFunctionType
ALU = mybir.AluOpType
AX = mybir.AxisListType

NCORES = 8
D = 1024
DC = D // 128
H16 = 16
DK = 64
HID0 = 4096
NEXP = 8
EHID = 2048
NKV = 4

STOP_AFTER = 99
SEQ = [1, 2, 3, 4, 5, 6]
TAP = None
ATT0_PHASE = 9
FFN0_PHASE = 9

ALLSEMS = ["T", "V", "S", "G", "D0", "D1", "D2", "D3", "DG", "DG2", "CC"]


def ts(i, n=512):
    return slice(i * n, (i + 1) * n)


class Prog:
    def __init__(self, nc, S, block, es):
        self.nc, self.S, self.block, self.es = nc, S, block, es
        self.sems = {}
        for nm in ALLSEMS:
            self.sems[nm] = [es.enter_context(nc.semaphore(nm)), 0]
        self.emit = {"t": block.tensor, "v": block.vector, "s": block.scalar,
                     "g": block.gpsimd, "y": block.sync}
        self.osem = {"t": "T", "v": "V", "s": "S", "g": "G", "y": None}
        self.nsteps = 0
        self.lastw = {k: {} for k in self.emit}

    def inc(self, name, instr, k=1):
        s = self.sems[name]
        instr.then_inc(s[0], k)
        s[1] += k
        return s[1]

    def val(self, name):
        return self.sems[name][1]

    def step(self, kind, fn, waits=None):
        if waits is None:
            wl = [(nm, self.sems[nm][1]) for nm in ALLSEMS
                  if self.sems[nm][1] > 0]
        else:
            wl = []
            for w in waits:
                if isinstance(w, tuple):
                    if w[1] > 0:
                        wl.append(w)
                elif self.sems[w][1] > 0:
                    wl.append((w, self.sems[w][1]))
        myname = self.osem[kind]
        self.nsteps += 1

        lw = self.lastw[kind]
        wl = [(nm, v) for nm, v in wl if lw.get(nm, 0) < v]
        for nm, v in wl:
            lw[nm] = v

        def _(e):
            for nm, v in wl:
                e.wait_ge(self.sems[nm][0], v)
            last = fn(e)
            if myname is not None and last is not None:
                self.inc(myname, last)
        self.emit[kind](_)

    def barrier_waits(self):
        return [(nm, self.sems[nm][1]) for nm in ALLSEMS]


def mm(eng, out, lhsT, rhs, start, stop, tp="auto"):
    if tp == "auto":
        bp_l = lhsT.base_partition()
        bp_o = out.base_partition()
        tp = (bp_l, bp_o) if (bp_l or bp_o) else None
    return eng.matmul(out, lhsT, rhs, start=start, stop=stop, tile_position=tp)


def build_program(S=2048, stop_after=None):
    stop = STOP_AFTER if stop_after is None else stop_after
    nc = bass.Bass("TRN2", target_bir_lowering=False, debug=False)
    es = ExitStack()
    NQ = S // 512

    dr = {}

    def din(name, shape, dt=F32):
        dr[name] = nc.dram_tensor(name, shape, dt, kind="ExternalInput")
        return dr[name]

    din("x_fm", [D, S])
    din("w1T", [D, HID0]); din("w2T", [HID0, D])
    din("convw", [128, 3 * HID0 // 128]); din("convb", [128, HID0 // 128])
    din("wqkvT", [D, D]); din("woT", [D, D]); din("qkv_b", [128, DC])
    din("u8", [128, 256]); din("cs4", [128, 4 * S])
    din("a1_qw", [D, D]); din("a1_kw", [D, D]); din("a1_vw", [D, D]); din("a1_ow", [D, D])
    din("g_qw", [D, D]); din("g_qb", [128, DC])
    din("g_kw", [D, NKV * DK]); din("g_kb", [128, 2])
    din("g_vw", [D, NKV * DK]); din("g_vb", [128, 2]); din("g_vbr", [1, NKV * DK])
    din("g_ow", [D, D]); din("g_ob", [128, DC])
    din("gate_w", [D, NEXP])
    din("moe_w1", [NEXP, D, EHID]); din("moe_w2", [NEXP, EHID, D])
    din("norm1_w", [128, DC]); din("norm2_w", [128, DC])
    din("ident_f", [128, 128]); din("ident_b", [128, 128], BF16)
    din("ones_b", [128, 128], BF16); din("ones_f", [128, 128])
    din("ut_b", [128, 128], BF16)
    din("bd64_b", [128, 128], BF16)
    din("bd32_b", [128, 128], BF16)
    din("eo_b", [128, 128], BF16)
    din("trsel_b", [128, 128], BF16)
    din("iotaE", [128, NEXP])
    din("cmask", [128, 2048], BF16)
    din("prefmask", [NEXP, 1])
    din("onecol_b", [128, 1], BF16)

    y_out = nc.dram_tensor("y_fm", [D, S], F32, kind="ExternalOutput")
    dbg_out = nc.dram_tensor("dbg", [128, 16384], F32, kind="ExternalOutput") if TAP else None
    x_mir = nc.dram_tensor("x_mirror", [D, S], F32, kind="Internal")
    w1q_d = nc.dram_tensor("w1q_stash", [D, HID0], BF16, kind="Internal")
    w2q_d = nc.dram_tensor("w2q_stash", [HID0, D], BF16, kind="Internal")
    wqkvq_d = nc.dram_tensor("wqkvq_stash", [D, D], BF16, kind="Internal")
    woq_d = nc.dram_tensor("woq_stash", [D, D], BF16, kind="Internal")
    cc_in_d = nc.dram_tensor("cc_in", [1, NEXP], F32, kind="Internal")
    cc_out_d = nc.dram_tensor("cc_out", [NCORES, NEXP], F32, kind="Internal")

    big = 16384
    B1 = es.enter_context(nc.sbuf_tensor("B1", [128, big], F32))
    C1 = es.enter_context(nc.sbuf_tensor("C1", [128, big], BF16))
    C2 = es.enter_context(nc.sbuf_tensor("C2", [128, big], BF16))
    W1 = es.enter_context(nc.sbuf_tensor("W1", [128, 16384], BF16))
    W2 = es.enter_context(nc.sbuf_tensor("W2", [128, 16384], BF16))
    SM = es.enter_context(nc.sbuf_tensor("SM", [128, max(S, 2048)], F32))
    SM2 = es.enter_context(nc.sbuf_tensor("SM2", [128, 960], F32))
    CNf = es.enter_context(nc.sbuf_tensor("CNf", [128, 512 + NEXP], F32))
    CNb = es.enter_context(nc.sbuf_tensor("CNb", [128, 897], BF16))
    PSA = es.enter_context(nc.psum_tensor("PSA", [128, 2048], F32))
    PSB = es.enter_context(nc.psum_tensor("PSB", [128, 2048], F32))

    block = es.enter_context(nc.Block())
    P = Prog(nc, S, block, es)

    def bank(i, n=1):
        assert 0 <= i and i + n <= 8
        if i < 4:
            assert i + n <= 4
            return PSA[:, i * 512:(i + n) * 512]
        assert i >= 4
        return PSB[:, (i - 4) * 512:(i - 4 + n) * 512]

    IDF = CNf[:, 0:128]
    ONESF = CNf[:, 128:256]
    IOTAE = CNf[:, 256:256 + NEXP]
    IDB = CNb[:, 0:128]
    ONESB = CNb[:, 128:256]
    UTB = CNb[:, 256:384]
    BD64 = CNb[:, 384:512]
    EOB4 = CNb[:, 512:640]
    ONECOL = CNb[:, 640:641]
    BD32 = CNb[:, 641:769]
    TRSEL = CNb[:, 769:897]
    U8 = CNf[:, 256 + NEXP:512 + NEXP]

    sm2p = {"n": 0}

    def sm2col(name, n=1):
        c = sm2p["n"]; sm2p["n"] = c + n
        assert sm2p["n"] <= 960, (name, sm2p["n"])
        sm2p[name] = SM2[:, c:c + n]
        return sm2p[name]

    def _(e):
        for dst, src in [(IDF, "ident_f"), (ONESF, "ones_f"), (IOTAE, "iotaE"),
                         (IDB, "ident_b"), (ONESB, "ones_b"), (UTB, "ut_b"),
                         (BD64, "bd64_b"), (BD32, "bd32_b"), (EOB4, "eo_b"),
                         (TRSEL, "trsel_b"),
                         (ONECOL, "onecol_b"), (U8, "u8")]:
            P.inc("D0", e.dma_start(out=dst, in_=dr[src].ap()), 16)
    P.step("y", _)

    def load_cols(src_ap, dst_cols):
        def _(e):
            P.inc("D0", e.dma_start(out=dst_cols, in_=src_ap), 16)
        P.step("y", _)

    def rows_to_cols(src_sbuf, n, dst_cols, scale=1.0, pbank=6):
        for g0 in range(0, n, 4):
            g = min(4, n - g0)

            def _t(e, g0=g0, g=g):
                last = None
                for j in range(g):
                    last = e.transpose(bank(pbank)[:, ts(j, 128)],
                                       src_sbuf[:, ts(g0 + j, 128)], IDF)
                return last
            P.step("t", _t)

            def _s(e, g0=g0, g=g):
                last = None
                for j in range(g):
                    last = e.activation(out=dst_cols[:, g0 + j:g0 + j + 1],
                                        in_=bank(pbank)[:, ts(j, 128)][:, 0:1],
                                        func=AF.Copy, scale=float(scale))
                return last
            P.step("s", _s)

    def bcast_tm_cols(tm_cols, nsub, scr128, out_psum, tbank=7):
        def _t1(e):
            last = None
            for j in range(nsub):
                last = e.transpose(bank(tbank)[0:1, j * 128:(j + 1) * 128],
                                   tm_cols[:, j:j + 1], IDF)
            return last
        P.step("t", _t1)
        P.step("v", lambda e: e.tensor_copy(out=scr128[0:1, 0:128 * nsub],
                                            in_=bank(tbank)[0:1, 0:128 * nsub]))

        def _t2(e):
            last = None
            for j in range(nsub):
                last = mm(e, out_psum[:, ts(j, 128)], ONESF[0:1, :],
                          scr128[0:1, j * 128:(j + 1) * 128], True, True)
            return last
        P.step("t", _t2)

    def stash_bitlinear(win, stash, rows, cols, tag):
        nch, npc = rows // 128, cols // 1024
        npan = nch * npc
        sl, scr, scr2 = B1[:, 0:1024], B1[:, 1024:2048], B1[:, 2048:3072]
        acc = sm2col(tag + "_abs", npan)
        tot = sm2col(tag + "_tot")
        inv = sm2col(tag + "_inv")
        scol = sm2col(tag + "_s")
        cs = sm2col(tag + "_cs", cols // 128)
        win_r = win.ap().rearrange("(a p) c -> p a c", p=128)
        st_r = stash.ap().rearrange("(a p) c -> p a c", p=128)

        for i in range(npan):
            a, j = divmod(i, npc)
            P.step("y", lambda e, a=a, j=j: P.inc(
                "D0", e.dma_start(out=sl, in_=win_r[:, a, ts(j, 1024)]), 16) and None)
            P.step("s", lambda e, i=i: e.activation(
                out=scr, in_=sl, func=AF.Abs, accum_out=acc[:, i:i + 1]))
        P.step("v", lambda e: e.tensor_reduce(out=tot, in_=acc, axis=AX.X, op=ALU.add))
        P.step("t", lambda e: mm(e, bank(7)[:, 0:1], ONESF, tot, True, True))
        P.step("v", lambda e: e.tensor_scalar(
            out=inv, in0=bank(7)[:, 0:1], scalar1=1.0 / (rows * cols),
            scalar2=1e-5, op0=ALU.mult, op1=ALU.max))
        P.step("v", lambda e: e.reciprocal(out=scol, in_=inv))

        qp = C1[:, 0:1024]
        for i in range(npan):
            a, j = divmod(i, npc)
            P.step("y", lambda e, a=a, j=j: P.inc(
                "D0", e.dma_start(out=sl, in_=win_r[:, a, ts(j, 1024)]), 16) and None)
            P.step("v", lambda e: e.tensor_scalar(
                out=scr, in0=sl, scalar1=scol, scalar2=1.0,
                op0=ALU.mult, op1=ALU.add))
            P.step("v", lambda e: e.tensor_copy(
                out=B1[:, 3072:4096].bitcast(I32), in_=scr))
            P.step("v", lambda e: e.tensor_copy(
                out=scr2, in_=B1[:, 3072:4096].bitcast(I32)))
            P.step("v", lambda e: e.tensor_scalar(
                out=scr2, in0=scr2, scalar1=1.0, scalar2=1.0,
                op0=ALU.subtract, op1=ALU.min))
            P.step("v", lambda e: e.tensor_scalar(
                out=qp, in0=scr2, scalar1=-1.0, scalar2=None, op0=ALU.max))
            P.step("y", lambda e, a=a, j=j: P.inc(
                "D0", e.dma_start(out=st_r[:, a, ts(j, 1024)], in_=qp), 16) and None)

        wp = W1[:, 0:1024]
        csb = B1[:, 0:1024]
        for j in range(npc):
            for a in range(nch):
                P.step("y", lambda e, a=a, j=j: P.inc(
                    "D0", e.dma_start(out=wp, in_=st_r[:, a, ts(j, 1024)]), 16) and None)
                P.step("t", lambda e, a=a: [
                    mm(e, bank(4), ONESB, wp[:, ts(0)], a == 0, a == nch - 1),
                    mm(e, bank(5), ONESB, wp[:, ts(1)], a == 0, a == nch - 1)][-1])
            P.step("v", lambda e: e.tensor_copy(out=csb[:, 0:512], in_=bank(4)))
            P.step("v", lambda e: e.tensor_copy(out=csb[:, 512:1024], in_=bank(5)))
            rows_to_cols(csb, 8, cs[:, j * 8:(j + 1) * 8], 127.0)
        return dict(inv=inv, s=scol, cs=cs)

    def stash_ternary(win, stash, tag, qoff=7.0):
        rows = cols = D
        nch, npc = rows // 128, cols // 1024
        sl, scr = B1[:, 0:1024], B1[:, 1024:2048]
        acc = sm2col(tag + "_abs", nch * npc)
        tot = sm2col(tag + "_tot")
        thr = sm2col(tag + "_thr")
        alpha = sm2col(tag + "_alpha", cols // 128)
        cs = sm2col(tag + "_cs", cols // 128)
        win_r = win.ap().rearrange("(a p) c -> p a c", p=128)
        st_r = stash.ap().rearrange("(a p) c -> p a c", p=128)
        absb = B1[:, 2048:3072]

        for j in range(npc):
            for a in range(nch):
                P.step("y", lambda e, a=a, j=j: P.inc(
                    "D0", e.dma_start(out=sl, in_=win_r[:, a, ts(j, 1024)]), 16) and None)
                P.step("s", lambda e, i=a * npc + j: e.activation(
                    out=scr, in_=sl, func=AF.Abs, accum_out=acc[:, i:i + 1]))
                P.step("v", lambda e: e.tensor_copy(out=C1[:, 0:1024], in_=scr))
                P.step("t", lambda e, a=a: [
                    mm(e, bank(4), ONESB, C1[:, 0:512], a == 0, a == nch - 1),
                    mm(e, bank(5), ONESB, C1[:, 512:1024], a == 0, a == nch - 1)][-1])
            P.step("v", lambda e: e.tensor_copy(out=absb[:, 0:512], in_=bank(4)))
            P.step("v", lambda e: e.tensor_copy(out=absb[:, 512:1024], in_=bank(5)))
            rows_to_cols(absb, 8, alpha[:, j * 8:(j + 1) * 8], 1.0 / rows)
        P.step("v", lambda e: e.tensor_reduce(out=tot, in_=acc, axis=AX.X, op=ALU.add))
        P.step("t", lambda e: mm(e, bank(7)[:, 0:1], ONESF, tot, True, True))
        P.step("v", lambda e: e.tensor_scalar(
            out=thr, in0=bank(7)[:, 0:1], scalar1=0.05 / (rows * cols),
            scalar2=None, op0=ALU.mult))

        qp = C1[:, 0:1024]
        sgn = B1[:, 2048:3072]
        msk = B1[:, 3072:4096]
        for i in range(nch * npc):
            a, j = divmod(i, npc)
            P.step("y", lambda e, a=a, j=j: P.inc(
                "D0", e.dma_start(out=sl, in_=win_r[:, a, ts(j, 1024)]), 16) and None)
            P.step("s", lambda e: e.sign(out=sgn, in_=sl))
            P.step("s", lambda e: e.activation(out=scr, in_=sl, func=AF.Abs))
            P.step("v", lambda e: e.tensor_scalar(
                out=msk, in0=scr, scalar1=thr, scalar2=None, op0=ALU.is_ge))
            P.step("v", lambda e: e.tensor_tensor(out=qp, in0=sgn, in1=msk, op=ALU.mult))
            P.step("y", lambda e, a=a, j=j: P.inc(
                "D0", e.dma_start(out=st_r[:, a, ts(j, 1024)], in_=qp), 16) and None)

        wp = W1[:, 0:1024]
        csb = B1[:, 0:1024]
        for j in range(npc):
            for a in range(nch):
                P.step("y", lambda e, a=a, j=j: P.inc(
                    "D0", e.dma_start(out=wp, in_=st_r[:, a, ts(j, 1024)]), 16) and None)
                P.step("t", lambda e, a=a: [
                    mm(e, bank(4), ONESB, wp[:, ts(0)], a == 0, a == nch - 1),
                    mm(e, bank(5), ONESB, wp[:, ts(1)], a == 0, a == nch - 1)][-1])
            P.step("v", lambda e: e.tensor_copy(out=csb[:, 0:512], in_=bank(4)))
            P.step("v", lambda e: e.tensor_copy(out=csb[:, 512:1024], in_=bank(5)))
            rows_to_cols(csb, 8, cs[:, j * 8:(j + 1) * 8], float(qoff))
        return dict(alpha=alpha, cs=cs)

    def stage_ln(x_src, mode, save_stats=False):
        Qv = {"q8": 127.0, "q4": 7.0}.get(mode)
        floor_ = {"q8": 1e-5, "q4": 1e-6}.get(mode)
        x_r = x_src.ap().rearrange("(c p) t -> p c t", p=128)
        C13 = C1.rearrange("p (c t) -> p c t", c=DC)
        for q in range(NQ):
            xq = B1[:, 0:4096]
            xq3 = xq.rearrange("p (c t) -> p c t", c=DC)
            lnq = B1[:, 4096:8192]
            lnq3 = lnq.rearrange("p (c t) -> p c t", c=DC)
            rsc = B1[:, 8192:10240]
            slotA = B1[:, 12288:12800]
            slotB = B1[:, 12800:13312]
            slotC = B1[:, 13312:13824]
            slotM = B1[:, 13824:14336]
            tmcol = B1[:, 14336:14464]
            tmcol2 = B1[:, 14464:14592]
            scr128 = B1[:, 14592:15104]
            xb = C2[:, 0:4096]
            xb3 = xb.rearrange("p (c t) -> p c t", c=DC)
            sq = C2[:, 4096:8192]
            sq3 = sq.rearrange("p (c t) -> p c t", c=DC)

            P.step("y", lambda e, q=q: P.inc(
                "D0", e.dma_start(out=xq3, in_=x_r[:, :, ts(q)]), 16) and None)
            P.step("v", lambda e: e.tensor_copy(out=xb, in_=xq))
            P.step("v", lambda e: e.tensor_tensor(out=sq, in0=xb, in1=xb, op=ALU.mult))

            def _mm(e):
                last = None
                for c in range(DC):
                    mm(e, bank(0), ONESB, xb3[:, c, :], c == 0, c == DC - 1)
                    last = mm(e, bank(1), ONESB, sq3[:, c, :], c == 0, c == DC - 1)
                return last
            P.step("t", _mm)
            P.step("v", lambda e: [
                e.tensor_scalar(out=slotA, in0=bank(0), scalar1=1.0 / D,
                                scalar2=None, op0=ALU.mult),
                e.tensor_scalar(out=slotB, in0=bank(1), scalar1=1.0 / D,
                                scalar2=None, op0=ALU.mult)][-1])
            P.step("v", lambda e: e.tensor_tensor(out=slotC, in0=slotA,
                                                  in1=slotA, op=ALU.mult))
            P.step("v", lambda e: e.scalar_tensor_tensor(
                out=slotB, in0=slotB, scalar=1e-5, in1=slotC,
                op0=ALU.add, op1=ALU.subtract))
            P.step("v", lambda e: e.reciprocal(out=slotB, in_=slotB))
            P.step("s", lambda e: e.activation(out=slotC, in_=slotB, func=AF.Sqrt))
            if save_stats:
                rows_to_cols(slotA, 4, sm2p["lnmean_tm"][:, q * 4:q * 4 + 4])
                rows_to_cols(slotC, 4, sm2p["lnrstd_tm"][:, q * 4:q * 4 + 4])

            for c in range(DC):
                P.step("v", lambda e, c=c: e.tensor_tensor(
                    out=lnq3[:, c, :], in0=xq3[:, c, :], in1=slotA, op=ALU.subtract))
                P.step("v", lambda e, c=c: e.tensor_tensor(
                    out=lnq3[:, c, :], in0=lnq3[:, c, :], in1=slotC, op=ALU.mult))
                if mode == "bf":
                    P.step("v", lambda e, c=c, q=q: e.tensor_copy(
                        out=C13[:, c, ts(q)], in_=lnq3[:, c, :]))
                else:
                    slotAB = B1[:, 10240:10752]
                    P.step("s", lambda e, c=c: e.activation(
                        out=slotAB, in_=lnq3[:, c, :], func=AF.Abs))
                    if c == 0:
                        P.step("v", lambda e: e.tensor_copy(out=slotM, in_=slotAB))
                    else:
                        P.step("v", lambda e: e.tensor_tensor(
                            out=slotM, in0=slotM, in1=slotAB, op=ALU.max))
            if mode == "bf":
                continue

            def _t(e):
                last = None
                for j in range(4):
                    last = e.transpose(bank(2)[:, ts(j, 128)],
                                       slotM[:, ts(j, 128)], IDF)
                return last
            P.step("t", _t)
            P.step("v", lambda e: [
                e.tensor_reduce(out=tmcol[:, j:j + 1],
                                in_=bank(2)[:, ts(j, 128)], axis=AX.X, op=ALU.max)
                for j in range(4)][-1])
            P.step("v", lambda e: e.tensor_single_scalar(
                out=tmcol[:, 0:4], in_=tmcol[:, 0:4], scalar=floor_, op=ALU.max))
            P.step("v", lambda e: e.tensor_scalar(
                out=tmcol2[:, 0:4], in0=tmcol[:, 0:4], scalar1=1.0 / Qv,
                scalar2=None, op0=ALU.mult))
            P.step("v", lambda e: e.reciprocal(out=tmcol[:, 0:4], in_=tmcol[:, 0:4]))
            P.step("v", lambda e: e.tensor_scalar(
                out=tmcol[:, 0:4], in0=tmcol[:, 0:4], scalar1=Qv,
                scalar2=None, op0=ALU.mult))
            bcast_tm_cols(tmcol[:, 0:4], 4, scr128, bank(3))
            bcast_tm_cols(tmcol2[:, 0:4], 4, scr128, bank(2))
            P.step("v", lambda e, q=q: e.tensor_copy(out=SM[:, ts(q)], in_=bank(2)))

            for hh in range(4):
                def _r1(e, hh=hh):
                    last = None
                    for c in range(2):
                        last = e.tensor_tensor(
                            out=rsc[:, ts(c)], in0=lnq3[:, hh * 2 + c, :],
                            in1=bank(3), op=ALU.mult)
                    return last
                P.step("v", _r1)
                P.step("v", lambda e: e.tensor_scalar(
                    out=rsc[:, 0:1024], in0=rsc[:, 0:1024], scalar1=Qv,
                    scalar2=None, op0=ALU.add))
                P.step("v", lambda e: e.tensor_copy(
                    out=B1[:, 15104:16128].bitcast(I32), in_=rsc[:, 0:1024]))

                def _w(e, hh=hh, q=q):
                    last = None
                    for c in range(2):
                        last = e.tensor_copy(
                            out=C13[:, hh * 2 + c, ts(q)],
                            in_=B1[:, 15104 + 512 * c:15104 + 512 * (c + 1)].bitcast(I32))
                    return last
                P.step("v", _w)

    def cq3_pa(mc):
        return B1.rearrange("p (m t) -> p m t", m=8)[:, mc, 0:S]

    def stage_ffn0(x_src, x_dst, w1m, w2m):
        r13 = C1.rearrange("p (c t) -> p c t", c=DC)
        w1q_r = w1q_d.ap().rearrange("(a p) c -> p a c", p=128)
        w2f_r = w2q_d.ap().rearrange("(a p) c -> p a c", p=128)
        inv1 = SM[:, 0:S]
        cw0, cw1, cw2 = sm2p["cw0"], sm2p["cw1"], sm2p["cw2"]
        cbc = sm2p["cb"]
        nb = NQ - 1
        hedge = sm2p["hedge"]
        amax2_tm = sm2p["amax2_tm"]
        W2f = W2.bitcast(F32)
        scrA = [W2f[:, 0:S], W2f[:, 2048:2048 + S]]
        GtmpA = W2f[:, 4096:4096 + S]
        AMX = W2f[:, 6144:6144 + S]
        W1UPf = W1.bitcast(F32)
        bw = P.barrier_waits()
        first = {k: True for k in "tvsgy"}

        def stw(kind, fn, waits):
            if first[kind]:
                first[kind] = False
                waits = bw + list(waits)
            P.step(kind, fn, waits=waits)

        if "c127" not in sm2p:
            sm2col("c127")
        c127 = sm2p["c127"]

        def _vinit(e):
            e.memset(c127, 127.0)
            return e.tensor_scalar(
                out=inv1, in0=inv1, scalar1=w1m["inv"], scalar2=None,
                op0=ALU.mult)
        stw("v", _vinit, waits=[])
        v_inv1 = P.val("V")
        stw("g", lambda e: e.memset(AMX, 0.0), waits=[])
        g_amx0 = P.val("G")

        d1_wb = {}
        tm_mm = {}
        vm_evac_ps = {0: 0, 1: 0}
        sm_abs = {}
        gm_conv2 = {}
        vm_max = 0
        for qh in range(4):
            half = qh % 2
            wb3 = W1[:, half * 8192:half * 8192 + 8192].rearrange(
                "p (k m) -> p k m", k=DC)
            lw = [("T", tm_mm[(qh - 2, 7)])] if qh >= 2 else []
            lw.append(("D1", P.val("D1")))
            stw("y", lambda e, wb3=wb3, qh=qh: P.inc("D1", e.dma_start(
                out=wb3, in_=w1q_r[:, :, ts(qh, 1024)]), 16) and None, waits=lw)
            d1_wb[qh] = P.val("D1")
            for mc in range(8):
                pid = mc % 2
                ps = (PSA if pid == 0 else PSB)[:, 0:S]
                wcol = w1m["cs"][:, qh * 8 + mc:qh * 8 + mc + 1]
                hcol = qh * 8 + mc
                src = cq3_pa(mc)

                def _m(e, wb3=wb3, mc=mc, ps=ps, qh=qh):
                    last = None
                    for nq_ in range(NQ):
                        for kc in range(DC):
                            last = mm(e, ps[:, ts(nq_)],
                                      wb3[:, kc, ts(mc, 128)],
                                      r13[:, kc, ts(nq_)], kc == 0, kc == DC - 1)
                    return last
                stw("t", _m, waits=[("D1", d1_wb[qh]),
                                    ("V", vm_evac_ps[pid])])
                tm_mm[(qh, mc)] = P.val("T")
                ew = [("T", tm_mm[(qh, mc)]), ("V", v_inv1), ("V", vm_max)]
                if (qh - 1, mc) in sm_abs:
                    ew.append(("S", sm_abs[(qh - 1, mc)]))
                    ew.append(("G", gm_conv2[(qh - 1, mc)]))
                stw("v", lambda e, src=src, ps=ps, wcol=wcol:
                    e.scalar_tensor_tensor(
                        out=src, in0=ps, scalar=wcol,
                        in1=inv1, op0=ALU.subtract, op1=ALU.mult), waits=ew)
                vm_evac_ps[pid] = P.val("V")
                vm_evac_cur = P.val("V")
                scr = scrA[mc % 2]
                stw("s", lambda e, src=src, scr=scr, hcol=hcol: e.activation(
                    out=scr, in_=src, func=AF.Identity,
                    scale=cw1[:, hcol:hcol + 1], bias=cbc[:, hcol:hcol + 1]),
                    waits=[("V", vm_evac_cur)])
                sm_c1 = P.val("S")

                def _g2a(e, src=src, hcol=hcol):
                    last = None
                    if nb:
                        srcb = src.rearrange("p (b u) -> p b u", u=512)
                        hv = hedge[:, hcol * 2 * nb:(hcol + 1) * 2 * nb]
                        hv2 = hv.rearrange("p (b u) -> p b u", u=2)
                        e.tensor_copy(out=hv2[:, :, 0:1], in_=srcb[:, 0:nb, 511:512])
                        last = e.tensor_copy(out=hv2[:, :, 1:2],
                                             in_=srcb[:, 1:nb + 1, 0:1])
                    return last
                stw("g", _g2a, waits=[("S", sm_c1)])
                gm_conv2[(qh, mc)] = P.val("G")
                stw("v", lambda e, src=src, scr=scr, hcol=hcol:
                    e.scalar_tensor_tensor(
                        out=scr[:, 1:S], in0=src[:, 0:S - 1],
                        scalar=cw0[:, hcol:hcol + 1], in1=scr[:, 1:S],
                        op0=ALU.mult, op1=ALU.add), waits=[("S", sm_c1)])
                v_c2 = P.val("V")
                stw("v", lambda e, src=src, scr=scr, hcol=hcol:
                    e.scalar_tensor_tensor(
                        out=scr[:, 0:S - 1], in0=src[:, 1:S],
                        scalar=cw2[:, hcol:hcol + 1], in1=scr[:, 0:S - 1],
                        op0=ALU.mult, op1=ALU.add),
                    waits=[("V", v_c2)])
                vm_c3 = P.val("V")

                stw("s", lambda e, src=src, scr=scr: e.activation(
                    out=src, in_=scr, func=AF.Silu), waits=[("V", vm_c3)])
                s_mid = P.val("S")
                stw("s", lambda e, src=src: e.activation(
                    out=src, in_=src, func=AF.Abs), waits=[("S", s_mid)])
                sm_abs[(qh, mc)] = P.val("S")
                stw("v", lambda e, src=src: e.tensor_tensor(
                    out=AMX, in0=AMX, in1=src, op=ALU.max),
                    waits=[("S", sm_abs[(qh, mc)]), ("G", g_amx0),
                           ("V", vm_max)])
                vm_max = P.val("V")

        if FFN0_PHASE <= 1:
            return
        nt = S // 128
        sc2_tm = sm2p["sc2_tm"]
        inv2_tm = sm2p["inv2_tm"]
        for j in range(0, nt, 4):
            g = min(4, nt - j)

            def _t(e, j=j, g=g):
                last = None
                for k in range(g):
                    last = e.transpose(bank(k)[:, 0:128], AMX[:, ts(j + k, 128)], IDF)
                return last
            P.step("t", _t, waits=[("V", vm_max), ("S", P.val("S")),
                                   ("T", P.val("T"))])
            tm_tr = P.val("T")

            def _v(e, j=j, g=g):
                return [e.tensor_reduce(
                    out=amax2_tm[:, j + k:j + k + 1], in_=bank(k)[:, 0:128],
                    axis=AX.X, op=ALU.max) for k in range(g)][-1]
            P.step("v", _v, waits=[("T", tm_tr)])
        P.step("v", lambda e: e.tensor_single_scalar(
            out=amax2_tm[:, 0:nt], in_=amax2_tm[:, 0:nt], scalar=1e-5, op=ALU.max))
        P.step("v", lambda e: e.tensor_scalar(
            out=inv2_tm[:, 0:nt], in0=amax2_tm[:, 0:nt],
            scalar1=w2m["inv"], scalar2=0.5 / 127.0, op0=ALU.mult, op1=ALU.mult))
        P.step("v", lambda e: e.reciprocal(out=sc2_tm[:, 0:nt], in_=amax2_tm[:, 0:nt]))
        P.step("v", lambda e: e.tensor_scalar(
            out=sc2_tm[:, 0:nt], in0=sc2_tm[:, 0:nt], scalar1=127.0,
            scalar2=None, op0=ALU.mult))
        scr128b = B1[:, 14336:14848]
        for tc in range(NQ):
            for j, tmc in enumerate([sc2_tm, inv2_tm]):
                dst = W1UPf[:, 4096 + tc * 1024 + j * 512:
                            4096 + tc * 1024 + j * 512 + 512]
                bcast_tm_cols(tmc[:, tc * 4:tc * 4 + 4], 4, scr128b, bank(4))
                P.step("v", lambda e, dst=dst: e.tensor_copy(
                    out=dst, in_=bank(4)))

        if FFN0_PHASE <= 2:
            return
        cch3 = B1[:, 0:4096].rearrange("p (m t) -> p m t", m=8)
        cscr = [B1[:, 4096:4608], B1[:, 4608:5120]]
        rscr = [B1[:, 5120:5632], B1[:, 5632:6144]]
        fscr = [B1[:, 6144:6656], B1[:, 6656:7168]]
        iscr = [B1[:, 15360:15872].bitcast(I32), B1[:, 15872:16384].bitcast(I32)]
        xq = B1[:, 8192:12288]
        xq3 = xq.rearrange("p (c t) -> p c t", c=DC)
        x_r = x_src.ap().rearrange("(c p) t -> p c t", p=128)
        xd_r = x_dst.ap().rearrange("(c p) t -> p c t", p=128)
        r2f = C2.rearrange("p (k t) -> p k t", k=32)
        wb3 = W1[:, 0:8192].rearrange("p (k m) -> p k m", k=DC)
        w2h = [W2[:, 0:8192].rearrange("p (k m) -> p k m", k=8),
               W2[:, 8192:16384].rearrange("p (k m) -> p k m", k=8)]

        paT, paV, paS, paG = (P.val(n) for n in "TVSG")
        d3_last = P.val("D3")
        d1_last = P.val("D1")
        tm_mm1 = paT
        tm_mm2 = {}
        vm_bank_free = {b: paV for b in range(8)}
        sm_silu = {m: paS for m in range(8)}
        sm_c1d = {m: paS for m in range(8)}
        gm_cvt = {m: paG for m in range(8)}
        gm_cvt_par = {0: paG, 1: paG}
        gm_fadd = {f: paG for f in range(8)}
        gm_fadd_par = {0: paG, 1: paG}
        d2_w2 = {}
        for tc in range(NQ):
            sc2sb = W1UPf[:, 4096 + tc * 1024:4096 + tc * 1024 + 512]
            inv2sb = W1UPf[:, 4096 + tc * 1024 + 512:4096 + tc * 1024 + 1024]
            xw = [("D3", d3_last)]
            if tc == 0:
                xw += [("V", paV), ("S", paS), ("G", paG)]
            stw("y", lambda e, tc=tc: P.inc(
                "D3", e.dma_start(out=xq3, in_=x_r[:, :, ts(tc)]), 16) and None,
                waits=xw)
            d3_xq = P.val("D3")
            for pp in range(2):
                pw = [("T", tm_mm2[(tc - 1, pp + 2)])] if (tc - 1, pp + 2) in tm_mm2 else []
                pw.append(("D2", P.val("D2")))
                if tc == 0:
                    pw += [("V", paV), ("S", paS), ("T", paT)]
                stw("g", lambda e, pp=pp: P.inc("D2", e.dma_start(
                    out=w2h[pp], in_=w2f_r[:, pp * 8:pp * 8 + 8, :]), 16) and None,
                    waits=pw)
                d2_w2[(tc, pp)] = P.val("D2")

            for qh in range(4):
                stw("y", lambda e, qh=qh: P.inc("D1", e.dma_start(
                    out=wb3, in_=w1q_r[:, :, ts(qh, 1024)]), 16) and None,
                    waits=[("T", tm_mm1), ("D1", d1_last)])
                d1_cur = P.val("D1")
                d1_last = d1_cur
                for mc in range(8):
                    bk = mc % 3
                    wcol = w1m["cs"][:, qh * 8 + mc:qh * 8 + mc + 1]
                    hcol = qh * 8 + mc
                    src = cch3[:, mc, :]
                    scr = cscr[mc % 2]
                    rs = rscr[mc % 2]
                    isc = iscr[mc % 2]

                    def _m(e, mc=mc, bk=bk, tc=tc):
                        last = None
                        for kc in range(DC):
                            last = mm(e, bank(bk), wb3[:, kc, ts(mc, 128)],
                                      r13[:, kc, ts(tc)], kc == 0, kc == DC - 1)
                        return last
                    mw1 = [("D1", d1_cur), ("V", vm_bank_free[bk])]
                    if tc == 0 and qh == 0 and mc == 0:
                        mw1.append(("T", P.val("T")))
                    stw("t", _m, waits=mw1)
                    tm_mm1 = P.val("T")
                    stw("v", lambda e, src=src, bk=bk, wcol=wcol, tc=tc:
                        e.scalar_tensor_tensor(
                            out=src, in0=bank(bk), scalar=wcol,
                            in1=inv1[:, ts(tc)], op0=ALU.subtract, op1=ALU.mult),
                        waits=[("T", tm_mm1), ("S", sm_silu[mc]), ("V", v_inv1)])
                    vm_bank_free[bk] = P.val("V")
                    vm_evac = P.val("V")
                    stw("s", lambda e, src=src, scr=scr, hcol=hcol: e.activation(
                        out=scr, in_=src, func=AF.Identity,
                        scale=cw1[:, hcol:hcol + 1], bias=cbc[:, hcol:hcol + 1]),
                        waits=[("V", vm_evac)])
                    sm_c1d[mc] = P.val("S")

                    stw("v", lambda e, src=src, scr=scr, hcol=hcol:
                        e.scalar_tensor_tensor(
                            out=scr[:, 1:512], in0=src[:, 0:511],
                            scalar=cw0[:, hcol:hcol + 1], in1=scr[:, 1:512],
                            op0=ALU.mult, op1=ALU.add),
                        waits=[("S", sm_c1d[mc])])
                    v_mid = P.val("V")

                    def _c3(e, src=src, scr=scr, hcol=hcol, tc=tc):
                        last = e.scalar_tensor_tensor(
                            out=scr[:, 0:511], in0=src[:, 1:512],
                            scalar=cw2[:, hcol:hcol + 1], in1=scr[:, 0:511],
                            op0=ALU.mult, op1=ALU.add)
                        if tc < NQ - 1:
                            last = e.scalar_tensor_tensor(
                                out=scr[:, 511:512],
                                in0=hedge[:, hcol * 2 * nb + 2 * tc + 1:
                                          hcol * 2 * nb + 2 * tc + 2],
                                scalar=cw2[:, hcol:hcol + 1], in1=scr[:, 511:512],
                                op0=ALU.mult, op1=ALU.add)
                        if tc > 0:
                            last = e.scalar_tensor_tensor(
                                out=scr[:, 0:1],
                                in0=hedge[:, hcol * 2 * nb + 2 * (tc - 1):
                                          hcol * 2 * nb + 2 * (tc - 1) + 1],
                                scalar=cw0[:, hcol:hcol + 1], in1=scr[:, 0:1],
                                op0=ALU.mult, op1=ALU.add)
                        return last
                    stw("v", _c3, waits=[("V", v_mid)])
                    vm_c23 = P.val("V")
                    stw("s", lambda e, src=src, scr=scr: e.activation(
                        out=src, in_=scr, func=AF.Silu), waits=[("V", vm_c23)])
                    sm_silu[mc] = P.val("S")
                    stw("v", lambda e, src=src, rs=rs, sc2sb=sc2sb:
                        e.tensor_tensor(out=rs, in0=src, in1=sc2sb, op=ALU.mult),
                        waits=[("S", sm_silu[mc]), ("V", gm_cvt_par[mc % 2])])
                    vm_rm = P.val("V")
                    stw("s", lambda e, rs=rs: e.activation(
                        out=rs, in_=rs, func=AF.Identity, bias=c127),
                        waits=[("V", vm_rm)])
                    sm_127 = P.val("S")
                    stw("v", lambda e, rs=rs, isc=isc: e.tensor_copy(
                        out=isc, in_=rs), waits=[("S", sm_127)])
                    gm_cvt_par[mc % 2] = P.val("V")
                    v_cv = P.val("V")
                    grw = [("V", v_cv)]
                    if (tc - 1, qh) in tm_mm2:
                        grw.append(("T", tm_mm2[(tc - 1, qh)]))
                    stw("v", lambda e, isc=isc, kk=qh * 8 + mc: e.tensor_copy(
                        out=r2f[:, kk, :], in_=isc), waits=grw)
                    gm_cvt[mc] = P.val("V")

            if FFN0_PHASE <= 3:
                continue
            gm_r2 = {qh: gm_cvt[7] for qh in range(4)}
            for p in range(4):
                if p >= 2:
                    stw("g", lambda e, p=p: P.inc("D2", e.dma_start(
                        out=w2h[p % 2], in_=w2f_r[:, p * 8:p * 8 + 8, :]), 16)
                        and None, waits=[("T", tm_mm2[(tc, p - 2)]),
                                         ("D2", P.val("D2"))])
                    d2_w2[(tc, p)] = P.val("D2")

                def _m2(e, tc=tc, p=p):
                    last = None
                    for fm in range(8):
                        for kc in range(8):
                            k = p * 8 + kc
                            last = mm(e, bank(fm), w2h[p % 2][:, kc, ts(fm, 128)],
                                      r2f[:, k, :], k == 0, k == 31)
                    return last
                mw = [("D2", d2_w2[(tc, p)]), ("V", gm_cvt[7])]
                if p == 0:
                    mw.append(("V", P.val("V")))
                    mw += [("V", vm_bank_free[b]) for b in range(3, 8)]
                stw("t", _m2, waits=mw)
                tm_mm2[(tc, p)] = P.val("T")
            for fc in range(DC):
                fs = fscr[fc % 2]
                stw("v", lambda e, fc=fc, fs=fs, inv2sb=inv2sb:
                    e.scalar_tensor_tensor(
                        out=fs, in0=bank(fc), scalar=w2m["cs"][:, fc:fc + 1],
                        in1=inv2sb, op0=ALU.subtract, op1=ALU.mult),
                    waits=[("T", tm_mm2[(tc, 3)]), ("V", gm_fadd_par[fc % 2])])
                vm_f = P.val("V")
                stw("v", lambda e, fc=fc, fs=fs: e.tensor_tensor(
                    out=xq3[:, fc, :], in0=xq3[:, fc, :], in1=fs, op=ALU.add),
                    waits=[("V", vm_f), ("D3", d3_xq)])
                gm_fadd[fc] = P.val("V")
                gm_fadd_par[fc % 2] = P.val("V")
                vm_bank_free[fc] = vm_f
            stw("y", lambda e, tc=tc: P.inc(
                "D3", e.dma_start(out=xd_r[:, :, ts(tc)], in_=xq3), 16) and None,
                waits=[("V", gm_fadd[7]), ("D3", d3_xq)])
            d3_last = P.val("D3")

    def tr(e, out, in_, ident, tp="auto"):
        if tp == "auto":
            bp_i = in_.base_partition()
            bp_o = out.base_partition()
            tp = (bp_i, bp_o) if (bp_i or bp_o) else None
        return e.transpose(out, in_, ident, tile_position=tp)

    def tap(name, src_ap):
        if TAP != name:
            return
        n = src_ap.shape[-1] if src_ap.ndim == 2 else None
        def _(e):
            P.inc("DG", e.dma_start(out=dbg_out.ap()[:, 0:n], in_=src_ap), 16)
        P.step("g", _)

    def stage_att0(x_src, x_dst, qm, om):
        NT = S // 128
        TB = S // 16
        BPT = 128 // TB
        r13 = C1.rearrange("p (c t) -> p c t", c=DC)
        wq_r = wqkvq_d.ap().rearrange("(a p) c -> p a c", p=128)
        wo_r = woq_d.ap().rearrange("(a p) c -> p a c", p=128)
        wbuf = W1[:, 0:8192]
        wbuf3 = wbuf.rearrange("p (k m) -> p k m", k=DC)
        projT = C2.rearrange("p (c t) -> p c t", c=DC)
        inv4 = SM[:, 0:S]
        scrA = B1[:, 8192:8704]

        P.step("y", lambda e: P.inc("D0", e.dma_start(out=wbuf3, in_=wq_r), 16) and None)
        for fc in range(DC):
            for nq in range(NQ):
                bk = (fc * NQ + nq) % 3

                def _m(e, fc=fc, nq=nq, bk=bk):
                    last = None
                    for kc in range(DC):
                        last = mm(e, bank(bk), wbuf3[:, kc, ts(fc, 128)],
                                  r13[:, kc, ts(nq)], kc == 0, kc == DC - 1)
                    return last
                P.step("t", _m)
                P.step("v", lambda e, fc=fc, nq=nq, bk=bk: e.scalar_tensor_tensor(
                    out=scrA, in0=bank(bk), scalar=qm["cs"][:, fc:fc + 1],
                    in1=inv4[:, ts(nq)], op0=ALU.subtract, op1=ALU.mult))
                P.step("s", lambda e, fc=fc, nq=nq: e.activation(
                    out=projT[:, fc, ts(nq)], in_=scrA, func=AF.Identity,
                    scale=qm["alpha"][:, fc:fc + 1],
                    bias=sm2p["qkvb"][:, fc:fc + 1]))

        tap("proj", C2[:, 0:16384])
        def att0_epilogue(e):
            for nq in range(NQ):
                P.inc("D0", e.dma_start(out=x_dst.ap()[:, ts(nq)],
                                        in_=x_src.ap()[:, ts(nq)]), 16)
        if ATT0_PHASE < 2:
            P.step("y", att0_epilogue)
            return
        P.step("y", lambda e: P.inc("D0", e.dma_start(
            out=B1[:, 0:4 * S], in_=dr["cs4"].ap()), 16) and None)
        COSQ, SINQ = B1[:, 0:S], B1[:, S:2 * S]
        COSK, SINK = B1[:, 2 * S:3 * S], B1[:, 3 * S:4 * S]
        rotT, rotB = B1[:, 4 * S:5 * S], B1[:, 5 * S:6 * S]
        x1b, x2b = B1[:, 6 * S:7 * S], B1[:, 7 * S:8 * S]
        qlT = C1[:, 0:4 * S].rearrange("p (g t) -> p g t", g=4)
        klT = W2[:, 0:4 * S].rearrange("p (g t) -> p g t", g=4)

        for side in ("q", "k"):
            COS = COSQ if side == "q" else COSK
            SIN = SINQ if side == "q" else SINK
            dstT = qlT if side == "q" else klT
            for g in range(4):
                def _eo(e, g=g, side=side):
                    last = None
                    for j in range(4):
                        h = 4 * g + j
                        par = h % 2
                        lE = EOB4[:, 64 * par:64 * par + 32]
                        lO = EOB4[:, 64 * par + 32:64 * par + 64]
                        if side == "q":
                            for nq in range(NQ):
                                rhs = projT[:, h // 2, ts(nq)]
                                mm(e, PSA[32 * j:32 * j + 32, ts(nq)], lE, rhs,
                                   True, True, tp=(0, 32 * j))
                                last = mm(e, PSB[32 * j:32 * j + 32, ts(nq)], lO,
                                          rhs, True, True, tp=(0, 32 * j))
                        else:
                            for cb in range(16):
                                par2 = cb % 2
                                lE2 = EOB4[:, 64 * par2:64 * par2 + 32]
                                lO2 = EOB4[:, 64 * par2 + 32:64 * par2 + 64]
                                rhs = projT[:, cb // 2, h * TB:(h + 1) * TB]
                                mm(e, PSA[32 * j:32 * j + 32, cb * TB:(cb + 1) * TB],
                                   lE2, rhs, True, True, tp=(0, 32 * j))
                                last = mm(e, PSB[32 * j:32 * j + 32, cb * TB:(cb + 1) * TB],
                                          lO2, rhs, True, True, tp=(0, 32 * j))
                    return last
                P.step("t", _eo)
                if ATT0_PHASE == 21:
                    continue
                P.step("v", lambda e: e.tensor_tensor(out=x1b, in0=PSA[:, 0:S], in1=COS, op=ALU.mult))
                P.step("v", lambda e: e.tensor_tensor(out=x2b, in0=PSB[:, 0:S], in1=SIN, op=ALU.mult))
                P.step("v", lambda e: e.tensor_tensor(out=rotT, in0=x1b, in1=x2b, op=ALU.subtract))
                P.step("v", lambda e: e.tensor_tensor(out=x1b, in0=PSA[:, 0:S], in1=SIN, op=ALU.mult))
                P.step("v", lambda e: e.tensor_tensor(out=x2b, in0=PSB[:, 0:S], in1=COS, op=ALU.mult))
                P.step("v", lambda e: e.tensor_tensor(out=rotB, in0=x1b, in1=x2b, op=ALU.add))
                if ATT0_PHASE == 22:
                    continue

                def _u(e, g=g):
                    last = None
                    for j in range(4):
                        for nq in range(NQ):
                            mm(e, PSA[32 * j:32 * j + 32, ts(nq)],
                               U8[:, 64 * j:64 * j + 32],
                               rotT[:, ts(nq)], True, False, tp=(0, 32 * j))
                            last = mm(e, PSA[32 * j:32 * j + 32, ts(nq)],
                                      U8[:, 64 * j + 32:64 * j + 64],
                                      rotB[:, ts(nq)], False, True, tp=(0, 32 * j))
                    return last
                P.step("t", _u)
                if ATT0_PHASE == 23:
                    continue
                P.step("v", lambda e, g=g, dstT=dstT: e.tensor_copy(
                    out=dstT[:, g, :], in_=PSA[:, 0:S]))

        if ATT0_PHASE < 3:
            P.step("y", att0_epilogue)
            return
        tap("qlT", C1[:, 0:4 * S])
        tap("klT", W2[:, 0:4 * S])
        shcol = sm2p["att0_sh"]
        t1c = sm2p["att0_t1"]
        t2c = sm2p["att0_t2"]
        sqb = W2[:, 4 * S + 256:5 * S + 256]
        for nm, src_ in (("q", qlT), ("k", klT)):
            dst = t1c if nm == "q" else t2c
            first = [True]
            for g in range(4):
                P.step("v", lambda e, g=g, src_=src_: e.tensor_tensor(
                    out=sqb, in0=src_[:, g, :], in1=src_[:, g, :], op=ALU.mult))
                for nq in range(NQ):
                    P.step("t", lambda e, nq=nq: mm(
                        e, bank(7), BD32, sqb[:, ts(nq)], True, True))
                    P.step("v", lambda e: e.tensor_reduce(
                        out=sm2p["att0_tmp"], in_=bank(7), axis=AX.X, op=ALU.max))
                    if first[0]:
                        P.step("v", lambda e, dst=dst: e.tensor_copy(
                            out=dst, in_=sm2p["att0_tmp"]))
                        first[0] = False
                    else:
                        P.step("v", lambda e, dst=dst: e.tensor_tensor(
                            out=dst, in0=dst, in1=sm2p["att0_tmp"], op=ALU.max))
        P.step("v", lambda e: e.tensor_tensor(out=t1c, in0=t1c, in1=t2c, op=ALU.mult))
        P.step("t", lambda e: tr(e, bank(7)[0:1, 0:128], t1c, IDF))
        P.step("v", lambda e: e.tensor_reduce(
            out=sm2p["att0_tmp"][0:1, :], in_=bank(7)[0:1, 0:128], axis=AX.X, op=ALU.max))
        P.step("s", lambda e: e.activation(out=sm2p["att0_tmp"][0:1, :],
                                           in_=sm2p["att0_tmp"][0:1, :], func=AF.Sqrt))
        P.step("v", lambda e: e.tensor_scalar(
            out=sm2p["att0_tmp"][0:1, :], in0=sm2p["att0_tmp"][0:1, :],
            scalar1=-0.125, scalar2=None, op0=ALU.mult))
        P.step("t", lambda e: mm(e, bank(7)[:, 0:1], ONESF[0:1, :],
                                 sm2p["att0_tmp"][0:1, :], True, True))
        P.step("v", lambda e: e.tensor_copy(out=shcol, in_=bank(7)[:, 0:1]))

        if ATT0_PHASE < 4:
            P.step("y", att0_epilogue)
            return
        ytm = B1[:, 0:NT * 1024].rearrange("p (i f) -> p i f", i=NT)
        Ptd = [W2[:, 4 * S + 256:4 * S + 256 + S],
               W2[:, 5 * S + 256:5 * S + 256 + S]]

        s_seed, v_seed, t_seed = P.val("S"), P.val("V"), P.val("T")
        vb3 = [W1[:, 0:NT * 65].rearrange("p (n c) -> p n c", c=65),
               W1[:, NT * 65:2 * NT * 65].rearrange("p (n c) -> p n c", c=65)]
        vm_bankuse = {5: v_seed, 6: v_seed}
        vm_vbuf = {0: v_seed, 1: v_seed}

        def fill_vbuf(h, w_extra):
            hb = h % 2
            for n0 in range(0, NT, 8):
                g_ = min(8, NT - n0)
                pb = 5 + (n0 // 8) % 2

                def _tvb(e, n0=n0, g_=g_, pb=pb, h=h):
                    last = None
                    for k in range(g_):
                        cb16 = n0 + k
                        for sub in range(BPT):
                            cb = cb16 * BPT + sub
                            src_ = projT[:, cb // 2, h * TB:(h + 1) * TB]
                            last = mm(e, bank(pb)[sub * TB:sub * TB + TB,
                                                  k * 64:k * 64 + 64], src_,
                                      TRSEL[:, 64 * (cb % 2):64 * (cb % 2) + 64],
                                      True, True, tp=(0, (sub * TB) % 128))
                    return last
                P.step("t", _tvb,
                       waits=w_extra + [("V", vm_bankuse[pb]),
                                        ("T", P.val("T"))])
                tv = P.val("T")

                def _vc(e, n0=n0, g_=g_, pb=pb, hb=hb):
                    e.tensor_copy(
                        out=vb3[hb][:, n0:n0 + g_, 0:64],
                        in_=bank(pb)[:, 0:g_ * 64].rearrange(
                            "p (n c) -> p n c", c=64))
                    return e.memset(vb3[hb][:, n0:n0 + g_, 64:65], 1.0)
                P.step("v", _vc, waits=[("T", tv)])
                vm_bankuse[pb] = P.val("V")
            vm_vbuf[hb] = P.val("V")

        fill_vbuf(0, [("S", s_seed), ("T", t_seed), ("V", v_seed)])

        sm_expq = {}
        s_seed2 = P.val("S")
        vm_evac_all = max(vm_vbuf.values())
        tm_y = {}
        tm_tt = {}
        vm_ts = {}
        first_st = True
        for h in range(16):
            j, g = h % 4, h // 4
            for cb16 in range(NT):
                Ptb = Ptd[cb16 % 2]
                for qn in range(NQ):
                    stww = [("S", sm_expq.get((cb16 - 1, qn), s_seed2))]
                    if cb16 == 0:
                        stww += [("V", vm_vbuf[h % 2]), ("V", vm_evac_all)]
                    if first_st:
                        stww += [("T", P.val("T"))]
                        first_st = False
                    P.step("t", lambda e, j=j, g=g, cb16=cb16, qn=qn: mm(
                        e, bank(4 + qn),
                        klT[:, g, cb16 * 128:cb16 * 128 + 128][32 * j:32 * j + 32, :],
                        qlT[:, g, ts(qn)][32 * j:32 * j + 32, :], True, True,
                        tp=(32 * j, 0)), waits=stww)
                    tm_st = P.val("T")
                    ew = [("T", tm_st)]
                    if (cb16 - 2, qn) in tm_y:
                        ew.append(("T", tm_y[(cb16 - 2, qn)]))
                    P.step("s", lambda e, Ptb=Ptb, qn=qn: e.activation(
                        out=Ptb[:, ts(qn)], in_=bank(4 + qn), func=AF.Exp,
                        bias=shcol, scale=0.125), waits=ew)
                    sm_expq[(cb16, qn)] = P.val("S")
                for qc in range(NQ):
                    P.step("t", lambda e, h=h, cb16=cb16, Ptb=Ptb, qc=qc: mm(
                        e, bank(qc)[0:65, :], vb3[h % 2][:, cb16, :],
                        Ptb[:, ts(qc)], cb16 == 0, cb16 == NT - 1),
                        waits=[("S", sm_expq[(cb16, qc)])])
                    tm_y[(cb16, qc)] = P.val("T")
            sm_exp = P.val("S")
            tm_y_fin = P.val("T")
            if h + 1 < 16:
                fill_vbuf(h + 1, [("S", sm_exp)])
            for qc in range(NQ):
                cw_ = [("T", tm_y_fin)]
                if qc - 1 in tm_tt:
                    cw_.append(("T", tm_tt[qc - 1]))
                P.step("v", lambda e, qc=qc: e.tensor_copy(
                    out=SM[0:65, 0:512], in_=bank(qc)[0:65, :]), waits=cw_)
                vm_cp = P.val("V")

                def _tt(e):
                    last = None
                    for u in range(4):
                        last = tr(e, bank(4)[:, 65 * u:65 * u + 65],
                                  SM[0:65, 128 * u:128 * u + 128], IDF[0:65, 0:65])
                    return last
                ttw = [("V", vm_cp), ("S", sm_exp), ("T", P.val("T"))]
                if (qc - 1) in vm_ts:
                    ttw.append(("V", vm_ts[qc - 1]))
                P.step("t", _tt, waits=ttw)
                tm_tt[qc] = P.val("T")
                P.step("v", lambda e: [
                    e.reciprocal(out=sm2p["att0_rc"][:, u:u + 1],
                                 in_=bank(4)[:, 65 * u + 64:65 * u + 65])
                    for u in range(4)][-1], waits=[("T", tm_tt[qc])])
                v_rc = P.val("V")
                P.step("v", lambda e, h=h, qc=qc: [
                    e.tensor_scalar(
                        out=ytm[:, qc * 4 + u, h * 64:(h + 1) * 64],
                        in0=bank(4)[:, 65 * u:65 * u + 64],
                        scalar1=sm2p["att0_rc"][:, u:u + 1], scalar2=None,
                        op0=ALU.mult)
                    for u in range(4)][-1], waits=[("V", v_rc)])
                vm_ts[qc] = P.val("V")
            vm_evac_all = P.val("V")
            tm_tt = {}
            vm_ts = {}

        tap("ytm", B1[:, 0:16384])
        if ATT0_PHASE < 5:
            P.step("y", att0_epilogue)
            return
        inv4y = sm2p["att0_inv4y"]
        for i in range(NT):
            P.step("v", lambda e, i=i: e.tensor_reduce(
                out=sm2p["att0_tmp"], in_=ytm[:, i, :], axis=AX.X, op=ALU.max,
                apply_absolute_value=True))
            P.step("v", lambda e: e.tensor_single_scalar(
                out=sm2p["att0_tmp"], in_=sm2p["att0_tmp"], scalar=1e-6, op=ALU.max))
            P.step("v", lambda e, i=i: e.tensor_scalar(
                out=inv4y[:, i:i + 1], in0=sm2p["att0_tmp"], scalar1=1.0 / 7.0,
                scalar2=None, op0=ALU.mult))
            P.step("v", lambda e: e.reciprocal(out=sm2p["att0_tmp"], in_=sm2p["att0_tmp"]))
            P.step("v", lambda e, i=i: e.tensor_scalar(
                out=SM[:, 0:1024], in0=ytm[:, i, :], scalar1=sm2p["att0_tmp"],
                scalar2=7.0, op0=ALU.mult, op1=ALU.mult))
            P.step("v", lambda e: e.tensor_scalar(
                out=SM[:, 0:1024], in0=SM[:, 0:1024], scalar1=7.0, scalar2=None,
                op0=ALU.add))
            P.step("v", lambda e: e.tensor_copy(
                out=SM[:, 1024:2048].bitcast(I32), in_=SM[:, 0:1024]))
            P.step("v", lambda e, i=i: e.tensor_copy(
                out=C1[:, i * 1024:(i + 1) * 1024], in_=SM[:, 1024:2048].bitcast(I32)))
        for i in range(NT):
            def _tq(e, i=i):
                last = None
                for fc in range(DC):
                    last = tr(e, bank(fc % 4)[:, 0:128],
                              C1[:, i * 1024 + fc * 128:i * 1024 + (fc + 1) * 128], IDB)
                    if fc % 4 == 3:
                        pass
                return last
            def _tq1(e, i=i):
                last = None
                for fc in range(4):
                    last = tr(e, bank(fc).bitcast(BF16)[:, 0:128],
                              C1[:, i * 1024 + fc * 128:i * 1024 + (fc + 1) * 128], IDB)
                return last
            P.step("t", _tq1)
            P.step("v", lambda e, i=i: [
                e.tensor_copy(out=projT[:, fc, i * 128:(i + 1) * 128],
                              in_=bank(fc).bitcast(BF16)[:, 0:128]) for fc in range(4)][-1])

            def _tq2(e, i=i):
                last = None
                for fc in range(4, DC):
                    last = tr(e, bank(fc - 4).bitcast(BF16)[:, 0:128],
                              C1[:, i * 1024 + fc * 128:i * 1024 + (fc + 1) * 128], IDB)
                return last
            P.step("t", _tq2)
            P.step("v", lambda e, i=i: [
                e.tensor_copy(out=projT[:, fc, i * 128:(i + 1) * 128],
                              in_=bank(fc - 4).bitcast(BF16)[:, 0:128]) for fc in range(4, DC)][-1])

        tap("r4y", C1[:, 0:16384])
        tap("r4yfm", C2[:, 0:16384])
        if TAP == "sm2":
            def _tp(e):
                P.inc("DG", e.dma_start(out=dbg_out.ap()[:, 0:8], in_=om["alpha"]), 16)
                P.inc("DG", e.dma_start(out=dbg_out.ap()[:, 8:16], in_=om["cs"]), 16)
                P.inc("DG", e.dma_start(out=dbg_out.ap()[:, 16:16 + NT], in_=inv4y[:, 0:NT]), 16)
            P.step("g", _tp)
        P.step("y", lambda e: P.inc("D0", e.dma_start(out=wbuf3, in_=wo_r), 16) and None)
        xq = B1[:, 8192:12288]
        xq3 = xq.rearrange("p (c t) -> p c t", c=DC)
        x_r = x_src.ap().rearrange("(c p) t -> p c t", p=128)
        xd_r = x_dst.ap().rearrange("(c p) t -> p c t", p=128)
        inv4ysb = B1[:, 12288:12800]
        scr128c = B1[:, 12800:13312]
        scrA2 = B1[:, 13312:13824]
        for nq in range(NQ):
            P.step("y", lambda e, nq=nq: P.inc(
                "D0", e.dma_start(out=xq3, in_=x_r[:, :, ts(nq)]), 16) and None)
            bcast_tm_cols(inv4y[:, nq * 4:nq * 4 + 4], 4, scr128c, bank(4))
            P.step("v", lambda e: e.tensor_copy(out=inv4ysb, in_=bank(4)))
            for fc in range(DC):
                bk = fc % 3

                def _m(e, fc=fc, nq=nq, bk=bk):
                    last = None
                    for kc in range(DC):
                        last = mm(e, bank(bk), wbuf3[:, kc, ts(fc, 128)],
                                  projT[:, kc, ts(nq)], kc == 0, kc == DC - 1)
                    return last
                P.step("t", _m)
                P.step("v", lambda e, fc=fc, bk=bk: e.scalar_tensor_tensor(
                    out=scrA2, in0=bank(bk), scalar=om["cs"][:, fc:fc + 1],
                    in1=inv4ysb, op0=ALU.subtract, op1=ALU.mult))
                P.step("v", lambda e, fc=fc: e.scalar_tensor_tensor(
                    out=xq3[:, fc, :], in0=scrA2, scalar=om["alpha"][:, fc:fc + 1],
                    in1=xq3[:, fc, :], op0=ALU.mult, op1=ALU.add))
            P.step("y", lambda e, nq=nq: P.inc(
                "D0", e.dma_start(out=xd_r[:, :, ts(nq)], in_=xq3), 16) and None)

    def attn_core(qT3, kT3, vtm, heads, scale, causal, Pt, shiftcol, ytm3,
                  sqb, NT, cmask=None):
        t1c = sm2p["ac_t1"]
        t2c = sm2p["ac_t2"]
        tmpc = sm2p["ac_tmp"]
        nkc = kT3.shape[1]
        for nm, src_, nch in (("q", qT3, 8), ("k", kT3, nkc)):
            dst = t1c if nm == "q" else t2c
            first = True
            for cch in range(nch):
                P.step("v", lambda e, cch=cch, src_=src_: e.tensor_tensor(
                    out=sqb[:, 0:S], in0=src_[:, cch, 0:S], in1=src_[:, cch, 0:S],
                    op=ALU.mult))
                for nq in range(NQ):
                    P.step("t", lambda e, nq=nq: mm(
                        e, bank(7), BD64, sqb[:, ts(nq)], True, True))
                    P.step("v", lambda e: e.tensor_reduce(
                        out=tmpc, in_=bank(7), axis=AX.X, op=ALU.max))
                    if first:
                        P.step("v", lambda e, dst=dst: e.tensor_copy(out=dst, in_=tmpc))
                        first = False
                    else:
                        P.step("v", lambda e, dst=dst: e.tensor_tensor(
                            out=dst, in0=dst, in1=tmpc, op=ALU.max))
        P.step("v", lambda e: e.tensor_tensor(out=t1c, in0=t1c, in1=t2c, op=ALU.mult))
        P.step("t", lambda e: tr(e, bank(7)[0:1, 0:128], t1c, IDF))
        P.step("v", lambda e: e.tensor_reduce(
            out=tmpc[0:1, :], in_=bank(7)[0:1, 0:128], axis=AX.X, op=ALU.max))
        P.step("s", lambda e: e.activation(out=tmpc[0:1, :], in_=tmpc[0:1, :],
                                           func=AF.Sqrt))
        P.step("v", lambda e, scale=scale: e.tensor_scalar(
            out=tmpc[0:1, :], in0=tmpc[0:1, :], scalar1=-scale, scalar2=None,
            op0=ALU.mult))
        P.step("t", lambda e: mm(e, bank(7)[:, 0:1], ONESF[0:1, :],
                                 tmpc[0:1, :], True, True))
        P.step("v", lambda e: e.tensor_copy(out=shiftcol, in_=bank(7)[:, 0:1]))

        ew = P.barrier_waits()
        first_on = {"t": True, "s": True, "v": True}

        def stw(kind, fn, w):
            if first_on.get(kind):
                w = list(w) + ew
                first_on[kind] = False
            P.step(kind, fn, waits=w)

        vm_end = P.val("V")
        s_seed2 = P.val("S")
        for (s, kchunk, kbase, vcol) in heads:
            sm_expq = {}
            tm_yq = {}
            vm_mask = {}
            prev_mt = None
            for mt in range(NT):
                qn0 = max(0, -(-(128 * mt - 511) // 512)) if causal else 0
                if causal and qn0 >= NQ:
                    continue
                pt = Pt[:, (mt % 2) * S:(mt % 2) * S + S]
                dq = mt // 4 if causal else -1
                for qn in range(qn0, NQ):
                    w = [("S", sm_expq.get((prev_mt, qn), s_seed2))]
                    if prev_mt is None:
                        w.append(("V", vm_end))
                    stw("t", lambda e, s=s, kchunk=kchunk, kbase=kbase, mt=mt,
                        qn=qn: mm(
                            e, bank(4 + qn),
                            kT3[:, kchunk, mt * 128:mt * 128 + 128][kbase:kbase + 64, :],
                            qT3[:, s // 2, ts(qn)][64 * (s % 2):64 * (s % 2) + 64, :],
                            True, True, tp=(kbase, 0)), w)
                    tm_stq = P.val("T")
                    w2_ = [("T", tm_stq)]
                    if (mt - 2, qn) in tm_yq:
                        w2_.append(("T", tm_yq[(mt - 2, qn)]))
                    stw("s", lambda e, pt=pt, qn=qn: e.activation(
                        out=pt[:, ts(qn)], in_=bank(4 + qn), func=AF.Exp,
                        bias=shiftcol, scale=scale), w2_)
                    sm_expq[(mt, qn)] = P.val("S")
                    if causal and qn == dq:
                        stw("v", lambda e, mt=mt, dq=dq, pt=pt: e.tensor_tensor(
                            out=pt[:, ts(dq)], in0=pt[:, ts(dq)],
                            in1=cmask[:, ts(mt % 4)], op=ALU.mult),
                            [("S", sm_expq[(mt, qn)])])
                        vm_mask[mt] = P.val("V")
                for qc in range(qn0, NQ):
                    stop_mt = min(NT - 1, 4 * qc + 3) if causal else NT - 1
                    w = [("S", sm_expq[(mt, qc)])]
                    if causal and qc == dq:
                        w.append(("V", vm_mask[mt]))
                    stw("t", lambda e, s=s, vcol=vcol, mt=mt, qc=qc, pt=pt,
                        stop_mt=stop_mt: mm(
                            e, bank(qc)[0:65, :], vtm(mt)[:, vcol:vcol + 65],
                            pt[:, ts(qc)], mt == 0, mt == stop_mt), w)
                    tm_yq[(mt, qc)] = P.val("T")
                prev_mt = mt
            tm_y = {prev_mt: P.val("T")}
            for qc in range(NQ):
                P.step("v", lambda e, qc=qc: e.tensor_copy(
                    out=SM[0:65, 0:512], in_=bank(qc)[0:65, :]),
                    waits=[("T", tm_y[prev_mt]), ("V", P.val("V"))])

                def _tt(e):
                    last = None
                    for u in range(4):
                        last = tr(e, bank(4)[:, 65 * u:65 * u + 65],
                                  SM[0:65, 128 * u:128 * u + 128], IDF[0:65, 0:65])
                    return last
                P.step("t", _tt, waits=[("V", P.val("V"))])
                P.step("v", lambda e: [
                    e.reciprocal(out=sm2p["ac_rc"][:, u:u + 1],
                                 in_=bank(4)[:, 65 * u + 64:65 * u + 65])
                    for u in range(4)][-1], waits=[("T", P.val("T"))])
                P.step("v", lambda e, s=s, qc=qc: [
                    e.tensor_scalar(
                        out=ytm3[:, qc * 4 + u, s * 64:(s + 1) * 64],
                        in0=bank(4)[:, 65 * u:65 * u + 64],
                        scalar1=sm2p["ac_rc"][:, u:u + 1], scalar2=None,
                        op0=ALU.mult)
                    for u in range(4)][-1], waits=[("V", P.val("V"))])
            vm_end = P.val("V")

    def y_to_fm(ytm3, NT):
        projT = C2.rearrange("p (c t) -> p c t", c=DC)
        for i in range(NT):
            P.step("v", lambda e, i=i: e.tensor_copy(
                out=C1[:, i * 1024:(i + 1) * 1024], in_=ytm3[:, i, :]))
        for i in range(NT):
            def _t1(e, i=i):
                last = None
                for fc in range(4):
                    last = tr(e, bank(fc).bitcast(BF16)[:, 0:128],
                              C1[:, i * 1024 + fc * 128:i * 1024 + (fc + 1) * 128], IDB)
                return last
            P.step("t", _t1)
            P.step("v", lambda e, i=i: [
                e.tensor_copy(out=projT[:, fc, i * 128:(i + 1) * 128],
                              in_=bank(fc).bitcast(BF16)[:, 0:128])
                for fc in range(4)][-1])

            def _t2(e, i=i):
                last = None
                for fc in range(4, DC):
                    last = tr(e, bank(fc - 4).bitcast(BF16)[:, 0:128],
                              C1[:, i * 1024 + fc * 128:i * 1024 + (fc + 1) * 128], IDB)
                return last
            P.step("t", _t2)
            P.step("v", lambda e, i=i: [
                e.tensor_copy(out=projT[:, fc, i * 128:(i + 1) * 128],
                              in_=bank(fc - 4).bitcast(BF16)[:, 0:128])
                for fc in range(4, DC)][-1])

    def proj_fm(rhs3, wbuf3, nmc, evac, kacc=DC):
        for fc in range(nmc):
            for nq in range(NQ):
                bk = (fc * NQ + nq) % 3

                def _m(e, fc=fc, nq=nq, bk=bk):
                    last = None
                    for kc in range(kacc):
                        last = mm(e, bank(bk), wbuf3[:, kc, ts(fc, 128)],
                                  rhs3[:, kc, ts(nq)], kc == 0, kc == kacc - 1)
                    return last
                P.step("t", _m)
                evac(fc, nq, bk)

    def stage_att1(x_src, x_dst):
        NT = S // 128
        ln3 = C1.rearrange("p (c t) -> p c t", c=DC)
        qT3 = C2.rearrange("p (c t) -> p c t", c=DC)
        kT3 = W2[:, 0:8 * S].rearrange("p (c t) -> p c t", c=DC)
        wb = W1[:, 0:8192]
        wb3 = wb.rearrange("p (k m) -> p k m", k=DC)

        Pt = W1[:, 8192:8192 + 2 * S]
        sqb = W1[:, 8192 + S:8192 + 2 * S]
        CMASK = W1[:, 8192 + 2 * S:8192 + 2 * S + 2048]
        vtm15 = W1[:, 8192 + 2 * S + 2048:8192 + 2 * S + 3088]
        P.step("g", lambda e: P.inc("DG", e.dma_start(
            out=CMASK, in_=dr["cmask"].ap()), 16) and None)

        P.step("g", lambda e: P.inc("DG", e.dma_start(
            out=wb3, in_=dr["a1_qw"].ap().rearrange("(a p) c -> p a c", p=128)), 16) and None)

        def ev_q(fc, nq, bk):
            P.step("v", lambda e, fc=fc, nq=nq, bk=bk: e.tensor_copy(
                out=qT3[:, fc, ts(nq)], in_=bank(bk)))
        proj_fm(ln3, wb3, DC, ev_q)
        P.step("g", lambda e: P.inc("DG", e.dma_start(
            out=wb3, in_=dr["a1_kw"].ap().rearrange("(a p) c -> p a c", p=128)), 16) and None)

        def ev_k(fc, nq, bk):
            P.step("v", lambda e, fc=fc, nq=nq, bk=bk: e.tensor_copy(
                out=kT3[:, fc, ts(nq)], in_=bank(bk)))
        proj_fm(ln3, wb3, DC, ev_k)
        vtmb = B1.bitcast(BF16)

        def vtm(i):
            if i == 15:
                return vtm15
            return vtmb[:, i * 1040:i * 1040 + 1040]
        P.step("g", lambda e: P.inc("DG", e.dma_start(
            out=wb3, in_=dr["a1_vw"].ap().rearrange("(a p) c -> p a c", p=128)), 16) and None)
        for i in range(NT):
            P.step("v", lambda e, i=i: e.memset(
                vtm(i).rearrange("p (h u) -> p h u", h=16)[:, :, 64:65], 1.0))
            for hv in range(16):
                def _mv(e, i=i, hv=hv):
                    last = None
                    for kc in range(DC):
                        last = mm(e, bank(3)[:, 0:64],
                                  ln3[:, kc, i * 128:(i + 1) * 128],
                                  wb3[:, kc, hv * 64:(hv + 1) * 64],
                                  kc == 0, kc == DC - 1)
                    return last
                P.step("t", _mv)
                P.step("v", lambda e, i=i, hv=hv: e.tensor_copy(
                    out=vtm(i)[:, hv * 65:hv * 65 + 64], in_=bank(3)[:, 0:64]))

        heads = [(h, h // 2, 64 * (h % 2), 65 * h) for h in range(16)]
        ytmb = vtmb[:, 16384:32768].rearrange("p (i f) -> p i f", i=16)
        attn_core(qT3, kT3, vtm, heads, 1.0 / 32.0, True, Pt,
                  sm2p["a1_sh"], ytmb, sqb, NT, cmask=CMASK)

        for i in range(NT):
            P.step("v", lambda e, i=i: e.tensor_copy(
                out=C1[:, i * 1024:(i + 1) * 1024], in_=ytmb[:, i, :]))
        projT = C2.rearrange("p (c t) -> p c t", c=DC)
        for i in range(NT):
            def _t1(e, i=i):
                last = None
                for fc in range(4):
                    last = tr(e, bank(fc).bitcast(BF16)[:, 0:128],
                              C1[:, i * 1024 + fc * 128:i * 1024 + (fc + 1) * 128], IDB)
                return last
            P.step("t", _t1)
            P.step("v", lambda e, i=i: [
                e.tensor_copy(out=projT[:, fc, i * 128:(i + 1) * 128],
                              in_=bank(fc).bitcast(BF16)[:, 0:128])
                for fc in range(4)][-1])

            def _t2(e, i=i):
                last = None
                for fc in range(4, DC):
                    last = tr(e, bank(fc - 4).bitcast(BF16)[:, 0:128],
                              C1[:, i * 1024 + fc * 128:i * 1024 + (fc + 1) * 128], IDB)
                return last
            P.step("t", _t2)
            P.step("v", lambda e, i=i: [
                e.tensor_copy(out=projT[:, fc, i * 128:(i + 1) * 128],
                              in_=bank(fc - 4).bitcast(BF16)[:, 0:128])
                for fc in range(4, DC)][-1])

        P.step("g", lambda e: P.inc("DG", e.dma_start(
            out=wb3, in_=dr["a1_ow"].ap().rearrange("(a p) c -> p a c", p=128)), 16) and None)
        xq = B1[:, 8192:12288]
        x_r = x_src.ap().rearrange("(c p) t -> p c t", p=128)
        xd_r = x_dst.ap().rearrange("(c p) t -> p c t", p=128)
        xq3 = xq.rearrange("p (c t) -> p c t", c=DC)
        for nq in range(NQ):
            P.step("y", lambda e, nq=nq: P.inc(
                "D0", e.dma_start(out=xq3, in_=x_r[:, :, ts(nq)]), 16) and None)
            for fc in range(DC):
                bk = fc % 3

                def _m(e, fc=fc, nq=nq, bk=bk):
                    last = None
                    for kc in range(DC):
                        last = mm(e, bank(bk), wb3[:, kc, ts(fc, 128)],
                                  projT[:, kc, ts(nq)], kc == 0, kc == DC - 1)
                    return last
                P.step("t", _m)
                P.step("v", lambda e, fc=fc, bk=bk: e.tensor_tensor(
                    out=xq3[:, fc, :], in0=xq3[:, fc, :], in1=bank(bk), op=ALU.add))
            P.step("y", lambda e, nq=nq: P.inc(
                "D0", e.dma_start(out=xd_r[:, :, ts(nq)], in_=xq3), 16) and None)

    def stage_mix1(x_src, x_dst):
        NT = S // 128
        CAP = float(int(1.25 * NCORES * S * 2 / NEXP))
        ln3 = C1.rearrange("p (c t) -> p c t", c=DC)
        qT3 = C2.rearrange("p (c t) -> p c t", c=DC)
        kT3 = W2[:, 0:2 * S].rearrange("p (c t) -> p c t", c=2)
        wb = W1[:, 0:8192]
        wb3 = wb.rearrange("p (k m) -> p k m", k=DC)
        Pt = W1[:, 8192:8192 + 2 * S]
        sqb = W1[:, 8192 + S:8192 + 2 * S]
        vbb = W1[:, 8192 + 2 * S:8192 + 2 * S + 256]
        gwsb = W1[:, 8192 + 2 * S + 256:8192 + 2 * S + 320]
        x_r = x_src.ap().rearrange("(c p) t -> p c t", p=128)
        xd_r = x_dst.ap().rearrange("(c p) t -> p c t", p=128)

        P.step("g", lambda e: P.inc("DG", e.dma_start(
            out=wb3, in_=dr["g_qw"].ap().rearrange("(a p) c -> p a c", p=128)), 16) and None)

        def ev_q(fc, nq, bk):
            P.step("s", lambda e, fc=fc, nq=nq, bk=bk: e.activation(
                out=qT3[:, fc, ts(nq)], in_=bank(bk), func=AF.Identity,
                bias=sm2p["g_qb"][:, fc:fc + 1]))
        proj_fm(ln3, wb3, DC, ev_q)
        P.step("g", lambda e: P.inc("DG", e.dma_start(
            out=wb3[:, :, 0:256],
            in_=dr["g_kw"].ap().rearrange("(a p) c -> p a c", p=128)), 16) and None)

        def ev_k(fc, nq, bk):
            P.step("s", lambda e, fc=fc, nq=nq, bk=bk: e.activation(
                out=kT3[:, fc, ts(nq)], in_=bank(bk), func=AF.Identity,
                bias=sm2p["g_kb"][:, fc:fc + 1]))
        for fc in range(2):
            for nq in range(NQ):
                bk = (fc * NQ + nq) % 3

                def _m(e, fc=fc, nq=nq, bk=bk):
                    last = None
                    for kc in range(DC):
                        last = mm(e, bank(bk), wb3[:, kc, fc * 128:fc * 128 + 128],
                                  ln3[:, kc, ts(nq)], kc == 0, kc == DC - 1)
                    return last
                P.step("t", _m)
                ev_k(fc, nq, bk)
        P.step("y", lambda e: P.inc("D0", e.dma_start(
            out=SM[0:1, 0:256], in_=dr["g_vbr"].ap()), 16) and None)
        P.step("t", lambda e: mm(e, bank(3)[:, 0:256], ONESF[0:1, :],
                                 SM[0:1, 0:256], True, True))
        P.step("v", lambda e: e.tensor_copy(out=vbb, in_=bank(3)[:, 0:256]))
        P.step("g", lambda e: P.inc("DG", e.dma_start(
            out=wb3[:, :, 0:256],
            in_=dr["g_vw"].ap().rearrange("(a p) c -> p a c", p=128)), 16) and None)
        vtmr = W2[:, 2 * S:2 * S + NT * 260]

        def vtm(i):
            return vtmr[:, i * 260:i * 260 + 260]
        for i in range(NT):
            P.step("v", lambda e, i=i: e.memset(
                vtm(i).rearrange("p (h u) -> p h u", h=4)[:, :, 64:65], 1.0))
            for kv in range(NKV):
                def _mv(e, i=i, kv=kv):
                    last = None
                    for kc in range(DC):
                        last = mm(e, bank(3)[:, 0:64],
                                  ln3[:, kc, i * 128:(i + 1) * 128],
                                  wb3[:, kc, kv * 64:kv * 64 + 64],
                                  kc == 0, kc == DC - 1)
                    return last
                P.step("t", _mv)
                P.step("v", lambda e, i=i, kv=kv: e.tensor_tensor(
                    out=vtm(i)[:, kv * 65:kv * 65 + 64], in0=bank(3)[:, 0:64],
                    in1=vbb[:, kv * 64:kv * 64 + 64], op=ALU.add))

        def sig(h):
            return 8 * (h // 8) + 2 * (h % 4) + (h // 4) % 2
        heads = [(sig(h), (h // 4) // 2, 64 * ((h // 4) % 2), 65 * (h // 4))
                 for h in range(16)]
        ytmf = B1[:, 0:NT * 1024].rearrange("p (i f) -> p i f", i=NT)
        attn_core(qT3, kT3, vtm, heads, 1.0 / 8.0, False, Pt,
                  sm2p["gq_sh"], ytmf, sqb, NT)
        y_to_fm(ytmf, NT)
        projT = C2.rearrange("p (c t) -> p c t", c=DC)

        P.step("g", lambda e: P.inc("DG", e.dma_start(
            out=wb3, in_=dr["g_ow"].ap().rearrange("(a p) c -> p a c", p=128)), 16) and None)
        s13 = B1.rearrange("p (c t) -> p c t", c=DC)
        xch = SM[:, 0:512]
        scrS = SM[:, 512:1024]
        meanb = SM[:, 1024:1536]
        rstdb = SM[:, 1536:2048]
        scr128m = C2[:, 0:1024].bitcast(F32) if False else None
        for nq in range(NQ):
            bcast_tm_cols(sm2p["lnmean_tm"][:, nq * 4:nq * 4 + 4], 4,
                          B1[:, 0:512] if False else SM[:, 512:1024], bank(4))
            P.step("v", lambda e: e.tensor_copy(out=meanb, in_=bank(4)))
            bcast_tm_cols(sm2p["lnrstd_tm"][:, nq * 4:nq * 4 + 4], 4,
                          SM[:, 512:1024], bank(4))
            P.step("v", lambda e: e.tensor_copy(out=rstdb, in_=bank(4)))
            for fc in range(DC):
                bk = fc % 3

                def _m(e, fc=fc, nq=nq, bk=bk):
                    last = None
                    for kc in range(DC):
                        last = mm(e, bank(bk), wb3[:, kc, ts(fc, 128)],
                                  projT[:, kc, ts(nq)], kc == 0, kc == DC - 1)
                    return last
                P.step("t", _m)
                P.step("y", lambda e, fc=fc, nq=nq: P.inc(
                    "D0", e.dma_start(out=xch, in_=x_r[:, fc, ts(nq)]), 16) and None)
                P.step("s", lambda e, fc=fc, bk=bk: e.activation(
                    out=scrS, in_=bank(bk), func=AF.Identity,
                    bias=sm2p["g_ob"][:, fc:fc + 1]))
                P.step("v", lambda e: e.tensor_tensor(
                    out=xch, in0=xch, in1=meanb, op=ALU.subtract))
                P.step("v", lambda e: e.tensor_tensor(
                    out=xch, in0=xch, in1=rstdb, op=ALU.mult))
                P.step("v", lambda e, fc=fc, nq=nq: e.tensor_tensor(
                    out=s13[:, fc, ts(nq)], in0=xch, in1=scrS, op=ALU.add))

        for nq in range(NQ):
            for fc in range(DC):
                P.step("v", lambda e, fc=fc, nq=nq: e.tensor_copy(
                    out=ln3[:, fc, ts(nq)], in_=s13[:, fc, ts(nq)]))
            for fc in range(DC):
                P.step("v", lambda e, fc=fc, nq=nq: e.tensor_tensor(
                    out=C2[:, fc * 512:(fc + 1) * 512], in0=ln3[:, fc, ts(nq)],
                    in1=ln3[:, fc, ts(nq)], op=ALU.mult))

            def _mmv(e, nq=nq):
                last = None
                for fc in range(DC):
                    last = mm(e, bank(0), ONESB, C2[:, fc * 512:(fc + 1) * 512],
                              fc == 0, fc == DC - 1)
                return last
            P.step("t", _mmv)
            P.step("v", lambda e: e.tensor_scalar(
                out=rstdb, in0=bank(0), scalar1=1.0 / D, scalar2=1e-6,
                op0=ALU.mult, op1=ALU.add))
            P.step("v", lambda e: e.reciprocal(out=rstdb, in_=rstdb))
            P.step("s", lambda e: e.activation(out=rstdb, in_=rstdb, func=AF.Sqrt))
            for fc in range(DC):
                P.step("v", lambda e, fc=fc, nq=nq: e.scalar_tensor_tensor(
                    out=s13[:, fc, ts(nq)], in0=s13[:, fc, ts(nq)],
                    scalar=sm2p["n1w"][:, fc:fc + 1], in1=rstdb,
                    op0=ALU.mult, op1=ALU.mult))
                P.step("v", lambda e, fc=fc, nq=nq: e.tensor_copy(
                    out=ln3[:, fc, ts(nq)], in_=s13[:, fc, ts(nq)]))

        P.step("y", lambda e: P.inc("D0", e.dma_start(
            out=sm2p["gwf"].rearrange("p (k g) -> p k g", k=DC),
            in_=dr["gate_w"].ap().rearrange("(a p) g -> p a g", p=128)), 16) and None)
        gw3 = sm2p["gwf"].rearrange("p (k g) -> p k g", k=DC)
        gpt = SM[:, 0:128]
        srt = SM[:, 128:256]
        em1 = SM[:, 384:512]
        em2 = SM[:, 512:640]
        vm = SM[:, 1152:1280]
        mfin = SM[:, 1280:1408]
        emtb = C2[:, 0:128]

        def _gmm(e):
            last = None
            for i in range(NT):
                for kc in range(DC):
                    last = mm(e, bank(0)[:, 8 * i:8 * i + 8],
                              s13[:, kc, i * 128:(i + 1) * 128],
                              gw3[:, kc, :], kc == 0, kc == DC - 1)
            return last
        P.step("t", _gmm)
        P.step("s", lambda e: e.activation(out=gpt, in_=bank(0)[:, 0:128],
                                           func=AF.Sigmoid))
        for i in range(NT):
            sl8 = slice(8 * i, 8 * i + 8)
            P.step("v", lambda e, sl8=sl8: e.max(out=srt[:, sl8], in_=gpt[:, sl8]))
            P.step("v", lambda e, sl8=sl8, i=i: e.tensor_scalar(
                out=em1[:, sl8], in0=gpt[:, sl8], scalar1=srt[:, 8 * i:8 * i + 1],
                scalar2=None, op0=ALU.is_equal))
            P.step("v", lambda e, sl8=sl8, i=i: e.tensor_scalar(
                out=em2[:, sl8], in0=gpt[:, sl8], scalar1=srt[:, 8 * i + 1:8 * i + 2],
                scalar2=None, op0=ALU.is_equal))
            P.step("v", lambda e, i=i: e.tensor_tensor(
                out=sm2p["mx_ws"][:, i:i + 1], in0=srt[:, 8 * i:8 * i + 1],
                in1=srt[:, 8 * i + 1:8 * i + 2], op=ALU.add))
            P.step("v", lambda e, i=i: e.tensor_scalar(
                out=sm2p["mx_ws"][:, i:i + 1], in0=sm2p["mx_ws"][:, i:i + 1],
                scalar1=1e-6, scalar2=None, op0=ALU.add))
            P.step("v", lambda e, i=i: e.reciprocal(
                out=sm2p["mx_ws"][:, i:i + 1], in_=sm2p["mx_ws"][:, i:i + 1]))
            P.step("v", lambda e, i=i: e.tensor_tensor(
                out=sm2p["mx_w1n"][:, i:i + 1], in0=srt[:, 8 * i:8 * i + 1],
                in1=sm2p["mx_ws"][:, i:i + 1], op=ALU.mult))
            P.step("v", lambda e, i=i: e.tensor_tensor(
                out=sm2p["mx_w2n"][:, i:i + 1], in0=srt[:, 8 * i + 1:8 * i + 2],
                in1=sm2p["mx_ws"][:, i:i + 1], op=ALU.mult))
            P.step("v", lambda e, sl8=sl8: e.tensor_tensor(
                out=emtb[:, sl8], in0=em1[:, sl8], in1=em2[:, sl8], op=ALU.add))
            P.step("t", lambda e, sl8=sl8: mm(
                e, bank(1)[:, sl8], UTB, emtb[:, sl8], True, True))
            P.step("v", lambda e, sl8=sl8: e.tensor_copy(
                out=SM[:, 256:384][:, sl8], in_=bank(1)[:, sl8]))
            P.step("t", lambda e, sl8=sl8: mm(
                e, bank(2)[:, sl8], ONESB, emtb[:, sl8], True, True))
            P.step("v", lambda e, sl8=sl8, i=i: e.tensor_copy(
                out=SM[0:1, 640 + 8 * i:640 + 8 * i + 8], in_=bank(2)[0:1, sl8]))
        totE = SM[0:1, 768:768 + 256].rearrange("p (g u) -> p g u", g=NEXP)
        P.step("v", lambda e: e.memset(SM[0:1, 768:1024], 0.0))
        P.step("v", lambda e: e.tensor_copy(
            out=totE[:, :, 0:NT],
            in_=SM[0:1, 640:640 + 8 * NT].rearrange("p (i g) -> p g i", g=NEXP)))
        for sh in (1, 2, 4, 8):
            if sh >= NT:
                continue
            P.step("v", lambda e, sh=sh: e.tensor_tensor(
                out=totE[:, :, sh:NT], in0=totE[:, :, sh:NT],
                in1=totE[:, :, 0:NT - sh], op=ALU.add))
        P.step("v", lambda e: e.tensor_copy(
            out=SM[0:1, 1041:1049],
            in_=totE[:, :, NT - 1:NT].rearrange("p g u -> p (g u)")))
        P.step("y", lambda e: P.inc("D0", e.dma_start(
            out=cc_in_d.ap(), in_=SM[0:1, 1041:1049]), 16) and None)
        P.step("g", lambda e: P.inc("CC", e.collective_compute(
            "AllGather", ALU.bypass, ins=[cc_in_d.ap()], outs=[cc_out_d.ap()],
            replica_groups=[list(range(NCORES))]), 1) and None)
        P.step("y", lambda e: P.inc("D0", e.dma_start(
            out=SM[0:NCORES, 1024:1032], in_=cc_out_d.ap()), 16) and None)
        P.step("y", lambda e: P.inc("D0", e.dma_start(
            out=SM[0:NEXP, 1040:1041], in_=dr["prefmask"].ap()), 16) and None)
        P.step("t", lambda e: mm(e, bank(3)[0:1, 0:8], SM[0:NEXP, 1040:1041],
                                 SM[0:NCORES, 1024:1032], True, True))
        P.step("v", lambda e: e.tensor_copy(out=SM[0:1, 1032:1040],
                                            in_=bank(3)[0:1, 0:8]))
        for i in range(NT):
            sl8 = slice(8 * i, 8 * i + 8)
            if i == 0:
                P.step("v", lambda e: e.tensor_copy(
                    out=SM[0:1, 1048:1056], in_=SM[0:1, 1032:1040]))
            else:
                P.step("v", lambda e, i=i: e.tensor_tensor(
                    out=SM[0:1, 1048:1056], in0=SM[0:1, 1032:1040],
                    in1=totE[:, :, i - 1:i].rearrange("p g u -> p (g u)"),
                    op=ALU.add))
            P.step("v", lambda e: e.tensor_scalar(
                out=SM[0:1, 1048:1056], in0=SM[0:1, 1048:1056],
                scalar1=-1.0, scalar2=CAP, op0=ALU.mult, op1=ALU.add))
            P.step("t", lambda e: mm(
                e, bank(3)[:, 0:8], ONESF[0:1, :], SM[0:1, 1048:1056],
                True, True))
            P.step("v", lambda e, sl8=sl8: e.tensor_tensor(
                out=vm[:, sl8], in0=SM[:, 256:384][:, sl8], in1=bank(3)[:, 0:8],
                op=ALU.is_lt))
            P.step("v", lambda e, sl8=sl8: e.tensor_tensor(
                out=mfin[:, sl8], in0=em1[:, sl8], in1=vm[:, sl8], op=ALU.mult))
            P.step("v", lambda e, sl8=sl8, i=i: e.tensor_scalar(
                out=mfin[:, sl8], in0=mfin[:, sl8],
                scalar1=sm2p["mx_w1n"][:, i:i + 1], scalar2=None, op0=ALU.mult))
            P.step("v", lambda e, sl8=sl8: e.tensor_tensor(
                out=vm[:, sl8], in0=em2[:, sl8], in1=vm[:, sl8], op=ALU.mult))
            P.step("v", lambda e, sl8=sl8, i=i: e.tensor_scalar(
                out=vm[:, sl8], in0=vm[:, sl8],
                scalar1=sm2p["mx_w2n"][:, i:i + 1], scalar2=None, op0=ALU.mult))
            P.step("v", lambda e, sl8=sl8: e.tensor_tensor(
                out=mfin[:, sl8], in0=mfin[:, sl8], in1=vm[:, sl8], op=ALU.add))

        w1e3 = W1.rearrange("p (k m) -> p k m", k=DC)
        w2e3 = W2.rearrange("p (k m) -> p k m", k=16)
        hb3 = C2[:, 0:8192].rearrange("p (k t) -> p k t", k=16)
        mfb = SM[:, 1408:1920]
        P.step("g", lambda e: P.inc("DG", e.dma_start(
            out=w1e3, in_=dr["moe_w1"].ap()[0].rearrange(
                "(a p) c -> p a c", p=128)), 16) and None)
        P.step("g", lambda e: P.inc("DG2", e.dma_start(
            out=w2e3, in_=dr["moe_w2"].ap()[0].rearrange(
                "(a p) c -> p a c", p=128)), 16) and None)
        dg_w1 = P.val("DG")
        dg_w2 = P.val("DG2")
        bank_user_v = [P.val("V")] * 8
        gelu_mark = [P.val("S")] * 16
        tm_mm2_prev = P.val("T")
        for ex in range(NEXP):
            tm_mm1_last = None
            for nq in range(NQ):
                bcast_tm_cols(SM[:, 1280 + ex:1280 + ex + 8 * NT:8][:, nq * 4:nq * 4 + 4],
                              4, SM[:, 512:1024], bank(4))
                P.step("v", lambda e: e.tensor_copy(out=mfb, in_=bank(4)))
                vm_mfb = P.val("V")
                bank_user_v[4] = vm_mfb
                for mc in range(16):
                    bk = mc % 3

                    def _m1(e, mc=mc, nq=nq, bk=bk):
                        last = None
                        for kc in range(DC):
                            last = mm(e, bank(bk), w1e3[:, kc, mc * 128:(mc + 1) * 128],
                                      ln3[:, kc, ts(nq)], kc == 0, kc == DC - 1)
                        return last
                    P.step("t", _m1, waits=[("DG", dg_w1),
                                            ("S", gelu_mark[mc - 3] if mc >= 3 else gelu_mark[13 + mc]),
                                            ("V", bank_user_v[bk])])
                    tmark = P.val("T")
                    P.step("s", lambda e, mc=mc, bk=bk: e.activation(
                        out=hb3[:, mc, :], in_=bank(bk), func=AF.Gelu),
                        waits=[("T", tmark), ("T", tm_mm2_prev)])
                    gelu_mark[mc] = P.val("S")
                tm_mm1_last = P.val("T")
                if nq == NQ - 1 and ex < NEXP - 1:
                    P.step("g", lambda e, ex=ex: P.inc("DG", e.dma_start(
                        out=w1e3, in_=dr["moe_w1"].ap()[ex + 1].rearrange(
                            "(a p) c -> p a c", p=128)), 16) and None,
                        waits=[("T", tm_mm1_last)])
                    dg_w1 = P.val("DG")
                for fm in range(DC):
                    def _m2(e, fm=fm, nq=nq):
                        last = None
                        for kc in range(16):
                            last = mm(e, bank(fm), w2e3[:, kc, fm * 128:(fm + 1) * 128],
                                      hb3[:, kc, :], kc == 0, kc == 15)
                        return last
                    P.step("t", _m2, waits=[("DG2", dg_w2), ("S", gelu_mark[15]),
                                            ("V", bank_user_v[fm])])
                    tm2 = P.val("T")
                    P.step("v", lambda e, fm=fm: e.tensor_tensor(
                        out=bank(fm), in0=bank(fm), in1=mfb, op=ALU.mult),
                        waits=[("T", tm2), ("V", vm_mfb)])
                    P.step("v", lambda e, fm=fm, nq=nq: e.tensor_tensor(
                        out=s13[:, fm, ts(nq)], in0=s13[:, fm, ts(nq)],
                        in1=bank(fm), op=ALU.add), waits=[("V", P.val("V"))])
                    bank_user_v[fm] = P.val("V")
                tm_mm2_prev = P.val("T")
                if nq == NQ - 1 and ex < NEXP - 1:
                    P.step("g", lambda e, ex=ex: P.inc("DG2", e.dma_start(
                        out=w2e3, in_=dr["moe_w2"].ap()[ex + 1].rearrange(
                            "(a p) c -> p a c", p=128)), 16) and None,
                        waits=[("T", tm_mm2_prev)])
                    dg_w2 = P.val("DG2")

        for nq in range(NQ):
            for fc in range(DC):
                P.step("v", lambda e, fc=fc, nq=nq: e.tensor_copy(
                    out=C1[:, fc * 512:(fc + 1) * 512], in_=s13[:, fc, ts(nq)]))
            for fc in range(DC):
                P.step("v", lambda e, fc=fc: e.tensor_tensor(
                    out=C2[:, fc * 512:(fc + 1) * 512], in0=C1[:, fc * 512:(fc + 1) * 512],
                    in1=C1[:, fc * 512:(fc + 1) * 512], op=ALU.mult))

            def _mmv2(e, nq=nq):
                last = None
                for fc in range(DC):
                    last = mm(e, bank(0), ONESB, C2[:, fc * 512:(fc + 1) * 512],
                              fc == 0, fc == DC - 1)
                return last
            P.step("t", _mmv2)
            P.step("v", lambda e: e.tensor_scalar(
                out=rstdb, in0=bank(0), scalar1=1.0 / D, scalar2=1e-6,
                op0=ALU.mult, op1=ALU.add))
            P.step("v", lambda e: e.reciprocal(out=rstdb, in_=rstdb))
            P.step("s", lambda e: e.activation(out=rstdb, in_=rstdb, func=AF.Sqrt))
            for fc in range(DC):
                P.step("y", lambda e, fc=fc, nq=nq: P.inc(
                    "D0", e.dma_start(out=xch, in_=x_r[:, fc, ts(nq)]), 16) and None)
                P.step("v", lambda e, fc=fc, nq=nq: e.scalar_tensor_tensor(
                    out=scrS, in0=s13[:, fc, ts(nq)],
                    scalar=sm2p["hn2w"][:, fc:fc + 1], in1=rstdb,
                    op0=ALU.mult, op1=ALU.mult))
                P.step("v", lambda e, fc=fc, nq=nq: e.tensor_tensor(
                    out=xch, in0=xch, in1=scrS, op=ALU.add))
                P.step("y", lambda e, fc=fc, nq=nq: P.inc(
                    "D0", e.dma_start(out=xd_r[:, fc, ts(nq)], in_=xch), 16) and None)

    for nm, wdt, ncol in [("cw0", None, 32), ("cw1", None, 32), ("cw2", None, 32),
                          ("cb", None, 32)]:
        sm2col(nm, ncol)
    load_cols(dr["convw"].ap()[:, 0:32], sm2p["cw0"])
    load_cols(dr["convw"].ap()[:, 32:64], sm2p["cw1"])
    load_cols(dr["convw"].ap()[:, 64:96], sm2p["cw2"])
    load_cols(dr["convb"].ap(), sm2p["cb"])
    sm2col("hedge", 32 * 2 * max(NQ - 1, 1))
    sm2col("amax2_tm", S // 128)
    sm2col("sc2_tm", S // 128)
    sm2col("inv2_tm", S // 128)
    sm2col("qkvb", DC)
    load_cols(dr["qkv_b"].ap(), sm2p["qkvb"])
    sm2col("att0_sh", 1)
    sm2col("att0_t1", 1)
    sm2col("att0_t2", 1)
    sm2col("att0_tmp", 1)
    sm2col("att0_rc", 4)
    sm2col("att0_inv4y", S // 128)
    sm2col("ac_t1", 1); sm2col("ac_t2", 1); sm2col("ac_tmp", 1)
    sm2col("ac_rc", 4); sm2col("a1_sh", 1); sm2col("gq_sh", 1)
    sm2col("lnmean_tm", S // 128); sm2col("lnrstd_tm", S // 128)
    sm2col("g_qb", DC); load_cols(dr["g_qb"].ap(), sm2p["g_qb"])
    sm2col("g_kb", 2); load_cols(dr["g_kb"].ap(), sm2p["g_kb"])
    sm2col("g_ob", DC); load_cols(dr["g_ob"].ap(), sm2p["g_ob"])
    sm2col("n1w", DC); load_cols(dr["norm1_w"].ap(), sm2p["n1w"])
    sm2col("hn2w", DC); load_cols(dr["norm2_w"].ap(), sm2p["hn2w"])
    P.step("v", lambda e: e.tensor_scalar(
        out=sm2p["hn2w"], in0=sm2p["hn2w"], scalar1=0.5, scalar2=None,
        op0=ALU.mult))
    sm2col("gwf", 64)
    sm2col("mx_ws", S // 128)
    sm2col("mx_w1n", S // 128)
    sm2col("mx_w2n", S // 128)

    w1m = stash_bitlinear(dr["w1T"], w1q_d, D, HID0, "w1")
    w2m = stash_bitlinear(dr["w2T"], w2q_d, HID0, D, "w2")
    qm = stash_ternary(dr["wqkvT"], wqkvq_d, "wq", 7.0)
    om = stash_ternary(dr["woT"], woq_d, "wo", 7.0)

    marks = {}

    def mark(name):
        marks[name] = P.nsteps
    mark("setup")
    seq = [i for i in SEQ if i <= stop]
    for pos, sid in enumerate(seq):
        mark(f"pre{pos}_{sid}")
        src_t = dr["x_fm"] if pos == 0 else x_mir
        dst_t = y_out if pos == len(seq) - 1 else x_mir
        if sid in (1, 3):
            stage_ln(src_t, "q8")
            stage_ffn0(src_t, dst_t, w1m, w2m)
        elif sid == 2:
            stage_ln(src_t, "q4")
            tap("r4", C1[:, 0:16384])
            tap("inv4", SM[:, 0:S])
            stage_att0(src_t, dst_t, qm, om)
        elif sid == 5:
            stage_ln(src_t, "bf")
            stage_att1(src_t, dst_t)
        elif sid in (4, 6):
            stage_ln(src_t, "bf", save_stats=True)
            stage_mix1(src_t, dst_t)

    mark("end")
    if os.environ.get("KPROF"):
        prev = 0
        for k2, v2 in marks.items():
            print(f"  steps {k2}: +{v2 - prev} (cum {v2})")
            prev = v2
    P.step("y", lambda e: None)
    es.close()
    return nc


def _prep_inputs(inputs, S):
    f32 = np.float32
    g = {}
    g["w1T"] = np.ascontiguousarray(inputs["L0_ffn_w1"].T.astype(f32))
    g["w2T"] = np.ascontiguousarray(inputs["L0_ffn_w2"].T.astype(f32))
    def cols(v):
        v = np.asarray(v, f32).reshape(-1)
        return np.ascontiguousarray(v.reshape(-1, 128).T)
    cw = np.asarray(inputs["L0_ffn_conv_w"], f32).reshape(HID0, 3)
    g["convw"] = np.concatenate([cols(cw[:, k]) for k in range(3)], 1)
    g["convb"] = cols(inputs["L0_ffn_conv_b"])
    g["wqkvT"] = np.ascontiguousarray(np.asarray(inputs["L0_attn_Wqkv_w"], f32).T)
    g["woT"] = np.ascontiguousarray(np.asarray(inputs["L0_attn_Wo_w"], f32).T)
    g["qkv_b"] = cols(inputs["L0_attn_Wqkv_b"])
    U = np.asarray(inputs["L0_attn_U"], f32)
    u8 = np.zeros((128, 256), f32)
    for j in range(4):
        u8[32 * j:32 * j + 32, 64 * j:64 * j + 32] = U[0:32, :]
        u8[32 * j:32 * j + 32, 64 * j + 32:64 * j + 64] = U[32:64, :]
    g["u8"] = u8
    half = 32
    inv = 1.0 / 10000.0 ** (np.arange(half, dtype=f32) / half)
    pos_q = np.arange(S, dtype=f32)
    cb = np.arange(S) // (S // 16)
    tl = np.arange(S) % (S // 16)
    pos_k = (tl * 16 + cb).astype(f32)
    cs4 = np.empty((128, 4 * S), f32)
    cs4[:, 0 * S:1 * S] = np.tile(np.cos(pos_q[None, :] * inv[:, None]), (4, 1))
    cs4[:, 1 * S:2 * S] = np.tile(np.sin(pos_q[None, :] * inv[:, None]), (4, 1))
    cs4[:, 2 * S:3 * S] = np.tile(np.cos(pos_k[None, :] * inv[:, None]), (4, 1))
    cs4[:, 3 * S:4 * S] = np.tile(np.sin(pos_k[None, :] * inv[:, None]), (4, 1))
    g["cs4"] = cs4
    perm = np.empty(D, np.int64)
    for h in range(H16):
        for j in range(DK):
            perm[64 * h + j] = j * 16 + h
    g["a1_qw"] = np.ascontiguousarray(np.asarray(inputs["L1_attn_qw"], f32)[:, perm])
    g["a1_kw"] = np.ascontiguousarray(np.asarray(inputs["L1_attn_kw"], f32)[:, perm])
    g["a1_vw"] = np.ascontiguousarray(np.asarray(inputs["L1_attn_vw"], f32)[:, perm])
    g["a1_ow"] = np.ascontiguousarray(np.asarray(inputs["L1_attn_ow"], f32)[perm, :])
    def sig(h):
        return 8 * (h // 8) + 2 * (h % 4) + (h // 4) % 2
    qperm = np.empty(D, np.int64)
    for h in range(H16):
        for j in range(DK):
            qperm[64 * sig(h) + j] = 64 * h + j
    g["g_qw"] = np.ascontiguousarray(np.asarray(inputs["L1_ffn_qw"], f32)[:, qperm])
    g["g_qb"] = cols(np.asarray(inputs["L1_ffn_qb"], f32)[qperm])
    g["g_kw"] = np.asarray(inputs["L1_ffn_kw"], f32)
    g["g_kb"] = cols(inputs["L1_ffn_kb"])
    g["g_vw"] = np.asarray(inputs["L1_ffn_vw"], f32)
    g["g_vb"] = cols(inputs["L1_ffn_vb"])
    g["g_vbr"] = np.asarray(inputs["L1_ffn_vb"], f32).reshape(1, NKV * DK)
    g["g_ow"] = np.ascontiguousarray(np.asarray(inputs["L1_ffn_ow"], f32)[qperm, :])
    g["g_ob"] = cols(inputs["L1_ffn_ob"])
    g["gate_w"] = np.asarray(inputs["L1_ffn_gate_w"], f32)
    g["moe_w1"] = np.asarray(inputs["L1_ffn_w1"], f32)
    g["moe_w2"] = np.asarray(inputs["L1_ffn_w2"], f32)
    g["norm1_w"] = cols(inputs["L1_ffn_norm1_w"])
    g["norm2_w"] = cols(inputs["L1_ffn_norm2_w"])
    g["ident_f"] = np.eye(128, dtype=f32)
    g["ident_b"] = np.eye(128).astype(ml_dtypes.bfloat16)
    g["ones_b"] = np.ones((128, 128), ml_dtypes.bfloat16)
    g["ones_f"] = np.ones((128, 128), f32)
    g["ut_b"] = np.triu(np.ones((128, 128)), 1).astype(ml_dtypes.bfloat16)
    bd = np.zeros((128, 128), f32)
    bd[0:64, 0:64] = 1.0
    bd[64:128, 64:128] = 1.0
    g["bd64_b"] = bd.astype(ml_dtypes.bfloat16)
    eo4 = np.zeros((128, 128), f32)
    for j in range(32):
        eo4[2 * j, j] = 1.0; eo4[2 * j + 1, 32 + j] = 1.0
        eo4[64 + 2 * j, 64 + j] = 1.0; eo4[64 + 2 * j + 1, 96 + j] = 1.0
    g["eo_b"] = eo4.astype(ml_dtypes.bfloat16)
    trs = np.zeros((128, 128), f32)
    trs[0:64, 0:64] = np.eye(64)
    trs[64:128, 64:128] = np.eye(64)
    g["trsel_b"] = trs.astype(ml_dtypes.bfloat16)
    bd32 = np.zeros((128, 128), f32)
    for j in range(4):
        bd32[32 * j:32 * (j + 1), 32 * j:32 * (j + 1)] = 1.0
    g["bd32_b"] = bd32.astype(ml_dtypes.bfloat16)
    g["iotaE"] = np.broadcast_to(np.arange(NEXP, dtype=f32), (128, NEXP)).copy()
    cm = np.zeros((128, 2048), f32)
    for r in range(4):
        mi = np.arange(128)[:, None]
        qi = np.arange(512)[None, :]
        cm[:, r * 512:(r + 1) * 512] = (qi >= 128 * r + mi)
    g["cmask"] = cm.astype(ml_dtypes.bfloat16)
    g["onecol_b"] = np.ones((128, 1), ml_dtypes.bfloat16)

    x = np.asarray(inputs["L0_attn_x"], f32)
    per_core = []
    for c in range(NCORES):
        m = dict(g)
        m["x_fm"] = np.ascontiguousarray(x[c].T)
        pm = np.zeros((NEXP, 1), f32)
        pm[:c if c <= NEXP else NEXP] = 1.0
        m["prefmask"] = pm
        per_core.append(m)
    return per_core


def kernel(**inputs):
    S = np.asarray(inputs["L0_attn_x"]).shape[1]
    per_core = _prep_inputs(inputs, S)
    nc = build_program(S=S)
    res = run_bass_kernel_spmd(nc, per_core, core_ids=list(range(NCORES)))
    outs = [r["y_fm"] for r in res.results]
    return np.stack([o.T for o in outs], 0).astype(np.float32)
